# version 11
# baseline (speedup 1.0000x reference)
import numpy as np
from contextlib import ExitStack

import concourse.bass as bass
import concourse.mybir as mybir
from concourse.bass_utils import run_bass_kernel_spmd

F32 = mybir.dt.float32
BF16 = mybir.dt.bfloat16
AF = mybir.ActivationFunctionType
ALU = mybir.AluOpType
AX = mybir.AxisListType

MAGIC = 12582912.0
NEG_BIG = -1.0e30
T, NT, H, KC, I3, MI = 4096, 32, 1024, 8, 3072, 24
EPS = 1e-5
QMAXI = 1.0 / 32767.0
ENGS = ("sync", "vector", "scalar", "tensor", "gpsimd")

PARITY = {
    "d_x", "d_h1w", "d_vw", "d_cw", "d_xr", "d_x2w", "d_tw", "d_d", "d_dw",
    "d_tt", "d_wdt", "d_f1w", "d_f2r", "d_f2w", "d_f3r", "d_f3u", "d_f3w",
    "d_u4", "d_x2r", "d_out", "d_da", "d_daw", "d_ds",
}
NOMERGE = {"s_ev2"}


class Prog:
    def __init__(self):
        self.cnt = {}
        self.tpos = 0
        self.steps = {E: [] for E in ENGS}
        self.stream_eng = {}

    def _reg(self, n):
        if n not in self.cnt:
            self.cnt[n] = 0

    def w(self, n, k=None):
        self._reg(n)
        c = self.cnt[n] if k is None else k
        return [(n, c)]

    def i(self, n):
        self._reg(n)
        self.cnt[n] += 1
        return (n, 1)

    def dcount(self, n):
        if n in PARITY:
            self._reg(n + "_A")
            self._reg(n + "_B")
            return (self.cnt[n + "_A"] + self.cnt[n + "_B"]) // 16
        self._reg(n)
        return self.cnt[n] // 16

    def dinc(self, n):
        tot = self.dcount(n)
        if n in PARITY:
            pn = n + ("_A" if tot % 2 == 0 else "_B")
            self.cnt[pn] += 16
            return (pn, 16)
        self.cnt[n] += 16
        return (n, 16)

    def wd(self, n, k=None):
        tot = self.dcount(n)
        if k is None:
            k = tot
        k = max(0, k)
        if k == 0:
            return []
        if n in PARITY:
            ka = (k + 1) // 2
            kb = k // 2
            out = [(n + "_A", 16 * ka)]
            if kb > 0:
                out.append((n + "_B", 16 * kb))
            return out
        return [(n, 16 * k)]

    def st(self, eng, emit, waits=(), incs=()):
        waits = tuple(waits)
        if eng == "tensor":
            ncomp = sum(v for (n, v) in incs if not n.startswith("d_"))
            if ncomp:
                if self.tpos > 0:
                    waits = waits + (("__tpos__", self.tpos),)
                self.tpos += ncomp
        self.steps[eng].append((waits, emit, tuple(incs)))


def att_chunks(i):
    out, j0 = [], 0
    while j0 <= i:
        nj = min(8, i + 1 - j0)
        out.append((j0, nj))
        j0 += nj
    return out


def build_nc(taps=()):
    nc = bass.Bass("TRN2", target_bir_lowering=False)

    x_in = nc.dram_tensor("x", [T, H], F32, kind="ExternalInput")
    ccol_in = nc.dram_tensor("ccol", [128, 8], F32, kind="ExternalInput")
    wmod_in = nc.dram_tensor("wmod", [H, 6 * H], F32, kind="ExternalInput")
    wq_in = nc.dram_tensor("wq", [H, H], F32, kind="ExternalInput")
    wk_in = nc.dram_tensor("wk", [H, H], F32, kind="ExternalInput")
    wv_in = nc.dram_tensor("wv", [H, H], F32, kind="ExternalInput")
    wo_in = nc.dram_tensor("wo", [H, H], F32, kind="ExternalInput")
    bq_in = nc.dram_tensor("bq", [1, H], F32, kind="ExternalInput")
    bk_in = nc.dram_tensor("bk", [1, H], F32, kind="ExternalInput")
    bv_in = nc.dram_tensor("bv", [1, H], F32, kind="ExternalInput")
    bo_in = nc.dram_tensor("bo", [1, H], F32, kind="ExternalInput")
    wg_in = nc.dram_tensor("wg", [I3, H], F32, kind="ExternalInput")
    wu_in = nc.dram_tensor("wu", [I3, H], F32, kind="ExternalInput")
    wd_in = nc.dram_tensor("wd", [H, I3], F32, kind="ExternalInput")
    out_hbm = nc.dram_tensor("out", [T, H], F32, kind="ExternalOutput")

    h1_hbm = nc.dram_tensor("h1s", [T, H], BF16)
    v_hbm = nc.dram_tensor("vs", [T, H], BF16)
    ctx_hbm = nc.dram_tensor("ctxs", [T, H], BF16)
    x2_hbm = nc.dram_tensor("x2s", [T, H], F32)
    t_hbm = nc.dram_tensor("ts_", [T, H], BF16)
    wo2_hbm = nc.dram_tensor("wo2s", [128, 8 * H], BF16)
    modraw = nc.dram_tensor("modraw", [1, 6 * H], F32)
    modh = nc.dram_tensor("modh", [12, H], F32)
    wgq_hbm = nc.dram_tensor("wgq", [I3, H], BF16)
    wuq_hbm = nc.dram_tensor("wuq", [I3, H], BF16)
    wdq_hbm = nc.dram_tensor("wdq", [H, I3], BF16)
    graw_hbm = nc.dram_tensor("graw", [I3, T], BF16)
    uraw_hbm = nc.dram_tensor("uraw", [I3, T], BF16)
    prod_hbm = nc.dram_tensor("prod", [I3, T], BF16)
    usc_hbm = nc.dram_tensor("usc", [I3, T], BF16)
    cci = [nc.dram_tensor(f"cci{j}", [8, 1], F32) for j in range(4)]
    cco = [nc.dram_tensor(f"cco{j}", [8, 1], F32) for j in range(4)]

    tap_srcs = {
        "h1": h1_hbm, "v": v_hbm, "ctx": ctx_hbm, "x2": x2_hbm,
        "modraw": modraw, "modh": modh, "wgq": wgq_hbm, "wuq": wuq_hbm,
        "wdq": wdq_hbm, "graw": graw_hbm, "uraw": uraw_hbm, "prod": prod_hbm,
        "usc": usc_hbm, "wo2": wo2_hbm,
    }
    tap_outs = {
        name: nc.dram_tensor("tap_" + name, list(tap_srcs[name].shape),
                             tap_srcs[name].dtype, kind="ExternalOutput")
        for name in taps
    }

    st_ = ExitStack()
    e_ = st_.enter_context

    R1 = e_(nc.sbuf_tensor("R1", [128, 32768], BF16))
    R2 = e_(nc.sbuf_tensor("R2", [128, 32768], BF16))
    R3 = e_(nc.sbuf_tensor("R3", [128, 32768], BF16))
    SG = e_(nc.sbuf_tensor("SGb", [128, 3072], BF16))
    FR = e_(nc.sbuf_tensor("FRf", [128, 2048], F32))
    idn = e_(nc.sbuf_tensor("idn", [128, 128], BF16))
    maskd = e_(nc.sbuf_tensor("maskd", [128, 128], BF16))
    onesrow = e_(nc.sbuf_tensor("onesrow", [1, 128], BF16))
    ccol = e_(nc.sbuf_tensor("ccolS", [128, 8], F32))
    cbf = e_(nc.sbuf_tensor("cbfS", [128, 8], BF16))
    ca1c = e_(nc.sbuf_tensor("ca1c", [128, 8], F32))
    sacbf = e_(nc.sbuf_tensor("sacbf", [128, 8], BF16))
    cb1c = e_(nc.sbuf_tensor("cb1c", [128, 8], F32))
    sbc = e_(nc.sbuf_tensor("sbc", [128, 8], F32))
    bqc = e_(nc.sbuf_tensor("bqc", [128, 8], F32))
    bkc = e_(nc.sbuf_tensor("bkc", [128, 8], F32))
    mxs = e_(nc.sbuf_tensor("mxs", [128, 8], F32))
    stats = e_(nc.sbuf_tensor("stats", [128, 16], F32))
    scl = e_(nc.sbuf_tensor("scl", [128, 24], F32))
    mrg = e_(nc.sbuf_tensor("mrg", [128, 8], F32))
    bnb = e_(nc.sbuf_tensor("bnb", [128, 24], F32))
    deq = e_(nc.sbuf_tensor("deq", [128, 128], F32))
    cst = e_(nc.sbuf_tensor("cst", [128, 6], F32))
    nc.const_aps.register(F32, EPS, cst[:, 0:1])
    nc.const_aps.register(F32, MAGIC, cst[:, 1:2])
    nc.const_aps.register(F32, 65536.0, cst[:, 2:3])
    nc.const_aps.register(F32, -MAGIC, cst[:, 3:4])
    nc.const_aps.register(F32, -1.0, cst[:, 4:5])

    PS0 = e_(nc.psum_tensor("PS0", [128, 1024], F32))
    PS1 = e_(nc.psum_tensor("PS1", [128, 1024], F32))
    PS2 = e_(nc.psum_tensor("PS2", [128, 1024], F32))
    PS3 = e_(nc.psum_tensor("PS3", [128, 2048], BF16))

    P = Prog()

    def r1(kc, a, b):
        return R1[:, kc * 4096 + a: kc * 4096 + b]

    def r2(kc, a, b):
        return R2[:, kc * 4096 + a: kc * 4096 + b]

    def r3(off, n):
        return R3[:, off: off + n]

    def fr(row, a, b):
        return FR[row:row + 1, a:b]

    def dma(eng, stream, out, in_, waits=(), slow=False):
        def em(e, out=out, in_=in_, slow=slow):
            if slow:
                return e.dma_start(out=out, in_=in_, allow_slow_non_contiguous=True)
            return e.dma_start(out=out, in_=in_)
        P.stream_eng[stream] = eng
        P.st(eng, em, waits, [P.dinc(stream)])

    def dmaT(eng, stream, out, in_, waits=()):
        P.stream_eng[stream] = eng
        P.st(eng, lambda e, o=out, i=in_: e.dma_start_transpose(o, i), waits,
             [P.dinc(stream)])

    nbar = [0]

    def barrier():
        nbar[0] += 1
        for E in ENGS:
            flush = []
            for sname, seng in P.stream_eng.items():
                if seng == E:
                    flush.extend(P.wd(sname))
            P.st(E, ("BARRIER", nbar[0], tuple(flush)))

    def V(emit, waits=(), incs=()):
        P.st("vector", emit, waits, incs)

    def SC(emit, waits=(), incs=()):
        P.st("scalar", emit, waits, incs)

    def TE(emit, waits=(), incs=()):
        P.st("tensor", emit, waits, incs)

    def GP(emit, waits=(), incs=()):
        P.st("gpsimd", emit, waits, incs)

    GP(lambda e: e.memset(idn[:, :], 0.0), [], [P.i("s_cst")])
    GP(lambda e: e.affine_select(out=idn[:, :], in_=idn[:, :],
                                 compare_op=ALU.not_equal, fill=1.0, base=0,
                                 pattern=[[-1, 128]], channel_multiplier=1),
       P.w("s_cst", 1), [P.i("s_cst")])
    GP(lambda e: e.memset(maskd[:, :], 0.0), [], [P.i("s_cst")])
    GP(lambda e: e.affine_select(out=maskd[:, :], in_=maskd[:, :],
                                 compare_op=ALU.is_ge, fill=NEG_BIG, base=0,
                                 pattern=[[-1, 128]], channel_multiplier=1),
       P.w("s_cst", 3), [P.i("s_cst")])
    GP(lambda e: e.memset(onesrow[:, :], 1.0), [], [P.i("s_cst")])
    GP(lambda e: e.memset(scl[:, 16:24], 0.0), [], [P.i("s_cst")])
    GP(lambda e: e.memset(mrg[:, 0:8], 0.0), [], [P.i("s_cst")])
    GP(lambda e: e.memset(cst[:, 0:1], EPS), [], [P.i("s_cst")])
    GP(lambda e: e.memset(cst[:, 1:2], MAGIC), [], [P.i("s_cst")])
    GP(lambda e: e.memset(cst[:, 2:3], 65536.0), [], [P.i("s_cst")])
    GP(lambda e: e.memset(cst[:, 3:4], -MAGIC), [], [P.i("s_cst")])
    GP(lambda e: e.memset(cst[:, 4:5], -1.0), [], [P.i("s_cst")])
    dma("sync", "d_p0", ccol[:, :], ccol_in[:, :])
    SC(lambda e: e.activation(cbf[:, :], ccol[:, :], AF.Silu),
       P.wd("d_p0", 1), [P.i("s_silu")])
    for kc in range(KC):
        dma("gpsimd", "d_wm", r3(kc * 2048, 2048),
            wmod_in[kc * 128:(kc + 1) * 128, 0:2048])

    for r in range(3):
        def mm0(e, r=r):
            last = None
            for w in range(4):
                ps = PS0 if w < 2 else PS1
                wo_ = (w % 2) * 512
                for kc in range(KC):
                    last = e.matmul(
                        ps[0:1, wo_:wo_ + 512], cbf[:, kc:kc + 1],
                        r3((r % 2) * 16384 + kc * 2048 + (w // 2) * 1024 + wo_, 512),
                        start=(kc == 0), stop=(kc == KC - 1))
            return last
        TE(mm0, P.wd("d_wm", 8 * (r + 1)) + P.w("s_silu", 1) + P.w("s_ev0", r),
           [P.i("s_mm0")])

    for t in range(NT):
        p = t % 2
        dma("sync", "d_x", FR[:, p * 1024:(p + 1) * 1024],
            x_in[t * 128:(t + 1) * 128, :],
            waits=(P.w("s_ln", t - 1) if t >= 2 else []))

        def bnst(e, p=p):
            e.bn_stats(bnb[:, p * 12:p * 12 + 6], FR[:, p * 1024:p * 1024 + 512])
            e.bn_stats(bnb[:, p * 12 + 6:p * 12 + 12],
                       FR[:, p * 1024 + 512:(p + 1) * 1024])
            return e.bn_aggr(stats[:, p * 2:p * 2 + 2],
                             bnb[:, p * 12:p * 12 + 12])
        V(bnst, P.wd("d_x", t + 1), [P.i("s_st")])
        SC(lambda e, p=p: e.activation(stats[:, 4 + p:5 + p],
                                       stats[:, p * 2 + 1:p * 2 + 2],
                                       AF.Sqrt, bias=EPS, scale=1.0),
           P.w("s_st", t + 1) + P.w("s_cst", 10), [P.i("s_sq")])

        def lnt(e, p=p):
            e.reciprocal(stats[:, 6 + p:7 + p], stats[:, 4 + p:5 + p])
            return e.tensor_scalar(
                SG[:, p * 1024:(p + 1) * 1024], FR[:, p * 1024:(p + 1) * 1024],
                stats[:, p * 2:p * 2 + 1], stats[:, 6 + p:7 + p],
                ALU.subtract, ALU.mult)
        V(lnt, P.w("s_sq", t + 1) + (P.wd("d_h1w", t - 1) if t >= 2 else []),
          [P.i("s_ln")])
        dma("gpsimd", "d_h1w", h1_hbm[t * 128:(t + 1) * 128, :],
            SG[:, p * 1024:(p + 1) * 1024], waits=P.w("s_ln", t + 1))

    for r in (1, 2):
        for kc in range(KC):
            dma("gpsimd", "d_wm", r3((r % 2) * 16384 + kc * 2048, 2048),
                wmod_in[kc * 128:(kc + 1) * 128, r * 2048:(r + 1) * 2048],
                waits=P.w("s_mm0", r))

    for r in range(3):
        V(lambda e: (e.tensor_copy(fr(0, 0, 1024), PS0[0:1, :]),
                     e.tensor_copy(fr(0, 1024, 2048), PS1[0:1, :]))[-1],
          P.w("s_mm0", r + 1) + P.w("s_ln", NT) + P.wd("d_mr", r),
          [P.i("s_ev0")])
        dma("scalar", "d_mr", modraw[0:1, r * 2048:(r + 1) * 2048],
            FR[0:1, 0:2048], waits=P.w("s_ev0", r + 1) + P.wd("d_mr", r))

    AS, BS = (0, 1024), (1024, 2048)

    def _row_sign_max(e, a, b):
        g = fr(0, *a)
        sgn = fr(0, *b)
        e.tensor_single_scalar(sgn, g, 0.0, ALU.is_ge)
        e.tensor_scalar(sgn, sgn, 2.0, -1.0, ALU.mult, ALU.add)
        e.scalar_tensor_tensor(g, g, -1.0, g, ALU.mult, ALU.max)
        e.tensor_scalar_max(g, g, 0.25)
        return e.tensor_tensor(g, g, sgn, op=ALU.mult)

    dma("sync", "d_rr", fr(0, *AS), modraw[0:1, 2 * H:3 * H], waits=P.wd("d_mr", 3))
    V(lambda e: _row_sign_max(e, AS, BS), P.wd("d_rr", 1), [P.i("s_row")])
    dma("scalar", "d_mw", modh[4:5, :], fr(0, *AS), waits=P.w("s_row", 1))
    dma("sync", "d_rr", fr(0, *AS), modraw[0:1, 5 * H:6 * H],
        waits=P.w("s_row", 1) + P.wd("d_mw", 1))
    V(lambda e: _row_sign_max(e, AS, BS), P.wd("d_rr", 2), [P.i("s_row")])
    dma("scalar", "d_mw", modh[5:6, :], fr(0, *AS), waits=P.w("s_row", 2))
    dma("sync", "d_rr", fr(0, *AS), modraw[0:1, H:2 * H],
        waits=P.w("s_row", 2) + P.wd("d_mw", 2))
    V(lambda e: e.tensor_scalar_add(fr(0, *AS), fr(0, *AS), 1.0),
      P.wd("d_rr", 3), [P.i("s_row")])
    dma("scalar", "d_mw", modh[0:1, :], fr(0, *AS), waits=P.w("s_row", 3))
    dma("sync", "d_rr", fr(0, *AS), modraw[0:1, 4 * H:5 * H],
        waits=P.w("s_row", 3) + P.wd("d_mw", 3))
    V(lambda e: e.tensor_scalar_add(fr(0, *AS), fr(0, *AS), 1.0),
      P.wd("d_rr", 4), [P.i("s_row")])
    dma("scalar", "d_mw", modh[2:3, :], fr(0, *AS), waits=P.w("s_row", 4))
    dma("gpsimd", "d_md", modh[1:2, :], modraw[0:1, 0:H], waits=P.wd("d_mr", 3))
    dma("gpsimd", "d_md", modh[3:4, :], modraw[0:1, 3 * H:4 * H],
        waits=P.wd("d_mr", 3))
    dma("sync", "d_rr", fr(0, *AS), bo_in[0:1, :],
        waits=P.w("s_row", 4) + P.wd("d_mw", 4))
    dma("sync", "d_rr", fr(0, *BS), modh[4:5, :],
        waits=P.w("s_row", 4) + P.wd("d_mw", 4))
    V(lambda e: e.tensor_tensor(fr(0, *AS), fr(0, *AS), fr(0, *BS), op=ALU.mult),
      P.wd("d_rr", 6), [P.i("s_row")])
    dma("scalar", "d_mw", modh[9:10, :], fr(0, *AS), waits=P.w("s_row", 5))
    for dst, row in ((ca1c, 0), (cb1c, 2), (sbc, 3)):
        dma("sync", "d_cols", dst[:, :],
            bass.AP(modh, row * H, [[1, 128], [128, 8]]),
            waits=P.wd("d_mw", 5) + P.wd("d_md", 2), slow=True)
    dma("gpsimd", "d_sac", sacbf[:, :],
        bass.AP(modh, 1 * H, [[1, 128], [128, 8]]),
        waits=P.wd("d_mw", 5) + P.wd("d_md", 2), slow=True)

    barrier()

    WSL = {"q": 8192, "k": 16384, "v": 24576}

    DUNITS = []
    for wsrc, wdst, npt, wcols in ((wg_in, wgq_hbm, 24, 1024),
                                   (wu_in, wuq_hbm, 24, 1024),
                                   (wd_in, wdq_hbm, 8, 3072)):
        for pt in range(npt):
            for hh in range(wcols // 512):
                DUNITS.append((wsrc, wdst, pt, hh))
    ND = len(DUNITS)
    nd_ld, nd_cp, npt_ = [0], [0], [0]

    def emit_dload(u):
        wsrc, wdst, pt, hh = DUNITS[u]
        fro = (u % 4) * 512
        dma("gpsimd", "d_d", FR[:, fro:fro + 512],
            wsrc[pt * 128:(pt + 1) * 128, hh * 512:(hh + 1) * 512],
            waits=(P.wd("d_d", u - 1) if u >= 2 else
                   P.w("s_w2", 5) + P.wd("d_bw", 3)) +
            (P.wd("d_dw", u - 3) if u >= 4 else []))

    def emit_dcomp(u):
        wsrc, wdst, pt, hh = DUNITS[u]
        fro = (u % 4) * 512
        c0 = (u % 4) * 32

        def dv(e, fro=fro, c0=c0):
            e.tensor_reduce(deq[:, c0:c0 + 16],
                            FR[:, fro:fro + 512].rearrange("q (b j) -> q b j", j=32),
                            axis=AX.X, op=ALU.max, apply_absolute_value=True)
            e.tensor_scalar(deq[:, c0:c0 + 16], deq[:, c0:c0 + 16],
                            1.0 / 7.0, 1e-12, ALU.mult, ALU.max)
            return e.reciprocal(deq[:, c0 + 16:c0 + 32], deq[:, c0:c0 + 16])
        V(dv, P.wd("d_d", u + 1) + (P.w("s_dg", u - 3) if u >= 4 else []),
          [P.i("s_da")])
        GP(lambda e, fro=fro, c0=c0:
           e.tensor_tensor(FR[:, fro:fro + 512].rearrange("q (b j) -> q b j", j=32),
                           FR[:, fro:fro + 512].rearrange("q (b j) -> q b j", j=32),
                           deq[:, c0 + 16:c0 + 32].unsqueeze(2)
                           .broadcast_to((128, 16, 32)), op=ALU.mult),
           P.w("s_da", u + 1) + P.w("s_dg1", u), [P.i("s_dg1")])
        SC(lambda e, fro=fro: (
            e.activation(FR[:, fro:fro + 512], FR[:, fro:fro + 512], AF.Identity,
                         bias=MAGIC, scale=1.0),
            e.activation(FR[:, fro:fro + 512], FR[:, fro:fro + 512], AF.Identity,
                         bias=-MAGIC, scale=1.0))[-1],
           P.w("s_dg1", u + 1), [P.i("s_dr")])
        GP(lambda e, fro=fro, c0=c0:
           e.tensor_tensor(FR[:, fro:fro + 512].rearrange("q (b j) -> q b j", j=32),
                           FR[:, fro:fro + 512].rearrange("q (b j) -> q b j", j=32),
                           deq[:, c0:c0 + 16].unsqueeze(2)
                           .broadcast_to((128, 16, 32)), op=ALU.mult),
           P.w("s_dr", u + 1) + P.w("s_dg", u), [P.i("s_dg")])
        dma("gpsimd", "d_dw",
            wdst[pt * 128:(pt + 1) * 128, hh * 512:(hh + 1) * 512],
            FR[:, fro:fro + 512], waits=P.w("s_dg", u + 1) +
            (P.wd("d_dw", u - 1) if u >= 2 else []))

    def dpoint():
        p = npt_[0]
        npt_[0] += 1
        lt = min(ND, 2 + (3 * (p + 1)) // 4)
        ct = min(ND, 2 + (3 * (p - 1)) // 4) if p >= 2 else 0
        while nd_ld[0] < lt:
            emit_dload(nd_ld[0])
            nd_ld[0] += 1
        while nd_cp[0] < ct:
            emit_dcomp(nd_cp[0])
            nd_cp[0] += 1

    def dflush():
        while nd_ld[0] < ND:
            emit_dload(nd_ld[0])
            nd_ld[0] += 1
        while nd_cp[0] < ND:
            emit_dcomp(nd_cp[0])
            nd_cp[0] += 1

    for wn, wsrc in (("q", wq_in), ("k", wk_in), ("v", wv_in)):
        dma("gpsimd", "d_w", r3(WSL[wn], 8192).rearrange("p (k c) -> p k c", c=1024),
            wsrc.ap().rearrange("(k p) c -> p k c", p=128))

    for wi, (wn, b_in, mrow, divq) in enumerate(
            (("q", bq_in, 6, True), ("k", bk_in, 7, False), ("v", bv_in, 8, False))):
        def sawmm(e, wn=wn):
            last = None
            for kc in range(KC):
                e.matmul(PS2[0:1, 0:512], sacbf[:, kc:kc + 1],
                         r3(WSL[wn] + kc * 1024, 512),
                         start=(kc == 0), stop=(kc == KC - 1))
                last = e.matmul(PS2[0:1, 512:1024], sacbf[:, kc:kc + 1],
                                r3(WSL[wn] + kc * 1024 + 512, 512),
                                start=(kc == 0), stop=(kc == KC - 1))
            return last
        TE(sawmm, P.wd("d_w", 3) + P.wd("d_cols", 3) + P.wd("d_sac", 1) + P.w("s_w2", 2 * wi),
           [P.i("s_saw")])
        dma("sync", "d_br", fr(0, *AS), b_in[0:1, :],
            waits=P.w("s_row", 5) + P.w("s_w2", 2 * wi))

        def sawev(e, divq=divq):
            last = e.scalar_tensor_tensor(fr(0, *BS), PS2[0:1, :], 1.0, fr(0, *AS),
                                          ALU.mult, ALU.add)
            if divq:
                last = e.tensor_scalar_mul(fr(0, *BS), fr(0, *BS), 1.0 / 32.0)
            return last
        V(sawev, P.w("s_saw", wi + 1) + P.wd("d_br", wi + 1) + P.wd("d_bw", wi),
          [P.i("s_w2")])
        dma("scalar", "d_bw", modh[mrow:mrow + 1, :], fr(0, *BS),
            waits=P.w("s_w2", 2 * wi + 1))

        def wscale(e, wn=wn, divq=divq):
            last = None
            for kc in range(KC):
                if divq:
                    last = e.tensor_scalar(
                        r3(WSL[wn] + kc * 1024, 1024),
                        r3(WSL[wn] + kc * 1024, 1024),
                        ca1c[:, kc:kc + 1], 1.0 / 32.0, ALU.mult, ALU.mult)
                else:
                    last = e.tensor_single_scalar(
                        r3(WSL[wn] + kc * 1024, 1024),
                        r3(WSL[wn] + kc * 1024, 1024),
                        ca1c[:, kc:kc + 1], ALU.mult)
            return last
        V(wscale, P.w("s_saw", wi + 1), [P.i("s_w2")])

    dma("sync", "d_bc", bqc[:, :],
        bass.AP(modh, 6 * H, [[1, 128], [128, 8]]), waits=P.wd("d_bw", 3), slow=True)
    dma("sync", "d_bc", bkc[:, :],
        bass.AP(modh, 7 * H, [[1, 128], [128, 8]]), waits=P.wd("d_bw", 3), slow=True)
    dma("gpsimd", "d_bvb", SG[:, 2048:3072],
        modh[8:9, :].partition_broadcast(128).squeeze(1), waits=P.wd("d_bw", 3))

    nwin = [0]
    WINFO = []
    nsc, nvv = [0], [0]

    def _reuse_wait(wdx):
        if wdx < 4:
            return []
        k, o = WINFO[wdx - 4]
        return P.w("s_ev2s" if k == "s" else "s_ev2v", o)

    for tq in range(4):
        for kc in range(KC):
            dmaT("sync", "d_h1t", r3(kc * 1024, 1024),
                 h1_hbm[tq * 1024:(tq + 1) * 1024, kc * 128:(kc + 1) * 128],
                 waits=P.wd("d_h1w", 8 * (tq + 1)) +
                 (P.w("s_saw", 3 + tq) if tq >= 1 else []))
        for wn, dstR in (("q", r1), ("k", r2)):
            bcol = bqc if wn == "q" else bkc
            for mc in range(KC):
                for wz in range(2):
                    wdx = nwin[0]
                    nwin[0] += 1
                    ps = (PS0, PS1)[(wdx // 2) % 2]
                    po = (wdx % 2) * 512
                    rw = _reuse_wait(wdx)
                    nsc[0] += 1
                    WINFO.append(("s", nsc[0]))

                    def qkm(e, wn=wn, mc=mc, wz=wz, ps=ps, po=po):
                        last = None
                        for kc in range(KC):
                            last = e.matmul(ps[:, po:po + 512],
                                            r3(WSL[wn] + kc * 1024 + mc * 128, 128),
                                            r3(kc * 1024 + wz * 512, 512),
                                            start=(kc == 0), stop=(kc == KC - 1))
                        return last
                    TE(qkm, P.wd("d_h1t", 8 * (tq + 1)) +
                       P.w("s_w2", 2 if wn == "q" else 4) + rw, [P.i("s_mm2")])
                    SC(lambda e, dstR=dstR, mc=mc, tq=tq, wz=wz, ps=ps, po=po, bcol=bcol:
                       e.activation(dstR(mc, tq * 1024 + wz * 512,
                                         tq * 1024 + wz * 512 + 512),
                                    ps[:, po:po + 512], AF.Identity,
                                    bias=bcol[:, mc:mc + 1], scale=1.0),
                       P.w("s_mm2", wdx + 1) + P.wd("d_bc", 2), [P.i("s_ev2s")])
                    dpoint()
        for tt in range(8):
            gt = tq * 8 + tt
            for wz in range(2):
                wdx = nwin[0]
                nwin[0] += 1
                ps = (PS0, PS1)[(wdx // 2) % 2]
                po = (wdx % 2) * 512
                rw = _reuse_wait(wdx)
                nvv[0] += 1
                WINFO.append(("v", nvv[0]))

                def vmm(e, tt=tt, wz=wz, ps=ps, po=po):
                    last = None
                    for kc in range(KC):
                        last = e.matmul(ps[:, po:po + 512],
                                        r3(kc * 1024 + tt * 128, 128),
                                        r3(WSL["v"] + kc * 1024 + wz * 512, 512),
                                        start=(kc == 0), stop=(kc == KC - 1))
                    return last
                TE(vmm, P.w("s_w2", 6) + rw, [P.i("s_mm2")])
                V(lambda e, gt=gt, wz=wz, ps=ps, po=po:
                  e.scalar_tensor_tensor(
                      SG[:, (gt % 2) * 1024 + wz * 512:(gt % 2) * 1024 + wz * 512 + 512],
                      ps[:, po:po + 512], 1.0,
                      SG[:, 2048 + wz * 512:2048 + (wz + 1) * 512],
                      ALU.mult, ALU.add),
                  P.w("s_mm2", wdx + 1) + P.wd("d_bvb", 1) +
                  (P.wd("d_vw", gt - 1) if gt >= 2 else []),
                  [P.i("s_ev2v")])
                dpoint()
            dma("gpsimd", "d_vw", v_hbm[gt * 128:(gt + 1) * 128, :],
                SG[:, (gt % 2) * 1024:(gt % 2) * 1024 + 1024],
                waits=P.w("s_ev2v", 16 * tq + 2 * tt + 2))
            dpoint()
        TE(lambda e: None, [], [P.i("s_saw")])

    dma("gpsimd", "d_w", r3(WSL["q"], 8192).rearrange("p (k c) -> p k c", c=1024),
        wo_in.ap().rearrange("(k p) c -> p k c", p=128), waits=P.w("s_mm2", 192))
    dma("gpsimd", "d_gab", SG[:, 0:1024],
        modh[4:5, :].partition_broadcast(128).squeeze(1), waits=P.wd("d_vw", NT))

    def woscale(e):
        last = None
        for kc in range(KC):
            last = e.tensor_tensor(r3(WSL["q"] + kc * 1024, 1024),
                                   r3(WSL["q"] + kc * 1024, 1024),
                                   SG[:, 0:1024], op=ALU.mult)
        return last
    V(woscale, P.wd("d_w", 4) + P.wd("d_gab", 1), [P.i("s_w2")])
    dma("gpsimd", "d_wow", wo2_hbm[:, :], r3(WSL["q"], 8192),
        waits=P.w("s_w2", 7))
    dflush()

    barrier()

    for jj in range(8):
        dma("sync", "d_vr", r3(jj * 4096, 4096).rearrange("p (j c) -> p j c", c=1024),
            v_hbm[jj * 512:(jj + 1) * 512, :].rearrange("(j p) c -> p j c", p=128),
            waits=P.wd("d_vw", NT) + P.wd("d_wow", 1))

    CHUNKS = []
    for i in range(NT):
        cs = att_chunks(i)
        for ci, (j0, nj) in enumerate(cs):
            CHUNKS.append((i, j0, nj, ci == 0, ci == len(cs) - 1))
    G = len(CHUNKS)
    NI = [0]


    TE_ORDER = []
    for g in range(G):
        TE_ORDER.append(("S", g))
        if g >= 1:
            TE_ORDER.append(("ptt", g - 1))
        if g >= 2:
            TE_ORDER.append(("ctx", g - 2))
    TE_ORDER += [("ptt", G - 1), ("ctx", G - 2), ("ctx", G - 1)]
    TPOS = {kg: i + 1 for i, kg in enumerate(TE_ORDER)}

    V_ORDER = []
    for g in range(G):
        V_ORDER += [("vmax", g), ("PTevac", g)]
        if g >= 1:
            V_ORDER.append(("vmerge", g - 1))
    V_ORDER.append(("vmerge", G - 1))
    VPOS = {kg: i + 1 for i, kg in enumerate(V_ORDER)}

    def emit_S(g):
        i, j0, nj, first, last = CHUNKS[g]
        ps = (PS0, PS1)[g % 2]

        def Smm(e, i=i, j0=j0, nj=nj, ps=ps):
            last_ = None
            for kc in range(KC):
                c0 = 0
                while c0 < nj * 128:
                    cw = min(512, nj * 128 - c0)
                    last_ = e.matmul(ps[:, c0:c0 + cw],
                                     r1(kc, i * 128, (i + 1) * 128),
                                     r2(kc, j0 * 128 + c0, j0 * 128 + c0 + cw),
                                     start=(kc == 0), stop=(kc == KC - 1))
                    c0 += cw
            return last_
        TE(Smm, P.w("s_exp", g - 1), [P.i("s_T3")])

    def emit_ptt(g):
        i, j0, nj, first, last = CHUNKS[g]
        pg = g % 2

        def ptt(e, nj=nj, pg=pg):
            last_ = None
            for jj in range(nj):
                last_ = e.transpose(
                    PS3[:, pg * 1024 + jj * 128:pg * 1024 + (jj + 1) * 128],
                    SG[:, pg * 1024 + jj * 128:pg * 1024 + (jj + 1) * 128],
                    idn[:, :])
            return last_
        TE(ptt, P.w("s_exp", g + 1) +
           (P.w("s_v3", VPOS[("PTevac", g - 1)]) if g >= 1 else []),
           [P.i("s_T3")])

    def emit_ctx(g):
        i, j0, nj, first, last = CHUNKS[g]
        pg = g % 2

        def ctxmm(e, j0=j0, nj=nj, pg=pg):
            last_ = None
            for jj in range(nj):
                for wz in range(2):
                    last_ = e.matmul(
                        PS2[:, wz * 512:(wz + 1) * 512],
                        SG[:, pg * 1024 + jj * 128:pg * 1024 + (jj + 1) * 128],
                        r3((j0 + jj) * 1024 + wz * 512, 512),
                        start=(jj == 0), stop=(jj == nj - 1))
            return last_
        cw = (P.w("s_v3", VPOS[("vmerge", g - 1)]) if g >= 1
              else P.w("s_v3", VPOS[("PTevac", 0)]))
        TE(ctxmm, cw + P.wd("d_vr", 8), [P.i("s_T3")])

    def emit_vme(g):
        i, j0, nj, first, last = CHUNKS[g]
        ps = (PS0, PS1)[g % 2]
        pg = g % 2
        w = nj * 128

        def vmax(e, i=i, j0=j0, first=first, last=last, ps=ps, pg=pg, w=w):
            if last:
                dof = (i - j0) * 128
                e.tensor_tensor(ps[:, dof:dof + 128], ps[:, dof:dof + 128],
                                maskd[:, :], op=ALU.add)
            if first:
                e.tensor_reduce(stats[:, 14 + pg:15 + pg], ps[:, 0:w], axis=AX.X,
                                op=ALU.max)
            else:
                e.tensor_reduce(stats[:, 4 + pg:5 + pg], ps[:, 0:w], axis=AX.X,
                                op=ALU.max)
                e.tensor_single_scalar(stats[:, 14 + pg:15 + pg],
                                       stats[:, 4 + pg:5 + pg], stats[:, 0:1],
                                       ALU.max)
                e.tensor_single_scalar(stats[:, 8 + pg:9 + pg], stats[:, 0:1],
                                       stats[:, 14 + pg:15 + pg], ALU.subtract)
            return e.tensor_copy(stats[:, 0:1], stats[:, 14 + pg:15 + pg])
        V(vmax, P.w("s_T3", TPOS[("S", g)]), [P.i("s_v3")])

        def sexp(e, first=first, ps=ps, pg=pg, w=w):
            e.activation(stats[:, 10 + pg:11 + pg], stats[:, 14 + pg:15 + pg],
                         AF.Identity, scale=-1.0)
            if not first:
                e.activation(stats[:, 12 + pg:13 + pg], stats[:, 8 + pg:9 + pg], AF.Exp)
            return e.activation(SG[:, pg * 1024:pg * 1024 + w], ps[:, 0:w], AF.Exp,
                                bias=stats[:, 10 + pg:11 + pg], scale=1.0,
                                accum_out=stats[:, 6 + pg:7 + pg])
        sw = P.w("s_v3", VPOS[("vmax", g)])
        if g >= 2:
            sw = sw + P.w("s_T3", TPOS[("ctx", g - 2)])
        SC(sexp, sw, [P.i("s_exp")])

        V(lambda e, nj=nj, pg=pg:
          e.tensor_copy(SG[:, pg * 1024:pg * 1024 + nj * 128],
                        PS3[:, pg * 1024:pg * 1024 + nj * 128]),
          P.w("s_T3", TPOS[("ptt", g)]), [P.i("s_v3")])

    def emit_vmerge(g):
        i, j0, nj, first, last = CHUNKS[g]
        pg = g % 2

        def vmerge(e, i=i, first=first, last=last, pg=pg):
            last_ = None
            if first:
                e.tensor_copy(FR[:, 0:1024], PS2[:, :])
                last_ = e.tensor_copy(stats[:, 1:2], stats[:, 6 + pg:7 + pg])
            else:
                e.scalar_tensor_tensor(FR[:, 0:1024], FR[:, 0:1024],
                                       stats[:, 12 + pg:13 + pg], PS2[:, :],
                                       ALU.mult, ALU.add)
                last_ = e.scalar_tensor_tensor(stats[:, 1:2], stats[:, 1:2],
                                               stats[:, 12 + pg:13 + pg],
                                               stats[:, 6 + pg:7 + pg],
                                               ALU.mult, ALU.add)
            if last:
                e.reciprocal(stats[:, 3:4], stats[:, 1:2])
                last_ = e.tensor_single_scalar(SG[:, 2048:3072], FR[:, 0:1024],
                                               stats[:, 3:4], ALU.mult)
            return last_
        mwaits = P.w("s_T3", TPOS[("ctx", g)])
        if last and i >= 1:
            mwaits = mwaits + P.wd("d_cw", i)
        V(vmerge, mwaits, [P.i("s_v3")] + ([P.i("s_cbf")] if last else []))
        if last:
            NI[0] += 1
            dma("gpsimd", "d_cw", ctx_hbm[i * 128:(i + 1) * 128, :],
                SG[:, 2048:3072], waits=P.w("s_cbf", NI[0]))

    for g in range(G):
        emit_S(g)
        if g >= 1:
            emit_ptt(g - 1)
        if g >= 2:
            emit_ctx(g - 2)
        emit_vme(g)
        if g >= 1:
            emit_vmerge(g - 1)
    emit_ptt(G - 1)
    emit_ctx(G - 2)
    emit_ctx(G - 1)
    emit_vmerge(G - 1)

    barrier()

    for kc in range(KC):
        dmaT("sync", "d_ct", r2(kc, 0, 4096),
             ctx_hbm[:, kc * 128:(kc + 1) * 128], waits=P.wd("d_cw", NT))
    dma("sync", "d_p4c", r3(WSL["q"], 8192), wo2_hbm[:, :], waits=P.wd("d_wow", 1))
    dma("gpsimd", "d_bor", SG[0:1, 2048:3072], modh[9:10, :],
        waits=P.wd("d_cw", NT))

    def emit_p4t(tt):
        p = tt % 2

        def ptt4(e, tt=tt, p=p):
            last = None
            for kc in range(KC):
                last = e.transpose(
                    PS3[:, p * 1024 + kc * 128:p * 1024 + (kc + 1) * 128],
                    SG[:, p * 1024 + kc * 128:p * 1024 + (kc + 1) * 128],
                    idn[:, :])
            return last
        TE(ptt4, P.w("s_ln", NT + tt + 1) +
           (P.w("s_p4e", tt - 1) if tt >= 2 else []), [P.i("s_p4t")])
        V(lambda e, tt=tt, p=p: e.tensor_copy(
            R1[:, :].rearrange("q (k c) -> q k c", c=4096)[:, :, tt * 128:(tt + 1) * 128],
            PS3[:, p * 1024:(p + 1) * 1024].rearrange("q (k c) -> q k c", c=128)),
          P.w("s_p4t", tt + 1), [P.i("s_p4e")])

    for tt in range(NT):
        ps = (PS0, PS1)[tt % 2]
        p = tt % 2
        dma("sync", "d_xr", FR[:, p * 1024:(p + 1) * 1024],
            x_in[tt * 128:(tt + 1) * 128, :],
            waits=(P.w("s_ln", NT + tt - 1) + P.wd("d_x2w", tt - 1)
                   if tt >= 2 else []))

        def atmm(e, tt=tt, ps=ps):
            last = None
            for wz in range(2):
                for kc in range(KC):
                    e.matmul(ps[:, wz * 512:(wz + 1) * 512],
                             r2(kc, tt * 128, (tt + 1) * 128),
                             r3(WSL["q"] + kc * 1024 + wz * 512, 512),
                             start=(kc == 0), stop=False)
                last = e.matmul(ps[:, wz * 512:(wz + 1) * 512], onesrow[0:1, :],
                                SG[0:1, 2048 + wz * 512:2048 + (wz + 1) * 512],
                                start=False, stop=True)
            return last
        TE(atmm, P.wd("d_ct", 8) + P.wd("d_p4c", 1) + P.wd("d_bor", 1) + P.w("s_x2", tt - 1),
           [P.i("s_at")])

        def x2ev(e, tt=tt, ps=ps, p=p):
            e.scalar_tensor_tensor(FR[:, p * 1024:(p + 1) * 1024],
                                   ps[:, :], 1.0, FR[:, p * 1024:(p + 1) * 1024],
                                   ALU.mult, ALU.add)
            e.bn_stats(bnb[:, p * 12:p * 12 + 6], FR[:, p * 1024:p * 1024 + 512])
            e.bn_stats(bnb[:, p * 12 + 6:p * 12 + 12],
                       FR[:, p * 1024 + 512:(p + 1) * 1024])
            return e.bn_aggr(stats[:, p * 2:p * 2 + 2],
                             bnb[:, p * 12:p * 12 + 12])
        V(x2ev, P.w("s_at", tt + 1) + P.wd("d_xr", tt + 1),
          [P.i("s_x2"), P.i("s_st")])
        SC(lambda e, p=p: e.activation(stats[:, 4 + p:5 + p],
                                       stats[:, p * 2 + 1:p * 2 + 2],
                                       AF.Sqrt, bias=EPS, scale=1.0),
           P.w("s_st", NT + tt + 1), [P.i("s_sq")])

        def tev(e, p=p):
            e.reciprocal(stats[:, 6 + p:7 + p], stats[:, 4 + p:5 + p])
            return e.tensor_scalar(
                SG[:, p * 1024:(p + 1) * 1024], FR[:, p * 1024:(p + 1) * 1024],
                stats[:, p * 2:p * 2 + 1], stats[:, 6 + p:7 + p],
                ALU.subtract, ALU.mult)
        V(tev, P.w("s_sq", NT + tt + 1) + (P.w("s_p4t", tt - 1) if tt >= 2 else []),
          [P.i("s_ln")])
        dma("gpsimd", "d_x2w", x2_hbm[tt * 128:(tt + 1) * 128, :],
            FR[:, p * 1024:(p + 1) * 1024], waits=P.w("s_x2", tt + 1))
        if tt >= 1:
            emit_p4t(tt - 1)
    emit_p4t(NT - 1)

    barrier()

    for kc in range(KC):
        V(lambda e, kc=kc: (
            e.tensor_scalar(r1(kc, 0, 4096), r1(kc, 0, 4096),
                            cb1c[:, kc:kc + 1], sbc[:, kc:kc + 1],
                            ALU.mult, ALU.add),
            e.tensor_reduce(mxs[:, kc:kc + 1], r1(kc, 0, 4096), axis=AX.X,
                            op=ALU.max, apply_absolute_value=True))[-1],
          P.w("s_p4e", NT), [P.i("s_f0a")])
    V(lambda e: e.tensor_reduce(stats[:, 14:15], mxs[:, :], axis=AX.X, op=ALU.max),
      P.w("s_f0a", 8), [P.i("s_f0a")])

    def add_collective(idx, reduce_emit, outcols, prwait, repk):
        GP(reduce_emit, prwait, [P.i("s_pr")])
        dma("gpsimd", "d_ccw", cci[idx][0:8, :], scl[0:1, 16:24],
            waits=P.w("s_pr", 2 * idx + 1))
        GP(lambda e, idx=idx: e.collective_compute(
            "AllReduce", ALU.max, replica_groups=[list(range(8))],
            ins=[cci[idx].ap().opt()], outs=[cco[idx].ap().opt()]),
           P.wd("d_ccw", idx + 1), [P.i("s_cc")])
        for rrow, dcol in outcols:
            dma("sync", "d_rep", scl[:, dcol:dcol + 1],
                cco[idx][rrow:rrow + 1, :].partition_broadcast(128).squeeze(1),
                waits=P.w("s_cc", idx + 1))
        return P.wd("d_rep", repk)

    rw = add_collective(0, lambda e: e.tensor_reduce(
        scl[0:1, 16:17], stats[:, 14:15], axis=AX.XYZWC, op=ALU.max),
        [(0, 0)], P.w("s_f0a", 9), 1)
    V(lambda e: (
        e.tensor_scalar(scl[:, 1:2], scl[:, 0:1], QMAXI, 1e-12, ALU.mult, ALU.max),
        e.reciprocal(scl[:, 2:3], scl[:, 1:2]))[-1],
      rw, [P.i("s_scl")])

    for kc in range(KC):
        V(lambda e, kc=kc: (
            e.tensor_scalar(FR[:, 0:2048], r1(kc, 0, 2048),
                            scl[:, 2:3], MAGIC, ALU.mult, ALU.add),
            e.tensor_scalar(r1(kc, 0, 2048), FR[:, 0:2048],
                            -MAGIC, scl[:, 1:2], ALU.add, ALU.mult),
            e.tensor_scalar(FR[:, 0:2048], r1(kc, 2048, 4096),
                            scl[:, 2:3], MAGIC, ALU.mult, ALU.add),
            e.tensor_scalar(r1(kc, 2048, 4096), FR[:, 0:2048],
                            -MAGIC, scl[:, 1:2], ALU.add, ALU.mult))[-1],
          P.w("s_scl", 1), [P.i("s_scl")])

    for kc in range(KC):
        dmaT("sync", "d_wgt", r2(kc, 0, 3072),
             wgq_hbm[:, kc * 128:(kc + 1) * 128], waits=P.wd("d_dw", 144))
    for kc in range(KC):
        dmaT("sync", "d_wgt", r3(kc * 4096, 3072),
             wuq_hbm[:, kc * 128:(kc + 1) * 128], waits=P.wd("d_dw", 144))

    barrier()

    f1w = [0]
    for mi in range(MI):
        for tc in range(8):
            for srcname, dhbm, mcol in (("g", graw_hbm, 0), ("u", uraw_hbm, 1)):
                wdx = f1w[0]
                f1w[0] += 1
                ps = (PS0, PS1, PS2)[(wdx // 2) % 3]
                po = (wdx % 2) * 512

                def f1mm(e, srcname=srcname, mi=mi, tc=tc, ps=ps, po=po):
                    last = None
                    for kc in range(KC):
                        if srcname == "g":
                            lhs = r2(kc, mi * 128, (mi + 1) * 128)
                        else:
                            lhs = r3(kc * 4096 + mi * 128, 128)
                        last = e.matmul(ps[:, po:po + 512], lhs,
                                        r1(kc, tc * 512, (tc + 1) * 512),
                                        start=(kc == 0), stop=(kc == KC - 1))
                    return last
                TE(f1mm, P.wd("d_wgt", 16) + P.w("s_scl", 9) +
                   P.w("s_f1e", wdx - 5), [P.i("s_f1m")])
                sgo = (wdx % 4) * 512
                SC(lambda e, ps=ps, po=po, sgo=sgo:
                   e.copy(SG[:, sgo:sgo + 512], ps[:, po:po + 512]),
                   P.w("s_f1m", wdx + 1) +
                   (P.wd("d_f1w", wdx - 3) + P.w("s_f1r", wdx - 3) if wdx >= 4 else []),
                   [P.i("s_f1e")])
                dma("scalar", "d_f1w",
                    dhbm[mi * 128:(mi + 1) * 128, tc * 512:(tc + 1) * 512],
                    SG[:, sgo:sgo + 512],
                    waits=(P.wd("d_f1w", wdx - 1) if wdx >= 2 else []))
                V(lambda e, sgo=sgo, mcol=mcol: (
                    e.tensor_reduce(mrg[:, 4:5], SG[:, sgo:sgo + 512], axis=AX.X,
                                    op=ALU.max, apply_absolute_value=True),
                    e.tensor_tensor(mrg[:, mcol:mcol + 1], mrg[:, mcol:mcol + 1],
                                    mrg[:, 4:5], op=ALU.max))[-1],
                  P.w("s_f1e", wdx + 1), [P.i("s_f1r")])

    GP(lambda e: None, P.w("s_f1r", 384), [P.i("s_pr")])
    rw = add_collective(1, lambda e: (
        e.tensor_reduce(scl[0:1, 16:17], mrg[:, 0:1], axis=AX.XYZWC, op=ALU.max),
        e.tensor_reduce(scl[0:1, 17:18], mrg[:, 1:2], axis=AX.XYZWC, op=ALU.max))[-1],
        [(0, 3), (1, 6)], P.w("s_pr", 2), 3)
    V(lambda e: (
        e.tensor_scalar(scl[:, 4:5], scl[:, 3:4], QMAXI, 1e-12, ALU.mult, ALU.max),
        e.reciprocal(scl[:, 5:6], scl[:, 4:5]),
        e.tensor_scalar(scl[:, 7:8], scl[:, 6:7], QMAXI, 1e-12, ALU.mult, ALU.max),
        e.reciprocal(scl[:, 8:9], scl[:, 7:8]),
        e.tensor_single_scalar(scl[:, 18:19], scl[:, 4:5], -MAGIC, ALU.mult))[-1],
      rw, [P.i("s_scl")])

    dma("gpsimd", "d_gbb", SG[:, 2048:3072],
        modh[5:6, :].partition_broadcast(128).squeeze(1), waits=P.w("s_f1m", 384))
    for mi in range(MI):
        dmaT("sync", "d_wdt", R2[:, mi * 1024:(mi + 1) * 1024],
             wdq_hbm[:, mi * 128:(mi + 1) * 128],
             waits=P.wd("d_dw", 144) + P.w("s_f1m", 384) +
             (P.w("s_wd", mi - 1) if mi >= 2 else []))
        V(lambda e, mi=mi:
          e.tensor_tensor(R2[:, mi * 1024:(mi + 1) * 1024],
                          R2[:, mi * 1024:(mi + 1) * 1024], SG[:, 2048:3072],
                          op=ALU.mult),
          P.wd("d_wdt", mi + 1) + P.wd("d_gbb", 1), [P.i("s_wd")])

    barrier()

    NU2 = MI * 4
    for u in range(NU2):
        mi, qq = divmod(u, 4)
        rin = (u % 4) * 1024
        psA = (PS0, PS1)[u % 2]
        frb = (u % 2) * 1024
        dma("sync", "d_f2r", R1[:, rin:rin + 1024],
            graw_hbm[mi * 128:(mi + 1) * 128, qq * 1024:(qq + 1) * 1024],
            waits=(P.wd("d_f2r", u - 1) if u >= 2 else []) +
            (P.wd("d_f2w", u - 3) if u >= 4 else []))
        SC(lambda e, rin=rin, psA=psA, frb=frb: (
            e.activation(psA[:, :], R1[:, rin:rin + 1024], AF.Identity,
                         bias=MAGIC, scale=scl[:, 5:6]),
            e.activation(FR[:, frb:frb + 1024], psA[:, :], AF.Sigmoid,
                         bias=scl[:, 18:19], scale=scl[:, 4:5]))[-1],
           P.wd("d_f2r", u + 1) + P.w("s_scl", 10) +
           (P.w("s_2v", u - 1) if u >= 2 else []),
           [P.i("s_2s")])
        V(lambda e, rin=rin, psA=psA, frb=frb: (
            e.tensor_scalar(FR[:, frb:frb + 1024], FR[:, frb:frb + 1024],
                            65536.0, MAGIC, ALU.mult, ALU.add),
            e.tensor_scalar(FR[:, frb:frb + 1024], FR[:, frb:frb + 1024],
                            -MAGIC, 65535.0, ALU.add, ALU.min),
            e.scalar_tensor_tensor(R1[:, rin:rin + 1024], psA[:, :], -MAGIC,
                                   FR[:, frb:frb + 1024], ALU.add, ALU.mult))[-1],
          P.w("s_2s", u + 1), [P.i("s_2v")])
        for v in ([u - 1] if u >= 1 else []) + ([u] if u == NU2 - 1 else []):
            vm, vq = divmod(v, 4)
            vr = (v % 4) * 1024
            V(lambda e, v=v, vr=vr: e.tensor_reduce(
                deq[:, v:v + 1], R1[:, vr:vr + 1024],
                axis=AX.X, op=ALU.max, apply_absolute_value=True),
              P.w("s_2v", v + 1), [P.i("s_2r")])
            dma("gpsimd", "d_f2w",
                prod_hbm[vm * 128:(vm + 1) * 128, vq * 1024:(vq + 1) * 1024],
                R1[:, vr:vr + 1024],
                waits=P.w("s_2r", v + 1) + (P.wd("d_f2w", v - 1) if v >= 2 else []))

    V(lambda e: e.tensor_reduce(mrg[:, 2:3], deq[:, 0:NU2], axis=AX.X,
                                op=ALU.max),
      P.w("s_2r", NU2), [P.i("s_2r")])
    GP(lambda e: None, P.w("s_2r", NU2 + 1), [P.i("s_pr")])
    rw = add_collective(2, lambda e: e.tensor_reduce(
        scl[0:1, 16:17], mrg[:, 2:3], axis=AX.XYZWC, op=ALU.max),
        [(0, 9)], P.w("s_pr", 4), 4)
    V(lambda e: (
        e.tensor_tensor(scl[:, 15:16], scl[:, 9:10], scl[:, 4:5], op=ALU.mult),
        e.tensor_scalar(scl[:, 10:11], scl[:, 15:16], QMAXI / 65536.0, 1e-12,
                        ALU.mult, ALU.max),
        e.tensor_single_scalar(scl[:, 15:16], scl[:, 10:11], 65536.0, ALU.mult),
        e.reciprocal(scl[:, 11:12], scl[:, 15:16]),
        e.tensor_tensor(scl[:, 11:12], scl[:, 11:12], scl[:, 4:5],
                        op=ALU.mult),
        e.tensor_single_scalar(scl[:, 19:20], scl[:, 10:11], -MAGIC,
                               ALU.mult))[-1],
      rw, [P.i("s_scl")])

    barrier()

    for u in range(NU2):
        mi, qq = divmod(u, 4)
        rp = (u % 4) * 1024
        ru = 4096 + (u % 4) * 1024
        fra = (u % 2) * 1024
        psB = (PS0, PS1)[u % 2]
        dma("sync", "d_f3r", R1[:, rp:rp + 1024],
            prod_hbm[mi * 128:(mi + 1) * 128, qq * 1024:(qq + 1) * 1024],
            waits=(P.wd("d_f3r", u - 1) if u >= 2 else []) +
            (P.wd("d_f3w", u - 3) if u >= 4 else []))
        dma("sync", "d_f3u", R1[:, ru:ru + 1024],
            uraw_hbm[mi * 128:(mi + 1) * 128, qq * 1024:(qq + 1) * 1024],
            waits=(P.wd("d_f3u", u - 1) if u >= 2 else []) +
            (P.w("s_31", u - 3) if u >= 4 else []))
        SC(lambda e, rp=rp, ru=ru, fra=fra, psB=psB: (
            e.activation(psB[:, :], R1[:, rp:rp + 1024], AF.Identity,
                         bias=MAGIC, scale=scl[:, 11:12]),
            e.activation(FR[:, fra:fra + 1024], R1[:, ru:ru + 1024], AF.Identity,
                         bias=MAGIC, scale=scl[:, 8:9]))[-1],
           P.wd("d_f3r", u + 1) + P.wd("d_f3u", u + 1) + P.w("s_scl", 11) +
           (P.w("s_3v", u - 1) if u >= 2 else []),
           [P.i("s_31")])
        V(lambda e, rp=rp, fra=fra, psB=psB: (
            e.tensor_scalar(FR[:, fra:fra + 1024], FR[:, fra:fra + 1024],
                            -MAGIC, scl[:, 7:8], ALU.add, ALU.mult),
            e.scalar_tensor_tensor(R1[:, rp:rp + 1024], psB[:, :], -MAGIC,
                                   FR[:, fra:fra + 1024], ALU.add, ALU.mult))[-1],
          P.w("s_31", u + 1), [P.i("s_3v")])
        for v in ([u - 1] if u >= 1 else []) + ([u] if u == NU2 - 1 else []):
            vm, vq = divmod(v, 4)
            vr = (v % 4) * 1024
            V(lambda e, v=v, vr=vr: e.tensor_reduce(
                deq[:, v:v + 1], R1[:, vr:vr + 1024],
                axis=AX.X, op=ALU.max, apply_absolute_value=True),
              P.w("s_3v", v + 1), [P.i("s_3r")])
            dma("gpsimd", "d_f3w",
                usc_hbm[vm * 128:(vm + 1) * 128, vq * 1024:(vq + 1) * 1024],
                R1[:, vr:vr + 1024],
                waits=P.w("s_3r", v + 1) + (P.wd("d_f3w", v - 1) if v >= 2 else []))

    V(lambda e: e.tensor_reduce(mrg[:, 3:4], deq[:, 0:NU2], axis=AX.X,
                                op=ALU.max),
      P.w("s_3r", NU2), [P.i("s_3r")])
    GP(lambda e: None, P.w("s_3r", NU2 + 1), [P.i("s_pr")])
    rw = add_collective(3, lambda e: e.tensor_reduce(
        scl[0:1, 16:17], mrg[:, 3:4], axis=AX.XYZWC, op=ALU.max),
        [(0, 12)], P.w("s_pr", 6), 5)
    V(lambda e: (
        e.tensor_tensor(scl[:, 21:22], scl[:, 12:13], scl[:, 10:11], op=ALU.mult),
        e.tensor_scalar(scl[:, 13:14], scl[:, 21:22], QMAXI, 1e-12, ALU.mult, ALU.max),
        e.reciprocal(scl[:, 14:15], scl[:, 13:14]),
        e.tensor_tensor(scl[:, 20:21], scl[:, 14:15], scl[:, 10:11],
                        op=ALU.mult))[-1],
      rw, [P.i("s_scl")])

    barrier()

    def emit_f4q(tc):
        sp = (tc % 2) * 12288
        for mi in range(MI):
            un = tc * MI + mi
            p = un % 2
            dma("sync", "d_u4", SG[:, p * 512:(p + 1) * 512],
                usc_hbm[mi * 128:(mi + 1) * 128, tc * 512:(tc + 1) * 512],
                waits=(P.w("s_41", un - 1) if un >= 2 else []))
            SC(lambda e, p=p:
               e.activation(PS2[:, p * 512:(p + 1) * 512],
                            SG[:, p * 512:(p + 1) * 512], AF.Identity,
                            bias=MAGIC, scale=scl[:, 20:21]),
               P.wd("d_u4", un + 1) + P.w("s_scl", 12) +
               (P.w("s_42", un - 1) if un >= 2 else []), [P.i("s_41")])
            V(lambda e, sp=sp, mi=mi, p=p:
              e.tensor_scalar(R3[:, sp + mi * 512:sp + (mi + 1) * 512],
                              PS2[:, p * 512:(p + 1) * 512],
                              -MAGIC, scl[:, 13:14], ALU.add, ALU.mult),
              P.w("s_41", un + 1) +
              (P.w("s_f4m", (tc - 1) * 4) if tc >= 2 and mi == 0 else []),
              [P.i("s_42")])

    def emit_f4mm(tc):
        sp = (tc % 2) * 12288
        for tt in range(4):
            tg = tc * 4 + tt
            ps = (PS0, PS1)[tg % 2]
            p = tg % 2
            dma("sync", "d_x2r", FR[:, p * 1024:(p + 1) * 1024],
                x2_hbm[tg * 128:(tg + 1) * 128, :],
                waits=(P.w("s_f4e", tg - 1) + P.wd("d_out", tg - 1)
                       if tg >= 2 else P.wd("d_x2w", NT)))

            def f4mm(e, sp=sp, tt=tt, ps=ps):
                last = None
                for wz in range(2):
                    for mi in range(MI):
                        last = e.matmul(
                            ps[:, wz * 512:(wz + 1) * 512],
                            R3[:, sp + mi * 512 + tt * 128:sp + mi * 512 + (tt + 1) * 128],
                            R2[:, mi * 1024 + wz * 512:mi * 1024 + (wz + 1) * 512],
                            start=(mi == 0), stop=(mi == MI - 1))
                return last
            TE(f4mm, P.w("s_42", (tc + 1) * MI) + P.w("s_wd", MI) +
               P.w("s_f4e", tg - 1), [P.i("s_f4m")])
            V(lambda e, p=p, ps=ps:
              e.scalar_tensor_tensor(FR[:, p * 1024:(p + 1) * 1024],
                                     ps[:, :], 1.0, FR[:, p * 1024:(p + 1) * 1024],
                                     ALU.mult, ALU.add),
              P.w("s_f4m", tg + 1) + P.wd("d_x2r", tg + 1),
              [P.i("s_f4e")])
            dma("scalar", "d_out", out_hbm[tg * 128:(tg + 1) * 128, :],
                FR[:, p * 1024:(p + 1) * 1024], waits=P.w("s_f4e", tg + 1))

    for tc in range(8):
        emit_f4q(tc)
        if tc >= 1:
            emit_f4mm(tc - 1)
    emit_f4mm(7)

    for name, dsttap in tap_outs.items():
        src = tap_srcs[name]
        dma("gpsimd", "d_tap", dsttap[:, :], src[:, :],
            waits=P.wd("d_out") + P.wd("d_f3w") + P.wd("d_dw") + P.wd("d_f2w") +
            P.wd("d_f1w") + P.wd("d_mr") + P.wd("d_mw") + P.wd("d_cw") +
            P.wd("d_x2w") + P.wd("d_tw") + P.wd("d_vw") + P.wd("d_h1w") +
            P.wd("d_wow"))

    SC(lambda e: None, P.wd("d_out"))
    if taps:
        GP(lambda e: None, P.wd("d_tap"))

    for E in ENGS:
        for waits, emit, incs in (
                (s for s in P.steps[E] if not isinstance(s[1], tuple))):
            for (n, v) in waits:
                if n == "__tpos__":
                    continue
                assert v <= P.cnt.get(n, 0), (
                    f"{E}: wait {n}>={v} unreachable (final {P.cnt.get(n, 0)})")

    inc_engs = {}
    for E in ENGS:
        for step in P.steps[E]:
            if isinstance(step[1], tuple):
                continue
            for (n, v) in step[2]:
                inc_engs.setdefault(n, set()).add(E)
    mergeable = {
        n for n, es in inc_engs.items()
        if len(es) == 1 and next(iter(es)) != "gpsimd"
        and not n.startswith("d_") and n not in NOMERGE
    }
    trans = {}
    phys_final = {E: 0 for E in ENGS}
    for E in ENGS:
        pos = 0
        lcnt = {}
        for step in P.steps[E]:
            if isinstance(step[1], tuple):
                continue
            stepm = [(n, v) for (n, v) in step[2] if n in mergeable]
            pos_end = pos + sum(v for (n, v) in stepm)
            for (n, v) in stepm:
                lcnt[n] = lcnt.get(n, 0) + v
                trans[(n, lcnt[n])] = pos_end
            pos = pos_end
        phys_final[E] = pos
    eng_of = {n: next(iter(es)) for n, es in inc_engs.items() if n in mergeable}

    sems = {}
    for E in ENGS:
        sems["@" + E] = e_(nc.semaphore("se_" + E))
    for n in P.cnt:
        if n not in mergeable:
            sems[n] = e_(nc.semaphore(n))
    sbar = e_(nc.semaphore("sbar"))

    def xlate(n, v):
        if n == "__tpos__":
            return ("@tensor", v)
        if n in mergeable:
            return ("@" + eng_of[n], trans[(n, v)])
        return (n, v)

    _NODRAIN = {"dma_start", "dma_start_transpose", "collective_compute",
                "wait_ge", "sem_inc", "drain", "nop", "engine_nop"}

    class _DrainProxy:

        def __init__(self, e):
            object.__setattr__(self, "_e", e)

        def __getattr__(self, name):
            f = getattr(self._e, name)
            if not callable(f) or name in _NODRAIN or name.startswith("_"):
                return f

            def wrap(*a, **kw):
                r = f(*a, **kw)
                if r is not None and hasattr(r, "then_inc"):
                    self._e.drain()
                return r
            return wrap

    def make_closure(E):
        def closure(raw_e):
            e = raw_e if E == "tensor" else _DrainProxy(raw_e)
            for waits, emit, incs in P.steps[E]:
                if isinstance(emit, tuple) and emit[0] == "BARRIER":
                    for (n, v) in emit[2]:
                        if v > 0:
                            raw_e.wait_ge(sems[n], v)
                    raw_e.sem_inc(sbar, 1)
                    raw_e.wait_ge(sbar, 5 * emit[1])
                    continue
                for (n, v) in waits:
                    if v > 0:
                        pn, pv = xlate(n, v)
                        raw_e.wait_ge(sems[pn], pv)
                inst = emit(e)
                agg = {}
                for (n, v) in incs:
                    pn = "@" + eng_of[n] if n in mergeable else n
                    agg[pn] = agg.get(pn, 0) + v
                if inst is None and agg:
                    inst = raw_e.nop()
                for pn, v in agg.items():
                    inst = inst.then_inc(sems[pn], v)
        return closure

    block = e_(nc.Block())
    block.sync(make_closure("sync"))
    block.vector(make_closure("vector"))
    block.scalar(make_closure("scalar"))
    block.tensor(make_closure("tensor"))
    block.gpsimd(make_closure("gpsimd"))

    st_.close()
    return nc


_NC_CACHE = {}


def _get_nc(taps=()):
    key = tuple(sorted(taps))
    if key not in _NC_CACHE:
        _NC_CACHE[key] = build_nc(taps=key)
    return _NC_CACHE[key]


def make_in_maps(inputs):
    y = np.ascontiguousarray(np.asarray(inputs["L0_attn_y_all"], dtype=np.float32))
    c = np.asarray(inputs["L0_c"], dtype=np.float32)
    base = {
        "wmod": np.ascontiguousarray(np.asarray(inputs["L0_wmod"], np.float32)),
        "wq": np.ascontiguousarray(np.asarray(inputs["L0_attn_Wq"], np.float32)),
        "wk": np.ascontiguousarray(np.asarray(inputs["L0_attn_Wk"], np.float32)),
        "wv": np.ascontiguousarray(np.asarray(inputs["L0_attn_Wv"], np.float32)),
        "wo": np.ascontiguousarray(np.asarray(inputs["L0_attn_Wo"], np.float32)),
        "bq": np.ascontiguousarray(np.asarray(inputs["L0_attn_bq"], np.float32).reshape(1, H)),
        "bk": np.ascontiguousarray(np.asarray(inputs["L0_attn_bk"], np.float32).reshape(1, H)),
        "bv": np.ascontiguousarray(np.asarray(inputs["L0_attn_bv"], np.float32).reshape(1, H)),
        "bo": np.ascontiguousarray(np.asarray(inputs["L0_attn_bo"], np.float32).reshape(1, H)),
        "wg": np.ascontiguousarray(np.asarray(inputs["L0_ffn_w_gate"], np.float32)),
        "wu": np.ascontiguousarray(np.asarray(inputs["L0_ffn_w_up"], np.float32)),
        "wd": np.ascontiguousarray(np.asarray(inputs["L0_ffn_w_down"], np.float32)),
    }
    in_maps = []
    for b in range(8):
        m = dict(base)
        m["x"] = np.ascontiguousarray(y[b])
        m["ccol"] = np.ascontiguousarray(c[b].reshape(8, 128).T)
        in_maps.append(m)
    return in_maps


def kernel(**inputs):
    nc = _get_nc()
    in_maps = make_in_maps(inputs)
    res = run_bass_kernel_spmd(nc, in_maps, core_ids=list(range(8)))
    return np.stack([np.asarray(r["out"], dtype=np.float32) for r in res.results])


# version 13
# speedup vs baseline: 1.0190x; 1.0190x over previous
import numpy as np
from contextlib import ExitStack

import concourse.bass as bass
import concourse.mybir as mybir
from concourse.bass_utils import run_bass_kernel_spmd

F32 = mybir.dt.float32
BF16 = mybir.dt.bfloat16
AF = mybir.ActivationFunctionType
ALU = mybir.AluOpType
AX = mybir.AxisListType

MAGIC = 12582912.0
NEG_BIG = -1.0e30
T, NT, H, KC, I3, MI = 4096, 32, 1024, 8, 3072, 24
EPS = 1e-5
QMAXI = 1.0 / 32767.0
ENGS = ("sync", "vector", "scalar", "tensor", "gpsimd")

PARITY = {
    "d_x", "d_h1w", "d_vw", "d_cw", "d_xr", "d_x2w", "d_tw", "d_d", "d_dw",
    "d_tt", "d_wdt", "d_f1w", "d_f2r", "d_f2w", "d_f3r", "d_f3u", "d_f3w",
    "d_u4", "d_x2r", "d_out", "d_da", "d_daw", "d_ds", "d_d2",
}
NOMERGE = {"s_ev2"}


class Prog:
    def __init__(self):
        self.cnt = {}
        self.tpos = 0
        self.steps = {E: [] for E in ENGS}
        self.stream_eng = {}

    def _reg(self, n):
        if n not in self.cnt:
            self.cnt[n] = 0

    def w(self, n, k=None):
        self._reg(n)
        c = self.cnt[n] if k is None else k
        return [(n, c)]

    def i(self, n):
        self._reg(n)
        self.cnt[n] += 1
        return (n, 1)

    def dcount(self, n):
        if n in PARITY:
            self._reg(n + "_A")
            self._reg(n + "_B")
            return (self.cnt[n + "_A"] + self.cnt[n + "_B"]) // 16
        self._reg(n)
        return self.cnt[n] // 16

    def dinc(self, n):
        tot = self.dcount(n)
        if n in PARITY:
            pn = n + ("_A" if tot % 2 == 0 else "_B")
            self.cnt[pn] += 16
            return (pn, 16)
        self.cnt[n] += 16
        return (n, 16)

    def wd(self, n, k=None):
        tot = self.dcount(n)
        if k is None:
            k = tot
        k = max(0, k)
        if k == 0:
            return []
        if n in PARITY:
            ka = (k + 1) // 2
            kb = k // 2
            out = [(n + "_A", 16 * ka)]
            if kb > 0:
                out.append((n + "_B", 16 * kb))
            return out
        return [(n, 16 * k)]

    def st(self, eng, emit, waits=(), incs=()):
        waits = tuple(waits)
        if eng == "tensor":
            ncomp = sum(v for (n, v) in incs if not n.startswith("d_"))
            if ncomp:
                if self.tpos > 0:
                    waits = waits + (("__tpos__", self.tpos),)
                self.tpos += ncomp
        self.steps[eng].append((waits, emit, tuple(incs)))


def att_chunks(i):
    out, j0 = [], 0
    while j0 <= i:
        nj = min(8, i + 1 - j0)
        out.append((j0, nj))
        j0 += nj
    return out


def build_nc(taps=()):
    nc = bass.Bass("TRN2", target_bir_lowering=False)

    x_in = nc.dram_tensor("x", [T, H], F32, kind="ExternalInput")
    ccol_in = nc.dram_tensor("ccol", [128, 8], F32, kind="ExternalInput")
    wmod_in = nc.dram_tensor("wmod", [H, 6 * H], F32, kind="ExternalInput")
    wq_in = nc.dram_tensor("wq", [H, H], F32, kind="ExternalInput")
    wk_in = nc.dram_tensor("wk", [H, H], F32, kind="ExternalInput")
    wv_in = nc.dram_tensor("wv", [H, H], F32, kind="ExternalInput")
    wo_in = nc.dram_tensor("wo", [H, H], F32, kind="ExternalInput")
    bq_in = nc.dram_tensor("bq", [1, H], F32, kind="ExternalInput")
    bk_in = nc.dram_tensor("bk", [1, H], F32, kind="ExternalInput")
    bv_in = nc.dram_tensor("bv", [1, H], F32, kind="ExternalInput")
    bo_in = nc.dram_tensor("bo", [1, H], F32, kind="ExternalInput")
    wg_in = nc.dram_tensor("wg", [I3, H], F32, kind="ExternalInput")
    wu_in = nc.dram_tensor("wu", [I3, H], F32, kind="ExternalInput")
    wd_in = nc.dram_tensor("wd", [H, I3], F32, kind="ExternalInput")
    out_hbm = nc.dram_tensor("out", [T, H], F32, kind="ExternalOutput")

    h1_hbm = nc.dram_tensor("h1s", [T, H], BF16)
    v_hbm = nc.dram_tensor("vs", [T, H], BF16)
    ctx_hbm = nc.dram_tensor("ctxs", [T, H], BF16)
    x2_hbm = nc.dram_tensor("x2s", [T, H], F32)
    t_hbm = nc.dram_tensor("ts_", [T, H], BF16)
    wo2_hbm = nc.dram_tensor("wo2s", [128, 8 * H], BF16)
    modraw = nc.dram_tensor("modraw", [1, 6 * H], F32)
    modh = nc.dram_tensor("modh", [12, H], F32)
    wgq_hbm = nc.dram_tensor("wgq", [I3, H], BF16)
    wuq_hbm = nc.dram_tensor("wuq", [I3, H], BF16)
    wdq_hbm = nc.dram_tensor("wdq", [H, I3], BF16)
    graw_hbm = nc.dram_tensor("graw", [I3, T], BF16)
    uraw_hbm = nc.dram_tensor("uraw", [I3, T], BF16)
    prod_hbm = nc.dram_tensor("prod", [I3, T], BF16)
    usc_hbm = nc.dram_tensor("usc", [I3, T], BF16)
    cci = [nc.dram_tensor(f"cci{j}", [8, 1], F32) for j in range(4)]
    cco = [nc.dram_tensor(f"cco{j}", [8, 1], F32) for j in range(4)]

    tap_srcs = {
        "h1": h1_hbm, "v": v_hbm, "ctx": ctx_hbm, "x2": x2_hbm,
        "modraw": modraw, "modh": modh, "wgq": wgq_hbm, "wuq": wuq_hbm,
        "wdq": wdq_hbm, "graw": graw_hbm, "uraw": uraw_hbm, "prod": prod_hbm,
        "usc": usc_hbm, "wo2": wo2_hbm,
    }
    tap_outs = {
        name: nc.dram_tensor("tap_" + name, list(tap_srcs[name].shape),
                             tap_srcs[name].dtype, kind="ExternalOutput")
        for name in taps
    }

    st_ = ExitStack()
    e_ = st_.enter_context

    R1 = e_(nc.sbuf_tensor("R1", [128, 32768], BF16))
    R2 = e_(nc.sbuf_tensor("R2", [128, 32768], BF16))
    R3 = e_(nc.sbuf_tensor("R3", [128, 32768], BF16))
    SG = e_(nc.sbuf_tensor("SGb", [128, 3072], BF16))
    FR = e_(nc.sbuf_tensor("FRf", [128, 2048], F32))
    idn = e_(nc.sbuf_tensor("idn", [128, 128], BF16))
    maskd = e_(nc.sbuf_tensor("maskd", [128, 128], BF16))
    onesrow = e_(nc.sbuf_tensor("onesrow", [1, 128], BF16))
    ccol = e_(nc.sbuf_tensor("ccolS", [128, 8], F32))
    cbf = e_(nc.sbuf_tensor("cbfS", [128, 8], BF16))
    ca1c = e_(nc.sbuf_tensor("ca1c", [128, 8], F32))
    sacbf = e_(nc.sbuf_tensor("sacbf", [128, 8], BF16))
    cb1c = e_(nc.sbuf_tensor("cb1c", [128, 8], F32))
    sbc = e_(nc.sbuf_tensor("sbc", [128, 8], F32))
    bqc = e_(nc.sbuf_tensor("bqc", [128, 8], F32))
    bkc = e_(nc.sbuf_tensor("bkc", [128, 8], F32))
    mxs = e_(nc.sbuf_tensor("mxs", [128, 8], F32))
    stats = e_(nc.sbuf_tensor("stats", [128, 16], F32))
    scl = e_(nc.sbuf_tensor("scl", [128, 24], F32))
    mrg = e_(nc.sbuf_tensor("mrg", [128, 8], F32))
    bnb = e_(nc.sbuf_tensor("bnb", [128, 24], F32))
    deq = e_(nc.sbuf_tensor("deq", [128, 128], F32))
    cst = e_(nc.sbuf_tensor("cst", [128, 6], F32))
    nc.const_aps.register(F32, EPS, cst[:, 0:1])
    nc.const_aps.register(F32, MAGIC, cst[:, 1:2])
    nc.const_aps.register(F32, 65536.0, cst[:, 2:3])
    nc.const_aps.register(F32, -MAGIC, cst[:, 3:4])
    nc.const_aps.register(F32, -1.0, cst[:, 4:5])

    PS0 = e_(nc.psum_tensor("PS0", [128, 1024], F32))
    PS1 = e_(nc.psum_tensor("PS1", [128, 1024], F32))
    PS2 = e_(nc.psum_tensor("PS2", [128, 1024], F32))
    PS3 = e_(nc.psum_tensor("PS3", [128, 2048], BF16))

    P = Prog()

    def r1(kc, a, b):
        return R1[:, kc * 4096 + a: kc * 4096 + b]

    def r2(kc, a, b):
        return R2[:, kc * 4096 + a: kc * 4096 + b]

    def r3(off, n):
        return R3[:, off: off + n]

    def fr(row, a, b):
        return FR[row:row + 1, a:b]

    def dma(eng, stream, out, in_, waits=(), slow=False):
        def em(e, out=out, in_=in_, slow=slow):
            if slow:
                return e.dma_start(out=out, in_=in_, allow_slow_non_contiguous=True)
            return e.dma_start(out=out, in_=in_)
        P.stream_eng[stream] = eng
        P.st(eng, em, waits, [P.dinc(stream)])

    def dmaT(eng, stream, out, in_, waits=()):
        P.stream_eng[stream] = eng
        P.st(eng, lambda e, o=out, i=in_: e.dma_start_transpose(o, i), waits,
             [P.dinc(stream)])

    nbar = [0]

    def barrier():
        nbar[0] += 1
        for E in ENGS:
            flush = []
            for sname, seng in P.stream_eng.items():
                if seng == E:
                    flush.extend(P.wd(sname))
            P.st(E, ("BARRIER", nbar[0], tuple(flush)))

    def V(emit, waits=(), incs=()):
        P.st("vector", emit, waits, incs)

    def SC(emit, waits=(), incs=()):
        P.st("scalar", emit, waits, incs)

    def TE(emit, waits=(), incs=()):
        P.st("tensor", emit, waits, incs)

    def GP(emit, waits=(), incs=()):
        P.st("gpsimd", emit, waits, incs)

    GP(lambda e: e.memset(idn[:, :], 0.0), [], [P.i("s_cst")])
    GP(lambda e: e.affine_select(out=idn[:, :], in_=idn[:, :],
                                 compare_op=ALU.not_equal, fill=1.0, base=0,
                                 pattern=[[-1, 128]], channel_multiplier=1),
       P.w("s_cst", 1), [P.i("s_cst")])
    GP(lambda e: e.memset(maskd[:, :], 0.0), [], [P.i("s_cst")])
    GP(lambda e: e.affine_select(out=maskd[:, :], in_=maskd[:, :],
                                 compare_op=ALU.is_ge, fill=NEG_BIG, base=0,
                                 pattern=[[-1, 128]], channel_multiplier=1),
       P.w("s_cst", 3), [P.i("s_cst")])
    GP(lambda e: e.memset(onesrow[:, :], 1.0), [], [P.i("s_cst")])
    GP(lambda e: e.memset(scl[:, 16:24], 0.0), [], [P.i("s_cst")])
    GP(lambda e: e.memset(mrg[:, 0:8], 0.0), [], [P.i("s_cst")])
    GP(lambda e: e.memset(cst[:, 0:1], EPS), [], [P.i("s_cst")])
    GP(lambda e: e.memset(cst[:, 1:2], MAGIC), [], [P.i("s_cst")])
    GP(lambda e: e.memset(cst[:, 2:3], 65536.0), [], [P.i("s_cst")])
    GP(lambda e: e.memset(cst[:, 3:4], -MAGIC), [], [P.i("s_cst")])
    GP(lambda e: e.memset(cst[:, 4:5], -1.0), [], [P.i("s_cst")])
    dma("sync", "d_p0", ccol[:, :], ccol_in[:, :])
    SC(lambda e: e.activation(cbf[:, :], ccol[:, :], AF.Silu),
       P.wd("d_p0", 1), [P.i("s_silu")])
    for kc in range(KC):
        dma("gpsimd", "d_wm", r3(kc * 2048, 2048),
            wmod_in[kc * 128:(kc + 1) * 128, 0:2048])

    for r in range(3):
        def mm0(e, r=r):
            last = None
            for w in range(4):
                ps = PS0 if w < 2 else PS1
                wo_ = (w % 2) * 512
                for kc in range(KC):
                    last = e.matmul(
                        ps[0:1, wo_:wo_ + 512], cbf[:, kc:kc + 1],
                        r3((r % 2) * 16384 + kc * 2048 + (w // 2) * 1024 + wo_, 512),
                        start=(kc == 0), stop=(kc == KC - 1))
            return last
        TE(mm0, P.wd("d_wm", 8 * (r + 1)) + P.w("s_silu", 1) + P.w("s_ev0", r),
           [P.i("s_mm0")])

    for t in range(NT):
        p = t % 2
        dma("sync", "d_x", FR[:, p * 1024:(p + 1) * 1024],
            x_in[t * 128:(t + 1) * 128, :],
            waits=(P.w("s_ln", t - 1) if t >= 2 else []))

        def bnst(e, p=p):
            e.bn_stats(bnb[:, p * 12:p * 12 + 6], FR[:, p * 1024:p * 1024 + 512])
            e.bn_stats(bnb[:, p * 12 + 6:p * 12 + 12],
                       FR[:, p * 1024 + 512:(p + 1) * 1024])
            return e.bn_aggr(stats[:, p * 2:p * 2 + 2],
                             bnb[:, p * 12:p * 12 + 12])
        V(bnst, P.wd("d_x", t + 1), [P.i("s_st")])
        SC(lambda e, p=p: e.activation(stats[:, 4 + p:5 + p],
                                       stats[:, p * 2 + 1:p * 2 + 2],
                                       AF.Sqrt, bias=EPS, scale=1.0),
           P.w("s_st", t + 1) + P.w("s_cst", 10), [P.i("s_sq")])

        def lnt(e, p=p):
            e.reciprocal(stats[:, 6 + p:7 + p], stats[:, 4 + p:5 + p])
            return e.tensor_scalar(
                SG[:, p * 1024:(p + 1) * 1024], FR[:, p * 1024:(p + 1) * 1024],
                stats[:, p * 2:p * 2 + 1], stats[:, 6 + p:7 + p],
                ALU.subtract, ALU.mult)
        V(lnt, P.w("s_sq", t + 1) + (P.wd("d_h1w", t - 1) if t >= 2 else []),
          [P.i("s_ln")])
        dma("gpsimd", "d_h1w", h1_hbm[t * 128:(t + 1) * 128, :],
            SG[:, p * 1024:(p + 1) * 1024], waits=P.w("s_ln", t + 1))

    for r in (1, 2):
        for kc in range(KC):
            dma("gpsimd", "d_wm", r3((r % 2) * 16384 + kc * 2048, 2048),
                wmod_in[kc * 128:(kc + 1) * 128, r * 2048:(r + 1) * 2048],
                waits=P.w("s_mm0", r))

    for r in range(3):
        V(lambda e: (e.tensor_copy(fr(0, 0, 1024), PS0[0:1, :]),
                     e.tensor_copy(fr(0, 1024, 2048), PS1[0:1, :]))[-1],
          P.w("s_mm0", r + 1) + P.w("s_ln", NT) + P.wd("d_mr", r),
          [P.i("s_ev0")])
        dma("scalar", "d_mr", modraw[0:1, r * 2048:(r + 1) * 2048],
            FR[0:1, 0:2048], waits=P.w("s_ev0", r + 1) + P.wd("d_mr", r))

    AS, BS = (0, 1024), (1024, 2048)

    def _row_sign_max(e, a, b):
        g = fr(0, *a)
        sgn = fr(0, *b)
        e.tensor_single_scalar(sgn, g, 0.0, ALU.is_ge)
        e.tensor_scalar(sgn, sgn, 2.0, -1.0, ALU.mult, ALU.add)
        e.scalar_tensor_tensor(g, g, -1.0, g, ALU.mult, ALU.max)
        e.tensor_scalar_max(g, g, 0.25)
        return e.tensor_tensor(g, g, sgn, op=ALU.mult)

    dma("sync", "d_rr", fr(0, *AS), modraw[0:1, 2 * H:3 * H], waits=P.wd("d_mr", 3))
    V(lambda e: _row_sign_max(e, AS, BS), P.wd("d_rr", 1), [P.i("s_row")])
    dma("scalar", "d_mw", modh[4:5, :], fr(0, *AS), waits=P.w("s_row", 1))
    dma("sync", "d_rr", fr(0, *AS), modraw[0:1, 5 * H:6 * H],
        waits=P.w("s_row", 1) + P.wd("d_mw", 1))
    V(lambda e: _row_sign_max(e, AS, BS), P.wd("d_rr", 2), [P.i("s_row")])
    dma("scalar", "d_mw", modh[5:6, :], fr(0, *AS), waits=P.w("s_row", 2))
    dma("sync", "d_rr", fr(0, *AS), modraw[0:1, H:2 * H],
        waits=P.w("s_row", 2) + P.wd("d_mw", 2))
    V(lambda e: e.tensor_scalar_add(fr(0, *AS), fr(0, *AS), 1.0),
      P.wd("d_rr", 3), [P.i("s_row")])
    dma("scalar", "d_mw", modh[0:1, :], fr(0, *AS), waits=P.w("s_row", 3))
    dma("sync", "d_rr", fr(0, *AS), modraw[0:1, 4 * H:5 * H],
        waits=P.w("s_row", 3) + P.wd("d_mw", 3))
    V(lambda e: e.tensor_scalar_add(fr(0, *AS), fr(0, *AS), 1.0),
      P.wd("d_rr", 4), [P.i("s_row")])
    dma("scalar", "d_mw", modh[2:3, :], fr(0, *AS), waits=P.w("s_row", 4))
    dma("gpsimd", "d_md", modh[1:2, :], modraw[0:1, 0:H], waits=P.wd("d_mr", 3))
    dma("gpsimd", "d_md", modh[3:4, :], modraw[0:1, 3 * H:4 * H],
        waits=P.wd("d_mr", 3))
    dma("sync", "d_rr", fr(0, *AS), bo_in[0:1, :],
        waits=P.w("s_row", 4) + P.wd("d_mw", 4))
    dma("sync", "d_rr", fr(0, *BS), modh[4:5, :],
        waits=P.w("s_row", 4) + P.wd("d_mw", 4))
    V(lambda e: e.tensor_tensor(fr(0, *AS), fr(0, *AS), fr(0, *BS), op=ALU.mult),
      P.wd("d_rr", 6), [P.i("s_row")])
    dma("scalar", "d_mw", modh[9:10, :], fr(0, *AS), waits=P.w("s_row", 5))
    for dst, row in ((ca1c, 0), (cb1c, 2), (sbc, 3)):
        dma("sync", "d_cols", dst[:, :],
            bass.AP(modh, row * H, [[1, 128], [128, 8]]),
            waits=P.wd("d_mw", 5) + P.wd("d_md", 2), slow=True)
    dma("gpsimd", "d_sac", sacbf[:, :],
        bass.AP(modh, 1 * H, [[1, 128], [128, 8]]),
        waits=P.wd("d_mw", 5) + P.wd("d_md", 2), slow=True)

    barrier()

    WSL = {"q": 8192, "k": 16384, "v": 24576}

    DUNITS = []
    for wsrc, wdst, npt, wcols in ((wg_in, wgq_hbm, 24, 1024),
                                   (wu_in, wuq_hbm, 24, 1024),
                                   (wd_in, wdq_hbm, 8, 3072)):
        for pt in range(npt):
            for hh in range(wcols // 512):
                DUNITS.append((wsrc, wdst, pt, hh))
    ND = len(DUNITS)
    nd_ld, nd_cp, npt_ = [0], [0], [0]

    NDA = 72

    def dslot(u):
        return (u % 4) * 512 if u < NDA else 1024 + (u % 2) * 512

    def emit_dload(u):
        wsrc, wdst, pt, hh = DUNITS[u]
        fro = dslot(u)
        if u < NDA:
            rw = ((P.wd("d_d", u - 1) if u >= 2 else
                   P.w("s_w2", 5) + P.wd("d_bw", 3)) +
                  (P.wd("d_dw", u - 3) if u >= 4 else []))
            dma("gpsimd", "d_d", FR[:, fro:fro + 512],
                wsrc[pt * 128:(pt + 1) * 128, hh * 512:(hh + 1) * 512], waits=rw)
        else:
            k = u - NDA
            rw = ((P.wd("d_d2", k - 1) if k >= 2 else P.wd("d_dw", NDA)) +
                  (P.wd("d_dw", u - 1) if k >= 2 else []))
            dma("sync", "d_d2", FR[:, fro:fro + 512],
                wsrc[pt * 128:(pt + 1) * 128, hh * 512:(hh + 1) * 512], waits=rw)

    def emit_dcomp(u):
        wsrc, wdst, pt, hh = DUNITS[u]
        fro = dslot(u)
        c0 = (u % 4) * 32

        def dv(e, fro=fro, c0=c0):
            e.tensor_reduce(deq[:, c0:c0 + 16],
                            FR[:, fro:fro + 512].rearrange("q (b j) -> q b j", j=32),
                            axis=AX.X, op=ALU.max, apply_absolute_value=True)
            e.tensor_scalar(deq[:, c0:c0 + 16], deq[:, c0:c0 + 16],
                            1.0 / 7.0, 1e-12, ALU.mult, ALU.max)
            return e.reciprocal(deq[:, c0 + 16:c0 + 32], deq[:, c0:c0 + 16])
        ldw = (P.wd("d_d", u + 1) if u < NDA else P.wd("d_d2", u - NDA + 1))
        V(dv, ldw + (P.w("s_dg", u - 3) if u >= 4 else []),
          [P.i("s_da")])
        GP(lambda e, fro=fro, c0=c0: (
            e.tensor_tensor(FR[:, fro:fro + 512].rearrange("q (b j) -> q b j", j=32),
                            FR[:, fro:fro + 512].rearrange("q (b j) -> q b j", j=32),
                            deq[:, c0 + 16:c0 + 32].unsqueeze(2)
                            .broadcast_to((128, 16, 32)), op=ALU.mult),
            e.tensor_tensor(FR[:, fro:fro + 512], FR[:, fro:fro + 512],
                            cst[:, 1:2].broadcast_to((128, 512)), op=ALU.add),
            e.tensor_tensor(FR[:, fro:fro + 512], FR[:, fro:fro + 512],
                            cst[:, 1:2].broadcast_to((128, 512)),
                            op=ALU.subtract),
            e.tensor_tensor(FR[:, fro:fro + 512].rearrange("q (b j) -> q b j", j=32),
                            FR[:, fro:fro + 512].rearrange("q (b j) -> q b j", j=32),
                            deq[:, c0:c0 + 16].unsqueeze(2)
                            .broadcast_to((128, 16, 32)), op=ALU.mult))[-1],
           P.w("s_da", u + 1) + P.w("s_dg", u), [P.i("s_dg")])
        dma("gpsimd", "d_dw",
            wdst[pt * 128:(pt + 1) * 128, hh * 512:(hh + 1) * 512],
            FR[:, fro:fro + 512], waits=P.w("s_dg", u + 1) +
            (P.wd("d_dw", u - 1) if u >= 2 else []))

    def dpoint():
        p = npt_[0]
        npt_[0] += 1
        lt = min(NDA, 2 + (p + 1) // 2)
        ct = min(NDA, 2 + (p - 1) // 2) if p >= 2 else 0
        while nd_ld[0] < lt:
            emit_dload(nd_ld[0])
            nd_ld[0] += 1
        while nd_cp[0] < ct:
            emit_dcomp(nd_cp[0])
            nd_cp[0] += 1

    def dflush():
        while nd_ld[0] < NDA:
            emit_dload(nd_ld[0])
            nd_ld[0] += 1
        while nd_cp[0] < NDA:
            emit_dcomp(nd_cp[0])
            nd_cp[0] += 1

    for wn, wsrc in (("q", wq_in), ("k", wk_in), ("v", wv_in)):
        dma("gpsimd", "d_w", r3(WSL[wn], 8192).rearrange("p (k c) -> p k c", c=1024),
            wsrc.ap().rearrange("(k p) c -> p k c", p=128))

    for wi, (wn, b_in, mrow, divq) in enumerate(
            (("q", bq_in, 6, True), ("k", bk_in, 7, False), ("v", bv_in, 8, False))):
        def sawmm(e, wn=wn):
            last = None
            for kc in range(KC):
                e.matmul(PS2[0:1, 0:512], sacbf[:, kc:kc + 1],
                         r3(WSL[wn] + kc * 1024, 512),
                         start=(kc == 0), stop=(kc == KC - 1))
                last = e.matmul(PS2[0:1, 512:1024], sacbf[:, kc:kc + 1],
                                r3(WSL[wn] + kc * 1024 + 512, 512),
                                start=(kc == 0), stop=(kc == KC - 1))
            return last
        TE(sawmm, P.wd("d_w", 3) + P.wd("d_cols", 3) + P.wd("d_sac", 1) + P.w("s_w2", 2 * wi),
           [P.i("s_saw")])
        dma("sync", "d_br", fr(0, *AS), b_in[0:1, :],
            waits=P.w("s_row", 5) + P.w("s_w2", 2 * wi))

        def sawev(e, divq=divq):
            last = e.scalar_tensor_tensor(fr(0, *BS), PS2[0:1, :], 1.0, fr(0, *AS),
                                          ALU.mult, ALU.add)
            if divq:
                last = e.tensor_scalar_mul(fr(0, *BS), fr(0, *BS), 1.0 / 32.0)
            return last
        V(sawev, P.w("s_saw", wi + 1) + P.wd("d_br", wi + 1) + P.wd("d_bw", wi),
          [P.i("s_w2")])
        dma("scalar", "d_bw", modh[mrow:mrow + 1, :], fr(0, *BS),
            waits=P.w("s_w2", 2 * wi + 1))

        def wscale(e, wn=wn, divq=divq):
            last = None
            for kc in range(KC):
                if divq:
                    last = e.tensor_scalar(
                        r3(WSL[wn] + kc * 1024, 1024),
                        r3(WSL[wn] + kc * 1024, 1024),
                        ca1c[:, kc:kc + 1], 1.0 / 32.0, ALU.mult, ALU.mult)
                else:
                    last = e.tensor_single_scalar(
                        r3(WSL[wn] + kc * 1024, 1024),
                        r3(WSL[wn] + kc * 1024, 1024),
                        ca1c[:, kc:kc + 1], ALU.mult)
            return last
        V(wscale, P.w("s_saw", wi + 1), [P.i("s_w2")])

    dma("sync", "d_bc", bqc[:, :],
        bass.AP(modh, 6 * H, [[1, 128], [128, 8]]), waits=P.wd("d_bw", 3), slow=True)
    dma("sync", "d_bc", bkc[:, :],
        bass.AP(modh, 7 * H, [[1, 128], [128, 8]]), waits=P.wd("d_bw", 3), slow=True)
    dma("gpsimd", "d_bvb", SG[:, 2048:3072],
        modh[8:9, :].partition_broadcast(128).squeeze(1), waits=P.wd("d_bw", 3))

    nwin = [0]
    WINFO = []
    nsc, nvv = [0], [0]

    def _reuse_wait(wdx):
        if wdx < 4:
            return []
        k, o = WINFO[wdx - 4]
        return P.w("s_ev2s" if k == "s" else "s_ev2v", o)

    for tq in range(4):
        for kc in range(KC):
            dmaT("sync", "d_h1t", r3(kc * 1024, 1024),
                 h1_hbm[tq * 1024:(tq + 1) * 1024, kc * 128:(kc + 1) * 128],
                 waits=P.wd("d_h1w", 8 * (tq + 1)) +
                 (P.w("s_saw", 3 + tq) if tq >= 1 else []))
        for wn, dstR in (("q", r1), ("k", r2)):
            bcol = bqc if wn == "q" else bkc
            for mc in range(KC):
                for wz in range(2):
                    wdx = nwin[0]
                    nwin[0] += 1
                    ps = (PS0, PS1)[(wdx // 2) % 2]
                    po = (wdx % 2) * 512
                    rw = _reuse_wait(wdx)
                    nsc[0] += 1
                    WINFO.append(("s", nsc[0]))

                    def qkm(e, wn=wn, mc=mc, wz=wz, ps=ps, po=po):
                        last = None
                        for kc in range(KC):
                            last = e.matmul(ps[:, po:po + 512],
                                            r3(WSL[wn] + kc * 1024 + mc * 128, 128),
                                            r3(kc * 1024 + wz * 512, 512),
                                            start=(kc == 0), stop=(kc == KC - 1))
                        return last
                    TE(qkm, P.wd("d_h1t", 8 * (tq + 1)) +
                       P.w("s_w2", 2 if wn == "q" else 4) + rw, [P.i("s_mm2")])
                    SC(lambda e, dstR=dstR, mc=mc, tq=tq, wz=wz, ps=ps, po=po, bcol=bcol:
                       e.activation(dstR(mc, tq * 1024 + wz * 512,
                                         tq * 1024 + wz * 512 + 512),
                                    ps[:, po:po + 512], AF.Identity,
                                    bias=bcol[:, mc:mc + 1], scale=1.0),
                       P.w("s_mm2", wdx + 1) + P.wd("d_bc", 2), [P.i("s_ev2s")])
                    dpoint()
        for tt in range(8):
            gt = tq * 8 + tt
            for wz in range(2):
                wdx = nwin[0]
                nwin[0] += 1
                ps = (PS0, PS1)[(wdx // 2) % 2]
                po = (wdx % 2) * 512
                rw = _reuse_wait(wdx)
                nvv[0] += 1
                WINFO.append(("v", nvv[0]))

                def vmm(e, tt=tt, wz=wz, ps=ps, po=po):
                    last = None
                    for kc in range(KC):
                        last = e.matmul(ps[:, po:po + 512],
                                        r3(kc * 1024 + tt * 128, 128),
                                        r3(WSL["v"] + kc * 1024 + wz * 512, 512),
                                        start=(kc == 0), stop=(kc == KC - 1))
                    return last
                TE(vmm, P.w("s_w2", 6) + rw, [P.i("s_mm2")])
                V(lambda e, gt=gt, wz=wz, ps=ps, po=po:
                  e.scalar_tensor_tensor(
                      SG[:, (gt % 2) * 1024 + wz * 512:(gt % 2) * 1024 + wz * 512 + 512],
                      ps[:, po:po + 512], 1.0,
                      SG[:, 2048 + wz * 512:2048 + (wz + 1) * 512],
                      ALU.mult, ALU.add),
                  P.w("s_mm2", wdx + 1) + P.wd("d_bvb", 1) +
                  (P.wd("d_vw", gt - 1) if gt >= 2 else []),
                  [P.i("s_ev2v")])
                dpoint()
            dma("gpsimd", "d_vw", v_hbm[gt * 128:(gt + 1) * 128, :],
                SG[:, (gt % 2) * 1024:(gt % 2) * 1024 + 1024],
                waits=P.w("s_ev2v", 16 * tq + 2 * tt + 2))
            dpoint()
        TE(lambda e: None, [], [P.i("s_saw")])

    dma("gpsimd", "d_w", r3(WSL["q"], 8192).rearrange("p (k c) -> p k c", c=1024),
        wo_in.ap().rearrange("(k p) c -> p k c", p=128), waits=P.w("s_mm2", 192))
    dma("gpsimd", "d_gab", SG[:, 0:1024],
        modh[4:5, :].partition_broadcast(128).squeeze(1), waits=P.wd("d_vw", NT))

    def woscale(e):
        last = None
        for kc in range(KC):
            last = e.tensor_tensor(r3(WSL["q"] + kc * 1024, 1024),
                                   r3(WSL["q"] + kc * 1024, 1024),
                                   SG[:, 0:1024], op=ALU.mult)
        return last
    V(woscale, P.wd("d_w", 4) + P.wd("d_gab", 1), [P.i("s_w2")])
    dma("gpsimd", "d_wow", wo2_hbm[:, :], r3(WSL["q"], 8192),
        waits=P.w("s_w2", 7))
    dflush()

    barrier()

    for jj in range(8):
        dma("sync", "d_vr", r3(jj * 4096, 4096).rearrange("p (j c) -> p j c", c=1024),
            v_hbm[jj * 512:(jj + 1) * 512, :].rearrange("(j p) c -> p j c", p=128),
            waits=P.wd("d_vw", NT) + P.wd("d_wow", 1))

    CHUNKS = []
    for i in range(NT):
        cs = att_chunks(i)
        for ci, (j0, nj) in enumerate(cs):
            CHUNKS.append((i, j0, nj, ci == 0, ci == len(cs) - 1))
    G = len(CHUNKS)
    NI = [0]


    TE_ORDER = []
    for g in range(G):
        TE_ORDER.append(("S", g))
        if g >= 1:
            TE_ORDER.append(("ptt", g - 1))
        if g >= 2:
            TE_ORDER.append(("ctx", g - 2))
    TE_ORDER += [("ptt", G - 1), ("ctx", G - 2), ("ctx", G - 1)]
    TPOS = {kg: i + 1 for i, kg in enumerate(TE_ORDER)}

    V_ORDER = []
    for g in range(G):
        V_ORDER += [("vmax", g), ("PTevac", g)]
        if g >= 1:
            V_ORDER.append(("vmerge", g - 1))
    V_ORDER.append(("vmerge", G - 1))
    VPOS = {kg: i + 1 for i, kg in enumerate(V_ORDER)}

    def emit_S(g):
        i, j0, nj, first, last = CHUNKS[g]
        ps = (PS0, PS1)[g % 2]

        def Smm(e, i=i, j0=j0, nj=nj, ps=ps):
            last_ = None
            for kc in range(KC):
                c0 = 0
                while c0 < nj * 128:
                    cw = min(512, nj * 128 - c0)
                    last_ = e.matmul(ps[:, c0:c0 + cw],
                                     r1(kc, i * 128, (i + 1) * 128),
                                     r2(kc, j0 * 128 + c0, j0 * 128 + c0 + cw),
                                     start=(kc == 0), stop=(kc == KC - 1))
                    c0 += cw
            return last_
        TE(Smm, P.w("s_exp", g - 1), [P.i("s_T3")])

    def emit_ptt(g):
        i, j0, nj, first, last = CHUNKS[g]
        pg = g % 2

        def ptt(e, nj=nj, pg=pg):
            last_ = None
            for jj in range(nj):
                last_ = e.transpose(
                    PS3[:, pg * 1024 + jj * 128:pg * 1024 + (jj + 1) * 128],
                    SG[:, pg * 1024 + jj * 128:pg * 1024 + (jj + 1) * 128],
                    idn[:, :])
            return last_
        TE(ptt, P.w("s_exp", g + 1) +
           (P.w("s_v3", VPOS[("PTevac", g - 1)]) if g >= 1 else []),
           [P.i("s_T3")])

    def emit_ctx(g):
        i, j0, nj, first, last = CHUNKS[g]
        pg = g % 2

        def ctxmm(e, j0=j0, nj=nj, pg=pg):
            last_ = None
            for jj in range(nj):
                for wz in range(2):
                    last_ = e.matmul(
                        PS2[:, wz * 512:(wz + 1) * 512],
                        SG[:, pg * 1024 + jj * 128:pg * 1024 + (jj + 1) * 128],
                        r3((j0 + jj) * 1024 + wz * 512, 512),
                        start=(jj == 0), stop=(jj == nj - 1))
            return last_
        cw = (P.w("s_v3", VPOS[("vmerge", g - 1)]) if g >= 1
              else P.w("s_v3", VPOS[("PTevac", 0)]))
        TE(ctxmm, cw + P.wd("d_vr", 8), [P.i("s_T3")])

    def emit_vme(g):
        i, j0, nj, first, last = CHUNKS[g]
        ps = (PS0, PS1)[g % 2]
        pg = g % 2
        w = nj * 128

        def vmax(e, i=i, j0=j0, first=first, last=last, ps=ps, pg=pg, w=w):
            if last:
                dof = (i - j0) * 128
                e.tensor_tensor(ps[:, dof:dof + 128], ps[:, dof:dof + 128],
                                maskd[:, :], op=ALU.add)
            if first:
                e.tensor_reduce(stats[:, 14 + pg:15 + pg], ps[:, 0:w], axis=AX.X,
                                op=ALU.max)
            else:
                e.tensor_reduce(stats[:, 4 + pg:5 + pg], ps[:, 0:w], axis=AX.X,
                                op=ALU.max)
                e.tensor_single_scalar(stats[:, 14 + pg:15 + pg],
                                       stats[:, 4 + pg:5 + pg], stats[:, 0:1],
                                       ALU.max)
                e.tensor_single_scalar(stats[:, 8 + pg:9 + pg], stats[:, 0:1],
                                       stats[:, 14 + pg:15 + pg], ALU.subtract)
            return e.tensor_copy(stats[:, 0:1], stats[:, 14 + pg:15 + pg])
        V(vmax, P.w("s_T3", TPOS[("S", g)]), [P.i("s_v3")])

        def sexp(e, first=first, ps=ps, pg=pg, w=w):
            e.activation(stats[:, 10 + pg:11 + pg], stats[:, 14 + pg:15 + pg],
                         AF.Identity, scale=-1.0)
            if not first:
                e.activation(stats[:, 12 + pg:13 + pg], stats[:, 8 + pg:9 + pg], AF.Exp)
            return e.activation(SG[:, pg * 1024:pg * 1024 + w], ps[:, 0:w], AF.Exp,
                                bias=stats[:, 10 + pg:11 + pg], scale=1.0,
                                accum_out=stats[:, 6 + pg:7 + pg])
        sw = P.w("s_v3", VPOS[("vmax", g)])
        if g >= 2:
            sw = sw + P.w("s_T3", TPOS[("ctx", g - 2)])
        SC(sexp, sw, [P.i("s_exp")])

        V(lambda e, nj=nj, pg=pg:
          e.tensor_copy(SG[:, pg * 1024:pg * 1024 + nj * 128],
                        PS3[:, pg * 1024:pg * 1024 + nj * 128]),
          P.w("s_T3", TPOS[("ptt", g)]), [P.i("s_v3")])

    def emit_vmerge(g):
        i, j0, nj, first, last = CHUNKS[g]
        pg = g % 2

        def vmerge(e, i=i, first=first, last=last, pg=pg):
            last_ = None
            if first:
                e.tensor_copy(FR[:, 0:1024], PS2[:, :])
                last_ = e.tensor_copy(stats[:, 1:2], stats[:, 6 + pg:7 + pg])
            else:
                e.scalar_tensor_tensor(FR[:, 0:1024], FR[:, 0:1024],
                                       stats[:, 12 + pg:13 + pg], PS2[:, :],
                                       ALU.mult, ALU.add)
                last_ = e.scalar_tensor_tensor(stats[:, 1:2], stats[:, 1:2],
                                               stats[:, 12 + pg:13 + pg],
                                               stats[:, 6 + pg:7 + pg],
                                               ALU.mult, ALU.add)
            if last:
                e.reciprocal(stats[:, 3:4], stats[:, 1:2])
                last_ = e.tensor_single_scalar(SG[:, 2048:3072], FR[:, 0:1024],
                                               stats[:, 3:4], ALU.mult)
            return last_
        mwaits = P.w("s_T3", TPOS[("ctx", g)])
        if last and i >= 1:
            mwaits = mwaits + P.wd("d_cw", i)
        V(vmerge, mwaits, [P.i("s_v3")] + ([P.i("s_cbf")] if last else []))
        if last:
            NI[0] += 1
            dma("gpsimd", "d_cw", ctx_hbm[i * 128:(i + 1) * 128, :],
                SG[:, 2048:3072], waits=P.w("s_cbf", NI[0]))

    for g in range(G):
        emit_S(g)
        if g >= 1:
            emit_ptt(g - 1)
        if g >= 2:
            emit_ctx(g - 2)
        emit_vme(g)
        if g >= 1:
            emit_vmerge(g - 1)
        t2 = NDA + min(ND - NDA, max(0, g - 1))
        while nd_ld[0] < t2:
            emit_dload(nd_ld[0])
            nd_ld[0] += 1
            emit_dcomp(nd_cp[0])
            nd_cp[0] += 1
    while nd_cp[0] < ND:
        if nd_ld[0] < ND:
            emit_dload(nd_ld[0])
            nd_ld[0] += 1
        emit_dcomp(nd_cp[0])
        nd_cp[0] += 1
    emit_ptt(G - 1)
    emit_ctx(G - 2)
    emit_ctx(G - 1)
    emit_vmerge(G - 1)

    barrier()

    for kc in range(KC):
        dmaT("sync", "d_ct", r2(kc, 0, 4096),
             ctx_hbm[:, kc * 128:(kc + 1) * 128], waits=P.wd("d_cw", NT))
    dma("sync", "d_p4c", r3(WSL["q"], 8192), wo2_hbm[:, :], waits=P.wd("d_wow", 1))
    dma("gpsimd", "d_bor", SG[0:1, 2048:3072], modh[9:10, :],
        waits=P.wd("d_cw", NT))

    def emit_p4t(tt):
        p = tt % 2

        def ptt4(e, tt=tt, p=p):
            last = None
            for kc in range(KC):
                last = e.transpose(
                    PS3[:, p * 1024 + kc * 128:p * 1024 + (kc + 1) * 128],
                    SG[:, p * 1024 + kc * 128:p * 1024 + (kc + 1) * 128],
                    idn[:, :])
            return last
        TE(ptt4, P.w("s_ln", NT + tt + 1) +
           (P.w("s_p4e", tt - 1) if tt >= 2 else []), [P.i("s_p4t")])
        V(lambda e, tt=tt, p=p: e.tensor_copy(
            R1[:, :].rearrange("q (k c) -> q k c", c=4096)[:, :, tt * 128:(tt + 1) * 128],
            PS3[:, p * 1024:(p + 1) * 1024].rearrange("q (k c) -> q k c", c=128)),
          P.w("s_p4t", tt + 1), [P.i("s_p4e")])

    for tt in range(NT):
        ps = (PS0, PS1)[tt % 2]
        p = tt % 2
        dma("sync", "d_xr", FR[:, p * 1024:(p + 1) * 1024],
            x_in[tt * 128:(tt + 1) * 128, :],
            waits=(P.w("s_ln", NT + tt - 1) + P.wd("d_x2w", tt - 1)
                   if tt >= 2 else []))

        def atmm(e, tt=tt, ps=ps):
            last = None
            for wz in range(2):
                for kc in range(KC):
                    e.matmul(ps[:, wz * 512:(wz + 1) * 512],
                             r2(kc, tt * 128, (tt + 1) * 128),
                             r3(WSL["q"] + kc * 1024 + wz * 512, 512),
                             start=(kc == 0), stop=False)
                last = e.matmul(ps[:, wz * 512:(wz + 1) * 512], onesrow[0:1, :],
                                SG[0:1, 2048 + wz * 512:2048 + (wz + 1) * 512],
                                start=False, stop=True)
            return last
        TE(atmm, P.wd("d_ct", 8) + P.wd("d_p4c", 1) + P.wd("d_bor", 1) + P.w("s_x2", tt - 1),
           [P.i("s_at")])

        def x2ev(e, tt=tt, ps=ps, p=p):
            e.scalar_tensor_tensor(FR[:, p * 1024:(p + 1) * 1024],
                                   ps[:, :], 1.0, FR[:, p * 1024:(p + 1) * 1024],
                                   ALU.mult, ALU.add)
            e.bn_stats(bnb[:, p * 12:p * 12 + 6], FR[:, p * 1024:p * 1024 + 512])
            e.bn_stats(bnb[:, p * 12 + 6:p * 12 + 12],
                       FR[:, p * 1024 + 512:(p + 1) * 1024])
            return e.bn_aggr(stats[:, p * 2:p * 2 + 2],
                             bnb[:, p * 12:p * 12 + 12])
        V(x2ev, P.w("s_at", tt + 1) + P.wd("d_xr", tt + 1),
          [P.i("s_x2"), P.i("s_st")])
        SC(lambda e, p=p: e.activation(stats[:, 4 + p:5 + p],
                                       stats[:, p * 2 + 1:p * 2 + 2],
                                       AF.Sqrt, bias=EPS, scale=1.0),
           P.w("s_st", NT + tt + 1), [P.i("s_sq")])

        def tev(e, p=p):
            e.reciprocal(stats[:, 6 + p:7 + p], stats[:, 4 + p:5 + p])
            return e.tensor_scalar(
                SG[:, p * 1024:(p + 1) * 1024], FR[:, p * 1024:(p + 1) * 1024],
                stats[:, p * 2:p * 2 + 1], stats[:, 6 + p:7 + p],
                ALU.subtract, ALU.mult)
        V(tev, P.w("s_sq", NT + tt + 1) + (P.w("s_p4t", tt - 1) if tt >= 2 else []),
          [P.i("s_ln")])
        dma("gpsimd", "d_x2w", x2_hbm[tt * 128:(tt + 1) * 128, :],
            FR[:, p * 1024:(p + 1) * 1024], waits=P.w("s_x2", tt + 1))
        if tt >= 1:
            emit_p4t(tt - 1)
    emit_p4t(NT - 1)

    barrier()

    for kc in range(KC):
        V(lambda e, kc=kc: (
            e.tensor_scalar(r1(kc, 0, 4096), r1(kc, 0, 4096),
                            cb1c[:, kc:kc + 1], sbc[:, kc:kc + 1],
                            ALU.mult, ALU.add),
            e.tensor_reduce(mxs[:, kc:kc + 1], r1(kc, 0, 4096), axis=AX.X,
                            op=ALU.max, apply_absolute_value=True))[-1],
          P.w("s_p4e", NT), [P.i("s_f0a")])
    V(lambda e: e.tensor_reduce(stats[:, 14:15], mxs[:, :], axis=AX.X, op=ALU.max),
      P.w("s_f0a", 8), [P.i("s_f0a")])

    def add_collective(idx, reduce_emit, outcols, prwait, repk):
        GP(reduce_emit, prwait, [P.i("s_pr")])
        dma("gpsimd", "d_ccw", cci[idx][0:8, :], scl[0:1, 16:24],
            waits=P.w("s_pr", 2 * idx + 1))
        GP(lambda e, idx=idx: e.collective_compute(
            "AllReduce", ALU.max, replica_groups=[list(range(8))],
            ins=[cci[idx].ap().opt()], outs=[cco[idx].ap().opt()]),
           P.wd("d_ccw", idx + 1), [P.i("s_cc")])
        for rrow, dcol in outcols:
            dma("sync", "d_rep", scl[:, dcol:dcol + 1],
                cco[idx][rrow:rrow + 1, :].partition_broadcast(128).squeeze(1),
                waits=P.w("s_cc", idx + 1))
        return P.wd("d_rep", repk)

    rw = add_collective(0, lambda e: e.tensor_reduce(
        scl[0:1, 16:17], stats[:, 14:15], axis=AX.XYZWC, op=ALU.max),
        [(0, 0)], P.w("s_f0a", 9), 1)
    V(lambda e: (
        e.tensor_scalar(scl[:, 1:2], scl[:, 0:1], QMAXI, 1e-12, ALU.mult, ALU.max),
        e.reciprocal(scl[:, 2:3], scl[:, 1:2]))[-1],
      rw, [P.i("s_scl")])

    for kc in range(KC):
        V(lambda e, kc=kc: (
            e.tensor_scalar(FR[:, 0:2048], r1(kc, 0, 2048),
                            scl[:, 2:3], MAGIC, ALU.mult, ALU.add),
            e.tensor_scalar(r1(kc, 0, 2048), FR[:, 0:2048],
                            -MAGIC, scl[:, 1:2], ALU.add, ALU.mult),
            e.tensor_scalar(FR[:, 0:2048], r1(kc, 2048, 4096),
                            scl[:, 2:3], MAGIC, ALU.mult, ALU.add),
            e.tensor_scalar(r1(kc, 2048, 4096), FR[:, 0:2048],
                            -MAGIC, scl[:, 1:2], ALU.add, ALU.mult))[-1],
          P.w("s_scl", 1), [P.i("s_scl")])

    for kc in range(KC):
        dmaT("sync", "d_wgt", r2(kc, 0, 3072),
             wgq_hbm[:, kc * 128:(kc + 1) * 128], waits=P.wd("d_dw", 144))
    for kc in range(KC):
        dmaT("sync", "d_wgt", r3(kc * 4096, 3072),
             wuq_hbm[:, kc * 128:(kc + 1) * 128], waits=P.wd("d_dw", 144))

    barrier()

    f1w = [0]
    for mi in range(MI):
        for tc in range(8):
            for srcname, dhbm, mcol in (("g", graw_hbm, 0), ("u", uraw_hbm, 1)):
                wdx = f1w[0]
                f1w[0] += 1
                ps = (PS0, PS1, PS2)[(wdx // 2) % 3]
                po = (wdx % 2) * 512

                def f1mm(e, srcname=srcname, mi=mi, tc=tc, ps=ps, po=po):
                    last = None
                    for kc in range(KC):
                        if srcname == "g":
                            lhs = r2(kc, mi * 128, (mi + 1) * 128)
                        else:
                            lhs = r3(kc * 4096 + mi * 128, 128)
                        last = e.matmul(ps[:, po:po + 512], lhs,
                                        r1(kc, tc * 512, (tc + 1) * 512),
                                        start=(kc == 0), stop=(kc == KC - 1))
                    return last
                TE(f1mm, P.wd("d_wgt", 16) + P.w("s_scl", 9) +
                   P.w("s_f1e", wdx - 5), [P.i("s_f1m")])
                sgo = (wdx % 4) * 512
                SC(lambda e, ps=ps, po=po, sgo=sgo:
                   e.copy(SG[:, sgo:sgo + 512], ps[:, po:po + 512]),
                   P.w("s_f1m", wdx + 1) +
                   (P.wd("d_f1w", wdx - 3) + P.w("s_f1r", wdx - 3) if wdx >= 4 else []),
                   [P.i("s_f1e")])
                dma("scalar", "d_f1w",
                    dhbm[mi * 128:(mi + 1) * 128, tc * 512:(tc + 1) * 512],
                    SG[:, sgo:sgo + 512],
                    waits=(P.wd("d_f1w", wdx - 1) if wdx >= 2 else []))
                V(lambda e, sgo=sgo, mcol=mcol: (
                    e.tensor_reduce(mrg[:, 4:5], SG[:, sgo:sgo + 512], axis=AX.X,
                                    op=ALU.max, apply_absolute_value=True),
                    e.tensor_tensor(mrg[:, mcol:mcol + 1], mrg[:, mcol:mcol + 1],
                                    mrg[:, 4:5], op=ALU.max))[-1],
                  P.w("s_f1e", wdx + 1), [P.i("s_f1r")])

    GP(lambda e: None, P.w("s_f1r", 384), [P.i("s_pr")])
    rw = add_collective(1, lambda e: (
        e.tensor_reduce(scl[0:1, 16:17], mrg[:, 0:1], axis=AX.XYZWC, op=ALU.max),
        e.tensor_reduce(scl[0:1, 17:18], mrg[:, 1:2], axis=AX.XYZWC, op=ALU.max))[-1],
        [(0, 3), (1, 6)], P.w("s_pr", 2), 3)
    V(lambda e: (
        e.tensor_scalar(scl[:, 4:5], scl[:, 3:4], QMAXI, 1e-12, ALU.mult, ALU.max),
        e.reciprocal(scl[:, 5:6], scl[:, 4:5]),
        e.tensor_scalar(scl[:, 7:8], scl[:, 6:7], QMAXI, 1e-12, ALU.mult, ALU.max),
        e.reciprocal(scl[:, 8:9], scl[:, 7:8]),
        e.tensor_single_scalar(scl[:, 18:19], scl[:, 4:5], -MAGIC, ALU.mult))[-1],
      rw, [P.i("s_scl")])

    dma("gpsimd", "d_gbb", SG[:, 2048:3072],
        modh[5:6, :].partition_broadcast(128).squeeze(1), waits=P.w("s_f1m", 384))
    for mi in range(MI):
        dmaT("sync", "d_wdt", R2[:, mi * 1024:(mi + 1) * 1024],
             wdq_hbm[:, mi * 128:(mi + 1) * 128],
             waits=P.wd("d_dw", 144) + P.w("s_f1m", 384) +
             (P.w("s_wd", mi - 1) if mi >= 2 else []))
        V(lambda e, mi=mi:
          e.tensor_tensor(R2[:, mi * 1024:(mi + 1) * 1024],
                          R2[:, mi * 1024:(mi + 1) * 1024], SG[:, 2048:3072],
                          op=ALU.mult),
          P.wd("d_wdt", mi + 1) + P.wd("d_gbb", 1), [P.i("s_wd")])

    barrier()

    NU2 = MI * 4
    for u in range(NU2):
        mi, qq = divmod(u, 4)
        rin = (u % 4) * 1024
        psA = (PS0, PS1)[u % 2]
        frb = (u % 2) * 1024
        dma("sync", "d_f2r", R1[:, rin:rin + 1024],
            graw_hbm[mi * 128:(mi + 1) * 128, qq * 1024:(qq + 1) * 1024],
            waits=(P.wd("d_f2r", u - 1) if u >= 2 else []) +
            (P.wd("d_f2w", u - 3) if u >= 4 else []))
        SC(lambda e, rin=rin, psA=psA, frb=frb: (
            e.activation(psA[:, :], R1[:, rin:rin + 1024], AF.Identity,
                         bias=MAGIC, scale=scl[:, 5:6]),
            e.activation(FR[:, frb:frb + 1024], psA[:, :], AF.Sigmoid,
                         bias=scl[:, 18:19], scale=scl[:, 4:5]))[-1],
           P.wd("d_f2r", u + 1) + P.w("s_scl", 10) +
           (P.w("s_2v", u - 1) if u >= 2 else []),
           [P.i("s_2s")])
        V(lambda e, rin=rin, psA=psA, frb=frb: (
            e.tensor_scalar(FR[:, frb:frb + 1024], FR[:, frb:frb + 1024],
                            65536.0, MAGIC, ALU.mult, ALU.add),
            e.tensor_scalar(FR[:, frb:frb + 1024], FR[:, frb:frb + 1024],
                            -MAGIC, 65535.0, ALU.add, ALU.min),
            e.scalar_tensor_tensor(R1[:, rin:rin + 1024], psA[:, :], -MAGIC,
                                   FR[:, frb:frb + 1024], ALU.add, ALU.mult))[-1],
          P.w("s_2s", u + 1), [P.i("s_2v")])
        for v in ([u - 1] if u >= 1 else []) + ([u] if u == NU2 - 1 else []):
            vm, vq = divmod(v, 4)
            vr = (v % 4) * 1024
            V(lambda e, v=v, vr=vr: e.tensor_reduce(
                deq[:, v:v + 1], R1[:, vr:vr + 1024],
                axis=AX.X, op=ALU.max, apply_absolute_value=True),
              P.w("s_2v", v + 1), [P.i("s_2r")])
            dma("gpsimd", "d_f2w",
                prod_hbm[vm * 128:(vm + 1) * 128, vq * 1024:(vq + 1) * 1024],
                R1[:, vr:vr + 1024],
                waits=P.w("s_2r", v + 1) + (P.wd("d_f2w", v - 1) if v >= 2 else []))

    V(lambda e: e.tensor_reduce(mrg[:, 2:3], deq[:, 0:NU2], axis=AX.X,
                                op=ALU.max),
      P.w("s_2r", NU2), [P.i("s_2r")])
    GP(lambda e: None, P.w("s_2r", NU2 + 1), [P.i("s_pr")])
    rw = add_collective(2, lambda e: e.tensor_reduce(
        scl[0:1, 16:17], mrg[:, 2:3], axis=AX.XYZWC, op=ALU.max),
        [(0, 9)], P.w("s_pr", 4), 4)
    V(lambda e: (
        e.tensor_tensor(scl[:, 15:16], scl[:, 9:10], scl[:, 4:5], op=ALU.mult),
        e.tensor_scalar(scl[:, 10:11], scl[:, 15:16], QMAXI / 65536.0, 1e-12,
                        ALU.mult, ALU.max),
        e.tensor_single_scalar(scl[:, 15:16], scl[:, 10:11], 65536.0, ALU.mult),
        e.reciprocal(scl[:, 11:12], scl[:, 15:16]),
        e.tensor_tensor(scl[:, 11:12], scl[:, 11:12], scl[:, 4:5],
                        op=ALU.mult),
        e.tensor_single_scalar(scl[:, 19:20], scl[:, 10:11], -MAGIC,
                               ALU.mult))[-1],
      rw, [P.i("s_scl")])

    barrier()

    for u in range(NU2):
        mi, qq = divmod(u, 4)
        rp = (u % 4) * 1024
        ru = 4096 + (u % 4) * 1024
        fra = (u % 2) * 1024
        psB = (PS0, PS1)[u % 2]
        dma("sync", "d_f3r", R1[:, rp:rp + 1024],
            prod_hbm[mi * 128:(mi + 1) * 128, qq * 1024:(qq + 1) * 1024],
            waits=(P.wd("d_f3r", u - 1) if u >= 2 else []) +
            (P.wd("d_f3w", u - 3) if u >= 4 else []))
        dma("sync", "d_f3u", R1[:, ru:ru + 1024],
            uraw_hbm[mi * 128:(mi + 1) * 128, qq * 1024:(qq + 1) * 1024],
            waits=(P.wd("d_f3u", u - 1) if u >= 2 else []) +
            (P.w("s_31", u - 3) if u >= 4 else []))
        SC(lambda e, rp=rp, ru=ru, fra=fra, psB=psB: (
            e.activation(psB[:, :], R1[:, rp:rp + 1024], AF.Identity,
                         bias=MAGIC, scale=scl[:, 11:12]),
            e.activation(FR[:, fra:fra + 1024], R1[:, ru:ru + 1024], AF.Identity,
                         bias=MAGIC, scale=scl[:, 8:9]))[-1],
           P.wd("d_f3r", u + 1) + P.wd("d_f3u", u + 1) + P.w("s_scl", 11) +
           (P.w("s_3v", u - 1) if u >= 2 else []),
           [P.i("s_31")])
        V(lambda e, rp=rp, fra=fra, psB=psB: (
            e.tensor_scalar(FR[:, fra:fra + 1024], FR[:, fra:fra + 1024],
                            -MAGIC, scl[:, 7:8], ALU.add, ALU.mult),
            e.scalar_tensor_tensor(R1[:, rp:rp + 1024], psB[:, :], -MAGIC,
                                   FR[:, fra:fra + 1024], ALU.add, ALU.mult))[-1],
          P.w("s_31", u + 1), [P.i("s_3v")])
        for v in ([u - 1] if u >= 1 else []) + ([u] if u == NU2 - 1 else []):
            vm, vq = divmod(v, 4)
            vr = (v % 4) * 1024
            V(lambda e, v=v, vr=vr: e.tensor_reduce(
                deq[:, v:v + 1], R1[:, vr:vr + 1024],
                axis=AX.X, op=ALU.max, apply_absolute_value=True),
              P.w("s_3v", v + 1), [P.i("s_3r")])
            dma("gpsimd", "d_f3w",
                usc_hbm[vm * 128:(vm + 1) * 128, vq * 1024:(vq + 1) * 1024],
                R1[:, vr:vr + 1024],
                waits=P.w("s_3r", v + 1) + (P.wd("d_f3w", v - 1) if v >= 2 else []))

    V(lambda e: e.tensor_reduce(mrg[:, 3:4], deq[:, 0:NU2], axis=AX.X,
                                op=ALU.max),
      P.w("s_3r", NU2), [P.i("s_3r")])
    GP(lambda e: None, P.w("s_3r", NU2 + 1), [P.i("s_pr")])
    rw = add_collective(3, lambda e: e.tensor_reduce(
        scl[0:1, 16:17], mrg[:, 3:4], axis=AX.XYZWC, op=ALU.max),
        [(0, 12)], P.w("s_pr", 6), 5)
    V(lambda e: (
        e.tensor_tensor(scl[:, 21:22], scl[:, 12:13], scl[:, 10:11], op=ALU.mult),
        e.tensor_scalar(scl[:, 13:14], scl[:, 21:22], QMAXI, 1e-12, ALU.mult, ALU.max),
        e.reciprocal(scl[:, 14:15], scl[:, 13:14]),
        e.tensor_tensor(scl[:, 20:21], scl[:, 14:15], scl[:, 10:11],
                        op=ALU.mult))[-1],
      rw, [P.i("s_scl")])

    barrier()

    def emit_f4q(tc):
        sp = (tc % 2) * 12288
        for mi in range(MI):
            un = tc * MI + mi
            p = un % 2
            dma("sync", "d_u4", SG[:, p * 512:(p + 1) * 512],
                usc_hbm[mi * 128:(mi + 1) * 128, tc * 512:(tc + 1) * 512],
                waits=(P.w("s_41", un - 1) if un >= 2 else []))
            SC(lambda e, p=p:
               e.activation(PS2[:, p * 512:(p + 1) * 512],
                            SG[:, p * 512:(p + 1) * 512], AF.Identity,
                            bias=MAGIC, scale=scl[:, 20:21]),
               P.wd("d_u4", un + 1) + P.w("s_scl", 12) +
               (P.w("s_42", un - 1) if un >= 2 else []), [P.i("s_41")])
            V(lambda e, sp=sp, mi=mi, p=p:
              e.tensor_scalar(R3[:, sp + mi * 512:sp + (mi + 1) * 512],
                              PS2[:, p * 512:(p + 1) * 512],
                              -MAGIC, scl[:, 13:14], ALU.add, ALU.mult),
              P.w("s_41", un + 1) +
              (P.w("s_f4m", (tc - 1) * 4) if tc >= 2 and mi == 0 else []),
              [P.i("s_42")])

    def emit_f4mm(tc):
        sp = (tc % 2) * 12288
        for tt in range(4):
            tg = tc * 4 + tt
            ps = (PS0, PS1)[tg % 2]
            p = tg % 2
            dma("sync", "d_x2r", FR[:, p * 1024:(p + 1) * 1024],
                x2_hbm[tg * 128:(tg + 1) * 128, :],
                waits=(P.w("s_f4e", tg - 1) + P.wd("d_out", tg - 1)
                       if tg >= 2 else P.wd("d_x2w", NT)))

            def f4mm(e, sp=sp, tt=tt, ps=ps):
                last = None
                for wz in range(2):
                    for mi in range(MI):
                        last = e.matmul(
                            ps[:, wz * 512:(wz + 1) * 512],
                            R3[:, sp + mi * 512 + tt * 128:sp + mi * 512 + (tt + 1) * 128],
                            R2[:, mi * 1024 + wz * 512:mi * 1024 + (wz + 1) * 512],
                            start=(mi == 0), stop=(mi == MI - 1))
                return last
            TE(f4mm, P.w("s_42", (tc + 1) * MI) + P.w("s_wd", MI) +
               P.w("s_f4e", tg - 1), [P.i("s_f4m")])
            V(lambda e, p=p, ps=ps:
              e.scalar_tensor_tensor(FR[:, p * 1024:(p + 1) * 1024],
                                     ps[:, :], 1.0, FR[:, p * 1024:(p + 1) * 1024],
                                     ALU.mult, ALU.add),
              P.w("s_f4m", tg + 1) + P.wd("d_x2r", tg + 1),
              [P.i("s_f4e")])
            dma("scalar", "d_out", out_hbm[tg * 128:(tg + 1) * 128, :],
                FR[:, p * 1024:(p + 1) * 1024], waits=P.w("s_f4e", tg + 1))

    for tc in range(8):
        emit_f4q(tc)
        if tc >= 1:
            emit_f4mm(tc - 1)
    emit_f4mm(7)

    for name, dsttap in tap_outs.items():
        src = tap_srcs[name]
        dma("gpsimd", "d_tap", dsttap[:, :], src[:, :],
            waits=P.wd("d_out") + P.wd("d_f3w") + P.wd("d_dw") + P.wd("d_f2w") +
            P.wd("d_f1w") + P.wd("d_mr") + P.wd("d_mw") + P.wd("d_cw") +
            P.wd("d_x2w") + P.wd("d_tw") + P.wd("d_vw") + P.wd("d_h1w") +
            P.wd("d_wow"))

    SC(lambda e: None, P.wd("d_out"))
    if taps:
        GP(lambda e: None, P.wd("d_tap"))

    for E in ENGS:
        for waits, emit, incs in (
                (s for s in P.steps[E] if not isinstance(s[1], tuple))):
            for (n, v) in waits:
                if n == "__tpos__":
                    continue
                assert v <= P.cnt.get(n, 0), (
                    f"{E}: wait {n}>={v} unreachable (final {P.cnt.get(n, 0)})")

    inc_engs = {}
    for E in ENGS:
        for step in P.steps[E]:
            if isinstance(step[1], tuple):
                continue
            for (n, v) in step[2]:
                inc_engs.setdefault(n, set()).add(E)
    mergeable = {
        n for n, es in inc_engs.items()
        if len(es) == 1 and next(iter(es)) != "gpsimd"
        and not n.startswith("d_") and n not in NOMERGE
    }
    trans = {}
    phys_final = {E: 0 for E in ENGS}
    for E in ENGS:
        pos = 0
        lcnt = {}
        for step in P.steps[E]:
            if isinstance(step[1], tuple):
                continue
            stepm = [(n, v) for (n, v) in step[2] if n in mergeable]
            pos_end = pos + sum(v for (n, v) in stepm)
            for (n, v) in stepm:
                lcnt[n] = lcnt.get(n, 0) + v
                trans[(n, lcnt[n])] = pos_end
            pos = pos_end
        phys_final[E] = pos
    eng_of = {n: next(iter(es)) for n, es in inc_engs.items() if n in mergeable}

    sems = {}
    for E in ENGS:
        sems["@" + E] = e_(nc.semaphore("se_" + E))
    for n in P.cnt:
        if n not in mergeable:
            sems[n] = e_(nc.semaphore(n))
    sbar = e_(nc.semaphore("sbar"))

    def xlate(n, v):
        if n == "__tpos__":
            return ("@tensor", v)
        if n in mergeable:
            return ("@" + eng_of[n], trans[(n, v)])
        return (n, v)

    _NODRAIN = {"dma_start", "dma_start_transpose", "collective_compute",
                "wait_ge", "sem_inc", "drain", "nop", "engine_nop"}

    class _DrainProxy:

        def __init__(self, e):
            object.__setattr__(self, "_e", e)

        def __getattr__(self, name):
            f = getattr(self._e, name)
            if not callable(f) or name in _NODRAIN or name.startswith("_"):
                return f

            def wrap(*a, **kw):
                r = f(*a, **kw)
                if r is not None and hasattr(r, "then_inc"):
                    self._e.drain()
                return r
            return wrap

    def make_closure(E):
        def closure(raw_e):
            e = raw_e if E == "tensor" else _DrainProxy(raw_e)
            for waits, emit, incs in P.steps[E]:
                if isinstance(emit, tuple) and emit[0] == "BARRIER":
                    for (n, v) in emit[2]:
                        if v > 0:
                            raw_e.wait_ge(sems[n], v)
                    raw_e.sem_inc(sbar, 1)
                    raw_e.wait_ge(sbar, 5 * emit[1])
                    continue
                for (n, v) in waits:
                    if v > 0:
                        pn, pv = xlate(n, v)
                        raw_e.wait_ge(sems[pn], pv)
                inst = emit(e)
                agg = {}
                for (n, v) in incs:
                    pn = "@" + eng_of[n] if n in mergeable else n
                    agg[pn] = agg.get(pn, 0) + v
                if inst is None and agg:
                    inst = raw_e.nop()
                for pn, v in agg.items():
                    inst = inst.then_inc(sems[pn], v)
        return closure

    block = e_(nc.Block())
    block.sync(make_closure("sync"))
    block.vector(make_closure("vector"))
    block.scalar(make_closure("scalar"))
    block.tensor(make_closure("tensor"))
    block.gpsimd(make_closure("gpsimd"))

    st_.close()
    return nc


_NC_CACHE = {}


def _get_nc(taps=()):
    key = tuple(sorted(taps))
    if key not in _NC_CACHE:
        _NC_CACHE[key] = build_nc(taps=key)
    return _NC_CACHE[key]


def make_in_maps(inputs):
    y = np.ascontiguousarray(np.asarray(inputs["L0_attn_y_all"], dtype=np.float32))
    c = np.asarray(inputs["L0_c"], dtype=np.float32)
    base = {
        "wmod": np.ascontiguousarray(np.asarray(inputs["L0_wmod"], np.float32)),
        "wq": np.ascontiguousarray(np.asarray(inputs["L0_attn_Wq"], np.float32)),
        "wk": np.ascontiguousarray(np.asarray(inputs["L0_attn_Wk"], np.float32)),
        "wv": np.ascontiguousarray(np.asarray(inputs["L0_attn_Wv"], np.float32)),
        "wo": np.ascontiguousarray(np.asarray(inputs["L0_attn_Wo"], np.float32)),
        "bq": np.ascontiguousarray(np.asarray(inputs["L0_attn_bq"], np.float32).reshape(1, H)),
        "bk": np.ascontiguousarray(np.asarray(inputs["L0_attn_bk"], np.float32).reshape(1, H)),
        "bv": np.ascontiguousarray(np.asarray(inputs["L0_attn_bv"], np.float32).reshape(1, H)),
        "bo": np.ascontiguousarray(np.asarray(inputs["L0_attn_bo"], np.float32).reshape(1, H)),
        "wg": np.ascontiguousarray(np.asarray(inputs["L0_ffn_w_gate"], np.float32)),
        "wu": np.ascontiguousarray(np.asarray(inputs["L0_ffn_w_up"], np.float32)),
        "wd": np.ascontiguousarray(np.asarray(inputs["L0_ffn_w_down"], np.float32)),
    }
    in_maps = []
    for b in range(8):
        m = dict(base)
        m["x"] = np.ascontiguousarray(y[b])
        m["ccol"] = np.ascontiguousarray(c[b].reshape(8, 128).T)
        in_maps.append(m)
    return in_maps


def kernel(**inputs):
    nc = _get_nc()
    in_maps = make_in_maps(inputs)
    res = run_bass_kernel_spmd(nc, in_maps, core_ids=list(range(8)))
    return np.stack([np.asarray(r["out"], dtype=np.float32) for r in res.results])


# version 14
# speedup vs baseline: 1.0272x; 1.0081x over previous
import numpy as np
from contextlib import ExitStack

import concourse.bass as bass
import concourse.mybir as mybir
from concourse.bass_utils import run_bass_kernel_spmd

F32 = mybir.dt.float32
BF16 = mybir.dt.bfloat16
AF = mybir.ActivationFunctionType
ALU = mybir.AluOpType
AX = mybir.AxisListType

MAGIC = 12582912.0
NEG_BIG = -1.0e30
T, NT, H, KC, I3, MI = 4096, 32, 1024, 8, 3072, 24
EPS = 1e-5
QMAXI = 1.0 / 32767.0
ENGS = ("sync", "vector", "scalar", "tensor", "gpsimd")

PARITY = {
    "d_x", "d_h1w", "d_vw", "d_cw", "d_xr", "d_x2w", "d_tw", "d_d", "d_dw",
    "d_tt", "d_wdt", "d_f1w", "d_f2r", "d_f2w", "d_f3r", "d_f3u", "d_f3w",
    "d_u4", "d_x2r", "d_out", "d_da", "d_daw", "d_ds", "d_d2",
}
NOMERGE = {"s_ev2"}


class Prog:
    def __init__(self):
        self.cnt = {}
        self.tpos = 0
        self.steps = {E: [] for E in ENGS}
        self.stream_eng = {}

    def _reg(self, n):
        if n not in self.cnt:
            self.cnt[n] = 0

    def w(self, n, k=None):
        self._reg(n)
        c = self.cnt[n] if k is None else k
        return [(n, c)]

    def i(self, n):
        self._reg(n)
        self.cnt[n] += 1
        return (n, 1)

    def dcount(self, n):
        if n in PARITY:
            self._reg(n + "_A")
            self._reg(n + "_B")
            return (self.cnt[n + "_A"] + self.cnt[n + "_B"]) // 16
        self._reg(n)
        return self.cnt[n] // 16

    def dinc(self, n):
        tot = self.dcount(n)
        if n in PARITY:
            pn = n + ("_A" if tot % 2 == 0 else "_B")
            self.cnt[pn] += 16
            return (pn, 16)
        self.cnt[n] += 16
        return (n, 16)

    def wd(self, n, k=None):
        tot = self.dcount(n)
        if k is None:
            k = tot
        k = max(0, k)
        if k == 0:
            return []
        if n in PARITY:
            ka = (k + 1) // 2
            kb = k // 2
            out = [(n + "_A", 16 * ka)]
            if kb > 0:
                out.append((n + "_B", 16 * kb))
            return out
        return [(n, 16 * k)]

    def st(self, eng, emit, waits=(), incs=()):
        waits = tuple(waits)
        if eng == "tensor":
            ncomp = sum(v for (n, v) in incs if not n.startswith("d_"))
            if ncomp:
                if self.tpos > 0:
                    waits = waits + (("__tpos__", self.tpos),)
                self.tpos += ncomp
        self.steps[eng].append((waits, emit, tuple(incs)))


def att_chunks(i):
    out, j0 = [], 0
    while j0 <= i:
        nj = min(8, i + 1 - j0)
        out.append((j0, nj))
        j0 += nj
    return out


def build_nc(taps=()):
    nc = bass.Bass("TRN2", target_bir_lowering=False)

    x_in = nc.dram_tensor("x", [T, H], F32, kind="ExternalInput")
    ccol_in = nc.dram_tensor("ccol", [128, 8], F32, kind="ExternalInput")
    wmod_in = nc.dram_tensor("wmod", [H, 6 * H], F32, kind="ExternalInput")
    wq_in = nc.dram_tensor("wq", [H, H], F32, kind="ExternalInput")
    wk_in = nc.dram_tensor("wk", [H, H], F32, kind="ExternalInput")
    wv_in = nc.dram_tensor("wv", [H, H], F32, kind="ExternalInput")
    wo_in = nc.dram_tensor("wo", [H, H], F32, kind="ExternalInput")
    bq_in = nc.dram_tensor("bq", [1, H], F32, kind="ExternalInput")
    bk_in = nc.dram_tensor("bk", [1, H], F32, kind="ExternalInput")
    bv_in = nc.dram_tensor("bv", [1, H], F32, kind="ExternalInput")
    bo_in = nc.dram_tensor("bo", [1, H], F32, kind="ExternalInput")
    wg_in = nc.dram_tensor("wg", [I3, H], F32, kind="ExternalInput")
    wu_in = nc.dram_tensor("wu", [I3, H], F32, kind="ExternalInput")
    wd_in = nc.dram_tensor("wd", [H, I3], F32, kind="ExternalInput")
    out_hbm = nc.dram_tensor("out", [T, H], F32, kind="ExternalOutput")

    h1_hbm = nc.dram_tensor("h1s", [T, H], BF16)
    v_hbm = nc.dram_tensor("vs", [T, H], BF16)
    ctx_hbm = nc.dram_tensor("ctxs", [T, H], BF16)
    x2_hbm = nc.dram_tensor("x2s", [T, H], F32)
    t_hbm = nc.dram_tensor("ts_", [T, H], BF16)
    wo2_hbm = nc.dram_tensor("wo2s", [128, 8 * H], BF16)
    modraw = nc.dram_tensor("modraw", [1, 6 * H], F32)
    modh = nc.dram_tensor("modh", [12, H], F32)
    wgq_hbm = nc.dram_tensor("wgq", [I3, H], BF16)
    wuq_hbm = nc.dram_tensor("wuq", [I3, H], BF16)
    wdq_hbm = nc.dram_tensor("wdq", [H, I3], BF16)
    graw_hbm = nc.dram_tensor("graw", [I3, T], BF16)
    uraw_hbm = nc.dram_tensor("uraw", [I3, T], BF16)
    prod_hbm = nc.dram_tensor("prod", [I3, T], BF16)
    usc_hbm = nc.dram_tensor("usc", [I3, T], BF16)
    cci = [nc.dram_tensor(f"cci{j}", [8, 1], F32) for j in range(4)]
    cco = [nc.dram_tensor(f"cco{j}", [8, 1], F32) for j in range(4)]

    tap_srcs = {
        "h1": h1_hbm, "v": v_hbm, "ctx": ctx_hbm, "x2": x2_hbm,
        "modraw": modraw, "modh": modh, "wgq": wgq_hbm, "wuq": wuq_hbm,
        "wdq": wdq_hbm, "graw": graw_hbm, "uraw": uraw_hbm, "prod": prod_hbm,
        "usc": usc_hbm, "wo2": wo2_hbm,
    }
    tap_outs = {
        name: nc.dram_tensor("tap_" + name, list(tap_srcs[name].shape),
                             tap_srcs[name].dtype, kind="ExternalOutput")
        for name in taps
    }

    st_ = ExitStack()
    e_ = st_.enter_context

    R1 = e_(nc.sbuf_tensor("R1", [128, 32768], BF16))
    R2 = e_(nc.sbuf_tensor("R2", [128, 32768], BF16))
    R3 = e_(nc.sbuf_tensor("R3", [128, 32768], BF16))
    SG = e_(nc.sbuf_tensor("SGb", [128, 3072], BF16))
    FR = e_(nc.sbuf_tensor("FRf", [128, 2048], F32))
    idn = e_(nc.sbuf_tensor("idn", [128, 128], BF16))
    maskd = e_(nc.sbuf_tensor("maskd", [128, 128], BF16))
    onesrow = e_(nc.sbuf_tensor("onesrow", [1, 128], BF16))
    ccol = e_(nc.sbuf_tensor("ccolS", [128, 8], F32))
    cbf = e_(nc.sbuf_tensor("cbfS", [128, 8], BF16))
    ca1c = e_(nc.sbuf_tensor("ca1c", [128, 8], F32))
    sacbf = e_(nc.sbuf_tensor("sacbf", [128, 8], BF16))
    cb1c = e_(nc.sbuf_tensor("cb1c", [128, 8], F32))
    sbc = e_(nc.sbuf_tensor("sbc", [128, 8], F32))
    bqc = e_(nc.sbuf_tensor("bqc", [128, 8], F32))
    bkc = e_(nc.sbuf_tensor("bkc", [128, 8], F32))
    mxs = e_(nc.sbuf_tensor("mxs", [128, 8], F32))
    stats = e_(nc.sbuf_tensor("stats", [128, 16], F32))
    scl = e_(nc.sbuf_tensor("scl", [128, 24], F32))
    mrg = e_(nc.sbuf_tensor("mrg", [128, 8], F32))
    bnb = e_(nc.sbuf_tensor("bnb", [128, 24], F32))
    deq = e_(nc.sbuf_tensor("deq", [128, 128], F32))
    cst = e_(nc.sbuf_tensor("cst", [128, 6], F32))
    nc.const_aps.register(F32, EPS, cst[:, 0:1])
    nc.const_aps.register(F32, MAGIC, cst[:, 1:2])
    nc.const_aps.register(F32, 65536.0, cst[:, 2:3])
    nc.const_aps.register(F32, -MAGIC, cst[:, 3:4])
    nc.const_aps.register(F32, -1.0, cst[:, 4:5])

    PS0 = e_(nc.psum_tensor("PS0", [128, 1024], F32))
    PS1 = e_(nc.psum_tensor("PS1", [128, 1024], F32))
    PS2 = e_(nc.psum_tensor("PS2", [128, 1024], F32))
    PS3 = e_(nc.psum_tensor("PS3", [128, 2048], BF16))

    P = Prog()

    def r1(kc, a, b):
        return R1[:, kc * 4096 + a: kc * 4096 + b]

    def r2(kc, a, b):
        return R2[:, kc * 4096 + a: kc * 4096 + b]

    def r3(off, n):
        return R3[:, off: off + n]

    def fr(row, a, b):
        return FR[row:row + 1, a:b]

    def dma(eng, stream, out, in_, waits=(), slow=False):
        def em(e, out=out, in_=in_, slow=slow):
            if slow:
                return e.dma_start(out=out, in_=in_, allow_slow_non_contiguous=True)
            return e.dma_start(out=out, in_=in_)
        P.stream_eng[stream] = eng
        P.st(eng, em, waits, [P.dinc(stream)])

    def dmaT(eng, stream, out, in_, waits=()):
        P.stream_eng[stream] = eng
        P.st(eng, lambda e, o=out, i=in_: e.dma_start_transpose(o, i), waits,
             [P.dinc(stream)])

    nbar = [0]

    def barrier():
        nbar[0] += 1
        for E in ENGS:
            flush = []
            for sname, seng in P.stream_eng.items():
                if seng == E:
                    flush.extend(P.wd(sname))
            P.st(E, ("BARRIER", nbar[0], tuple(flush)))

    def V(emit, waits=(), incs=()):
        P.st("vector", emit, waits, incs)

    def SC(emit, waits=(), incs=()):
        P.st("scalar", emit, waits, incs)

    def TE(emit, waits=(), incs=()):
        P.st("tensor", emit, waits, incs)

    def GP(emit, waits=(), incs=()):
        P.st("gpsimd", emit, waits, incs)

    GP(lambda e: e.memset(idn[:, :], 0.0), [], [P.i("s_cst")])
    GP(lambda e: e.affine_select(out=idn[:, :], in_=idn[:, :],
                                 compare_op=ALU.not_equal, fill=1.0, base=0,
                                 pattern=[[-1, 128]], channel_multiplier=1),
       P.w("s_cst", 1), [P.i("s_cst")])
    GP(lambda e: e.memset(maskd[:, :], 0.0), [], [P.i("s_cst")])
    GP(lambda e: e.affine_select(out=maskd[:, :], in_=maskd[:, :],
                                 compare_op=ALU.is_ge, fill=NEG_BIG, base=0,
                                 pattern=[[-1, 128]], channel_multiplier=1),
       P.w("s_cst", 3), [P.i("s_cst")])
    GP(lambda e: e.memset(onesrow[:, :], 1.0), [], [P.i("s_cst")])
    GP(lambda e: e.memset(scl[:, 16:24], 0.0), [], [P.i("s_cst")])
    GP(lambda e: e.memset(mrg[:, 0:8], 0.0), [], [P.i("s_cst")])
    GP(lambda e: e.memset(cst[:, 0:1], EPS), [], [P.i("s_cst")])
    GP(lambda e: e.memset(cst[:, 1:2], MAGIC), [], [P.i("s_cst")])
    GP(lambda e: e.memset(cst[:, 2:3], 65536.0), [], [P.i("s_cst")])
    GP(lambda e: e.memset(cst[:, 3:4], -MAGIC), [], [P.i("s_cst")])
    GP(lambda e: e.memset(cst[:, 4:5], -1.0), [], [P.i("s_cst")])
    dma("sync", "d_p0", ccol[:, :], ccol_in[:, :])
    SC(lambda e: e.activation(cbf[:, :], ccol[:, :], AF.Silu),
       P.wd("d_p0", 1), [P.i("s_silu")])
    for kc in range(KC):
        dma("gpsimd", "d_wm", r3(kc * 2048, 2048),
            wmod_in[kc * 128:(kc + 1) * 128, 0:2048])

    for r in range(3):
        def mm0(e, r=r):
            last = None
            for w in range(4):
                ps = PS0 if w < 2 else PS1
                wo_ = (w % 2) * 512
                for kc in range(KC):
                    last = e.matmul(
                        ps[0:1, wo_:wo_ + 512], cbf[:, kc:kc + 1],
                        r3((r % 2) * 16384 + kc * 2048 + (w // 2) * 1024 + wo_, 512),
                        start=(kc == 0), stop=(kc == KC - 1))
            return last
        TE(mm0, P.wd("d_wm", 8 * (r + 1)) + P.w("s_silu", 1) + P.w("s_ev0", r),
           [P.i("s_mm0")])

    for t in range(NT):
        p = t % 2
        dma("sync", "d_x", FR[:, p * 1024:(p + 1) * 1024],
            x_in[t * 128:(t + 1) * 128, :],
            waits=(P.w("s_ln", t - 1) if t >= 2 else []))

        def bnst(e, p=p):
            e.bn_stats(bnb[:, p * 12:p * 12 + 6], FR[:, p * 1024:p * 1024 + 512])
            e.bn_stats(bnb[:, p * 12 + 6:p * 12 + 12],
                       FR[:, p * 1024 + 512:(p + 1) * 1024])
            return e.bn_aggr(stats[:, p * 2:p * 2 + 2],
                             bnb[:, p * 12:p * 12 + 12])
        V(bnst, P.wd("d_x", t + 1), [P.i("s_st")])
        SC(lambda e, p=p: e.activation(stats[:, 4 + p:5 + p],
                                       stats[:, p * 2 + 1:p * 2 + 2],
                                       AF.Sqrt, bias=EPS, scale=1.0),
           P.w("s_st", t + 1) + P.w("s_cst", 10), [P.i("s_sq")])

        def lnt(e, p=p):
            e.reciprocal(stats[:, 6 + p:7 + p], stats[:, 4 + p:5 + p])
            return e.tensor_scalar(
                SG[:, p * 1024:(p + 1) * 1024], FR[:, p * 1024:(p + 1) * 1024],
                stats[:, p * 2:p * 2 + 1], stats[:, 6 + p:7 + p],
                ALU.subtract, ALU.mult)
        V(lnt, P.w("s_sq", t + 1) + (P.wd("d_h1w", t - 1) if t >= 2 else []),
          [P.i("s_ln")])
        dma("gpsimd", "d_h1w", h1_hbm[t * 128:(t + 1) * 128, :],
            SG[:, p * 1024:(p + 1) * 1024], waits=P.w("s_ln", t + 1))

    for r in (1, 2):
        for kc in range(KC):
            dma("gpsimd", "d_wm", r3((r % 2) * 16384 + kc * 2048, 2048),
                wmod_in[kc * 128:(kc + 1) * 128, r * 2048:(r + 1) * 2048],
                waits=P.w("s_mm0", r))

    for r in range(3):
        V(lambda e: (e.tensor_copy(fr(0, 0, 1024), PS0[0:1, :]),
                     e.tensor_copy(fr(0, 1024, 2048), PS1[0:1, :]))[-1],
          P.w("s_mm0", r + 1) + P.w("s_ln", NT) + P.wd("d_mr", r),
          [P.i("s_ev0")])
        dma("scalar", "d_mr", modraw[0:1, r * 2048:(r + 1) * 2048],
            FR[0:1, 0:2048], waits=P.w("s_ev0", r + 1) + P.wd("d_mr", r))

    AS, BS = (0, 1024), (1024, 2048)

    def _row_sign_max(e, a, b):
        g = fr(0, *a)
        sgn = fr(0, *b)
        e.tensor_single_scalar(sgn, g, 0.0, ALU.is_ge)
        e.tensor_scalar(sgn, sgn, 2.0, -1.0, ALU.mult, ALU.add)
        e.scalar_tensor_tensor(g, g, -1.0, g, ALU.mult, ALU.max)
        e.tensor_scalar_max(g, g, 0.25)
        return e.tensor_tensor(g, g, sgn, op=ALU.mult)

    dma("sync", "d_rr", fr(0, *AS), modraw[0:1, 2 * H:3 * H], waits=P.wd("d_mr", 3))
    V(lambda e: _row_sign_max(e, AS, BS), P.wd("d_rr", 1), [P.i("s_row")])
    dma("scalar", "d_mw", modh[4:5, :], fr(0, *AS), waits=P.w("s_row", 1))
    dma("sync", "d_rr", fr(0, *AS), modraw[0:1, 5 * H:6 * H],
        waits=P.w("s_row", 1) + P.wd("d_mw", 1))
    V(lambda e: _row_sign_max(e, AS, BS), P.wd("d_rr", 2), [P.i("s_row")])
    dma("scalar", "d_mw", modh[5:6, :], fr(0, *AS), waits=P.w("s_row", 2))
    dma("sync", "d_rr", fr(0, *AS), modraw[0:1, H:2 * H],
        waits=P.w("s_row", 2) + P.wd("d_mw", 2))
    V(lambda e: e.tensor_scalar_add(fr(0, *AS), fr(0, *AS), 1.0),
      P.wd("d_rr", 3), [P.i("s_row")])
    dma("scalar", "d_mw", modh[0:1, :], fr(0, *AS), waits=P.w("s_row", 3))
    dma("sync", "d_rr", fr(0, *AS), modraw[0:1, 4 * H:5 * H],
        waits=P.w("s_row", 3) + P.wd("d_mw", 3))
    V(lambda e: e.tensor_scalar_add(fr(0, *AS), fr(0, *AS), 1.0),
      P.wd("d_rr", 4), [P.i("s_row")])
    dma("scalar", "d_mw", modh[2:3, :], fr(0, *AS), waits=P.w("s_row", 4))
    dma("gpsimd", "d_md", modh[1:2, :], modraw[0:1, 0:H], waits=P.wd("d_mr", 3))
    dma("gpsimd", "d_md", modh[3:4, :], modraw[0:1, 3 * H:4 * H],
        waits=P.wd("d_mr", 3))
    dma("sync", "d_rr", fr(0, *AS), bo_in[0:1, :],
        waits=P.w("s_row", 4) + P.wd("d_mw", 4))
    dma("sync", "d_rr", fr(0, *BS), modh[4:5, :],
        waits=P.w("s_row", 4) + P.wd("d_mw", 4))
    V(lambda e: e.tensor_tensor(fr(0, *AS), fr(0, *AS), fr(0, *BS), op=ALU.mult),
      P.wd("d_rr", 6), [P.i("s_row")])
    dma("scalar", "d_mw", modh[9:10, :], fr(0, *AS), waits=P.w("s_row", 5))
    for dst, row in ((ca1c, 0), (cb1c, 2), (sbc, 3)):
        dma("sync", "d_cols", dst[:, :],
            bass.AP(modh, row * H, [[1, 128], [128, 8]]),
            waits=P.wd("d_mw", 5) + P.wd("d_md", 2), slow=True)
    dma("gpsimd", "d_sac", sacbf[:, :],
        bass.AP(modh, 1 * H, [[1, 128], [128, 8]]),
        waits=P.wd("d_mw", 5) + P.wd("d_md", 2), slow=True)

    barrier()

    WSL = {"q": 8192, "k": 16384, "v": 24576}

    DUNITS = []
    for wsrc, wdst, npt, wcols in ((wg_in, wgq_hbm, 24, 1024),
                                   (wu_in, wuq_hbm, 24, 1024),
                                   (wd_in, wdq_hbm, 8, 3072)):
        for pt in range(npt):
            for hh in range(wcols // 512):
                DUNITS.append((wsrc, wdst, pt, hh))
    ND = len(DUNITS)
    nd_ld, nd_cp, npt_ = [0], [0], [0]

    NDA = 48

    def dslot(u):
        return (u % 4) * 512 if u < NDA else 1024 + (u % 2) * 512

    def emit_dload(u):
        wsrc, wdst, pt, hh = DUNITS[u]
        fro = dslot(u)
        if u < NDA:
            rw = ((P.wd("d_d", u - 1) if u >= 2 else
                   P.w("s_w2", 5) + P.wd("d_bw", 3)) +
                  (P.wd("d_dw", u - 3) if u >= 4 else []))
            dma("gpsimd", "d_d", FR[:, fro:fro + 512],
                wsrc[pt * 128:(pt + 1) * 128, hh * 512:(hh + 1) * 512], waits=rw)
        else:
            k = u - NDA
            rw = ((P.wd("d_d2", k - 1) if k >= 2 else P.wd("d_dw", NDA)) +
                  (P.wd("d_dw", u - 1) if k >= 2 else []))
            dma("sync", "d_d2", FR[:, fro:fro + 512],
                wsrc[pt * 128:(pt + 1) * 128, hh * 512:(hh + 1) * 512], waits=rw)

    def emit_dcomp(u):
        wsrc, wdst, pt, hh = DUNITS[u]
        fro = dslot(u)
        c0 = (u % 4) * 32

        def dv(e, fro=fro, c0=c0):
            e.tensor_reduce(deq[:, c0:c0 + 16],
                            FR[:, fro:fro + 512].rearrange("q (b j) -> q b j", j=32),
                            axis=AX.X, op=ALU.max, apply_absolute_value=True)
            e.tensor_scalar(deq[:, c0:c0 + 16], deq[:, c0:c0 + 16],
                            1.0 / 7.0, 1e-12, ALU.mult, ALU.max)
            return e.reciprocal(deq[:, c0 + 16:c0 + 32], deq[:, c0:c0 + 16])
        ldw = (P.wd("d_d", u + 1) if u < NDA else P.wd("d_d2", u - NDA + 1))
        V(dv, ldw + (P.w("s_dg", u - 3) if u >= 4 else []),
          [P.i("s_da")])
        GP(lambda e, fro=fro, c0=c0: (
            e.tensor_tensor(FR[:, fro:fro + 512].rearrange("q (b j) -> q b j", j=32),
                            FR[:, fro:fro + 512].rearrange("q (b j) -> q b j", j=32),
                            deq[:, c0 + 16:c0 + 32].unsqueeze(2)
                            .broadcast_to((128, 16, 32)), op=ALU.mult),
            e.tensor_tensor(FR[:, fro:fro + 512], FR[:, fro:fro + 512],
                            cst[:, 1:2].broadcast_to((128, 512)), op=ALU.add),
            e.tensor_tensor(FR[:, fro:fro + 512], FR[:, fro:fro + 512],
                            cst[:, 1:2].broadcast_to((128, 512)),
                            op=ALU.subtract),
            e.tensor_tensor(FR[:, fro:fro + 512].rearrange("q (b j) -> q b j", j=32),
                            FR[:, fro:fro + 512].rearrange("q (b j) -> q b j", j=32),
                            deq[:, c0:c0 + 16].unsqueeze(2)
                            .broadcast_to((128, 16, 32)), op=ALU.mult))[-1],
           P.w("s_da", u + 1) + P.w("s_dg", u), [P.i("s_dg")])
        dma("gpsimd", "d_dw",
            wdst[pt * 128:(pt + 1) * 128, hh * 512:(hh + 1) * 512],
            FR[:, fro:fro + 512], waits=P.w("s_dg", u + 1) +
            (P.wd("d_dw", u - 1) if u >= 2 else []))

    def dpoint():
        p = npt_[0]
        npt_[0] += 1
        lt = min(NDA, 2 + (p + 1) // 2)
        ct = min(NDA, 2 + (p - 1) // 2) if p >= 2 else 0
        while nd_ld[0] < lt:
            emit_dload(nd_ld[0])
            nd_ld[0] += 1
        while nd_cp[0] < ct:
            emit_dcomp(nd_cp[0])
            nd_cp[0] += 1

    def dflush():
        while nd_ld[0] < NDA:
            emit_dload(nd_ld[0])
            nd_ld[0] += 1
        while nd_cp[0] < NDA:
            emit_dcomp(nd_cp[0])
            nd_cp[0] += 1

    for wn, wsrc in (("q", wq_in), ("k", wk_in), ("v", wv_in)):
        dma("gpsimd", "d_w", r3(WSL[wn], 8192).rearrange("p (k c) -> p k c", c=1024),
            wsrc.ap().rearrange("(k p) c -> p k c", p=128))

    for wi, (wn, b_in, mrow, divq) in enumerate(
            (("q", bq_in, 6, True), ("k", bk_in, 7, False), ("v", bv_in, 8, False))):
        def sawmm(e, wn=wn):
            last = None
            for kc in range(KC):
                e.matmul(PS2[0:1, 0:512], sacbf[:, kc:kc + 1],
                         r3(WSL[wn] + kc * 1024, 512),
                         start=(kc == 0), stop=(kc == KC - 1))
                last = e.matmul(PS2[0:1, 512:1024], sacbf[:, kc:kc + 1],
                                r3(WSL[wn] + kc * 1024 + 512, 512),
                                start=(kc == 0), stop=(kc == KC - 1))
            return last
        TE(sawmm, P.wd("d_w", 3) + P.wd("d_cols", 3) + P.wd("d_sac", 1) + P.w("s_w2", 2 * wi),
           [P.i("s_saw")])
        dma("sync", "d_br", fr(0, *AS), b_in[0:1, :],
            waits=P.w("s_row", 5) + P.w("s_w2", 2 * wi))

        def sawev(e, divq=divq):
            last = e.scalar_tensor_tensor(fr(0, *BS), PS2[0:1, :], 1.0, fr(0, *AS),
                                          ALU.mult, ALU.add)
            if divq:
                last = e.tensor_scalar_mul(fr(0, *BS), fr(0, *BS), 1.0 / 32.0)
            return last
        V(sawev, P.w("s_saw", wi + 1) + P.wd("d_br", wi + 1) + P.wd("d_bw", wi),
          [P.i("s_w2")])
        dma("scalar", "d_bw", modh[mrow:mrow + 1, :], fr(0, *BS),
            waits=P.w("s_w2", 2 * wi + 1))

        def wscale(e, wn=wn, divq=divq):
            last = None
            for kc in range(KC):
                if divq:
                    last = e.tensor_scalar(
                        r3(WSL[wn] + kc * 1024, 1024),
                        r3(WSL[wn] + kc * 1024, 1024),
                        ca1c[:, kc:kc + 1], 1.0 / 32.0, ALU.mult, ALU.mult)
                else:
                    last = e.tensor_single_scalar(
                        r3(WSL[wn] + kc * 1024, 1024),
                        r3(WSL[wn] + kc * 1024, 1024),
                        ca1c[:, kc:kc + 1], ALU.mult)
            return last
        V(wscale, P.w("s_saw", wi + 1), [P.i("s_w2")])

    dma("sync", "d_bc", bqc[:, :],
        bass.AP(modh, 6 * H, [[1, 128], [128, 8]]), waits=P.wd("d_bw", 3), slow=True)
    dma("sync", "d_bc", bkc[:, :],
        bass.AP(modh, 7 * H, [[1, 128], [128, 8]]), waits=P.wd("d_bw", 3), slow=True)
    dma("gpsimd", "d_bvb", SG[:, 2048:3072],
        modh[8:9, :].partition_broadcast(128).squeeze(1), waits=P.wd("d_bw", 3))

    nwin = [0]
    WINFO = []
    nsc, nvv = [0], [0]

    def _reuse_wait(wdx):
        if wdx < 4:
            return []
        k, o = WINFO[wdx - 4]
        return P.w("s_ev2s" if k == "s" else "s_ev2v", o)

    for tq in range(4):
        for kc in range(KC):
            dmaT("sync", "d_h1t", r3(kc * 1024, 1024),
                 h1_hbm[tq * 1024:(tq + 1) * 1024, kc * 128:(kc + 1) * 128],
                 waits=P.wd("d_h1w", 8 * (tq + 1)) +
                 (P.w("s_saw", 3 + tq) if tq >= 1 else []))
        for wn, dstR in (("q", r1), ("k", r2)):
            bcol = bqc if wn == "q" else bkc
            for mc in range(KC):
                for wz in range(2):
                    wdx = nwin[0]
                    nwin[0] += 1
                    ps = (PS0, PS1)[(wdx // 2) % 2]
                    po = (wdx % 2) * 512
                    rw = _reuse_wait(wdx)
                    nsc[0] += 1
                    WINFO.append(("s", nsc[0]))

                    def qkm(e, wn=wn, mc=mc, wz=wz, ps=ps, po=po):
                        last = None
                        for kc in range(KC):
                            last = e.matmul(ps[:, po:po + 512],
                                            r3(WSL[wn] + kc * 1024 + mc * 128, 128),
                                            r3(kc * 1024 + wz * 512, 512),
                                            start=(kc == 0), stop=(kc == KC - 1))
                        return last
                    TE(qkm, P.wd("d_h1t", 8 * (tq + 1)) +
                       P.w("s_w2", 2 if wn == "q" else 4) + rw, [P.i("s_mm2")])
                    SC(lambda e, dstR=dstR, mc=mc, tq=tq, wz=wz, ps=ps, po=po, bcol=bcol:
                       e.activation(dstR(mc, tq * 1024 + wz * 512,
                                         tq * 1024 + wz * 512 + 512),
                                    ps[:, po:po + 512], AF.Identity,
                                    bias=bcol[:, mc:mc + 1], scale=1.0),
                       P.w("s_mm2", wdx + 1) + P.wd("d_bc", 2), [P.i("s_ev2s")])
                    dpoint()
        for tt in range(8):
            gt = tq * 8 + tt
            for wz in range(2):
                wdx = nwin[0]
                nwin[0] += 1
                ps = (PS0, PS1)[(wdx // 2) % 2]
                po = (wdx % 2) * 512
                rw = _reuse_wait(wdx)
                nvv[0] += 1
                WINFO.append(("v", nvv[0]))

                def vmm(e, tt=tt, wz=wz, ps=ps, po=po):
                    last = None
                    for kc in range(KC):
                        last = e.matmul(ps[:, po:po + 512],
                                        r3(kc * 1024 + tt * 128, 128),
                                        r3(WSL["v"] + kc * 1024 + wz * 512, 512),
                                        start=(kc == 0), stop=(kc == KC - 1))
                    return last
                TE(vmm, P.w("s_w2", 6) + rw, [P.i("s_mm2")])
                V(lambda e, gt=gt, wz=wz, ps=ps, po=po:
                  e.scalar_tensor_tensor(
                      SG[:, (gt % 2) * 1024 + wz * 512:(gt % 2) * 1024 + wz * 512 + 512],
                      ps[:, po:po + 512], 1.0,
                      SG[:, 2048 + wz * 512:2048 + (wz + 1) * 512],
                      ALU.mult, ALU.add),
                  P.w("s_mm2", wdx + 1) + P.wd("d_bvb", 1) +
                  (P.wd("d_vw", gt - 1) if gt >= 2 else []),
                  [P.i("s_ev2v")])
                dpoint()
            dma("gpsimd", "d_vw", v_hbm[gt * 128:(gt + 1) * 128, :],
                SG[:, (gt % 2) * 1024:(gt % 2) * 1024 + 1024],
                waits=P.w("s_ev2v", 16 * tq + 2 * tt + 2))
            dpoint()
        TE(lambda e: None, [], [P.i("s_saw")])

    dma("gpsimd", "d_w", r3(WSL["q"], 8192).rearrange("p (k c) -> p k c", c=1024),
        wo_in.ap().rearrange("(k p) c -> p k c", p=128), waits=P.w("s_mm2", 192))
    dma("gpsimd", "d_gab", SG[:, 0:1024],
        modh[4:5, :].partition_broadcast(128).squeeze(1), waits=P.wd("d_vw", NT))

    def woscale(e):
        last = None
        for kc in range(KC):
            last = e.tensor_tensor(r3(WSL["q"] + kc * 1024, 1024),
                                   r3(WSL["q"] + kc * 1024, 1024),
                                   SG[:, 0:1024], op=ALU.mult)
        return last
    V(woscale, P.wd("d_w", 4) + P.wd("d_gab", 1), [P.i("s_w2")])
    dma("gpsimd", "d_wow", wo2_hbm[:, :], r3(WSL["q"], 8192),
        waits=P.w("s_w2", 7))
    dflush()

    barrier()

    for jj in range(8):
        dma("sync", "d_vr", r3(jj * 4096, 4096).rearrange("p (j c) -> p j c", c=1024),
            v_hbm[jj * 512:(jj + 1) * 512, :].rearrange("(j p) c -> p j c", p=128),
            waits=P.wd("d_vw", NT) + P.wd("d_wow", 1))

    CHUNKS = []
    for i in range(NT):
        cs = att_chunks(i)
        for ci, (j0, nj) in enumerate(cs):
            CHUNKS.append((i, j0, nj, ci == 0, ci == len(cs) - 1))
    G = len(CHUNKS)
    NI = [0]


    TE_ORDER = []
    for g in range(G):
        TE_ORDER.append(("S", g))
        if g >= 1:
            TE_ORDER.append(("ptt", g - 1))
        if g >= 2:
            TE_ORDER.append(("ctx", g - 2))
    TE_ORDER += [("ptt", G - 1), ("ctx", G - 2), ("ctx", G - 1)]
    TPOS = {kg: i + 1 for i, kg in enumerate(TE_ORDER)}

    V_ORDER = []
    for g in range(G):
        V_ORDER += [("vmax", g), ("PTevac", g)]
        if g >= 1:
            V_ORDER.append(("vmerge", g - 1))
    V_ORDER.append(("vmerge", G - 1))
    VPOS = {kg: i + 1 for i, kg in enumerate(V_ORDER)}

    def emit_S(g):
        i, j0, nj, first, last = CHUNKS[g]
        ps = (PS0, PS1)[g % 2]

        def Smm(e, i=i, j0=j0, nj=nj, ps=ps):
            last_ = None
            for kc in range(KC):
                c0 = 0
                while c0 < nj * 128:
                    cw = min(512, nj * 128 - c0)
                    last_ = e.matmul(ps[:, c0:c0 + cw],
                                     r1(kc, i * 128, (i + 1) * 128),
                                     r2(kc, j0 * 128 + c0, j0 * 128 + c0 + cw),
                                     start=(kc == 0), stop=(kc == KC - 1))
                    c0 += cw
            return last_
        TE(Smm, P.w("s_exp", g - 1), [P.i("s_T3")])

    def emit_ptt(g):
        i, j0, nj, first, last = CHUNKS[g]
        pg = g % 2

        def ptt(e, nj=nj, pg=pg):
            last_ = None
            for jj in range(nj):
                last_ = e.transpose(
                    PS3[:, pg * 1024 + jj * 128:pg * 1024 + (jj + 1) * 128],
                    SG[:, pg * 1024 + jj * 128:pg * 1024 + (jj + 1) * 128],
                    idn[:, :])
            return last_
        TE(ptt, P.w("s_exp", g + 1) +
           (P.w("s_v3", VPOS[("PTevac", g - 1)]) if g >= 1 else []),
           [P.i("s_T3")])

    def emit_ctx(g):
        i, j0, nj, first, last = CHUNKS[g]
        pg = g % 2

        def ctxmm(e, j0=j0, nj=nj, pg=pg):
            last_ = None
            for jj in range(nj):
                for wz in range(2):
                    last_ = e.matmul(
                        PS2[:, wz * 512:(wz + 1) * 512],
                        SG[:, pg * 1024 + jj * 128:pg * 1024 + (jj + 1) * 128],
                        r3((j0 + jj) * 1024 + wz * 512, 512),
                        start=(jj == 0), stop=(jj == nj - 1))
            return last_
        cw = (P.w("s_v3", VPOS[("vmerge", g - 1)]) if g >= 1
              else P.w("s_v3", VPOS[("PTevac", 0)]))
        TE(ctxmm, cw + P.wd("d_vr", 8), [P.i("s_T3")])

    def emit_vme(g):
        i, j0, nj, first, last = CHUNKS[g]
        ps = (PS0, PS1)[g % 2]
        pg = g % 2
        w = nj * 128

        def vmax(e, i=i, j0=j0, first=first, last=last, ps=ps, pg=pg, w=w):
            if last:
                dof = (i - j0) * 128
                e.tensor_tensor(ps[:, dof:dof + 128], ps[:, dof:dof + 128],
                                maskd[:, :], op=ALU.add)
            if first:
                e.tensor_reduce(stats[:, 14 + pg:15 + pg], ps[:, 0:w], axis=AX.X,
                                op=ALU.max)
            else:
                e.tensor_reduce(stats[:, 4 + pg:5 + pg], ps[:, 0:w], axis=AX.X,
                                op=ALU.max)
                e.tensor_single_scalar(stats[:, 14 + pg:15 + pg],
                                       stats[:, 4 + pg:5 + pg], stats[:, 0:1],
                                       ALU.max)
                e.tensor_single_scalar(stats[:, 8 + pg:9 + pg], stats[:, 0:1],
                                       stats[:, 14 + pg:15 + pg], ALU.subtract)
            return e.tensor_copy(stats[:, 0:1], stats[:, 14 + pg:15 + pg])
        V(vmax, P.w("s_T3", TPOS[("S", g)]), [P.i("s_v3")])

        def sexp(e, first=first, ps=ps, pg=pg, w=w):
            e.activation(stats[:, 10 + pg:11 + pg], stats[:, 14 + pg:15 + pg],
                         AF.Identity, scale=-1.0)
            if not first:
                e.activation(stats[:, 12 + pg:13 + pg], stats[:, 8 + pg:9 + pg], AF.Exp)
            return e.activation(SG[:, pg * 1024:pg * 1024 + w], ps[:, 0:w], AF.Exp,
                                bias=stats[:, 10 + pg:11 + pg], scale=1.0,
                                accum_out=stats[:, 6 + pg:7 + pg])
        sw = P.w("s_v3", VPOS[("vmax", g)])
        if g >= 2:
            sw = sw + P.w("s_T3", TPOS[("ctx", g - 2)])
        SC(sexp, sw, [P.i("s_exp")])

        V(lambda e, nj=nj, pg=pg:
          e.tensor_copy(SG[:, pg * 1024:pg * 1024 + nj * 128],
                        PS3[:, pg * 1024:pg * 1024 + nj * 128]),
          P.w("s_T3", TPOS[("ptt", g)]), [P.i("s_v3")])

    def emit_vmerge(g):
        i, j0, nj, first, last = CHUNKS[g]
        pg = g % 2

        def vmerge(e, i=i, first=first, last=last, pg=pg):
            last_ = None
            if first:
                e.tensor_copy(FR[:, 0:1024], PS2[:, :])
                last_ = e.tensor_copy(stats[:, 1:2], stats[:, 6 + pg:7 + pg])
            else:
                e.scalar_tensor_tensor(FR[:, 0:1024], FR[:, 0:1024],
                                       stats[:, 12 + pg:13 + pg], PS2[:, :],
                                       ALU.mult, ALU.add)
                last_ = e.scalar_tensor_tensor(stats[:, 1:2], stats[:, 1:2],
                                               stats[:, 12 + pg:13 + pg],
                                               stats[:, 6 + pg:7 + pg],
                                               ALU.mult, ALU.add)
            if last:
                e.reciprocal(stats[:, 3:4], stats[:, 1:2])
                last_ = e.tensor_single_scalar(SG[:, 2048:3072], FR[:, 0:1024],
                                               stats[:, 3:4], ALU.mult)
            return last_
        mwaits = P.w("s_T3", TPOS[("ctx", g)])
        if last and i >= 1:
            mwaits = mwaits + P.wd("d_cw", i)
        V(vmerge, mwaits, [P.i("s_v3")] + ([P.i("s_cbf")] if last else []))
        if last:
            NI[0] += 1
            dma("gpsimd", "d_cw", ctx_hbm[i * 128:(i + 1) * 128, :],
                SG[:, 2048:3072], waits=P.w("s_cbf", NI[0]))

    for g in range(G):
        emit_S(g)
        if g >= 1:
            emit_ptt(g - 1)
        if g >= 2:
            emit_ctx(g - 2)
        emit_vme(g)
        if g >= 1:
            emit_vmerge(g - 1)
        t2 = NDA + min(ND - NDA, max(0, ((g - 1) * 5) // 4))
        while nd_ld[0] < t2:
            emit_dload(nd_ld[0])
            nd_ld[0] += 1
            emit_dcomp(nd_cp[0])
            nd_cp[0] += 1
    while nd_cp[0] < ND:
        if nd_ld[0] < ND:
            emit_dload(nd_ld[0])
            nd_ld[0] += 1
        emit_dcomp(nd_cp[0])
        nd_cp[0] += 1
    emit_ptt(G - 1)
    emit_ctx(G - 2)
    emit_ctx(G - 1)
    emit_vmerge(G - 1)

    barrier()

    for kc in range(KC):
        dmaT("sync", "d_ct", r2(kc, 0, 4096),
             ctx_hbm[:, kc * 128:(kc + 1) * 128], waits=P.wd("d_cw", NT))
    dma("sync", "d_p4c", r3(WSL["q"], 8192), wo2_hbm[:, :], waits=P.wd("d_wow", 1))
    dma("gpsimd", "d_bor", SG[0:1, 2048:3072], modh[9:10, :],
        waits=P.wd("d_cw", NT))

    def emit_p4t(tt):
        p = tt % 2

        def ptt4(e, tt=tt, p=p):
            last = None
            for kc in range(KC):
                last = e.transpose(
                    PS3[:, p * 1024 + kc * 128:p * 1024 + (kc + 1) * 128],
                    SG[:, p * 1024 + kc * 128:p * 1024 + (kc + 1) * 128],
                    idn[:, :])
            return last
        TE(ptt4, P.w("s_ln", NT + tt + 1) +
           (P.w("s_p4e", tt - 1) if tt >= 2 else []), [P.i("s_p4t")])
        V(lambda e, tt=tt, p=p: e.tensor_copy(
            R1[:, :].rearrange("q (k c) -> q k c", c=4096)[:, :, tt * 128:(tt + 1) * 128],
            PS3[:, p * 1024:(p + 1) * 1024].rearrange("q (k c) -> q k c", c=128)),
          P.w("s_p4t", tt + 1), [P.i("s_p4e")])

    for tt in range(NT):
        ps = (PS0, PS1)[tt % 2]
        p = tt % 2
        dma("sync", "d_xr", FR[:, p * 1024:(p + 1) * 1024],
            x_in[tt * 128:(tt + 1) * 128, :],
            waits=(P.w("s_ln", NT + tt - 1) + P.wd("d_x2w", tt - 1)
                   if tt >= 2 else []))

        def atmm(e, tt=tt, ps=ps):
            last = None
            for wz in range(2):
                for kc in range(KC):
                    e.matmul(ps[:, wz * 512:(wz + 1) * 512],
                             r2(kc, tt * 128, (tt + 1) * 128),
                             r3(WSL["q"] + kc * 1024 + wz * 512, 512),
                             start=(kc == 0), stop=False)
                last = e.matmul(ps[:, wz * 512:(wz + 1) * 512], onesrow[0:1, :],
                                SG[0:1, 2048 + wz * 512:2048 + (wz + 1) * 512],
                                start=False, stop=True)
            return last
        TE(atmm, P.wd("d_ct", 8) + P.wd("d_p4c", 1) + P.wd("d_bor", 1) + P.w("s_x2", tt - 1),
           [P.i("s_at")])

        def x2ev(e, tt=tt, ps=ps, p=p):
            e.scalar_tensor_tensor(FR[:, p * 1024:(p + 1) * 1024],
                                   ps[:, :], 1.0, FR[:, p * 1024:(p + 1) * 1024],
                                   ALU.mult, ALU.add)
            e.bn_stats(bnb[:, p * 12:p * 12 + 6], FR[:, p * 1024:p * 1024 + 512])
            e.bn_stats(bnb[:, p * 12 + 6:p * 12 + 12],
                       FR[:, p * 1024 + 512:(p + 1) * 1024])
            return e.bn_aggr(stats[:, p * 2:p * 2 + 2],
                             bnb[:, p * 12:p * 12 + 12])
        V(x2ev, P.w("s_at", tt + 1) + P.wd("d_xr", tt + 1),
          [P.i("s_x2"), P.i("s_st")])
        SC(lambda e, p=p: e.activation(stats[:, 4 + p:5 + p],
                                       stats[:, p * 2 + 1:p * 2 + 2],
                                       AF.Sqrt, bias=EPS, scale=1.0),
           P.w("s_st", NT + tt + 1), [P.i("s_sq")])

        def tev(e, p=p):
            e.reciprocal(stats[:, 6 + p:7 + p], stats[:, 4 + p:5 + p])
            return e.tensor_scalar(
                SG[:, p * 1024:(p + 1) * 1024], FR[:, p * 1024:(p + 1) * 1024],
                stats[:, p * 2:p * 2 + 1], stats[:, 6 + p:7 + p],
                ALU.subtract, ALU.mult)
        V(tev, P.w("s_sq", NT + tt + 1) + (P.w("s_p4t", tt - 1) if tt >= 2 else []),
          [P.i("s_ln")])
        dma("gpsimd", "d_x2w", x2_hbm[tt * 128:(tt + 1) * 128, :],
            FR[:, p * 1024:(p + 1) * 1024], waits=P.w("s_x2", tt + 1))
        if tt >= 1:
            emit_p4t(tt - 1)
    emit_p4t(NT - 1)

    barrier()

    for kc in range(KC):
        V(lambda e, kc=kc: (
            e.tensor_scalar(r1(kc, 0, 4096), r1(kc, 0, 4096),
                            cb1c[:, kc:kc + 1], sbc[:, kc:kc + 1],
                            ALU.mult, ALU.add),
            e.tensor_reduce(mxs[:, kc:kc + 1], r1(kc, 0, 4096), axis=AX.X,
                            op=ALU.max, apply_absolute_value=True))[-1],
          P.w("s_p4e", NT), [P.i("s_f0a")])
    V(lambda e: e.tensor_reduce(stats[:, 14:15], mxs[:, :], axis=AX.X, op=ALU.max),
      P.w("s_f0a", 8), [P.i("s_f0a")])

    def add_collective(idx, reduce_emit, outcols, prwait, repk):
        GP(reduce_emit, prwait, [P.i("s_pr")])
        dma("gpsimd", "d_ccw", cci[idx][0:8, :], scl[0:1, 16:24],
            waits=P.w("s_pr", 2 * idx + 1))
        GP(lambda e, idx=idx: e.collective_compute(
            "AllReduce", ALU.max, replica_groups=[list(range(8))],
            ins=[cci[idx].ap().opt()], outs=[cco[idx].ap().opt()]),
           P.wd("d_ccw", idx + 1), [P.i("s_cc")])
        for rrow, dcol in outcols:
            dma("sync", "d_rep", scl[:, dcol:dcol + 1],
                cco[idx][rrow:rrow + 1, :].partition_broadcast(128).squeeze(1),
                waits=P.w("s_cc", idx + 1))
        return P.wd("d_rep", repk)

    rw = add_collective(0, lambda e: e.tensor_reduce(
        scl[0:1, 16:17], stats[:, 14:15], axis=AX.XYZWC, op=ALU.max),
        [(0, 0)], P.w("s_f0a", 9), 1)
    V(lambda e: (
        e.tensor_scalar(scl[:, 1:2], scl[:, 0:1], QMAXI, 1e-12, ALU.mult, ALU.max),
        e.reciprocal(scl[:, 2:3], scl[:, 1:2]))[-1],
      rw, [P.i("s_scl")])

    for kc in range(KC):
        V(lambda e, kc=kc: (
            e.tensor_scalar(FR[:, 0:2048], r1(kc, 0, 2048),
                            scl[:, 2:3], MAGIC, ALU.mult, ALU.add),
            e.tensor_scalar(r1(kc, 0, 2048), FR[:, 0:2048],
                            -MAGIC, scl[:, 1:2], ALU.add, ALU.mult),
            e.tensor_scalar(FR[:, 0:2048], r1(kc, 2048, 4096),
                            scl[:, 2:3], MAGIC, ALU.mult, ALU.add),
            e.tensor_scalar(r1(kc, 2048, 4096), FR[:, 0:2048],
                            -MAGIC, scl[:, 1:2], ALU.add, ALU.mult))[-1],
          P.w("s_scl", 1), [P.i("s_scl")])

    for kc in range(KC):
        dmaT("sync", "d_wgt", r2(kc, 0, 3072),
             wgq_hbm[:, kc * 128:(kc + 1) * 128], waits=P.wd("d_dw", 144))
    for kc in range(KC):
        dmaT("sync", "d_wgt", r3(kc * 4096, 3072),
             wuq_hbm[:, kc * 128:(kc + 1) * 128], waits=P.wd("d_dw", 144))

    barrier()

    f1w = [0]
    for mi in range(MI):
        for tc in range(8):
            for srcname, dhbm, mcol in (("g", graw_hbm, 0), ("u", uraw_hbm, 1)):
                wdx = f1w[0]
                f1w[0] += 1
                ps = (PS0, PS1, PS2)[(wdx // 2) % 3]
                po = (wdx % 2) * 512

                def f1mm(e, srcname=srcname, mi=mi, tc=tc, ps=ps, po=po):
                    last = None
                    for kc in range(KC):
                        if srcname == "g":
                            lhs = r2(kc, mi * 128, (mi + 1) * 128)
                        else:
                            lhs = r3(kc * 4096 + mi * 128, 128)
                        last = e.matmul(ps[:, po:po + 512], lhs,
                                        r1(kc, tc * 512, (tc + 1) * 512),
                                        start=(kc == 0), stop=(kc == KC - 1))
                    return last
                TE(f1mm, P.wd("d_wgt", 16) + P.w("s_scl", 9) +
                   P.w("s_f1e", wdx - 5), [P.i("s_f1m")])
                sgo = (wdx % 4) * 512
                SC(lambda e, ps=ps, po=po, sgo=sgo:
                   e.copy(SG[:, sgo:sgo + 512], ps[:, po:po + 512]),
                   P.w("s_f1m", wdx + 1) +
                   (P.wd("d_f1w", wdx - 3) + P.w("s_f1r", wdx - 3) if wdx >= 4 else []),
                   [P.i("s_f1e")])
                dma("scalar", "d_f1w",
                    dhbm[mi * 128:(mi + 1) * 128, tc * 512:(tc + 1) * 512],
                    SG[:, sgo:sgo + 512],
                    waits=(P.wd("d_f1w", wdx - 1) if wdx >= 2 else []))
                V(lambda e, sgo=sgo, mcol=mcol: (
                    e.tensor_reduce(mrg[:, 4:5], SG[:, sgo:sgo + 512], axis=AX.X,
                                    op=ALU.max, apply_absolute_value=True),
                    e.tensor_tensor(mrg[:, mcol:mcol + 1], mrg[:, mcol:mcol + 1],
                                    mrg[:, 4:5], op=ALU.max))[-1],
                  P.w("s_f1e", wdx + 1), [P.i("s_f1r")])

    GP(lambda e: None, P.w("s_f1r", 384), [P.i("s_pr")])
    rw = add_collective(1, lambda e: (
        e.tensor_reduce(scl[0:1, 16:17], mrg[:, 0:1], axis=AX.XYZWC, op=ALU.max),
        e.tensor_reduce(scl[0:1, 17:18], mrg[:, 1:2], axis=AX.XYZWC, op=ALU.max))[-1],
        [(0, 3), (1, 6)], P.w("s_pr", 2), 3)
    V(lambda e: (
        e.tensor_scalar(scl[:, 4:5], scl[:, 3:4], QMAXI, 1e-12, ALU.mult, ALU.max),
        e.reciprocal(scl[:, 5:6], scl[:, 4:5]),
        e.tensor_scalar(scl[:, 7:8], scl[:, 6:7], QMAXI, 1e-12, ALU.mult, ALU.max),
        e.reciprocal(scl[:, 8:9], scl[:, 7:8]),
        e.tensor_single_scalar(scl[:, 18:19], scl[:, 4:5], -MAGIC, ALU.mult))[-1],
      rw, [P.i("s_scl")])

    dma("gpsimd", "d_gbb", SG[:, 2048:3072],
        modh[5:6, :].partition_broadcast(128).squeeze(1), waits=P.w("s_f1m", 384))
    for mi in range(MI):
        dmaT("sync", "d_wdt", R2[:, mi * 1024:(mi + 1) * 1024],
             wdq_hbm[:, mi * 128:(mi + 1) * 128],
             waits=P.wd("d_dw", 144) + P.w("s_f1m", 384) +
             (P.w("s_wd", mi - 1) if mi >= 2 else []))
        V(lambda e, mi=mi:
          e.tensor_tensor(R2[:, mi * 1024:(mi + 1) * 1024],
                          R2[:, mi * 1024:(mi + 1) * 1024], SG[:, 2048:3072],
                          op=ALU.mult),
          P.wd("d_wdt", mi + 1) + P.wd("d_gbb", 1), [P.i("s_wd")])

    barrier()

    NU2 = MI * 4
    for u in range(NU2):
        mi, qq = divmod(u, 4)
        rin = (u % 4) * 1024
        psA = (PS0, PS1)[u % 2]
        frb = (u % 2) * 1024
        dma("sync", "d_f2r", R1[:, rin:rin + 1024],
            graw_hbm[mi * 128:(mi + 1) * 128, qq * 1024:(qq + 1) * 1024],
            waits=(P.wd("d_f2r", u - 1) if u >= 2 else []) +
            (P.wd("d_f2w", u - 3) if u >= 4 else []))
        SC(lambda e, rin=rin, psA=psA, frb=frb: (
            e.activation(psA[:, :], R1[:, rin:rin + 1024], AF.Identity,
                         bias=MAGIC, scale=scl[:, 5:6]),
            e.activation(FR[:, frb:frb + 1024], psA[:, :], AF.Sigmoid,
                         bias=scl[:, 18:19], scale=scl[:, 4:5]))[-1],
           P.wd("d_f2r", u + 1) + P.w("s_scl", 10) +
           (P.w("s_2v", u - 1) if u >= 2 else []),
           [P.i("s_2s")])
        V(lambda e, rin=rin, psA=psA, frb=frb: (
            e.tensor_scalar(FR[:, frb:frb + 1024], FR[:, frb:frb + 1024],
                            65536.0, MAGIC, ALU.mult, ALU.add),
            e.tensor_scalar(FR[:, frb:frb + 1024], FR[:, frb:frb + 1024],
                            -MAGIC, 65535.0, ALU.add, ALU.min),
            e.scalar_tensor_tensor(R1[:, rin:rin + 1024], psA[:, :], -MAGIC,
                                   FR[:, frb:frb + 1024], ALU.add, ALU.mult))[-1],
          P.w("s_2s", u + 1), [P.i("s_2v")])
        for v in ([u - 1] if u >= 1 else []) + ([u] if u == NU2 - 1 else []):
            vm, vq = divmod(v, 4)
            vr = (v % 4) * 1024
            V(lambda e, v=v, vr=vr: e.tensor_reduce(
                deq[:, v:v + 1], R1[:, vr:vr + 1024],
                axis=AX.X, op=ALU.max, apply_absolute_value=True),
              P.w("s_2v", v + 1), [P.i("s_2r")])
            dma("gpsimd", "d_f2w",
                prod_hbm[vm * 128:(vm + 1) * 128, vq * 1024:(vq + 1) * 1024],
                R1[:, vr:vr + 1024],
                waits=P.w("s_2r", v + 1) + (P.wd("d_f2w", v - 1) if v >= 2 else []))

    V(lambda e: e.tensor_reduce(mrg[:, 2:3], deq[:, 0:NU2], axis=AX.X,
                                op=ALU.max),
      P.w("s_2r", NU2), [P.i("s_2r")])
    GP(lambda e: None, P.w("s_2r", NU2 + 1), [P.i("s_pr")])
    rw = add_collective(2, lambda e: e.tensor_reduce(
        scl[0:1, 16:17], mrg[:, 2:3], axis=AX.XYZWC, op=ALU.max),
        [(0, 9)], P.w("s_pr", 4), 4)
    V(lambda e: (
        e.tensor_tensor(scl[:, 15:16], scl[:, 9:10], scl[:, 4:5], op=ALU.mult),
        e.tensor_scalar(scl[:, 10:11], scl[:, 15:16], QMAXI / 65536.0, 1e-12,
                        ALU.mult, ALU.max),
        e.tensor_single_scalar(scl[:, 15:16], scl[:, 10:11], 65536.0, ALU.mult),
        e.reciprocal(scl[:, 11:12], scl[:, 15:16]),
        e.tensor_tensor(scl[:, 11:12], scl[:, 11:12], scl[:, 4:5],
                        op=ALU.mult),
        e.tensor_single_scalar(scl[:, 19:20], scl[:, 10:11], -MAGIC,
                               ALU.mult))[-1],
      rw, [P.i("s_scl")])

    barrier()

    for u in range(NU2):
        mi, qq = divmod(u, 4)
        rp = (u % 4) * 1024
        ru = 4096 + (u % 4) * 1024
        fra = (u % 2) * 1024
        psB = (PS0, PS1)[u % 2]
        dma("sync", "d_f3r", R1[:, rp:rp + 1024],
            prod_hbm[mi * 128:(mi + 1) * 128, qq * 1024:(qq + 1) * 1024],
            waits=(P.wd("d_f3r", u - 1) if u >= 2 else []) +
            (P.wd("d_f3w", u - 3) if u >= 4 else []))
        dma("sync", "d_f3u", R1[:, ru:ru + 1024],
            uraw_hbm[mi * 128:(mi + 1) * 128, qq * 1024:(qq + 1) * 1024],
            waits=(P.wd("d_f3u", u - 1) if u >= 2 else []) +
            (P.w("s_31", u - 3) if u >= 4 else []))
        SC(lambda e, rp=rp, ru=ru, fra=fra, psB=psB: (
            e.activation(psB[:, :], R1[:, rp:rp + 1024], AF.Identity,
                         bias=MAGIC, scale=scl[:, 11:12]),
            e.activation(FR[:, fra:fra + 1024], R1[:, ru:ru + 1024], AF.Identity,
                         bias=MAGIC, scale=scl[:, 8:9]))[-1],
           P.wd("d_f3r", u + 1) + P.wd("d_f3u", u + 1) + P.w("s_scl", 11) +
           (P.w("s_3v", u - 1) if u >= 2 else []),
           [P.i("s_31")])
        V(lambda e, rp=rp, fra=fra, psB=psB: (
            e.tensor_scalar(FR[:, fra:fra + 1024], FR[:, fra:fra + 1024],
                            -MAGIC, scl[:, 7:8], ALU.add, ALU.mult),
            e.scalar_tensor_tensor(R1[:, rp:rp + 1024], psB[:, :], -MAGIC,
                                   FR[:, fra:fra + 1024], ALU.add, ALU.mult))[-1],
          P.w("s_31", u + 1), [P.i("s_3v")])
        for v in ([u - 1] if u >= 1 else []) + ([u] if u == NU2 - 1 else []):
            vm, vq = divmod(v, 4)
            vr = (v % 4) * 1024
            V(lambda e, v=v, vr=vr: e.tensor_reduce(
                deq[:, v:v + 1], R1[:, vr:vr + 1024],
                axis=AX.X, op=ALU.max, apply_absolute_value=True),
              P.w("s_3v", v + 1), [P.i("s_3r")])
            dma("gpsimd", "d_f3w",
                usc_hbm[vm * 128:(vm + 1) * 128, vq * 1024:(vq + 1) * 1024],
                R1[:, vr:vr + 1024],
                waits=P.w("s_3r", v + 1) + (P.wd("d_f3w", v - 1) if v >= 2 else []))

    V(lambda e: e.tensor_reduce(mrg[:, 3:4], deq[:, 0:NU2], axis=AX.X,
                                op=ALU.max),
      P.w("s_3r", NU2), [P.i("s_3r")])
    GP(lambda e: None, P.w("s_3r", NU2 + 1), [P.i("s_pr")])
    rw = add_collective(3, lambda e: e.tensor_reduce(
        scl[0:1, 16:17], mrg[:, 3:4], axis=AX.XYZWC, op=ALU.max),
        [(0, 12)], P.w("s_pr", 6), 5)
    V(lambda e: (
        e.tensor_tensor(scl[:, 21:22], scl[:, 12:13], scl[:, 10:11], op=ALU.mult),
        e.tensor_scalar(scl[:, 13:14], scl[:, 21:22], QMAXI, 1e-12, ALU.mult, ALU.max),
        e.reciprocal(scl[:, 14:15], scl[:, 13:14]),
        e.tensor_tensor(scl[:, 20:21], scl[:, 14:15], scl[:, 10:11],
                        op=ALU.mult))[-1],
      rw, [P.i("s_scl")])

    barrier()

    def emit_f4q(tc):
        sp = (tc % 2) * 12288
        for mi in range(MI):
            un = tc * MI + mi
            p = un % 2
            dma("sync", "d_u4", SG[:, p * 512:(p + 1) * 512],
                usc_hbm[mi * 128:(mi + 1) * 128, tc * 512:(tc + 1) * 512],
                waits=(P.w("s_41", un - 1) if un >= 2 else []))
            SC(lambda e, p=p:
               e.activation(PS2[:, p * 512:(p + 1) * 512],
                            SG[:, p * 512:(p + 1) * 512], AF.Identity,
                            bias=MAGIC, scale=scl[:, 20:21]),
               P.wd("d_u4", un + 1) + P.w("s_scl", 12) +
               (P.w("s_42", un - 1) if un >= 2 else []), [P.i("s_41")])
            V(lambda e, sp=sp, mi=mi, p=p:
              e.tensor_scalar(R3[:, sp + mi * 512:sp + (mi + 1) * 512],
                              PS2[:, p * 512:(p + 1) * 512],
                              -MAGIC, scl[:, 13:14], ALU.add, ALU.mult),
              P.w("s_41", un + 1) +
              (P.w("s_f4m", (tc - 1) * 4) if tc >= 2 and mi == 0 else []),
              [P.i("s_42")])

    def emit_f4mm(tc):
        sp = (tc % 2) * 12288
        for tt in range(4):
            tg = tc * 4 + tt
            ps = (PS0, PS1)[tg % 2]
            p = tg % 2
            dma("sync", "d_x2r", FR[:, p * 1024:(p + 1) * 1024],
                x2_hbm[tg * 128:(tg + 1) * 128, :],
                waits=(P.w("s_f4e", tg - 1) + P.wd("d_out", tg - 1)
                       if tg >= 2 else P.wd("d_x2w", NT)))

            def f4mm(e, sp=sp, tt=tt, ps=ps):
                last = None
                for wz in range(2):
                    for mi in range(MI):
                        last = e.matmul(
                            ps[:, wz * 512:(wz + 1) * 512],
                            R3[:, sp + mi * 512 + tt * 128:sp + mi * 512 + (tt + 1) * 128],
                            R2[:, mi * 1024 + wz * 512:mi * 1024 + (wz + 1) * 512],
                            start=(mi == 0), stop=(mi == MI - 1))
                return last
            TE(f4mm, P.w("s_42", (tc + 1) * MI) + P.w("s_wd", MI) +
               P.w("s_f4e", tg - 1), [P.i("s_f4m")])
            V(lambda e, p=p, ps=ps:
              e.scalar_tensor_tensor(FR[:, p * 1024:(p + 1) * 1024],
                                     ps[:, :], 1.0, FR[:, p * 1024:(p + 1) * 1024],
                                     ALU.mult, ALU.add),
              P.w("s_f4m", tg + 1) + P.wd("d_x2r", tg + 1),
              [P.i("s_f4e")])
            dma("scalar", "d_out", out_hbm[tg * 128:(tg + 1) * 128, :],
                FR[:, p * 1024:(p + 1) * 1024], waits=P.w("s_f4e", tg + 1))

    for tc in range(8):
        emit_f4q(tc)
        if tc >= 1:
            emit_f4mm(tc - 1)
    emit_f4mm(7)

    for name, dsttap in tap_outs.items():
        src = tap_srcs[name]
        dma("gpsimd", "d_tap", dsttap[:, :], src[:, :],
            waits=P.wd("d_out") + P.wd("d_f3w") + P.wd("d_dw") + P.wd("d_f2w") +
            P.wd("d_f1w") + P.wd("d_mr") + P.wd("d_mw") + P.wd("d_cw") +
            P.wd("d_x2w") + P.wd("d_tw") + P.wd("d_vw") + P.wd("d_h1w") +
            P.wd("d_wow"))

    SC(lambda e: None, P.wd("d_out"))
    if taps:
        GP(lambda e: None, P.wd("d_tap"))

    for E in ENGS:
        for waits, emit, incs in (
                (s for s in P.steps[E] if not isinstance(s[1], tuple))):
            for (n, v) in waits:
                if n == "__tpos__":
                    continue
                assert v <= P.cnt.get(n, 0), (
                    f"{E}: wait {n}>={v} unreachable (final {P.cnt.get(n, 0)})")

    inc_engs = {}
    for E in ENGS:
        for step in P.steps[E]:
            if isinstance(step[1], tuple):
                continue
            for (n, v) in step[2]:
                inc_engs.setdefault(n, set()).add(E)
    mergeable = {
        n for n, es in inc_engs.items()
        if len(es) == 1 and next(iter(es)) != "gpsimd"
        and not n.startswith("d_") and n not in NOMERGE
    }
    trans = {}
    phys_final = {E: 0 for E in ENGS}
    for E in ENGS:
        pos = 0
        lcnt = {}
        for step in P.steps[E]:
            if isinstance(step[1], tuple):
                continue
            stepm = [(n, v) for (n, v) in step[2] if n in mergeable]
            pos_end = pos + sum(v for (n, v) in stepm)
            for (n, v) in stepm:
                lcnt[n] = lcnt.get(n, 0) + v
                trans[(n, lcnt[n])] = pos_end
            pos = pos_end
        phys_final[E] = pos
    eng_of = {n: next(iter(es)) for n, es in inc_engs.items() if n in mergeable}

    sems = {}
    for E in ENGS:
        sems["@" + E] = e_(nc.semaphore("se_" + E))
    for n in P.cnt:
        if n not in mergeable:
            sems[n] = e_(nc.semaphore(n))
    sbar = e_(nc.semaphore("sbar"))

    def xlate(n, v):
        if n == "__tpos__":
            return ("@tensor", v)
        if n in mergeable:
            return ("@" + eng_of[n], trans[(n, v)])
        return (n, v)

    _NODRAIN = {"dma_start", "dma_start_transpose", "collective_compute",
                "wait_ge", "sem_inc", "drain", "nop", "engine_nop"}

    class _DrainProxy:

        def __init__(self, e):
            object.__setattr__(self, "_e", e)

        def __getattr__(self, name):
            f = getattr(self._e, name)
            if not callable(f) or name in _NODRAIN or name.startswith("_"):
                return f

            def wrap(*a, **kw):
                r = f(*a, **kw)
                if r is not None and hasattr(r, "then_inc"):
                    self._e.drain()
                return r
            return wrap

    def make_closure(E):
        def closure(raw_e):
            e = raw_e if E == "tensor" else _DrainProxy(raw_e)
            for waits, emit, incs in P.steps[E]:
                if isinstance(emit, tuple) and emit[0] == "BARRIER":
                    for (n, v) in emit[2]:
                        if v > 0:
                            raw_e.wait_ge(sems[n], v)
                    raw_e.sem_inc(sbar, 1)
                    raw_e.wait_ge(sbar, 5 * emit[1])
                    continue
                for (n, v) in waits:
                    if v > 0:
                        pn, pv = xlate(n, v)
                        raw_e.wait_ge(sems[pn], pv)
                inst = emit(e)
                agg = {}
                for (n, v) in incs:
                    pn = "@" + eng_of[n] if n in mergeable else n
                    agg[pn] = agg.get(pn, 0) + v
                if inst is None and agg:
                    inst = raw_e.nop()
                for pn, v in agg.items():
                    inst = inst.then_inc(sems[pn], v)
        return closure

    block = e_(nc.Block())
    block.sync(make_closure("sync"))
    block.vector(make_closure("vector"))
    block.scalar(make_closure("scalar"))
    block.tensor(make_closure("tensor"))
    block.gpsimd(make_closure("gpsimd"))

    st_.close()
    return nc


_NC_CACHE = {}


def _get_nc(taps=()):
    key = tuple(sorted(taps))
    if key not in _NC_CACHE:
        _NC_CACHE[key] = build_nc(taps=key)
    return _NC_CACHE[key]


def make_in_maps(inputs):
    y = np.ascontiguousarray(np.asarray(inputs["L0_attn_y_all"], dtype=np.float32))
    c = np.asarray(inputs["L0_c"], dtype=np.float32)
    base = {
        "wmod": np.ascontiguousarray(np.asarray(inputs["L0_wmod"], np.float32)),
        "wq": np.ascontiguousarray(np.asarray(inputs["L0_attn_Wq"], np.float32)),
        "wk": np.ascontiguousarray(np.asarray(inputs["L0_attn_Wk"], np.float32)),
        "wv": np.ascontiguousarray(np.asarray(inputs["L0_attn_Wv"], np.float32)),
        "wo": np.ascontiguousarray(np.asarray(inputs["L0_attn_Wo"], np.float32)),
        "bq": np.ascontiguousarray(np.asarray(inputs["L0_attn_bq"], np.float32).reshape(1, H)),
        "bk": np.ascontiguousarray(np.asarray(inputs["L0_attn_bk"], np.float32).reshape(1, H)),
        "bv": np.ascontiguousarray(np.asarray(inputs["L0_attn_bv"], np.float32).reshape(1, H)),
        "bo": np.ascontiguousarray(np.asarray(inputs["L0_attn_bo"], np.float32).reshape(1, H)),
        "wg": np.ascontiguousarray(np.asarray(inputs["L0_ffn_w_gate"], np.float32)),
        "wu": np.ascontiguousarray(np.asarray(inputs["L0_ffn_w_up"], np.float32)),
        "wd": np.ascontiguousarray(np.asarray(inputs["L0_ffn_w_down"], np.float32)),
    }
    in_maps = []
    for b in range(8):
        m = dict(base)
        m["x"] = np.ascontiguousarray(y[b])
        m["ccol"] = np.ascontiguousarray(c[b].reshape(8, 128).T)
        in_maps.append(m)
    return in_maps


def kernel(**inputs):
    nc = _get_nc()
    in_maps = make_in_maps(inputs)
    res = run_bass_kernel_spmd(nc, in_maps, core_ids=list(range(8)))
    return np.stack([np.asarray(r["out"], dtype=np.float32) for r in res.results])


# version 15
# speedup vs baseline: 1.0400x; 1.0124x over previous
import numpy as np
from contextlib import ExitStack

import concourse.bass as bass
import concourse.mybir as mybir
from concourse.bass_utils import run_bass_kernel_spmd

F32 = mybir.dt.float32
BF16 = mybir.dt.bfloat16
AF = mybir.ActivationFunctionType
ALU = mybir.AluOpType
AX = mybir.AxisListType

MAGIC = 12582912.0
NEG_BIG = -1.0e30
T, NT, H, KC, I3, MI = 4096, 32, 1024, 8, 3072, 24
EPS = 1e-5
QMAXI = 1.0 / 32767.0
ENGS = ("sync", "vector", "scalar", "tensor", "gpsimd")

PARITY = {
    "d_x", "d_h1w", "d_vw", "d_cw", "d_xr", "d_x2w", "d_tw", "d_d", "d_dw",
    "d_tt", "d_wdt", "d_f1w", "d_f2r", "d_f2w", "d_f3r", "d_f3u", "d_f3w",
    "d_u4", "d_x2r", "d_out", "d_da", "d_daw", "d_ds", "d_d2",
}
NOMERGE = {"s_ev2"}


class Prog:
    def __init__(self):
        self.cnt = {}
        self.tpos = 0
        self.steps = {E: [] for E in ENGS}
        self.stream_eng = {}

    def _reg(self, n):
        if n not in self.cnt:
            self.cnt[n] = 0

    def w(self, n, k=None):
        self._reg(n)
        c = self.cnt[n] if k is None else k
        return [(n, c)]

    def i(self, n):
        self._reg(n)
        self.cnt[n] += 1
        return (n, 1)

    def dcount(self, n):
        if n in PARITY:
            self._reg(n + "_A")
            self._reg(n + "_B")
            return (self.cnt[n + "_A"] + self.cnt[n + "_B"]) // 16
        self._reg(n)
        return self.cnt[n] // 16

    def dinc(self, n):
        tot = self.dcount(n)
        if n in PARITY:
            pn = n + ("_A" if tot % 2 == 0 else "_B")
            self.cnt[pn] += 16
            return (pn, 16)
        self.cnt[n] += 16
        return (n, 16)

    def wd(self, n, k=None):
        tot = self.dcount(n)
        if k is None:
            k = tot
        k = max(0, k)
        if k == 0:
            return []
        if n in PARITY:
            ka = (k + 1) // 2
            kb = k // 2
            out = [(n + "_A", 16 * ka)]
            if kb > 0:
                out.append((n + "_B", 16 * kb))
            return out
        return [(n, 16 * k)]

    def st(self, eng, emit, waits=(), incs=()):
        waits = tuple(waits)
        if eng == "tensor":
            ncomp = sum(v for (n, v) in incs if not n.startswith("d_"))
            if ncomp:
                if self.tpos > 0:
                    waits = waits + (("__tpos__", self.tpos),)
                self.tpos += ncomp
        self.steps[eng].append((waits, emit, tuple(incs)))


def att_chunks(i):
    out, j0 = [], 0
    while j0 <= i:
        nj = min(8, i + 1 - j0)
        out.append((j0, nj))
        j0 += nj
    return out


def build_nc(taps=()):
    nc = bass.Bass("TRN2", target_bir_lowering=False)

    x_in = nc.dram_tensor("x", [T, H], F32, kind="ExternalInput")
    ccol_in = nc.dram_tensor("ccol", [128, 8], F32, kind="ExternalInput")
    wmod_in = nc.dram_tensor("wmod", [H, 6 * H], F32, kind="ExternalInput")
    wq_in = nc.dram_tensor("wq", [H, H], F32, kind="ExternalInput")
    wk_in = nc.dram_tensor("wk", [H, H], F32, kind="ExternalInput")
    wv_in = nc.dram_tensor("wv", [H, H], F32, kind="ExternalInput")
    wo_in = nc.dram_tensor("wo", [H, H], F32, kind="ExternalInput")
    bq_in = nc.dram_tensor("bq", [1, H], F32, kind="ExternalInput")
    bk_in = nc.dram_tensor("bk", [1, H], F32, kind="ExternalInput")
    bv_in = nc.dram_tensor("bv", [1, H], F32, kind="ExternalInput")
    bo_in = nc.dram_tensor("bo", [1, H], F32, kind="ExternalInput")
    wg_in = nc.dram_tensor("wg", [I3, H], F32, kind="ExternalInput")
    wu_in = nc.dram_tensor("wu", [I3, H], F32, kind="ExternalInput")
    wd_in = nc.dram_tensor("wd", [H, I3], F32, kind="ExternalInput")
    out_hbm = nc.dram_tensor("out", [T, H], F32, kind="ExternalOutput")

    h1_hbm = nc.dram_tensor("h1s", [T, H], BF16)
    v_hbm = nc.dram_tensor("vs", [T, H], BF16)
    ctx_hbm = nc.dram_tensor("ctxs", [T, H], BF16)
    x2_hbm = nc.dram_tensor("x2s", [T, H], F32)
    t_hbm = nc.dram_tensor("ts_", [T, H], BF16)
    wo2_hbm = nc.dram_tensor("wo2s", [128, 8 * H], BF16)
    modraw = nc.dram_tensor("modraw", [1, 6 * H], F32)
    modh = nc.dram_tensor("modh", [12, H], F32)
    wgq_hbm = nc.dram_tensor("wgq", [I3, H], BF16)
    wuq_hbm = nc.dram_tensor("wuq", [I3, H], BF16)
    wdq_hbm = nc.dram_tensor("wdq", [H, I3], BF16)
    graw_hbm = nc.dram_tensor("graw", [I3, T], BF16)
    uraw_hbm = nc.dram_tensor("uraw", [I3, T], BF16)
    prod_hbm = nc.dram_tensor("prod", [I3, T], BF16)
    usc_hbm = nc.dram_tensor("usc", [I3, T], BF16)
    cci = [nc.dram_tensor(f"cci{j}", [8, 1], F32) for j in range(4)]
    cco = [nc.dram_tensor(f"cco{j}", [8, 1], F32) for j in range(4)]

    tap_srcs = {
        "h1": h1_hbm, "v": v_hbm, "ctx": ctx_hbm, "x2": x2_hbm,
        "modraw": modraw, "modh": modh, "wgq": wgq_hbm, "wuq": wuq_hbm,
        "wdq": wdq_hbm, "graw": graw_hbm, "uraw": uraw_hbm, "prod": prod_hbm,
        "usc": usc_hbm, "wo2": wo2_hbm,
    }
    tap_outs = {
        name: nc.dram_tensor("tap_" + name, list(tap_srcs[name].shape),
                             tap_srcs[name].dtype, kind="ExternalOutput")
        for name in taps
    }

    st_ = ExitStack()
    e_ = st_.enter_context

    R1 = e_(nc.sbuf_tensor("R1", [128, 32768], BF16))
    R2 = e_(nc.sbuf_tensor("R2", [128, 32768], BF16))
    R3 = e_(nc.sbuf_tensor("R3", [128, 32768], BF16))
    SG = e_(nc.sbuf_tensor("SGb", [128, 3072], BF16))
    FR = e_(nc.sbuf_tensor("FRf", [128, 2048], F32))
    idn = e_(nc.sbuf_tensor("idn", [128, 128], BF16))
    maskd = e_(nc.sbuf_tensor("maskd", [128, 128], BF16))
    onesrow = e_(nc.sbuf_tensor("onesrow", [1, 128], BF16))
    ccol = e_(nc.sbuf_tensor("ccolS", [128, 8], F32))
    cbf = e_(nc.sbuf_tensor("cbfS", [128, 8], BF16))
    ca1c = e_(nc.sbuf_tensor("ca1c", [128, 8], F32))
    sacbf = e_(nc.sbuf_tensor("sacbf", [128, 8], BF16))
    cb1c = e_(nc.sbuf_tensor("cb1c", [128, 8], F32))
    sbc = e_(nc.sbuf_tensor("sbc", [128, 8], F32))
    bqc = e_(nc.sbuf_tensor("bqc", [128, 8], F32))
    bkc = e_(nc.sbuf_tensor("bkc", [128, 8], F32))
    mxs = e_(nc.sbuf_tensor("mxs", [128, 8], F32))
    stats = e_(nc.sbuf_tensor("stats", [128, 16], F32))
    scl = e_(nc.sbuf_tensor("scl", [128, 24], F32))
    mrg = e_(nc.sbuf_tensor("mrg", [128, 8], F32))
    bnb = e_(nc.sbuf_tensor("bnb", [128, 24], F32))
    deq = e_(nc.sbuf_tensor("deq", [128, 128], F32))
    cst = e_(nc.sbuf_tensor("cst", [128, 6], F32))
    nc.const_aps.register(F32, EPS, cst[:, 0:1])
    nc.const_aps.register(F32, MAGIC, cst[:, 1:2])
    nc.const_aps.register(F32, 65536.0, cst[:, 2:3])
    nc.const_aps.register(F32, -MAGIC, cst[:, 3:4])
    nc.const_aps.register(F32, -1.0, cst[:, 4:5])

    PS0 = e_(nc.psum_tensor("PS0", [128, 1024], F32))
    PS1 = e_(nc.psum_tensor("PS1", [128, 1024], F32))
    PS2 = e_(nc.psum_tensor("PS2", [128, 1024], F32))
    PS3 = e_(nc.psum_tensor("PS3", [128, 2048], BF16))

    P = Prog()

    def r1(kc, a, b):
        return R1[:, kc * 4096 + a: kc * 4096 + b]

    def r2(kc, a, b):
        return R2[:, kc * 4096 + a: kc * 4096 + b]

    def r3(off, n):
        return R3[:, off: off + n]

    def fr(row, a, b):
        return FR[row:row + 1, a:b]

    def dma(eng, stream, out, in_, waits=(), slow=False):
        def em(e, out=out, in_=in_, slow=slow):
            if slow:
                return e.dma_start(out=out, in_=in_, allow_slow_non_contiguous=True)
            return e.dma_start(out=out, in_=in_)
        P.stream_eng[stream] = eng
        P.st(eng, em, waits, [P.dinc(stream)])

    def dmaT(eng, stream, out, in_, waits=()):
        P.stream_eng[stream] = eng
        P.st(eng, lambda e, o=out, i=in_: e.dma_start_transpose(o, i), waits,
             [P.dinc(stream)])

    nbar = [0]

    def barrier():
        nbar[0] += 1
        for E in ENGS:
            flush = []
            for sname, seng in P.stream_eng.items():
                if seng == E:
                    flush.extend(P.wd(sname))
            P.st(E, ("BARRIER", nbar[0], tuple(flush)))

    def V(emit, waits=(), incs=()):
        P.st("vector", emit, waits, incs)

    def SC(emit, waits=(), incs=()):
        P.st("scalar", emit, waits, incs)

    def TE(emit, waits=(), incs=()):
        P.st("tensor", emit, waits, incs)

    def GP(emit, waits=(), incs=()):
        P.st("gpsimd", emit, waits, incs)

    GP(lambda e: e.memset(idn[:, :], 0.0), [], [P.i("s_cst")])
    GP(lambda e: e.affine_select(out=idn[:, :], in_=idn[:, :],
                                 compare_op=ALU.not_equal, fill=1.0, base=0,
                                 pattern=[[-1, 128]], channel_multiplier=1),
       P.w("s_cst", 1), [P.i("s_cst")])
    GP(lambda e: e.memset(maskd[:, :], 0.0), [], [P.i("s_cst")])
    GP(lambda e: e.affine_select(out=maskd[:, :], in_=maskd[:, :],
                                 compare_op=ALU.is_ge, fill=NEG_BIG, base=0,
                                 pattern=[[-1, 128]], channel_multiplier=1),
       P.w("s_cst", 3), [P.i("s_cst")])
    GP(lambda e: e.memset(onesrow[:, :], 1.0), [], [P.i("s_cst")])
    GP(lambda e: e.memset(scl[:, 16:24], 0.0), [], [P.i("s_cst")])
    GP(lambda e: e.memset(mrg[:, 0:8], 0.0), [], [P.i("s_cst")])
    GP(lambda e: e.memset(cst[:, 0:1], EPS), [], [P.i("s_cst")])
    GP(lambda e: e.memset(cst[:, 1:2], MAGIC), [], [P.i("s_cst")])
    GP(lambda e: e.memset(cst[:, 2:3], 65536.0), [], [P.i("s_cst")])
    GP(lambda e: e.memset(cst[:, 3:4], -MAGIC), [], [P.i("s_cst")])
    GP(lambda e: e.memset(cst[:, 4:5], -1.0), [], [P.i("s_cst")])
    dma("sync", "d_p0", ccol[:, :], ccol_in[:, :])
    SC(lambda e: e.activation(cbf[:, :], ccol[:, :], AF.Silu),
       P.wd("d_p0", 1), [P.i("s_silu")])
    for kc in range(KC):
        dma("gpsimd", "d_wm", r3(kc * 2048, 2048),
            wmod_in[kc * 128:(kc + 1) * 128, 0:2048])

    for r in range(3):
        def mm0(e, r=r):
            last = None
            for w in range(4):
                ps = PS0 if w < 2 else PS1
                wo_ = (w % 2) * 512
                for kc in range(KC):
                    last = e.matmul(
                        ps[0:1, wo_:wo_ + 512], cbf[:, kc:kc + 1],
                        r3((r % 2) * 16384 + kc * 2048 + (w // 2) * 1024 + wo_, 512),
                        start=(kc == 0), stop=(kc == KC - 1))
            return last
        TE(mm0, P.wd("d_wm", 8 * (r + 1)) + P.w("s_silu", 1) + P.w("s_ev0", r),
           [P.i("s_mm0")])

    for t in range(NT):
        p = t % 2
        dma("sync", "d_x", FR[:, p * 1024:(p + 1) * 1024],
            x_in[t * 128:(t + 1) * 128, :],
            waits=(P.w("s_ln", t - 1) if t >= 2 else []))

        def bnst(e, p=p):
            e.bn_stats(bnb[:, p * 12:p * 12 + 6], FR[:, p * 1024:p * 1024 + 512])
            e.bn_stats(bnb[:, p * 12 + 6:p * 12 + 12],
                       FR[:, p * 1024 + 512:(p + 1) * 1024])
            return e.bn_aggr(stats[:, p * 2:p * 2 + 2],
                             bnb[:, p * 12:p * 12 + 12])
        V(bnst, P.wd("d_x", t + 1), [P.i("s_st")])
        SC(lambda e, p=p: e.activation(stats[:, 4 + p:5 + p],
                                       stats[:, p * 2 + 1:p * 2 + 2],
                                       AF.Sqrt, bias=EPS, scale=1.0),
           P.w("s_st", t + 1) + P.w("s_cst", 10), [P.i("s_sq")])

        def lnt(e, p=p):
            e.reciprocal(stats[:, 6 + p:7 + p], stats[:, 4 + p:5 + p])
            return e.tensor_scalar(
                SG[:, p * 1024:(p + 1) * 1024], FR[:, p * 1024:(p + 1) * 1024],
                stats[:, p * 2:p * 2 + 1], stats[:, 6 + p:7 + p],
                ALU.subtract, ALU.mult)
        V(lnt, P.w("s_sq", t + 1) + (P.wd("d_h1w", t - 1) if t >= 2 else []),
          [P.i("s_ln")])
        dma("gpsimd", "d_h1w", h1_hbm[t * 128:(t + 1) * 128, :],
            SG[:, p * 1024:(p + 1) * 1024], waits=P.w("s_ln", t + 1))

    for r in (1, 2):
        for kc in range(KC):
            dma("gpsimd", "d_wm", r3((r % 2) * 16384 + kc * 2048, 2048),
                wmod_in[kc * 128:(kc + 1) * 128, r * 2048:(r + 1) * 2048],
                waits=P.w("s_mm0", r))

    for r in range(3):
        V(lambda e: (e.tensor_copy(fr(0, 0, 1024), PS0[0:1, :]),
                     e.tensor_copy(fr(0, 1024, 2048), PS1[0:1, :]))[-1],
          P.w("s_mm0", r + 1) + P.w("s_ln", NT) + P.wd("d_mr", r),
          [P.i("s_ev0")])
        dma("scalar", "d_mr", modraw[0:1, r * 2048:(r + 1) * 2048],
            FR[0:1, 0:2048], waits=P.w("s_ev0", r + 1) + P.wd("d_mr", r))

    AS, BS = (0, 1024), (1024, 2048)

    def _row_sign_max(e, a, b):
        g = fr(0, *a)
        sgn = fr(0, *b)
        e.tensor_single_scalar(sgn, g, 0.0, ALU.is_ge)
        e.tensor_scalar(sgn, sgn, 2.0, -1.0, ALU.mult, ALU.add)
        e.scalar_tensor_tensor(g, g, -1.0, g, ALU.mult, ALU.max)
        e.tensor_scalar_max(g, g, 0.25)
        return e.tensor_tensor(g, g, sgn, op=ALU.mult)

    dma("sync", "d_rr", fr(0, *AS), modraw[0:1, 2 * H:3 * H], waits=P.wd("d_mr", 3))
    V(lambda e: _row_sign_max(e, AS, BS), P.wd("d_rr", 1), [P.i("s_row")])
    dma("scalar", "d_mw", modh[4:5, :], fr(0, *AS), waits=P.w("s_row", 1))
    dma("sync", "d_rr", fr(0, *AS), modraw[0:1, 5 * H:6 * H],
        waits=P.w("s_row", 1) + P.wd("d_mw", 1))
    V(lambda e: _row_sign_max(e, AS, BS), P.wd("d_rr", 2), [P.i("s_row")])
    dma("scalar", "d_mw", modh[5:6, :], fr(0, *AS), waits=P.w("s_row", 2))
    dma("sync", "d_rr", fr(0, *AS), modraw[0:1, H:2 * H],
        waits=P.w("s_row", 2) + P.wd("d_mw", 2))
    V(lambda e: e.tensor_scalar_add(fr(0, *AS), fr(0, *AS), 1.0),
      P.wd("d_rr", 3), [P.i("s_row")])
    dma("scalar", "d_mw", modh[0:1, :], fr(0, *AS), waits=P.w("s_row", 3))
    dma("sync", "d_rr", fr(0, *AS), modraw[0:1, 4 * H:5 * H],
        waits=P.w("s_row", 3) + P.wd("d_mw", 3))
    V(lambda e: e.tensor_scalar_add(fr(0, *AS), fr(0, *AS), 1.0),
      P.wd("d_rr", 4), [P.i("s_row")])
    dma("scalar", "d_mw", modh[2:3, :], fr(0, *AS), waits=P.w("s_row", 4))
    dma("gpsimd", "d_md", modh[1:2, :], modraw[0:1, 0:H], waits=P.wd("d_mr", 3))
    dma("gpsimd", "d_md", modh[3:4, :], modraw[0:1, 3 * H:4 * H],
        waits=P.wd("d_mr", 3))
    dma("sync", "d_rr", fr(0, *AS), bo_in[0:1, :],
        waits=P.w("s_row", 4) + P.wd("d_mw", 4))
    dma("sync", "d_rr", fr(0, *BS), modh[4:5, :],
        waits=P.w("s_row", 4) + P.wd("d_mw", 4))
    V(lambda e: e.tensor_tensor(fr(0, *AS), fr(0, *AS), fr(0, *BS), op=ALU.mult),
      P.wd("d_rr", 6), [P.i("s_row")])
    dma("scalar", "d_mw", modh[9:10, :], fr(0, *AS), waits=P.w("s_row", 5))
    for dst, row in ((ca1c, 0), (cb1c, 2), (sbc, 3)):
        dma("sync", "d_cols", dst[:, :],
            bass.AP(modh, row * H, [[1, 128], [128, 8]]),
            waits=P.wd("d_mw", 5) + P.wd("d_md", 2), slow=True)
    dma("gpsimd", "d_sac", sacbf[:, :],
        bass.AP(modh, 1 * H, [[1, 128], [128, 8]]),
        waits=P.wd("d_mw", 5) + P.wd("d_md", 2), slow=True)

    barrier()

    WSL = {"q": 8192, "k": 16384, "v": 24576}

    DUNITS = []
    for wsrc, wdst, npt, wcols in ((wg_in, wgq_hbm, 24, 1024),
                                   (wu_in, wuq_hbm, 24, 1024),
                                   (wd_in, wdq_hbm, 8, 3072)):
        for pt in range(npt):
            for hh in range(wcols // 512):
                DUNITS.append((wsrc, wdst, pt, hh))
    ND = len(DUNITS)
    nd_ld, nd_cp, npt_ = [0], [0], [0]

    NDA = 72

    def dslot(u):
        return (u % 4) * 512 if u < NDA else 1024 + (u % 2) * 512

    def emit_dload(u):
        wsrc, wdst, pt, hh = DUNITS[u]
        fro = dslot(u)
        if u < NDA:
            rw = ((P.wd("d_d", u - 1) if u >= 2 else
                   P.w("s_w2", 5) + P.wd("d_bw", 3)) +
                  (P.wd("d_dw", u - 3) if u >= 4 else []))
            dma("gpsimd", "d_d", FR[:, fro:fro + 512],
                wsrc[pt * 128:(pt + 1) * 128, hh * 512:(hh + 1) * 512], waits=rw)
        else:
            k = u - NDA
            rw = ((P.wd("d_d2", k - 1) if k >= 2 else P.wd("d_dw", NDA)) +
                  (P.wd("d_dw", u - 1) if k >= 2 else []))
            dma("sync", "d_d2", FR[:, fro:fro + 512],
                wsrc[pt * 128:(pt + 1) * 128, hh * 512:(hh + 1) * 512], waits=rw)

    def emit_dcomp(u):
        wsrc, wdst, pt, hh = DUNITS[u]
        fro = dslot(u)
        c0 = (u % 4) * 32

        def dv(e, fro=fro, c0=c0):
            e.tensor_reduce(deq[:, c0:c0 + 16],
                            FR[:, fro:fro + 512].rearrange("q (b j) -> q b j", j=32),
                            axis=AX.X, op=ALU.max, apply_absolute_value=True)
            e.tensor_scalar(deq[:, c0:c0 + 16], deq[:, c0:c0 + 16],
                            1.0 / 7.0, 1e-12, ALU.mult, ALU.max)
            return e.reciprocal(deq[:, c0 + 16:c0 + 32], deq[:, c0:c0 + 16])
        ldw = (P.wd("d_d", u + 1) if u < NDA else P.wd("d_d2", u - NDA + 1))
        V(dv, ldw + (P.w("s_dg", u - 3) if u >= 4 else []),
          [P.i("s_da")])
        GP(lambda e, fro=fro, c0=c0: (
            e.tensor_tensor(FR[:, fro:fro + 512].rearrange("q (b j) -> q b j", j=32),
                            FR[:, fro:fro + 512].rearrange("q (b j) -> q b j", j=32),
                            deq[:, c0 + 16:c0 + 32].unsqueeze(2)
                            .broadcast_to((128, 16, 32)), op=ALU.mult),
            e.tensor_tensor(FR[:, fro:fro + 512], FR[:, fro:fro + 512],
                            cst[:, 1:2].broadcast_to((128, 512)), op=ALU.add),
            e.tensor_tensor(FR[:, fro:fro + 512], FR[:, fro:fro + 512],
                            cst[:, 1:2].broadcast_to((128, 512)),
                            op=ALU.subtract),
            e.tensor_tensor(FR[:, fro:fro + 512].rearrange("q (b j) -> q b j", j=32),
                            FR[:, fro:fro + 512].rearrange("q (b j) -> q b j", j=32),
                            deq[:, c0:c0 + 16].unsqueeze(2)
                            .broadcast_to((128, 16, 32)), op=ALU.mult))[-1],
           P.w("s_da", u + 1) + P.w("s_dg", u), [P.i("s_dg")])
        dma("gpsimd", "d_dw",
            wdst[pt * 128:(pt + 1) * 128, hh * 512:(hh + 1) * 512],
            FR[:, fro:fro + 512], waits=P.w("s_dg", u + 1) +
            (P.wd("d_dw", u - 1) if u >= 2 else []))

    def dpoint():
        p = npt_[0]
        npt_[0] += 1
        lt = min(NDA, 2 + (p + 1) // 2)
        ct = min(NDA, 2 + (p - 1) // 2) if p >= 2 else 0
        while nd_ld[0] < lt:
            emit_dload(nd_ld[0])
            nd_ld[0] += 1
        while nd_cp[0] < ct:
            emit_dcomp(nd_cp[0])
            nd_cp[0] += 1

    def dflush():
        while nd_ld[0] < NDA:
            emit_dload(nd_ld[0])
            nd_ld[0] += 1
        while nd_cp[0] < NDA:
            emit_dcomp(nd_cp[0])
            nd_cp[0] += 1

    for wn, wsrc in (("q", wq_in), ("k", wk_in), ("v", wv_in)):
        dma("gpsimd", "d_w", r3(WSL[wn], 8192).rearrange("p (k c) -> p k c", c=1024),
            wsrc.ap().rearrange("(k p) c -> p k c", p=128))

    for wi, (wn, b_in, mrow, divq) in enumerate(
            (("q", bq_in, 6, True), ("k", bk_in, 7, False), ("v", bv_in, 8, False))):
        def sawmm(e, wn=wn):
            last = None
            for kc in range(KC):
                e.matmul(PS2[0:1, 0:512], sacbf[:, kc:kc + 1],
                         r3(WSL[wn] + kc * 1024, 512),
                         start=(kc == 0), stop=(kc == KC - 1))
                last = e.matmul(PS2[0:1, 512:1024], sacbf[:, kc:kc + 1],
                                r3(WSL[wn] + kc * 1024 + 512, 512),
                                start=(kc == 0), stop=(kc == KC - 1))
            return last
        TE(sawmm, P.wd("d_w", 3) + P.wd("d_cols", 3) + P.wd("d_sac", 1) + P.w("s_w2", 2 * wi),
           [P.i("s_saw")])
        dma("sync", "d_br", fr(0, *AS), b_in[0:1, :],
            waits=P.w("s_row", 5) + P.w("s_w2", 2 * wi))

        def sawev(e, divq=divq):
            last = e.scalar_tensor_tensor(fr(0, *BS), PS2[0:1, :], 1.0, fr(0, *AS),
                                          ALU.mult, ALU.add)
            if divq:
                last = e.tensor_scalar_mul(fr(0, *BS), fr(0, *BS), 1.0 / 32.0)
            return last
        V(sawev, P.w("s_saw", wi + 1) + P.wd("d_br", wi + 1) + P.wd("d_bw", wi),
          [P.i("s_w2")])
        dma("scalar", "d_bw", modh[mrow:mrow + 1, :], fr(0, *BS),
            waits=P.w("s_w2", 2 * wi + 1))

        def wscale(e, wn=wn, divq=divq):
            last = None
            for kc in range(KC):
                if divq:
                    last = e.tensor_scalar(
                        r3(WSL[wn] + kc * 1024, 1024),
                        r3(WSL[wn] + kc * 1024, 1024),
                        ca1c[:, kc:kc + 1], 1.0 / 32.0, ALU.mult, ALU.mult)
                else:
                    last = e.tensor_single_scalar(
                        r3(WSL[wn] + kc * 1024, 1024),
                        r3(WSL[wn] + kc * 1024, 1024),
                        ca1c[:, kc:kc + 1], ALU.mult)
            return last
        V(wscale, P.w("s_saw", wi + 1), [P.i("s_w2")])

    dma("sync", "d_bc", bqc[:, :],
        bass.AP(modh, 6 * H, [[1, 128], [128, 8]]), waits=P.wd("d_bw", 3), slow=True)
    dma("sync", "d_bc", bkc[:, :],
        bass.AP(modh, 7 * H, [[1, 128], [128, 8]]), waits=P.wd("d_bw", 3), slow=True)
    dma("gpsimd", "d_bvb", SG[:, 2048:3072],
        modh[8:9, :].partition_broadcast(128).squeeze(1), waits=P.wd("d_bw", 3))

    nwin = [0]
    WINFO = []
    nsc, nvv = [0], [0]

    def _reuse_wait(wdx):
        if wdx < 4:
            return []
        k, o = WINFO[wdx - 4]
        return P.w("s_ev2s" if k == "s" else "s_ev2v", o)

    for tq in range(4):
        for kc in range(KC):
            dmaT("sync", "d_h1t", r3(kc * 1024, 1024),
                 h1_hbm[tq * 1024:(tq + 1) * 1024, kc * 128:(kc + 1) * 128],
                 waits=P.wd("d_h1w", 8 * (tq + 1)) +
                 (P.w("s_saw", 3 + tq) if tq >= 1 else []))
        for wn, dstR in (("q", r1), ("k", r2)):
            bcol = bqc if wn == "q" else bkc
            for mc in range(KC):
                for wz in range(2):
                    wdx = nwin[0]
                    nwin[0] += 1
                    ps = (PS0, PS1)[(wdx // 2) % 2]
                    po = (wdx % 2) * 512
                    rw = _reuse_wait(wdx)
                    nsc[0] += 1
                    WINFO.append(("s", nsc[0]))

                    def qkm(e, wn=wn, mc=mc, wz=wz, ps=ps, po=po):
                        last = None
                        for kc in range(KC):
                            last = e.matmul(ps[:, po:po + 512],
                                            r3(WSL[wn] + kc * 1024 + mc * 128, 128),
                                            r3(kc * 1024 + wz * 512, 512),
                                            start=(kc == 0), stop=(kc == KC - 1))
                        return last
                    TE(qkm, P.wd("d_h1t", 8 * (tq + 1)) +
                       P.w("s_w2", 2 if wn == "q" else 4) + rw, [P.i("s_mm2")])
                    SC(lambda e, dstR=dstR, mc=mc, tq=tq, wz=wz, ps=ps, po=po, bcol=bcol:
                       e.activation(dstR(mc, tq * 1024 + wz * 512,
                                         tq * 1024 + wz * 512 + 512),
                                    ps[:, po:po + 512], AF.Identity,
                                    bias=bcol[:, mc:mc + 1], scale=1.0),
                       P.w("s_mm2", wdx + 1) + P.wd("d_bc", 2), [P.i("s_ev2s")])
                    dpoint()
        for tt in range(8):
            gt = tq * 8 + tt
            for wz in range(2):
                wdx = nwin[0]
                nwin[0] += 1
                ps = (PS0, PS1)[(wdx // 2) % 2]
                po = (wdx % 2) * 512
                rw = _reuse_wait(wdx)
                nvv[0] += 1
                WINFO.append(("v", nvv[0]))

                def vmm(e, tt=tt, wz=wz, ps=ps, po=po):
                    last = None
                    for kc in range(KC):
                        last = e.matmul(ps[:, po:po + 512],
                                        r3(kc * 1024 + tt * 128, 128),
                                        r3(WSL["v"] + kc * 1024 + wz * 512, 512),
                                        start=(kc == 0), stop=(kc == KC - 1))
                    return last
                TE(vmm, P.w("s_w2", 6) + rw, [P.i("s_mm2")])
                V(lambda e, gt=gt, wz=wz, ps=ps, po=po:
                  e.scalar_tensor_tensor(
                      SG[:, (gt % 2) * 1024 + wz * 512:(gt % 2) * 1024 + wz * 512 + 512],
                      ps[:, po:po + 512], 1.0,
                      SG[:, 2048 + wz * 512:2048 + (wz + 1) * 512],
                      ALU.mult, ALU.add),
                  P.w("s_mm2", wdx + 1) + P.wd("d_bvb", 1) +
                  (P.wd("d_vw", gt - 1) if gt >= 2 else []),
                  [P.i("s_ev2v")])
                dpoint()
            dma("gpsimd", "d_vw", v_hbm[gt * 128:(gt + 1) * 128, :],
                SG[:, (gt % 2) * 1024:(gt % 2) * 1024 + 1024],
                waits=P.w("s_ev2v", 16 * tq + 2 * tt + 2))
            dpoint()
        TE(lambda e: None, [], [P.i("s_saw")])

    dma("gpsimd", "d_w", r3(WSL["q"], 8192).rearrange("p (k c) -> p k c", c=1024),
        wo_in.ap().rearrange("(k p) c -> p k c", p=128), waits=P.w("s_mm2", 192))
    dma("gpsimd", "d_gab", SG[:, 0:1024],
        modh[4:5, :].partition_broadcast(128).squeeze(1), waits=P.wd("d_vw", NT))

    def woscale(e):
        last = None
        for kc in range(KC):
            last = e.tensor_tensor(r3(WSL["q"] + kc * 1024, 1024),
                                   r3(WSL["q"] + kc * 1024, 1024),
                                   SG[:, 0:1024], op=ALU.mult)
        return last
    V(woscale, P.wd("d_w", 4) + P.wd("d_gab", 1), [P.i("s_w2")])
    dma("gpsimd", "d_wow", wo2_hbm[:, :], r3(WSL["q"], 8192),
        waits=P.w("s_w2", 7))
    dflush()

    barrier()

    for jj in range(8):
        dma("sync", "d_vr", r3(jj * 4096, 4096).rearrange("p (j c) -> p j c", c=1024),
            v_hbm[jj * 512:(jj + 1) * 512, :].rearrange("(j p) c -> p j c", p=128),
            waits=P.wd("d_vw", NT) + P.wd("d_wow", 1))

    CHUNKS = []
    for i in range(NT):
        cs = att_chunks(i)
        for ci, (j0, nj) in enumerate(cs):
            CHUNKS.append((i, j0, nj, ci == 0, ci == len(cs) - 1))
    G = len(CHUNKS)
    NI = [0]


    TE_ORDER = []
    for g in range(G):
        TE_ORDER.append(("S", g))
        if g >= 1:
            TE_ORDER.append(("ptt", g - 1))
        if g >= 2:
            TE_ORDER.append(("ctx", g - 2))
    TE_ORDER += [("ptt", G - 1), ("ctx", G - 2), ("ctx", G - 1)]
    TPOS = {kg: i + 1 for i, kg in enumerate(TE_ORDER)}

    V_ORDER = []
    for g in range(G):
        V_ORDER += [("vmax", g), ("PTevac", g)]
        if g >= 1:
            V_ORDER.append(("vmerge", g - 1))
    V_ORDER.append(("vmerge", G - 1))
    VPOS = {kg: i + 1 for i, kg in enumerate(V_ORDER)}

    def emit_S(g):
        i, j0, nj, first, last = CHUNKS[g]
        ps = (PS0, PS1)[g % 2]

        def Smm(e, i=i, j0=j0, nj=nj, ps=ps):
            last_ = None
            for kc in range(KC):
                c0 = 0
                while c0 < nj * 128:
                    cw = min(512, nj * 128 - c0)
                    last_ = e.matmul(ps[:, c0:c0 + cw],
                                     r1(kc, i * 128, (i + 1) * 128),
                                     r2(kc, j0 * 128 + c0, j0 * 128 + c0 + cw),
                                     start=(kc == 0), stop=(kc == KC - 1))
                    c0 += cw
            return last_
        TE(Smm, P.w("s_exp", g - 1), [P.i("s_T3")])

    def emit_ptt(g):
        i, j0, nj, first, last = CHUNKS[g]
        pg = g % 2

        def ptt(e, nj=nj, pg=pg):
            last_ = None
            for jj in range(nj):
                last_ = e.transpose(
                    PS3[:, pg * 1024 + jj * 128:pg * 1024 + (jj + 1) * 128],
                    SG[:, pg * 1024 + jj * 128:pg * 1024 + (jj + 1) * 128],
                    idn[:, :])
            return last_
        TE(ptt, P.w("s_exp", g + 1) +
           (P.w("s_v3", VPOS[("PTevac", g - 1)]) if g >= 1 else []),
           [P.i("s_T3")])

    def emit_ctx(g):
        i, j0, nj, first, last = CHUNKS[g]
        pg = g % 2

        def ctxmm(e, j0=j0, nj=nj, pg=pg):
            last_ = None
            for jj in range(nj):
                for wz in range(2):
                    last_ = e.matmul(
                        PS2[:, wz * 512:(wz + 1) * 512],
                        SG[:, pg * 1024 + jj * 128:pg * 1024 + (jj + 1) * 128],
                        r3((j0 + jj) * 1024 + wz * 512, 512),
                        start=(jj == 0), stop=(jj == nj - 1))
            return last_
        cw = (P.w("s_v3", VPOS[("vmerge", g - 1)]) if g >= 1
              else P.w("s_v3", VPOS[("PTevac", 0)]))
        TE(ctxmm, cw + P.wd("d_vr", 8), [P.i("s_T3")])

    def emit_vme(g):
        i, j0, nj, first, last = CHUNKS[g]
        ps = (PS0, PS1)[g % 2]
        pg = g % 2
        w = nj * 128

        def vmax(e, i=i, j0=j0, first=first, last=last, ps=ps, pg=pg, w=w):
            if last:
                dof = (i - j0) * 128
                e.tensor_tensor(ps[:, dof:dof + 128], ps[:, dof:dof + 128],
                                maskd[:, :], op=ALU.add)
            if first:
                e.tensor_reduce(stats[:, 14 + pg:15 + pg], ps[:, 0:w], axis=AX.X,
                                op=ALU.max)
            else:
                e.tensor_reduce(stats[:, 4 + pg:5 + pg], ps[:, 0:w], axis=AX.X,
                                op=ALU.max)
                e.tensor_single_scalar(stats[:, 14 + pg:15 + pg],
                                       stats[:, 4 + pg:5 + pg], stats[:, 0:1],
                                       ALU.max)
                e.tensor_single_scalar(stats[:, 8 + pg:9 + pg], stats[:, 0:1],
                                       stats[:, 14 + pg:15 + pg], ALU.subtract)
            return e.tensor_copy(stats[:, 0:1], stats[:, 14 + pg:15 + pg])
        V(vmax, P.w("s_T3", TPOS[("S", g)]), [P.i("s_v3")])

        def sexp(e, first=first, ps=ps, pg=pg, w=w):
            e.activation(stats[:, 10 + pg:11 + pg], stats[:, 14 + pg:15 + pg],
                         AF.Identity, scale=-1.0)
            if not first:
                e.activation(stats[:, 12 + pg:13 + pg], stats[:, 8 + pg:9 + pg], AF.Exp)
            return e.activation(SG[:, pg * 1024:pg * 1024 + w], ps[:, 0:w], AF.Exp,
                                bias=stats[:, 10 + pg:11 + pg], scale=1.0,
                                accum_out=stats[:, 6 + pg:7 + pg])
        sw = P.w("s_v3", VPOS[("vmax", g)])
        if g >= 2:
            sw = sw + P.w("s_T3", TPOS[("ctx", g - 2)])
        SC(sexp, sw, [P.i("s_exp")])

        V(lambda e, nj=nj, pg=pg:
          e.tensor_copy(SG[:, pg * 1024:pg * 1024 + nj * 128],
                        PS3[:, pg * 1024:pg * 1024 + nj * 128]),
          P.w("s_T3", TPOS[("ptt", g)]), [P.i("s_v3")])

    def emit_vmerge(g):
        i, j0, nj, first, last = CHUNKS[g]
        pg = g % 2

        def vmerge(e, i=i, first=first, last=last, pg=pg):
            last_ = None
            if first:
                e.tensor_copy(FR[:, 0:1024], PS2[:, :])
                last_ = e.tensor_copy(stats[:, 1:2], stats[:, 6 + pg:7 + pg])
            else:
                e.scalar_tensor_tensor(FR[:, 0:1024], FR[:, 0:1024],
                                       stats[:, 12 + pg:13 + pg], PS2[:, :],
                                       ALU.mult, ALU.add)
                last_ = e.scalar_tensor_tensor(stats[:, 1:2], stats[:, 1:2],
                                               stats[:, 12 + pg:13 + pg],
                                               stats[:, 6 + pg:7 + pg],
                                               ALU.mult, ALU.add)
            if last:
                e.reciprocal(stats[:, 3:4], stats[:, 1:2])
                last_ = e.tensor_single_scalar(SG[:, 2048:3072], FR[:, 0:1024],
                                               stats[:, 3:4], ALU.mult)
            return last_
        mwaits = P.w("s_T3", TPOS[("ctx", g)])
        if last and i >= 1:
            mwaits = mwaits + P.wd("d_cw", i)
        V(vmerge, mwaits, [P.i("s_v3")] + ([P.i("s_cbf")] if last else []))
        if last:
            NI[0] += 1
            dma("gpsimd", "d_cw", ctx_hbm[i * 128:(i + 1) * 128, :],
                SG[:, 2048:3072], waits=P.w("s_cbf", NI[0]))

    for g in range(G):
        emit_S(g)
        if g >= 1:
            emit_ptt(g - 1)
        if g >= 2:
            emit_ctx(g - 2)
        emit_vme(g)
        if g >= 1:
            emit_vmerge(g - 1)
        t2 = NDA + min(ND - NDA, max(0, g - 1))
        while nd_ld[0] < t2:
            emit_dload(nd_ld[0])
            nd_ld[0] += 1
            emit_dcomp(nd_cp[0])
            nd_cp[0] += 1
    while nd_cp[0] < ND:
        if nd_ld[0] < ND:
            emit_dload(nd_ld[0])
            nd_ld[0] += 1
        emit_dcomp(nd_cp[0])
        nd_cp[0] += 1
    emit_ptt(G - 1)
    emit_ctx(G - 2)
    emit_ctx(G - 1)
    emit_vmerge(G - 1)

    barrier()

    for kc in range(KC):
        dmaT("sync", "d_ct", r2(kc, 0, 4096),
             ctx_hbm[:, kc * 128:(kc + 1) * 128], waits=P.wd("d_cw", NT))
    dma("sync", "d_p4c", r3(WSL["q"], 8192), wo2_hbm[:, :], waits=P.wd("d_wow", 1))
    dma("gpsimd", "d_bor", SG[0:1, 2048:3072], modh[9:10, :],
        waits=P.wd("d_cw", NT))

    def emit_p4t(tt):
        p = tt % 2

        def ptt4(e, tt=tt, p=p):
            last = None
            for kc in range(KC):
                last = e.transpose(
                    PS3[:, p * 1024 + kc * 128:p * 1024 + (kc + 1) * 128],
                    SG[:, p * 1024 + kc * 128:p * 1024 + (kc + 1) * 128],
                    idn[:, :])
            return last
        TE(ptt4, P.w("s_ln", NT + tt + 1) +
           (P.w("s_p4e", tt - 1) if tt >= 2 else []), [P.i("s_p4t")])
        V(lambda e, tt=tt, p=p: e.tensor_copy(
            R1[:, :].rearrange("q (k c) -> q k c", c=4096)[:, :, tt * 128:(tt + 1) * 128],
            PS3[:, p * 1024:(p + 1) * 1024].rearrange("q (k c) -> q k c", c=128)),
          P.w("s_p4t", tt + 1), [P.i("s_p4e")])

    for tt in range(NT):
        ps = (PS0, PS1)[tt % 2]
        p = tt % 2
        dma("sync", "d_xr", FR[:, p * 1024:(p + 1) * 1024],
            x_in[tt * 128:(tt + 1) * 128, :],
            waits=(P.w("s_ln", NT + tt - 1) + P.wd("d_x2w", tt - 1)
                   if tt >= 2 else []))

        def atmm(e, tt=tt, ps=ps):
            last = None
            for wz in range(2):
                for kc in range(KC):
                    e.matmul(ps[:, wz * 512:(wz + 1) * 512],
                             r2(kc, tt * 128, (tt + 1) * 128),
                             r3(WSL["q"] + kc * 1024 + wz * 512, 512),
                             start=(kc == 0), stop=False)
                last = e.matmul(ps[:, wz * 512:(wz + 1) * 512], onesrow[0:1, :],
                                SG[0:1, 2048 + wz * 512:2048 + (wz + 1) * 512],
                                start=False, stop=True)
            return last
        TE(atmm, P.wd("d_ct", 8) + P.wd("d_p4c", 1) + P.wd("d_bor", 1) + P.w("s_x2", tt - 1),
           [P.i("s_at")])

        def x2ev(e, tt=tt, ps=ps, p=p):
            e.scalar_tensor_tensor(FR[:, p * 1024:(p + 1) * 1024],
                                   ps[:, :], 1.0, FR[:, p * 1024:(p + 1) * 1024],
                                   ALU.mult, ALU.add)
            e.bn_stats(bnb[:, p * 12:p * 12 + 6], FR[:, p * 1024:p * 1024 + 512])
            e.bn_stats(bnb[:, p * 12 + 6:p * 12 + 12],
                       FR[:, p * 1024 + 512:(p + 1) * 1024])
            return e.bn_aggr(stats[:, p * 2:p * 2 + 2],
                             bnb[:, p * 12:p * 12 + 12])
        V(x2ev, P.w("s_at", tt + 1) + P.wd("d_xr", tt + 1),
          [P.i("s_x2"), P.i("s_st")])
        SC(lambda e, p=p: e.activation(stats[:, 4 + p:5 + p],
                                       stats[:, p * 2 + 1:p * 2 + 2],
                                       AF.Sqrt, bias=EPS, scale=1.0),
           P.w("s_st", NT + tt + 1), [P.i("s_sq")])

        def tev(e, p=p):
            e.reciprocal(stats[:, 6 + p:7 + p], stats[:, 4 + p:5 + p])
            return e.tensor_scalar(
                SG[:, p * 1024:(p + 1) * 1024], FR[:, p * 1024:(p + 1) * 1024],
                stats[:, p * 2:p * 2 + 1], stats[:, 6 + p:7 + p],
                ALU.subtract, ALU.mult)
        V(tev, P.w("s_sq", NT + tt + 1) + (P.w("s_p4t", tt - 1) if tt >= 2 else []),
          [P.i("s_ln")])
        dma("gpsimd", "d_x2w", x2_hbm[tt * 128:(tt + 1) * 128, :],
            FR[:, p * 1024:(p + 1) * 1024], waits=P.w("s_x2", tt + 1))
        if tt >= 1:
            emit_p4t(tt - 1)
    emit_p4t(NT - 1)

    barrier()

    for kc in range(KC):
        V(lambda e, kc=kc: (
            e.tensor_scalar(r1(kc, 0, 4096), r1(kc, 0, 4096),
                            cb1c[:, kc:kc + 1], sbc[:, kc:kc + 1],
                            ALU.mult, ALU.add),
            e.tensor_reduce(mxs[:, kc:kc + 1], r1(kc, 0, 4096), axis=AX.X,
                            op=ALU.max, apply_absolute_value=True))[-1],
          P.w("s_p4e", NT), [P.i("s_f0a")])
    V(lambda e: e.tensor_reduce(stats[:, 14:15], mxs[:, :], axis=AX.X, op=ALU.max),
      P.w("s_f0a", 8), [P.i("s_f0a")])

    def add_collective(idx, reduce_emit, outcols, prwait, repk):
        GP(reduce_emit, prwait, [P.i("s_pr")])
        dma("gpsimd", "d_ccw", cci[idx][0:8, :], scl[0:1, 16:24],
            waits=P.w("s_pr", 2 * idx + 1))
        GP(lambda e, idx=idx: e.collective_compute(
            "AllReduce", ALU.max, replica_groups=[list(range(8))],
            ins=[cci[idx].ap().opt()], outs=[cco[idx].ap().opt()]),
           P.wd("d_ccw", idx + 1), [P.i("s_cc")])
        for rrow, dcol in outcols:
            dma("sync", "d_rep", scl[:, dcol:dcol + 1],
                cco[idx][rrow:rrow + 1, :].partition_broadcast(128).squeeze(1),
                waits=P.w("s_cc", idx + 1))
        return P.wd("d_rep", repk)

    rw = add_collective(0, lambda e: e.tensor_reduce(
        scl[0:1, 16:17], stats[:, 14:15], axis=AX.XYZWC, op=ALU.max),
        [(0, 0)], P.w("s_f0a", 9), 1)
    V(lambda e: (
        e.tensor_scalar(scl[:, 1:2], scl[:, 0:1], QMAXI, 1e-12, ALU.mult, ALU.max),
        e.reciprocal(scl[:, 2:3], scl[:, 1:2]))[-1],
      rw, [P.i("s_scl")])

    for kc in range(KC):
        V(lambda e, kc=kc: (
            e.tensor_scalar(FR[:, 0:2048], r1(kc, 0, 2048),
                            scl[:, 2:3], MAGIC, ALU.mult, ALU.add),
            e.tensor_scalar(r1(kc, 0, 2048), FR[:, 0:2048],
                            -MAGIC, scl[:, 1:2], ALU.add, ALU.mult),
            e.tensor_scalar(FR[:, 0:2048], r1(kc, 2048, 4096),
                            scl[:, 2:3], MAGIC, ALU.mult, ALU.add),
            e.tensor_scalar(r1(kc, 2048, 4096), FR[:, 0:2048],
                            -MAGIC, scl[:, 1:2], ALU.add, ALU.mult))[-1],
          P.w("s_scl", 1), [P.i("s_scl")])

    for kc in range(KC):
        dmaT("sync", "d_wgt", r2(kc, 0, 3072),
             wgq_hbm[:, kc * 128:(kc + 1) * 128], waits=P.wd("d_dw", 144))
    for kc in range(KC):
        dmaT("sync", "d_wgt", r3(kc * 4096, 3072),
             wuq_hbm[:, kc * 128:(kc + 1) * 128], waits=P.wd("d_dw", 144))

    barrier()

    f1w = [0]
    for mi in range(MI):
        for tc in range(8):
            for srcname, dhbm, mcol in (("g", graw_hbm, 0), ("u", uraw_hbm, 1)):
                wdx = f1w[0]
                f1w[0] += 1
                ps = (PS0, PS1, PS2)[(wdx // 2) % 3]
                po = (wdx % 2) * 512

                def f1mm(e, srcname=srcname, mi=mi, tc=tc, ps=ps, po=po):
                    last = None
                    for kc in range(KC):
                        if srcname == "g":
                            lhs = r2(kc, mi * 128, (mi + 1) * 128)
                        else:
                            lhs = r3(kc * 4096 + mi * 128, 128)
                        last = e.matmul(ps[:, po:po + 512], lhs,
                                        r1(kc, tc * 512, (tc + 1) * 512),
                                        start=(kc == 0), stop=(kc == KC - 1))
                    return last
                TE(f1mm, P.wd("d_wgt", 16) + P.w("s_scl", 9) +
                   P.w("s_f1e", wdx - 5), [P.i("s_f1m")])
                sgo = (wdx % 4) * 512
                SC(lambda e, ps=ps, po=po, sgo=sgo:
                   e.copy(SG[:, sgo:sgo + 512], ps[:, po:po + 512]),
                   P.w("s_f1m", wdx + 1) +
                   (P.wd("d_f1w", wdx - 3) + P.w("s_f1r", wdx - 3) if wdx >= 4 else []),
                   [P.i("s_f1e")])
                dma("scalar", "d_f1w",
                    dhbm[mi * 128:(mi + 1) * 128, tc * 512:(tc + 1) * 512],
                    SG[:, sgo:sgo + 512],
                    waits=(P.wd("d_f1w", wdx - 1) if wdx >= 2 else []))
                V(lambda e, sgo=sgo, mcol=mcol: (
                    e.tensor_reduce(mrg[:, 4:5], SG[:, sgo:sgo + 512], axis=AX.X,
                                    op=ALU.max, apply_absolute_value=True),
                    e.tensor_tensor(mrg[:, mcol:mcol + 1], mrg[:, mcol:mcol + 1],
                                    mrg[:, 4:5], op=ALU.max))[-1],
                  P.w("s_f1e", wdx + 1), [P.i("s_f1r")])

    GP(lambda e: None, P.w("s_f1r", 384), [P.i("s_pr")])
    rw = add_collective(1, lambda e: (
        e.tensor_reduce(scl[0:1, 16:17], mrg[:, 0:1], axis=AX.XYZWC, op=ALU.max),
        e.tensor_reduce(scl[0:1, 17:18], mrg[:, 1:2], axis=AX.XYZWC, op=ALU.max))[-1],
        [(0, 3), (1, 6)], P.w("s_pr", 2), 3)
    V(lambda e: (
        e.tensor_scalar(scl[:, 4:5], scl[:, 3:4], QMAXI, 1e-12, ALU.mult, ALU.max),
        e.reciprocal(scl[:, 5:6], scl[:, 4:5]),
        e.tensor_scalar(scl[:, 7:8], scl[:, 6:7], QMAXI, 1e-12, ALU.mult, ALU.max),
        e.reciprocal(scl[:, 8:9], scl[:, 7:8]),
        e.tensor_single_scalar(scl[:, 18:19], scl[:, 4:5], -MAGIC, ALU.mult))[-1],
      rw, [P.i("s_scl")])

    dma("gpsimd", "d_gbb", SG[:, 2048:3072],
        modh[5:6, :].partition_broadcast(128).squeeze(1), waits=P.w("s_f1m", 384))
    for mi in range(MI):
        dmaT("sync", "d_wdt", R2[:, mi * 1024:(mi + 1) * 1024],
             wdq_hbm[:, mi * 128:(mi + 1) * 128],
             waits=P.wd("d_dw", 144) + P.w("s_f1m", 384) +
             (P.w("s_wd", mi - 1) if mi >= 2 else []))
        V(lambda e, mi=mi:
          e.tensor_tensor(R2[:, mi * 1024:(mi + 1) * 1024],
                          R2[:, mi * 1024:(mi + 1) * 1024], SG[:, 2048:3072],
                          op=ALU.mult),
          P.wd("d_wdt", mi + 1) + P.wd("d_gbb", 1), [P.i("s_wd")])

    barrier()

    NU2 = MI * 4
    for u in range(NU2):
        mi, qq = divmod(u, 4)
        rin = (u % 4) * 1024
        psA = (PS0, PS1, PS2)[u % 3]
        frb = (u % 2) * 1024
        dma("sync", "d_f2r", R1[:, rin:rin + 1024],
            graw_hbm[mi * 128:(mi + 1) * 128, qq * 1024:(qq + 1) * 1024],
            waits=(P.wd("d_f2r", u - 1) if u >= 2 else []) +
            (P.wd("d_f2w", u - 3) if u >= 4 else []))
        SC(lambda e, rin=rin, psA=psA:
           e.activation(psA[:, :], R1[:, rin:rin + 1024], AF.Identity,
                        bias=MAGIC, scale=scl[:, 5:6]),
           P.wd("d_f2r", u + 1) + P.w("s_scl", 10) +
           (P.w("s_2v", u - 2) if u >= 3 else []),
           [P.i("s_2a")])
        SC(lambda e, psA=psA, frb=frb:
           e.activation(FR[:, frb:frb + 1024], psA[:, :], AF.Sigmoid,
                        bias=scl[:, 18:19], scale=scl[:, 4:5]),
           P.w("s_2a", u + 1) + (P.w("s_2v", u - 1) if u >= 2 else []),
           [P.i("s_2s")])
        V(lambda e, rin=rin, psA=psA, frb=frb: (
            e.tensor_scalar(FR[:, frb:frb + 1024], FR[:, frb:frb + 1024],
                            65536.0, MAGIC, ALU.mult, ALU.add),
            e.tensor_scalar(FR[:, frb:frb + 1024], FR[:, frb:frb + 1024],
                            -MAGIC, 65535.0, ALU.add, ALU.min),
            e.scalar_tensor_tensor(R1[:, rin:rin + 1024], psA[:, :], -MAGIC,
                                   FR[:, frb:frb + 1024], ALU.add, ALU.mult))[-1],
          P.w("s_2s", u + 1), [P.i("s_2v")])
        for v in ([u - 1] if u >= 1 else []) + ([u] if u == NU2 - 1 else []):
            vm, vq = divmod(v, 4)
            vr = (v % 4) * 1024
            V(lambda e, v=v, vr=vr: e.tensor_reduce(
                deq[:, v:v + 1], R1[:, vr:vr + 1024],
                axis=AX.X, op=ALU.max, apply_absolute_value=True),
              P.w("s_2v", v + 1), [P.i("s_2r")])
            dma("gpsimd", "d_f2w",
                prod_hbm[vm * 128:(vm + 1) * 128, vq * 1024:(vq + 1) * 1024],
                R1[:, vr:vr + 1024],
                waits=P.w("s_2r", v + 1) + (P.wd("d_f2w", v - 1) if v >= 2 else []))

    V(lambda e: e.tensor_reduce(mrg[:, 2:3], deq[:, 0:NU2], axis=AX.X,
                                op=ALU.max),
      P.w("s_2r", NU2), [P.i("s_2r")])
    GP(lambda e: None, P.w("s_2r", NU2 + 1), [P.i("s_pr")])
    rw = add_collective(2, lambda e: e.tensor_reduce(
        scl[0:1, 16:17], mrg[:, 2:3], axis=AX.XYZWC, op=ALU.max),
        [(0, 9)], P.w("s_pr", 4), 4)
    V(lambda e: (
        e.tensor_tensor(scl[:, 15:16], scl[:, 9:10], scl[:, 4:5], op=ALU.mult),
        e.tensor_scalar(scl[:, 10:11], scl[:, 15:16], QMAXI / 65536.0, 1e-12,
                        ALU.mult, ALU.max),
        e.tensor_single_scalar(scl[:, 15:16], scl[:, 10:11], 65536.0, ALU.mult),
        e.reciprocal(scl[:, 11:12], scl[:, 15:16]),
        e.tensor_tensor(scl[:, 11:12], scl[:, 11:12], scl[:, 4:5],
                        op=ALU.mult),
        e.tensor_single_scalar(scl[:, 19:20], scl[:, 10:11], -MAGIC,
                               ALU.mult))[-1],
      rw, [P.i("s_scl")])

    barrier()

    for u in range(NU2):
        mi, qq = divmod(u, 4)
        rp = (u % 4) * 1024
        ru = 4096 + (u % 4) * 1024
        fra = (u % 2) * 1024
        psB = (PS0, PS1, PS2)[u % 3]
        dma("sync", "d_f3r", R1[:, rp:rp + 1024],
            prod_hbm[mi * 128:(mi + 1) * 128, qq * 1024:(qq + 1) * 1024],
            waits=(P.wd("d_f3r", u - 1) if u >= 2 else []) +
            (P.wd("d_f3w", u - 3) if u >= 4 else []))
        dma("sync", "d_f3u", R1[:, ru:ru + 1024],
            uraw_hbm[mi * 128:(mi + 1) * 128, qq * 1024:(qq + 1) * 1024],
            waits=(P.wd("d_f3u", u - 1) if u >= 2 else []) +
            (P.w("s_31", u - 3) if u >= 4 else []))
        SC(lambda e, rp=rp, psB=psB:
           e.activation(psB[:, :], R1[:, rp:rp + 1024], AF.Identity,
                        bias=MAGIC, scale=scl[:, 11:12]),
           P.wd("d_f3r", u + 1) + P.w("s_scl", 11) +
           (P.w("s_3v", u - 2) if u >= 3 else []),
           [P.i("s_3a")])
        SC(lambda e, ru=ru, fra=fra:
           e.activation(FR[:, fra:fra + 1024], R1[:, ru:ru + 1024], AF.Identity,
                        bias=MAGIC, scale=scl[:, 8:9]),
           P.w("s_3a", u + 1) + P.wd("d_f3u", u + 1) +
           (P.w("s_3v", u - 1) if u >= 2 else []),
           [P.i("s_31")])
        V(lambda e, rp=rp, fra=fra, psB=psB: (
            e.tensor_scalar(FR[:, fra:fra + 1024], FR[:, fra:fra + 1024],
                            -MAGIC, scl[:, 7:8], ALU.add, ALU.mult),
            e.scalar_tensor_tensor(R1[:, rp:rp + 1024], psB[:, :], -MAGIC,
                                   FR[:, fra:fra + 1024], ALU.add, ALU.mult))[-1],
          P.w("s_31", u + 1), [P.i("s_3v")])
        for v in ([u - 1] if u >= 1 else []) + ([u] if u == NU2 - 1 else []):
            vm, vq = divmod(v, 4)
            vr = (v % 4) * 1024
            V(lambda e, v=v, vr=vr: e.tensor_reduce(
                deq[:, v:v + 1], R1[:, vr:vr + 1024],
                axis=AX.X, op=ALU.max, apply_absolute_value=True),
              P.w("s_3v", v + 1), [P.i("s_3r")])
            dma("gpsimd", "d_f3w",
                usc_hbm[vm * 128:(vm + 1) * 128, vq * 1024:(vq + 1) * 1024],
                R1[:, vr:vr + 1024],
                waits=P.w("s_3r", v + 1) + (P.wd("d_f3w", v - 1) if v >= 2 else []))

    V(lambda e: e.tensor_reduce(mrg[:, 3:4], deq[:, 0:NU2], axis=AX.X,
                                op=ALU.max),
      P.w("s_3r", NU2), [P.i("s_3r")])
    GP(lambda e: None, P.w("s_3r", NU2 + 1), [P.i("s_pr")])
    rw = add_collective(3, lambda e: e.tensor_reduce(
        scl[0:1, 16:17], mrg[:, 3:4], axis=AX.XYZWC, op=ALU.max),
        [(0, 12)], P.w("s_pr", 6), 5)
    V(lambda e: (
        e.tensor_tensor(scl[:, 21:22], scl[:, 12:13], scl[:, 10:11], op=ALU.mult),
        e.tensor_scalar(scl[:, 13:14], scl[:, 21:22], QMAXI, 1e-12, ALU.mult, ALU.max),
        e.reciprocal(scl[:, 14:15], scl[:, 13:14]),
        e.tensor_tensor(scl[:, 20:21], scl[:, 14:15], scl[:, 10:11],
                        op=ALU.mult))[-1],
      rw, [P.i("s_scl")])

    barrier()

    def emit_f4q(tc):
        sp = (tc % 2) * 12288
        for mi in range(MI):
            un = tc * MI + mi
            p = un % 2
            dma("sync", "d_u4", SG[:, p * 512:(p + 1) * 512],
                usc_hbm[mi * 128:(mi + 1) * 128, tc * 512:(tc + 1) * 512],
                waits=(P.w("s_41", un - 1) if un >= 2 else []))
            SC(lambda e, p=p:
               e.activation(PS2[:, p * 512:(p + 1) * 512],
                            SG[:, p * 512:(p + 1) * 512], AF.Identity,
                            bias=MAGIC, scale=scl[:, 20:21]),
               P.wd("d_u4", un + 1) + P.w("s_scl", 12) +
               (P.w("s_42", un - 1) if un >= 2 else []), [P.i("s_41")])
            V(lambda e, sp=sp, mi=mi, p=p:
              e.tensor_scalar(R3[:, sp + mi * 512:sp + (mi + 1) * 512],
                              PS2[:, p * 512:(p + 1) * 512],
                              -MAGIC, scl[:, 13:14], ALU.add, ALU.mult),
              P.w("s_41", un + 1) +
              (P.w("s_f4m", (tc - 1) * 4) if tc >= 2 and mi == 0 else []),
              [P.i("s_42")])

    def emit_f4mm(tc):
        sp = (tc % 2) * 12288
        for tt in range(4):
            tg = tc * 4 + tt
            ps = (PS0, PS1)[tg % 2]
            p = tg % 2
            dma("sync", "d_x2r", FR[:, p * 1024:(p + 1) * 1024],
                x2_hbm[tg * 128:(tg + 1) * 128, :],
                waits=(P.w("s_f4e", tg - 1) + P.wd("d_out", tg - 1)
                       if tg >= 2 else P.wd("d_x2w", NT)))

            def f4mm(e, sp=sp, tt=tt, ps=ps):
                last = None
                for wz in range(2):
                    for mi in range(MI):
                        last = e.matmul(
                            ps[:, wz * 512:(wz + 1) * 512],
                            R3[:, sp + mi * 512 + tt * 128:sp + mi * 512 + (tt + 1) * 128],
                            R2[:, mi * 1024 + wz * 512:mi * 1024 + (wz + 1) * 512],
                            start=(mi == 0), stop=(mi == MI - 1))
                return last
            TE(f4mm, P.w("s_42", (tc + 1) * MI) + P.w("s_wd", MI) +
               P.w("s_f4e", tg - 1), [P.i("s_f4m")])
            V(lambda e, p=p, ps=ps:
              e.scalar_tensor_tensor(FR[:, p * 1024:(p + 1) * 1024],
                                     ps[:, :], 1.0, FR[:, p * 1024:(p + 1) * 1024],
                                     ALU.mult, ALU.add),
              P.w("s_f4m", tg + 1) + P.wd("d_x2r", tg + 1),
              [P.i("s_f4e")])
            dma("scalar", "d_out", out_hbm[tg * 128:(tg + 1) * 128, :],
                FR[:, p * 1024:(p + 1) * 1024], waits=P.w("s_f4e", tg + 1))

    for tc in range(8):
        emit_f4q(tc)
        if tc >= 1:
            emit_f4mm(tc - 1)
    emit_f4mm(7)

    for name, dsttap in tap_outs.items():
        src = tap_srcs[name]
        dma("gpsimd", "d_tap", dsttap[:, :], src[:, :],
            waits=P.wd("d_out") + P.wd("d_f3w") + P.wd("d_dw") + P.wd("d_f2w") +
            P.wd("d_f1w") + P.wd("d_mr") + P.wd("d_mw") + P.wd("d_cw") +
            P.wd("d_x2w") + P.wd("d_tw") + P.wd("d_vw") + P.wd("d_h1w") +
            P.wd("d_wow"))

    SC(lambda e: None, P.wd("d_out"))
    if taps:
        GP(lambda e: None, P.wd("d_tap"))

    for E in ENGS:
        for waits, emit, incs in (
                (s for s in P.steps[E] if not isinstance(s[1], tuple))):
            for (n, v) in waits:
                if n == "__tpos__":
                    continue
                assert v <= P.cnt.get(n, 0), (
                    f"{E}: wait {n}>={v} unreachable (final {P.cnt.get(n, 0)})")

    inc_engs = {}
    for E in ENGS:
        for step in P.steps[E]:
            if isinstance(step[1], tuple):
                continue
            for (n, v) in step[2]:
                inc_engs.setdefault(n, set()).add(E)
    mergeable = {
        n for n, es in inc_engs.items()
        if len(es) == 1 and next(iter(es)) != "gpsimd"
        and not n.startswith("d_") and n not in NOMERGE
    }
    trans = {}
    phys_final = {E: 0 for E in ENGS}
    for E in ENGS:
        pos = 0
        lcnt = {}
        for step in P.steps[E]:
            if isinstance(step[1], tuple):
                continue
            stepm = [(n, v) for (n, v) in step[2] if n in mergeable]
            pos_end = pos + sum(v for (n, v) in stepm)
            for (n, v) in stepm:
                lcnt[n] = lcnt.get(n, 0) + v
                trans[(n, lcnt[n])] = pos_end
            pos = pos_end
        phys_final[E] = pos
    eng_of = {n: next(iter(es)) for n, es in inc_engs.items() if n in mergeable}

    sems = {}
    for E in ENGS:
        sems["@" + E] = e_(nc.semaphore("se_" + E))
    for n in P.cnt:
        if n not in mergeable:
            sems[n] = e_(nc.semaphore(n))
    sbar = e_(nc.semaphore("sbar"))

    def xlate(n, v):
        if n == "__tpos__":
            return ("@tensor", v)
        if n in mergeable:
            return ("@" + eng_of[n], trans[(n, v)])
        return (n, v)

    _NODRAIN = {"dma_start", "dma_start_transpose", "collective_compute",
                "wait_ge", "sem_inc", "drain", "nop", "engine_nop"}

    class _DrainProxy:

        def __init__(self, e):
            object.__setattr__(self, "_e", e)

        def __getattr__(self, name):
            f = getattr(self._e, name)
            if not callable(f) or name in _NODRAIN or name.startswith("_"):
                return f

            def wrap(*a, **kw):
                r = f(*a, **kw)
                if r is not None and hasattr(r, "then_inc"):
                    self._e.drain()
                return r
            return wrap

    def make_closure(E):
        def closure(raw_e):
            e = raw_e if E == "tensor" else _DrainProxy(raw_e)
            for waits, emit, incs in P.steps[E]:
                if isinstance(emit, tuple) and emit[0] == "BARRIER":
                    for (n, v) in emit[2]:
                        if v > 0:
                            raw_e.wait_ge(sems[n], v)
                    raw_e.sem_inc(sbar, 1)
                    raw_e.wait_ge(sbar, 5 * emit[1])
                    continue
                for (n, v) in waits:
                    if v > 0:
                        pn, pv = xlate(n, v)
                        raw_e.wait_ge(sems[pn], pv)
                inst = emit(e)
                agg = {}
                for (n, v) in incs:
                    pn = "@" + eng_of[n] if n in mergeable else n
                    agg[pn] = agg.get(pn, 0) + v
                if inst is None and agg:
                    inst = raw_e.nop()
                for pn, v in agg.items():
                    inst = inst.then_inc(sems[pn], v)
        return closure

    block = e_(nc.Block())
    block.sync(make_closure("sync"))
    block.vector(make_closure("vector"))
    block.scalar(make_closure("scalar"))
    block.tensor(make_closure("tensor"))
    block.gpsimd(make_closure("gpsimd"))

    st_.close()
    return nc


_NC_CACHE = {}


def _get_nc(taps=()):
    key = tuple(sorted(taps))
    if key not in _NC_CACHE:
        _NC_CACHE[key] = build_nc(taps=key)
    return _NC_CACHE[key]


def make_in_maps(inputs):
    y = np.ascontiguousarray(np.asarray(inputs["L0_attn_y_all"], dtype=np.float32))
    c = np.asarray(inputs["L0_c"], dtype=np.float32)
    base = {
        "wmod": np.ascontiguousarray(np.asarray(inputs["L0_wmod"], np.float32)),
        "wq": np.ascontiguousarray(np.asarray(inputs["L0_attn_Wq"], np.float32)),
        "wk": np.ascontiguousarray(np.asarray(inputs["L0_attn_Wk"], np.float32)),
        "wv": np.ascontiguousarray(np.asarray(inputs["L0_attn_Wv"], np.float32)),
        "wo": np.ascontiguousarray(np.asarray(inputs["L0_attn_Wo"], np.float32)),
        "bq": np.ascontiguousarray(np.asarray(inputs["L0_attn_bq"], np.float32).reshape(1, H)),
        "bk": np.ascontiguousarray(np.asarray(inputs["L0_attn_bk"], np.float32).reshape(1, H)),
        "bv": np.ascontiguousarray(np.asarray(inputs["L0_attn_bv"], np.float32).reshape(1, H)),
        "bo": np.ascontiguousarray(np.asarray(inputs["L0_attn_bo"], np.float32).reshape(1, H)),
        "wg": np.ascontiguousarray(np.asarray(inputs["L0_ffn_w_gate"], np.float32)),
        "wu": np.ascontiguousarray(np.asarray(inputs["L0_ffn_w_up"], np.float32)),
        "wd": np.ascontiguousarray(np.asarray(inputs["L0_ffn_w_down"], np.float32)),
    }
    in_maps = []
    for b in range(8):
        m = dict(base)
        m["x"] = np.ascontiguousarray(y[b])
        m["ccol"] = np.ascontiguousarray(c[b].reshape(8, 128).T)
        in_maps.append(m)
    return in_maps


def kernel(**inputs):
    nc = _get_nc()
    in_maps = make_in_maps(inputs)
    res = run_bass_kernel_spmd(nc, in_maps, core_ids=list(range(8)))
    return np.stack([np.asarray(r["out"], dtype=np.float32) for r in res.results])


# version 17 (final)
# speedup vs baseline: 1.0616x; 1.0208x over previous
import numpy as np
from contextlib import ExitStack

import concourse.bass as bass
import concourse.mybir as mybir
from concourse.bass_utils import run_bass_kernel_spmd

F32 = mybir.dt.float32
BF16 = mybir.dt.bfloat16
AF = mybir.ActivationFunctionType
ALU = mybir.AluOpType
AX = mybir.AxisListType

MAGIC = 12582912.0
NEG_BIG = -1.0e30
T, NT, H, KC, I3, MI = 4096, 32, 1024, 8, 3072, 24
EPS = 1e-5
QMAXI = 1.0 / 32767.0
ENGS = ("sync", "vector", "scalar", "tensor", "gpsimd")

PARITY = {
    "d_x", "d_h1w", "d_vw", "d_cw", "d_xr", "d_x2w", "d_tw", "d_d", "d_dw",
    "d_tt", "d_wdt", "d_f1w", "d_f2r", "d_f2w", "d_f3r", "d_f3u", "d_f3w",
    "d_u4", "d_x2r", "d_out", "d_da", "d_daw", "d_ds", "d_d2",
}
NOMERGE = {"s_ev2"}


class Prog:
    def __init__(self):
        self.cnt = {}
        self.tpos = 0
        self.steps = {E: [] for E in ENGS}
        self.stream_eng = {}

    def _reg(self, n):
        if n not in self.cnt:
            self.cnt[n] = 0

    def w(self, n, k=None):
        self._reg(n)
        c = self.cnt[n] if k is None else k
        return [(n, c)]

    def i(self, n):
        self._reg(n)
        self.cnt[n] += 1
        return (n, 1)

    def dcount(self, n):
        if n in PARITY:
            self._reg(n + "_A")
            self._reg(n + "_B")
            return (self.cnt[n + "_A"] + self.cnt[n + "_B"]) // 16
        self._reg(n)
        return self.cnt[n] // 16

    def dinc(self, n):
        tot = self.dcount(n)
        if n in PARITY:
            pn = n + ("_A" if tot % 2 == 0 else "_B")
            self.cnt[pn] += 16
            return (pn, 16)
        self.cnt[n] += 16
        return (n, 16)

    def wd(self, n, k=None):
        tot = self.dcount(n)
        if k is None:
            k = tot
        k = max(0, k)
        if k == 0:
            return []
        if n in PARITY:
            ka = (k + 1) // 2
            kb = k // 2
            out = [(n + "_A", 16 * ka)]
            if kb > 0:
                out.append((n + "_B", 16 * kb))
            return out
        return [(n, 16 * k)]

    def st(self, eng, emit, waits=(), incs=()):
        waits = tuple(waits)
        if eng == "tensor":
            ncomp = sum(v for (n, v) in incs if not n.startswith("d_"))
            if ncomp:
                if self.tpos > 0:
                    waits = waits + (("__tpos__", self.tpos),)
                self.tpos += ncomp
        self.steps[eng].append((waits, emit, tuple(incs)))


def att_chunks(i):
    out, j0 = [], 0
    while j0 <= i:
        nj = min(8, i + 1 - j0)
        out.append((j0, nj))
        j0 += nj
    return out


def build_nc(taps=()):
    nc = bass.Bass("TRN2", target_bir_lowering=False)

    x_in = nc.dram_tensor("x", [T, H], F32, kind="ExternalInput")
    ccol_in = nc.dram_tensor("ccol", [128, 8], F32, kind="ExternalInput")
    wmod_in = nc.dram_tensor("wmod", [H, 6 * H], F32, kind="ExternalInput")
    wq_in = nc.dram_tensor("wq", [H, H], F32, kind="ExternalInput")
    wk_in = nc.dram_tensor("wk", [H, H], F32, kind="ExternalInput")
    wv_in = nc.dram_tensor("wv", [H, H], F32, kind="ExternalInput")
    wo_in = nc.dram_tensor("wo", [H, H], F32, kind="ExternalInput")
    bq_in = nc.dram_tensor("bq", [1, H], F32, kind="ExternalInput")
    bk_in = nc.dram_tensor("bk", [1, H], F32, kind="ExternalInput")
    bv_in = nc.dram_tensor("bv", [1, H], F32, kind="ExternalInput")
    bo_in = nc.dram_tensor("bo", [1, H], F32, kind="ExternalInput")
    wg_in = nc.dram_tensor("wg", [I3, H], F32, kind="ExternalInput")
    wu_in = nc.dram_tensor("wu", [I3, H], F32, kind="ExternalInput")
    wd_in = nc.dram_tensor("wd", [H, I3], F32, kind="ExternalInput")
    out_hbm = nc.dram_tensor("out", [T, H], F32, kind="ExternalOutput")

    h1_hbm = nc.dram_tensor("h1s", [T, H], BF16)
    v_hbm = nc.dram_tensor("vs", [T, H], BF16)
    ctx_hbm = nc.dram_tensor("ctxs", [T, H], BF16)
    x2_hbm = nc.dram_tensor("x2s", [T, H], F32)
    t_hbm = nc.dram_tensor("ts_", [T, H], BF16)
    wo2_hbm = nc.dram_tensor("wo2s", [128, 8 * H], BF16)
    modraw = nc.dram_tensor("modraw", [1, 6 * H], F32)
    modh = nc.dram_tensor("modh", [12, H], F32)
    wgq_hbm = nc.dram_tensor("wgq", [I3, H], BF16)
    wuq_hbm = nc.dram_tensor("wuq", [I3, H], BF16)
    wdq_hbm = nc.dram_tensor("wdq", [H, I3], BF16)
    graw_hbm = nc.dram_tensor("graw", [I3, T], BF16)
    uraw_hbm = nc.dram_tensor("uraw", [I3, T], BF16)
    prod_hbm = nc.dram_tensor("prod", [I3, T], BF16)
    usc_hbm = nc.dram_tensor("usc", [I3, T], BF16)
    cci = [nc.dram_tensor(f"cci{j}", [8, 1], F32) for j in range(4)]
    cco = [nc.dram_tensor(f"cco{j}", [8, 1], F32) for j in range(4)]

    tap_srcs = {
        "h1": h1_hbm, "v": v_hbm, "ctx": ctx_hbm, "x2": x2_hbm,
        "modraw": modraw, "modh": modh, "wgq": wgq_hbm, "wuq": wuq_hbm,
        "wdq": wdq_hbm, "graw": graw_hbm, "uraw": uraw_hbm, "prod": prod_hbm,
        "usc": usc_hbm, "wo2": wo2_hbm,
    }
    tap_outs = {
        name: nc.dram_tensor("tap_" + name, list(tap_srcs[name].shape),
                             tap_srcs[name].dtype, kind="ExternalOutput")
        for name in taps
    }

    st_ = ExitStack()
    e_ = st_.enter_context

    R1 = e_(nc.sbuf_tensor("R1", [128, 32768], BF16))
    R2 = e_(nc.sbuf_tensor("R2", [128, 32768], BF16))
    R3 = e_(nc.sbuf_tensor("R3", [128, 32768], BF16))
    SG = e_(nc.sbuf_tensor("SGb", [128, 3072], BF16))
    FR = e_(nc.sbuf_tensor("FRf", [128, 2048], F32))
    idn = e_(nc.sbuf_tensor("idn", [128, 128], BF16))
    maskd = e_(nc.sbuf_tensor("maskd", [128, 128], BF16))
    onesrow = e_(nc.sbuf_tensor("onesrow", [1, 128], BF16))
    ccol = e_(nc.sbuf_tensor("ccolS", [128, 8], F32))
    cbf = e_(nc.sbuf_tensor("cbfS", [128, 8], BF16))
    ca1c = e_(nc.sbuf_tensor("ca1c", [128, 8], F32))
    sacbf = e_(nc.sbuf_tensor("sacbf", [128, 8], BF16))
    cb1c = e_(nc.sbuf_tensor("cb1c", [128, 8], F32))
    sbc = e_(nc.sbuf_tensor("sbc", [128, 8], F32))
    bqc = e_(nc.sbuf_tensor("bqc", [128, 8], F32))
    bkc = e_(nc.sbuf_tensor("bkc", [128, 8], F32))
    mxs = e_(nc.sbuf_tensor("mxs", [128, 8], F32))
    stats = e_(nc.sbuf_tensor("stats", [128, 16], F32))
    scl = e_(nc.sbuf_tensor("scl", [128, 24], F32))
    mrg = e_(nc.sbuf_tensor("mrg", [128, 8], F32))
    bnb = e_(nc.sbuf_tensor("bnb", [128, 24], F32))
    deq = e_(nc.sbuf_tensor("deq", [128, 128], F32))
    cst = e_(nc.sbuf_tensor("cst", [128, 6], F32))
    nc.const_aps.register(F32, EPS, cst[:, 0:1])
    nc.const_aps.register(F32, MAGIC, cst[:, 1:2])
    nc.const_aps.register(F32, 65536.0, cst[:, 2:3])
    nc.const_aps.register(F32, -MAGIC, cst[:, 3:4])
    nc.const_aps.register(F32, -1.0, cst[:, 4:5])

    PS0 = e_(nc.psum_tensor("PS0", [128, 1024], F32))
    PS1 = e_(nc.psum_tensor("PS1", [128, 1024], F32))
    PS2 = e_(nc.psum_tensor("PS2", [128, 1024], F32))
    PS3 = e_(nc.psum_tensor("PS3", [128, 2048], BF16))

    P = Prog()

    def r1(kc, a, b):
        return R1[:, kc * 4096 + a: kc * 4096 + b]

    def r2(kc, a, b):
        return R2[:, kc * 4096 + a: kc * 4096 + b]

    def r3(off, n):
        return R3[:, off: off + n]

    def fr(row, a, b):
        return FR[row:row + 1, a:b]

    def dma(eng, stream, out, in_, waits=(), slow=False):
        def em(e, out=out, in_=in_, slow=slow):
            if slow:
                return e.dma_start(out=out, in_=in_, allow_slow_non_contiguous=True)
            return e.dma_start(out=out, in_=in_)
        P.stream_eng[stream] = eng
        P.st(eng, em, waits, [P.dinc(stream)])

    def dmaT(eng, stream, out, in_, waits=()):
        P.stream_eng[stream] = eng
        P.st(eng, lambda e, o=out, i=in_: e.dma_start_transpose(o, i), waits,
             [P.dinc(stream)])

    nbar = [0]

    def barrier():
        nbar[0] += 1
        for E in ENGS:
            flush = []
            for sname, seng in P.stream_eng.items():
                if seng == E:
                    flush.extend(P.wd(sname))
            P.st(E, ("BARRIER", nbar[0], tuple(flush)))

    def V(emit, waits=(), incs=()):
        P.st("vector", emit, waits, incs)

    def SC(emit, waits=(), incs=()):
        P.st("scalar", emit, waits, incs)

    def TE(emit, waits=(), incs=()):
        P.st("tensor", emit, waits, incs)

    def GP(emit, waits=(), incs=()):
        P.st("gpsimd", emit, waits, incs)

    GP(lambda e: e.memset(idn[:, :], 0.0), [], [P.i("s_cst")])
    GP(lambda e: e.affine_select(out=idn[:, :], in_=idn[:, :],
                                 compare_op=ALU.not_equal, fill=1.0, base=0,
                                 pattern=[[-1, 128]], channel_multiplier=1),
       P.w("s_cst", 1), [P.i("s_cst")])
    GP(lambda e: e.memset(maskd[:, :], 0.0), [], [P.i("s_cst")])
    GP(lambda e: e.affine_select(out=maskd[:, :], in_=maskd[:, :],
                                 compare_op=ALU.is_ge, fill=NEG_BIG, base=0,
                                 pattern=[[-1, 128]], channel_multiplier=1),
       P.w("s_cst", 3), [P.i("s_cst")])
    GP(lambda e: e.memset(onesrow[:, :], 1.0), [], [P.i("s_cst")])
    GP(lambda e: e.memset(scl[:, 16:24], 0.0), [], [P.i("s_cst")])
    GP(lambda e: e.memset(mrg[:, 0:8], 0.0), [], [P.i("s_cst")])
    GP(lambda e: e.memset(cst[:, 0:1], EPS), [], [P.i("s_cst")])
    GP(lambda e: e.memset(cst[:, 1:2], MAGIC), [], [P.i("s_cst")])
    GP(lambda e: e.memset(cst[:, 2:3], 65536.0), [], [P.i("s_cst")])
    GP(lambda e: e.memset(cst[:, 3:4], -MAGIC), [], [P.i("s_cst")])
    GP(lambda e: e.memset(cst[:, 4:5], -1.0), [], [P.i("s_cst")])
    dma("sync", "d_p0", ccol[:, :], ccol_in[:, :])
    SC(lambda e: e.activation(cbf[:, :], ccol[:, :], AF.Silu),
       P.wd("d_p0", 1), [P.i("s_silu")])
    for kc in range(KC):
        dma("gpsimd", "d_wm", r3(kc * 2048, 2048),
            wmod_in[kc * 128:(kc + 1) * 128, 0:2048])

    for r in range(3):
        def mm0(e, r=r):
            last = None
            for w in range(4):
                ps = PS0 if w < 2 else PS1
                wo_ = (w % 2) * 512
                for kc in range(KC):
                    last = e.matmul(
                        ps[0:1, wo_:wo_ + 512], cbf[:, kc:kc + 1],
                        r3((r % 2) * 16384 + kc * 2048 + (w // 2) * 1024 + wo_, 512),
                        start=(kc == 0), stop=(kc == KC - 1))
            return last
        TE(mm0, P.wd("d_wm", 8 * (r + 1)) + P.w("s_silu", 1) + P.w("s_ev0", r),
           [P.i("s_mm0")])

    for t in range(NT):
        p = t % 2
        dma("sync", "d_x", FR[:, p * 1024:(p + 1) * 1024],
            x_in[t * 128:(t + 1) * 128, :],
            waits=(P.w("s_ln", t - 1) if t >= 2 else []))

        def bnst(e, p=p):
            e.bn_stats(bnb[:, p * 12:p * 12 + 6], FR[:, p * 1024:p * 1024 + 512])
            e.bn_stats(bnb[:, p * 12 + 6:p * 12 + 12],
                       FR[:, p * 1024 + 512:(p + 1) * 1024])
            return e.bn_aggr(stats[:, p * 2:p * 2 + 2],
                             bnb[:, p * 12:p * 12 + 12])
        V(bnst, P.wd("d_x", t + 1), [P.i("s_st")])
        SC(lambda e, p=p: e.activation(stats[:, 4 + p:5 + p],
                                       stats[:, p * 2 + 1:p * 2 + 2],
                                       AF.Sqrt, bias=EPS, scale=1.0),
           P.w("s_st", t + 1) + P.w("s_cst", 10), [P.i("s_sq")])

        def lnt(e, p=p):
            e.reciprocal(stats[:, 6 + p:7 + p], stats[:, 4 + p:5 + p])
            return e.tensor_scalar(
                SG[:, p * 1024:(p + 1) * 1024], FR[:, p * 1024:(p + 1) * 1024],
                stats[:, p * 2:p * 2 + 1], stats[:, 6 + p:7 + p],
                ALU.subtract, ALU.mult)
        V(lnt, P.w("s_sq", t + 1) + (P.wd("d_h1w", t - 1) if t >= 2 else []),
          [P.i("s_ln")])
        dma("gpsimd", "d_h1w", h1_hbm[t * 128:(t + 1) * 128, :],
            SG[:, p * 1024:(p + 1) * 1024], waits=P.w("s_ln", t + 1))

    for r in (1, 2):
        for kc in range(KC):
            dma("gpsimd", "d_wm", r3((r % 2) * 16384 + kc * 2048, 2048),
                wmod_in[kc * 128:(kc + 1) * 128, r * 2048:(r + 1) * 2048],
                waits=P.w("s_mm0", r))

    for r in range(3):
        V(lambda e: (e.tensor_copy(fr(0, 0, 1024), PS0[0:1, :]),
                     e.tensor_copy(fr(0, 1024, 2048), PS1[0:1, :]))[-1],
          P.w("s_mm0", r + 1) + P.w("s_ln", NT) + P.wd("d_mr", r),
          [P.i("s_ev0")])
        dma("scalar", "d_mr", modraw[0:1, r * 2048:(r + 1) * 2048],
            FR[0:1, 0:2048], waits=P.w("s_ev0", r + 1) + P.wd("d_mr", r))

    AS, BS = (0, 1024), (1024, 2048)

    def _row_sign_max(e, a, b):
        g = fr(0, *a)
        sgn = fr(0, *b)
        e.tensor_single_scalar(sgn, g, 0.0, ALU.is_ge)
        e.tensor_scalar(sgn, sgn, 2.0, -1.0, ALU.mult, ALU.add)
        e.scalar_tensor_tensor(g, g, -1.0, g, ALU.mult, ALU.max)
        e.tensor_scalar_max(g, g, 0.25)
        return e.tensor_tensor(g, g, sgn, op=ALU.mult)

    dma("sync", "d_rr", fr(0, *AS), modraw[0:1, 2 * H:3 * H], waits=P.wd("d_mr", 3))
    V(lambda e: _row_sign_max(e, AS, BS), P.wd("d_rr", 1), [P.i("s_row")])
    dma("scalar", "d_mw", modh[4:5, :], fr(0, *AS), waits=P.w("s_row", 1))
    dma("sync", "d_rr", fr(0, *AS), modraw[0:1, 5 * H:6 * H],
        waits=P.w("s_row", 1) + P.wd("d_mw", 1))
    V(lambda e: _row_sign_max(e, AS, BS), P.wd("d_rr", 2), [P.i("s_row")])
    dma("scalar", "d_mw", modh[5:6, :], fr(0, *AS), waits=P.w("s_row", 2))
    dma("sync", "d_rr", fr(0, *AS), modraw[0:1, H:2 * H],
        waits=P.w("s_row", 2) + P.wd("d_mw", 2))
    V(lambda e: e.tensor_scalar_add(fr(0, *AS), fr(0, *AS), 1.0),
      P.wd("d_rr", 3), [P.i("s_row")])
    dma("scalar", "d_mw", modh[0:1, :], fr(0, *AS), waits=P.w("s_row", 3))
    dma("sync", "d_rr", fr(0, *AS), modraw[0:1, 4 * H:5 * H],
        waits=P.w("s_row", 3) + P.wd("d_mw", 3))
    V(lambda e: e.tensor_scalar_add(fr(0, *AS), fr(0, *AS), 1.0),
      P.wd("d_rr", 4), [P.i("s_row")])
    dma("scalar", "d_mw", modh[2:3, :], fr(0, *AS), waits=P.w("s_row", 4))
    dma("gpsimd", "d_md", modh[1:2, :], modraw[0:1, 0:H], waits=P.wd("d_mr", 3))
    dma("gpsimd", "d_md", modh[3:4, :], modraw[0:1, 3 * H:4 * H],
        waits=P.wd("d_mr", 3))
    dma("sync", "d_rr", fr(0, *AS), bo_in[0:1, :],
        waits=P.w("s_row", 4) + P.wd("d_mw", 4))
    dma("sync", "d_rr", fr(0, *BS), modh[4:5, :],
        waits=P.w("s_row", 4) + P.wd("d_mw", 4))
    V(lambda e: e.tensor_tensor(fr(0, *AS), fr(0, *AS), fr(0, *BS), op=ALU.mult),
      P.wd("d_rr", 6), [P.i("s_row")])
    dma("scalar", "d_mw", modh[9:10, :], fr(0, *AS), waits=P.w("s_row", 5))
    for dst, row in ((ca1c, 0), (cb1c, 2), (sbc, 3)):
        dma("sync", "d_cols", dst[:, :],
            bass.AP(modh, row * H, [[1, 128], [128, 8]]),
            waits=P.wd("d_mw", 5) + P.wd("d_md", 2), slow=True)
    dma("gpsimd", "d_sac", sacbf[:, :],
        bass.AP(modh, 1 * H, [[1, 128], [128, 8]]),
        waits=P.wd("d_mw", 5) + P.wd("d_md", 2), slow=True)

    barrier()

    WSL = {"q": 8192, "k": 16384, "v": 24576}

    DUNITS = []
    for wsrc, wdst, npt, wcols in ((wg_in, wgq_hbm, 24, 1024),
                                   (wu_in, wuq_hbm, 24, 1024),
                                   (wd_in, wdq_hbm, 8, 3072)):
        for pt in range(npt):
            for hh in range(wcols // 512):
                DUNITS.append((wsrc, wdst, pt, hh))
    ND = len(DUNITS)
    nd_ld, nd_cp, npt_ = [0], [0], [0]

    NDA = 72

    def dslot(u):
        return (u % 4) * 512 if u < NDA else 1024 + (u % 2) * 512

    def emit_dload(u):
        wsrc, wdst, pt, hh = DUNITS[u]
        fro = dslot(u)
        if u < NDA:
            rw = ((P.wd("d_d", u - 1) if u >= 2 else
                   P.w("s_w2", 5) + P.wd("d_bw", 3)) +
                  (P.wd("d_dw", u - 3) if u >= 4 else []))
            dma("gpsimd", "d_d", FR[:, fro:fro + 512],
                wsrc[pt * 128:(pt + 1) * 128, hh * 512:(hh + 1) * 512], waits=rw)
        else:
            k = u - NDA
            rw = ((P.wd("d_d2", k - 1) if k >= 2 else P.wd("d_dw", NDA)) +
                  (P.wd("d_dw", u - 1) if k >= 2 else []))
            dma("sync", "d_d2", FR[:, fro:fro + 512],
                wsrc[pt * 128:(pt + 1) * 128, hh * 512:(hh + 1) * 512], waits=rw)

    def emit_dcomp(u):
        wsrc, wdst, pt, hh = DUNITS[u]
        fro = dslot(u)
        c0 = (u % 4) * 32

        def dv(e, fro=fro, c0=c0):
            e.tensor_reduce(deq[:, c0:c0 + 16],
                            FR[:, fro:fro + 512].rearrange("q (b j) -> q b j", j=32),
                            axis=AX.X, op=ALU.max, apply_absolute_value=True)
            e.tensor_scalar(deq[:, c0:c0 + 16], deq[:, c0:c0 + 16],
                            1.0 / 7.0, 1e-12, ALU.mult, ALU.max)
            return e.reciprocal(deq[:, c0 + 16:c0 + 32], deq[:, c0:c0 + 16])
        ldw = (P.wd("d_d", u + 1) if u < NDA else P.wd("d_d2", u - NDA + 1))
        V(dv, ldw + (P.w("s_dg", u - 3) if u >= 4 else []),
          [P.i("s_da")])
        if u < NDA:
            GP(lambda e, fro=fro, c0=c0:
               e.tensor_tensor(FR[:, fro:fro + 512].rearrange("q (b j) -> q b j", j=32),
                               FR[:, fro:fro + 512].rearrange("q (b j) -> q b j", j=32),
                               deq[:, c0 + 16:c0 + 32].unsqueeze(2)
                               .broadcast_to((128, 16, 32)), op=ALU.mult),
               P.w("s_da", u + 1) + P.w("s_dg1", u), [P.i("s_dg1")])
            V(lambda e, fro=fro, c0=c0: (
                e.tensor_scalar(FR[:, fro:fro + 512], FR[:, fro:fro + 512],
                                MAGIC, -MAGIC, ALU.add, ALU.add),
                e.tensor_tensor(FR[:, fro:fro + 512].rearrange("q (b j) -> q b j", j=32),
                                FR[:, fro:fro + 512].rearrange("q (b j) -> q b j", j=32),
                                deq[:, c0:c0 + 16].unsqueeze(2)
                                .broadcast_to((128, 16, 32)), op=ALU.mult))[-1],
              P.w("s_dg1", u + 1) + P.w("s_dg", u), [P.i("s_dg")])
        else:
            GP(lambda e, fro=fro, c0=c0: (
                e.tensor_tensor(FR[:, fro:fro + 512].rearrange("q (b j) -> q b j", j=32),
                                FR[:, fro:fro + 512].rearrange("q (b j) -> q b j", j=32),
                                deq[:, c0 + 16:c0 + 32].unsqueeze(2)
                                .broadcast_to((128, 16, 32)), op=ALU.mult),
                e.tensor_tensor(FR[:, fro:fro + 512], FR[:, fro:fro + 512],
                                cst[:, 1:2].broadcast_to((128, 512)), op=ALU.add),
                e.tensor_tensor(FR[:, fro:fro + 512], FR[:, fro:fro + 512],
                                cst[:, 1:2].broadcast_to((128, 512)),
                                op=ALU.subtract),
                e.tensor_tensor(FR[:, fro:fro + 512].rearrange("q (b j) -> q b j", j=32),
                                FR[:, fro:fro + 512].rearrange("q (b j) -> q b j", j=32),
                                deq[:, c0:c0 + 16].unsqueeze(2)
                                .broadcast_to((128, 16, 32)), op=ALU.mult))[-1],
               P.w("s_da", u + 1) + P.w("s_dg", u), [P.i("s_dg")])
        dma("gpsimd", "d_dw",
            wdst[pt * 128:(pt + 1) * 128, hh * 512:(hh + 1) * 512],
            FR[:, fro:fro + 512], waits=P.w("s_dg", u + 1) +
            (P.wd("d_dw", u - 1) if u >= 2 else []))

    def dpoint():
        p = npt_[0]
        npt_[0] += 1
        lt = min(NDA, 2 + (p + 1) // 2)
        ct = min(NDA, 2 + (p - 1) // 2) if p >= 2 else 0
        while nd_ld[0] < lt:
            emit_dload(nd_ld[0])
            nd_ld[0] += 1
        while nd_cp[0] < ct:
            emit_dcomp(nd_cp[0])
            nd_cp[0] += 1

    def dflush():
        while nd_ld[0] < NDA:
            emit_dload(nd_ld[0])
            nd_ld[0] += 1
        while nd_cp[0] < NDA:
            emit_dcomp(nd_cp[0])
            nd_cp[0] += 1

    for wn, wsrc in (("q", wq_in), ("k", wk_in), ("v", wv_in)):
        dma("gpsimd", "d_w", r3(WSL[wn], 8192).rearrange("p (k c) -> p k c", c=1024),
            wsrc.ap().rearrange("(k p) c -> p k c", p=128))

    for wi, (wn, b_in, mrow, divq) in enumerate(
            (("q", bq_in, 6, True), ("k", bk_in, 7, False), ("v", bv_in, 8, False))):
        def sawmm(e, wn=wn):
            last = None
            for kc in range(KC):
                e.matmul(PS2[0:1, 0:512], sacbf[:, kc:kc + 1],
                         r3(WSL[wn] + kc * 1024, 512),
                         start=(kc == 0), stop=(kc == KC - 1))
                last = e.matmul(PS2[0:1, 512:1024], sacbf[:, kc:kc + 1],
                                r3(WSL[wn] + kc * 1024 + 512, 512),
                                start=(kc == 0), stop=(kc == KC - 1))
            return last
        TE(sawmm, P.wd("d_w", 3) + P.wd("d_cols", 3) + P.wd("d_sac", 1) + P.w("s_w2", 2 * wi),
           [P.i("s_saw")])
        dma("sync", "d_br", fr(0, *AS), b_in[0:1, :],
            waits=P.w("s_row", 5) + P.w("s_w2", 2 * wi))

        def sawev(e, divq=divq):
            last = e.scalar_tensor_tensor(fr(0, *BS), PS2[0:1, :], 1.0, fr(0, *AS),
                                          ALU.mult, ALU.add)
            if divq:
                last = e.tensor_scalar_mul(fr(0, *BS), fr(0, *BS), 1.0 / 32.0)
            return last
        V(sawev, P.w("s_saw", wi + 1) + P.wd("d_br", wi + 1) + P.wd("d_bw", wi),
          [P.i("s_w2")])
        dma("scalar", "d_bw", modh[mrow:mrow + 1, :], fr(0, *BS),
            waits=P.w("s_w2", 2 * wi + 1))

        def wscale(e, wn=wn, divq=divq):
            last = None
            for kc in range(KC):
                if divq:
                    last = e.tensor_scalar(
                        r3(WSL[wn] + kc * 1024, 1024),
                        r3(WSL[wn] + kc * 1024, 1024),
                        ca1c[:, kc:kc + 1], 1.0 / 32.0, ALU.mult, ALU.mult)
                else:
                    last = e.tensor_single_scalar(
                        r3(WSL[wn] + kc * 1024, 1024),
                        r3(WSL[wn] + kc * 1024, 1024),
                        ca1c[:, kc:kc + 1], ALU.mult)
            return last
        V(wscale, P.w("s_saw", wi + 1), [P.i("s_w2")])

    dma("sync", "d_bc", bqc[:, :],
        bass.AP(modh, 6 * H, [[1, 128], [128, 8]]), waits=P.wd("d_bw", 3), slow=True)
    dma("sync", "d_bc", bkc[:, :],
        bass.AP(modh, 7 * H, [[1, 128], [128, 8]]), waits=P.wd("d_bw", 3), slow=True)
    dma("gpsimd", "d_bvb", SG[:, 2048:3072],
        modh[8:9, :].partition_broadcast(128).squeeze(1), waits=P.wd("d_bw", 3))

    nwin = [0]
    WINFO = []
    nsc, nvv = [0], [0]

    def _reuse_wait(wdx):
        if wdx < 4:
            return []
        k, o = WINFO[wdx - 4]
        return P.w("s_ev2s" if k == "s" else "s_ev2v", o)

    for tq in range(4):
        for kc in range(KC):
            dmaT("sync", "d_h1t", r3(kc * 1024, 1024),
                 h1_hbm[tq * 1024:(tq + 1) * 1024, kc * 128:(kc + 1) * 128],
                 waits=P.wd("d_h1w", 8 * (tq + 1)) +
                 (P.w("s_saw", 3 + tq) if tq >= 1 else []))
        for wn, dstR in (("q", r1), ("k", r2)):
            bcol = bqc if wn == "q" else bkc
            for mc in range(KC):
                for wz in range(2):
                    wdx = nwin[0]
                    nwin[0] += 1
                    ps = (PS0, PS1)[(wdx // 2) % 2]
                    po = (wdx % 2) * 512
                    rw = _reuse_wait(wdx)
                    nsc[0] += 1
                    WINFO.append(("s", nsc[0]))

                    def qkm(e, wn=wn, mc=mc, wz=wz, ps=ps, po=po):
                        last = None
                        for kc in range(KC):
                            last = e.matmul(ps[:, po:po + 512],
                                            r3(WSL[wn] + kc * 1024 + mc * 128, 128),
                                            r3(kc * 1024 + wz * 512, 512),
                                            start=(kc == 0), stop=(kc == KC - 1))
                        return last
                    TE(qkm, P.wd("d_h1t", 8 * (tq + 1)) +
                       P.w("s_w2", 2 if wn == "q" else 4) + rw, [P.i("s_mm2")])
                    SC(lambda e, dstR=dstR, mc=mc, tq=tq, wz=wz, ps=ps, po=po, bcol=bcol:
                       e.activation(dstR(mc, tq * 1024 + wz * 512,
                                         tq * 1024 + wz * 512 + 512),
                                    ps[:, po:po + 512], AF.Identity,
                                    bias=bcol[:, mc:mc + 1], scale=1.0),
                       P.w("s_mm2", wdx + 1) + P.wd("d_bc", 2), [P.i("s_ev2s")])
                    dpoint()
        for tt in range(8):
            gt = tq * 8 + tt
            for wz in range(2):
                wdx = nwin[0]
                nwin[0] += 1
                ps = (PS0, PS1)[(wdx // 2) % 2]
                po = (wdx % 2) * 512
                rw = _reuse_wait(wdx)
                nvv[0] += 1
                WINFO.append(("v", nvv[0]))

                def vmm(e, tt=tt, wz=wz, ps=ps, po=po):
                    last = None
                    for kc in range(KC):
                        last = e.matmul(ps[:, po:po + 512],
                                        r3(kc * 1024 + tt * 128, 128),
                                        r3(WSL["v"] + kc * 1024 + wz * 512, 512),
                                        start=(kc == 0), stop=(kc == KC - 1))
                    return last
                TE(vmm, P.w("s_w2", 6) + rw, [P.i("s_mm2")])
                V(lambda e, gt=gt, wz=wz, ps=ps, po=po:
                  e.scalar_tensor_tensor(
                      SG[:, (gt % 2) * 1024 + wz * 512:(gt % 2) * 1024 + wz * 512 + 512],
                      ps[:, po:po + 512], 1.0,
                      SG[:, 2048 + wz * 512:2048 + (wz + 1) * 512],
                      ALU.mult, ALU.add),
                  P.w("s_mm2", wdx + 1) + P.wd("d_bvb", 1) +
                  (P.wd("d_vw", gt - 1) if gt >= 2 else []),
                  [P.i("s_ev2v")])
                dpoint()
            dma("gpsimd", "d_vw", v_hbm[gt * 128:(gt + 1) * 128, :],
                SG[:, (gt % 2) * 1024:(gt % 2) * 1024 + 1024],
                waits=P.w("s_ev2v", 16 * tq + 2 * tt + 2))
            dpoint()
        TE(lambda e: None, [], [P.i("s_saw")])

    dma("gpsimd", "d_w", r3(WSL["q"], 8192).rearrange("p (k c) -> p k c", c=1024),
        wo_in.ap().rearrange("(k p) c -> p k c", p=128), waits=P.w("s_mm2", 192))
    dma("gpsimd", "d_gab", SG[:, 0:1024],
        modh[4:5, :].partition_broadcast(128).squeeze(1), waits=P.wd("d_vw", NT))

    def woscale(e):
        last = None
        for kc in range(KC):
            last = e.tensor_tensor(r3(WSL["q"] + kc * 1024, 1024),
                                   r3(WSL["q"] + kc * 1024, 1024),
                                   SG[:, 0:1024], op=ALU.mult)
        return last
    V(woscale, P.wd("d_w", 4) + P.wd("d_gab", 1), [P.i("s_w2")])
    dma("gpsimd", "d_wow", wo2_hbm[:, :], r3(WSL["q"], 8192),
        waits=P.w("s_w2", 7))
    dflush()

    barrier()

    for jj in range(8):
        dma("sync", "d_vr", r3(jj * 4096, 4096).rearrange("p (j c) -> p j c", c=1024),
            v_hbm[jj * 512:(jj + 1) * 512, :].rearrange("(j p) c -> p j c", p=128),
            waits=P.wd("d_vw", NT) + P.wd("d_wow", 1))

    CHUNKS = []
    for i in range(NT):
        cs = att_chunks(i)
        for ci, (j0, nj) in enumerate(cs):
            CHUNKS.append((i, j0, nj, ci == 0, ci == len(cs) - 1))
    G = len(CHUNKS)
    NI = [0]


    TE_ORDER = []
    for g in range(G):
        TE_ORDER.append(("S", g))
        if g >= 1:
            TE_ORDER.append(("ptt", g - 1))
        if g >= 2:
            TE_ORDER.append(("ctx", g - 2))
    TE_ORDER += [("ptt", G - 1), ("ctx", G - 2), ("ctx", G - 1)]
    TPOS = {kg: i + 1 for i, kg in enumerate(TE_ORDER)}

    V_ORDER = []
    for g in range(G):
        V_ORDER += [("vmax", g), ("PTevac", g)]
        if g >= 1:
            V_ORDER.append(("vmerge", g - 1))
    V_ORDER.append(("vmerge", G - 1))
    VPOS = {kg: i + 1 for i, kg in enumerate(V_ORDER)}

    def emit_S(g):
        i, j0, nj, first, last = CHUNKS[g]
        ps = (PS0, PS1)[g % 2]

        def Smm(e, i=i, j0=j0, nj=nj, ps=ps):
            last_ = None
            for kc in range(KC):
                c0 = 0
                while c0 < nj * 128:
                    cw = min(512, nj * 128 - c0)
                    last_ = e.matmul(ps[:, c0:c0 + cw],
                                     r1(kc, i * 128, (i + 1) * 128),
                                     r2(kc, j0 * 128 + c0, j0 * 128 + c0 + cw),
                                     start=(kc == 0), stop=(kc == KC - 1))
                    c0 += cw
            return last_
        TE(Smm, P.w("s_exp", g - 1), [P.i("s_T3")])

    def emit_ptt(g):
        i, j0, nj, first, last = CHUNKS[g]
        pg = g % 2

        def ptt(e, nj=nj, pg=pg):
            last_ = None
            for jj in range(nj):
                last_ = e.transpose(
                    PS3[:, pg * 1024 + jj * 128:pg * 1024 + (jj + 1) * 128],
                    SG[:, pg * 1024 + jj * 128:pg * 1024 + (jj + 1) * 128],
                    idn[:, :])
            return last_
        TE(ptt, P.w("s_exp", g + 1) +
           (P.w("s_v3", VPOS[("PTevac", g - 1)]) if g >= 1 else []),
           [P.i("s_T3")])

    def emit_ctx(g):
        i, j0, nj, first, last = CHUNKS[g]
        pg = g % 2

        def ctxmm(e, j0=j0, nj=nj, pg=pg):
            last_ = None
            for jj in range(nj):
                for wz in range(2):
                    last_ = e.matmul(
                        PS2[:, wz * 512:(wz + 1) * 512],
                        SG[:, pg * 1024 + jj * 128:pg * 1024 + (jj + 1) * 128],
                        r3((j0 + jj) * 1024 + wz * 512, 512),
                        start=(jj == 0), stop=(jj == nj - 1))
            return last_
        cw = (P.w("s_v3", VPOS[("vmerge", g - 1)]) if g >= 1
              else P.w("s_v3", VPOS[("PTevac", 0)]))
        TE(ctxmm, cw + P.wd("d_vr", 8), [P.i("s_T3")])

    def emit_vme(g):
        i, j0, nj, first, last = CHUNKS[g]
        ps = (PS0, PS1)[g % 2]
        pg = g % 2
        w = nj * 128

        def vmax(e, i=i, j0=j0, first=first, last=last, ps=ps, pg=pg, w=w):
            if last:
                dof = (i - j0) * 128
                e.tensor_tensor(ps[:, dof:dof + 128], ps[:, dof:dof + 128],
                                maskd[:, :], op=ALU.add)
            if first:
                e.tensor_reduce(stats[:, 14 + pg:15 + pg], ps[:, 0:w], axis=AX.X,
                                op=ALU.max)
            else:
                e.tensor_reduce(stats[:, 4 + pg:5 + pg], ps[:, 0:w], axis=AX.X,
                                op=ALU.max)
                e.tensor_single_scalar(stats[:, 14 + pg:15 + pg],
                                       stats[:, 4 + pg:5 + pg], stats[:, 0:1],
                                       ALU.max)
                e.tensor_single_scalar(stats[:, 8 + pg:9 + pg], stats[:, 0:1],
                                       stats[:, 14 + pg:15 + pg], ALU.subtract)
            return e.tensor_copy(stats[:, 0:1], stats[:, 14 + pg:15 + pg])
        V(vmax, P.w("s_T3", TPOS[("S", g)]), [P.i("s_v3")])

        def sexp(e, first=first, ps=ps, pg=pg, w=w):
            e.activation(stats[:, 10 + pg:11 + pg], stats[:, 14 + pg:15 + pg],
                         AF.Identity, scale=-1.0)
            if not first:
                e.activation(stats[:, 12 + pg:13 + pg], stats[:, 8 + pg:9 + pg], AF.Exp)
            return e.activation(SG[:, pg * 1024:pg * 1024 + w], ps[:, 0:w], AF.Exp,
                                bias=stats[:, 10 + pg:11 + pg], scale=1.0,
                                accum_out=stats[:, 6 + pg:7 + pg])
        sw = P.w("s_v3", VPOS[("vmax", g)])
        if g >= 2:
            sw = sw + P.w("s_T3", TPOS[("ctx", g - 2)])
        SC(sexp, sw, [P.i("s_exp")])

        V(lambda e, nj=nj, pg=pg:
          e.tensor_copy(SG[:, pg * 1024:pg * 1024 + nj * 128],
                        PS3[:, pg * 1024:pg * 1024 + nj * 128]),
          P.w("s_T3", TPOS[("ptt", g)]), [P.i("s_v3")])

    def emit_vmerge(g):
        i, j0, nj, first, last = CHUNKS[g]
        pg = g % 2

        def vmerge(e, i=i, first=first, last=last, pg=pg):
            last_ = None
            if first:
                e.tensor_copy(FR[:, 0:1024], PS2[:, :])
                last_ = e.tensor_copy(stats[:, 1:2], stats[:, 6 + pg:7 + pg])
            else:
                e.scalar_tensor_tensor(FR[:, 0:1024], FR[:, 0:1024],
                                       stats[:, 12 + pg:13 + pg], PS2[:, :],
                                       ALU.mult, ALU.add)
                last_ = e.scalar_tensor_tensor(stats[:, 1:2], stats[:, 1:2],
                                               stats[:, 12 + pg:13 + pg],
                                               stats[:, 6 + pg:7 + pg],
                                               ALU.mult, ALU.add)
            if last:
                e.reciprocal(stats[:, 3:4], stats[:, 1:2])
                last_ = e.tensor_single_scalar(SG[:, 2048:3072], FR[:, 0:1024],
                                               stats[:, 3:4], ALU.mult)
            return last_
        mwaits = P.w("s_T3", TPOS[("ctx", g)])
        if last and i >= 1:
            mwaits = mwaits + P.wd("d_cw", i)
        V(vmerge, mwaits, [P.i("s_v3")] + ([P.i("s_cbf")] if last else []))
        if last:
            NI[0] += 1
            dma("gpsimd", "d_cw", ctx_hbm[i * 128:(i + 1) * 128, :],
                SG[:, 2048:3072], waits=P.w("s_cbf", NI[0]))

    for g in range(G):
        emit_S(g)
        if g >= 1:
            emit_ptt(g - 1)
        if g >= 2:
            emit_ctx(g - 2)
        emit_vme(g)
        if g >= 1:
            emit_vmerge(g - 1)
        t2 = NDA + min(ND - NDA, max(0, g - 7))
        while nd_ld[0] < t2:
            emit_dload(nd_ld[0])
            nd_ld[0] += 1
            emit_dcomp(nd_cp[0])
            nd_cp[0] += 1
    while nd_cp[0] < ND:
        if nd_ld[0] < ND:
            emit_dload(nd_ld[0])
            nd_ld[0] += 1
        emit_dcomp(nd_cp[0])
        nd_cp[0] += 1
    emit_ptt(G - 1)
    emit_ctx(G - 2)
    emit_ctx(G - 1)
    emit_vmerge(G - 1)

    barrier()

    for kc in range(KC):
        dmaT("sync", "d_ct", r2(kc, 0, 4096),
             ctx_hbm[:, kc * 128:(kc + 1) * 128], waits=P.wd("d_cw", NT))
    dma("sync", "d_p4c", r3(WSL["q"], 8192), wo2_hbm[:, :], waits=P.wd("d_wow", 1))
    dma("gpsimd", "d_bor", SG[0:1, 2048:3072], modh[9:10, :],
        waits=P.wd("d_cw", NT))

    def emit_p4t(tt):
        p = tt % 2

        def ptt4(e, tt=tt, p=p):
            last = None
            for kc in range(KC):
                last = e.transpose(
                    PS3[:, p * 1024 + kc * 128:p * 1024 + (kc + 1) * 128],
                    SG[:, p * 1024 + kc * 128:p * 1024 + (kc + 1) * 128],
                    idn[:, :])
            return last
        TE(ptt4, P.w("s_ln", NT + tt + 1) +
           (P.w("s_p4e", tt - 1) if tt >= 2 else []), [P.i("s_p4t")])
        V(lambda e, tt=tt, p=p: e.tensor_copy(
            R1[:, :].rearrange("q (k c) -> q k c", c=4096)[:, :, tt * 128:(tt + 1) * 128],
            PS3[:, p * 1024:(p + 1) * 1024].rearrange("q (k c) -> q k c", c=128)),
          P.w("s_p4t", tt + 1), [P.i("s_p4e")])

    for tt in range(NT):
        ps = (PS0, PS1)[tt % 2]
        p = tt % 2
        dma("sync", "d_xr", FR[:, p * 1024:(p + 1) * 1024],
            x_in[tt * 128:(tt + 1) * 128, :],
            waits=(P.w("s_ln", NT + tt - 1) + P.wd("d_x2w", tt - 1)
                   if tt >= 2 else []))

        def atmm(e, tt=tt, ps=ps):
            last = None
            for wz in range(2):
                for kc in range(KC):
                    e.matmul(ps[:, wz * 512:(wz + 1) * 512],
                             r2(kc, tt * 128, (tt + 1) * 128),
                             r3(WSL["q"] + kc * 1024 + wz * 512, 512),
                             start=(kc == 0), stop=False)
                last = e.matmul(ps[:, wz * 512:(wz + 1) * 512], onesrow[0:1, :],
                                SG[0:1, 2048 + wz * 512:2048 + (wz + 1) * 512],
                                start=False, stop=True)
            return last
        TE(atmm, P.wd("d_ct", 8) + P.wd("d_p4c", 1) + P.wd("d_bor", 1) + P.w("s_x2", tt - 1),
           [P.i("s_at")])

        def x2ev(e, tt=tt, ps=ps, p=p):
            e.scalar_tensor_tensor(FR[:, p * 1024:(p + 1) * 1024],
                                   ps[:, :], 1.0, FR[:, p * 1024:(p + 1) * 1024],
                                   ALU.mult, ALU.add)
            e.bn_stats(bnb[:, p * 12:p * 12 + 6], FR[:, p * 1024:p * 1024 + 512])
            e.bn_stats(bnb[:, p * 12 + 6:p * 12 + 12],
                       FR[:, p * 1024 + 512:(p + 1) * 1024])
            return e.bn_aggr(stats[:, p * 2:p * 2 + 2],
                             bnb[:, p * 12:p * 12 + 12])
        V(x2ev, P.w("s_at", tt + 1) + P.wd("d_xr", tt + 1),
          [P.i("s_x2"), P.i("s_st")])
        SC(lambda e, p=p: e.activation(stats[:, 4 + p:5 + p],
                                       stats[:, p * 2 + 1:p * 2 + 2],
                                       AF.Sqrt, bias=EPS, scale=1.0),
           P.w("s_st", NT + tt + 1), [P.i("s_sq")])

        def tev(e, p=p):
            e.reciprocal(stats[:, 6 + p:7 + p], stats[:, 4 + p:5 + p])
            return e.tensor_scalar(
                SG[:, p * 1024:(p + 1) * 1024], FR[:, p * 1024:(p + 1) * 1024],
                stats[:, p * 2:p * 2 + 1], stats[:, 6 + p:7 + p],
                ALU.subtract, ALU.mult)
        V(tev, P.w("s_sq", NT + tt + 1) + (P.w("s_p4t", tt - 1) if tt >= 2 else []),
          [P.i("s_ln")])
        dma("gpsimd", "d_x2w", x2_hbm[tt * 128:(tt + 1) * 128, :],
            FR[:, p * 1024:(p + 1) * 1024], waits=P.w("s_x2", tt + 1))
        if tt >= 1:
            emit_p4t(tt - 1)
    emit_p4t(NT - 1)

    barrier()

    for kc in range(KC):
        V(lambda e, kc=kc: (
            e.tensor_scalar(r1(kc, 0, 4096), r1(kc, 0, 4096),
                            cb1c[:, kc:kc + 1], sbc[:, kc:kc + 1],
                            ALU.mult, ALU.add),
            e.tensor_reduce(mxs[:, kc:kc + 1], r1(kc, 0, 4096), axis=AX.X,
                            op=ALU.max, apply_absolute_value=True))[-1],
          P.w("s_p4e", NT), [P.i("s_f0a")])
    V(lambda e: e.tensor_reduce(stats[:, 14:15], mxs[:, :], axis=AX.X, op=ALU.max),
      P.w("s_f0a", 8), [P.i("s_f0a")])

    def add_collective(idx, reduce_emit, outcols, prwait, repk):
        GP(reduce_emit, prwait, [P.i("s_pr")])
        dma("gpsimd", "d_ccw", cci[idx][0:8, :], scl[0:1, 16:24],
            waits=P.w("s_pr", 2 * idx + 1))
        GP(lambda e, idx=idx: e.collective_compute(
            "AllReduce", ALU.max, replica_groups=[list(range(8))],
            ins=[cci[idx].ap().opt()], outs=[cco[idx].ap().opt()]),
           P.wd("d_ccw", idx + 1), [P.i("s_cc")])
        for rrow, dcol in outcols:
            dma("sync", "d_rep", scl[:, dcol:dcol + 1],
                cco[idx][rrow:rrow + 1, :].partition_broadcast(128).squeeze(1),
                waits=P.w("s_cc", idx + 1))
        return P.wd("d_rep", repk)

    rw = add_collective(0, lambda e: e.tensor_reduce(
        scl[0:1, 16:17], stats[:, 14:15], axis=AX.XYZWC, op=ALU.max),
        [(0, 0)], P.w("s_f0a", 9), 1)
    V(lambda e: (
        e.tensor_scalar(scl[:, 1:2], scl[:, 0:1], QMAXI, 1e-12, ALU.mult, ALU.max),
        e.reciprocal(scl[:, 2:3], scl[:, 1:2]))[-1],
      rw, [P.i("s_scl")])

    for kc in range(KC):
        V(lambda e, kc=kc: (
            e.tensor_scalar(FR[:, 0:2048], r1(kc, 0, 2048),
                            scl[:, 2:3], MAGIC, ALU.mult, ALU.add),
            e.tensor_scalar(r1(kc, 0, 2048), FR[:, 0:2048],
                            -MAGIC, scl[:, 1:2], ALU.add, ALU.mult),
            e.tensor_scalar(FR[:, 0:2048], r1(kc, 2048, 4096),
                            scl[:, 2:3], MAGIC, ALU.mult, ALU.add),
            e.tensor_scalar(r1(kc, 2048, 4096), FR[:, 0:2048],
                            -MAGIC, scl[:, 1:2], ALU.add, ALU.mult))[-1],
          P.w("s_scl", 1), [P.i("s_scl")])

    for kc in range(KC):
        dmaT("sync", "d_wgt", r2(kc, 0, 3072),
             wgq_hbm[:, kc * 128:(kc + 1) * 128], waits=P.wd("d_dw", 144))
    for kc in range(KC):
        dmaT("sync", "d_wgt", r3(kc * 4096, 3072),
             wuq_hbm[:, kc * 128:(kc + 1) * 128], waits=P.wd("d_dw", 144))

    barrier()

    f1w = [0]
    for mi in range(MI):
        for tc in range(8):
            for srcname, dhbm, mcol in (("g", graw_hbm, 0), ("u", uraw_hbm, 1)):
                wdx = f1w[0]
                f1w[0] += 1
                ps = (PS0, PS1, PS2)[(wdx // 2) % 3]
                po = (wdx % 2) * 512

                def f1mm(e, srcname=srcname, mi=mi, tc=tc, ps=ps, po=po):
                    last = None
                    for kc in range(KC):
                        if srcname == "g":
                            lhs = r2(kc, mi * 128, (mi + 1) * 128)
                        else:
                            lhs = r3(kc * 4096 + mi * 128, 128)
                        last = e.matmul(ps[:, po:po + 512], lhs,
                                        r1(kc, tc * 512, (tc + 1) * 512),
                                        start=(kc == 0), stop=(kc == KC - 1))
                    return last
                TE(f1mm, P.wd("d_wgt", 16) + P.w("s_scl", 9) +
                   P.w("s_f1e", wdx - 5), [P.i("s_f1m")])
                sgo = (wdx % 4) * 512
                SC(lambda e, ps=ps, po=po, sgo=sgo:
                   e.copy(SG[:, sgo:sgo + 512], ps[:, po:po + 512]),
                   P.w("s_f1m", wdx + 1) +
                   (P.wd("d_f1w", wdx - 3) + P.w("s_f1r", wdx - 3) if wdx >= 4 else []),
                   [P.i("s_f1e")])
                dma("scalar", "d_f1w",
                    dhbm[mi * 128:(mi + 1) * 128, tc * 512:(tc + 1) * 512],
                    SG[:, sgo:sgo + 512],
                    waits=(P.wd("d_f1w", wdx - 1) if wdx >= 2 else []))
                V(lambda e, sgo=sgo, mcol=mcol: (
                    e.tensor_reduce(mrg[:, 4:5], SG[:, sgo:sgo + 512], axis=AX.X,
                                    op=ALU.max, apply_absolute_value=True),
                    e.tensor_tensor(mrg[:, mcol:mcol + 1], mrg[:, mcol:mcol + 1],
                                    mrg[:, 4:5], op=ALU.max))[-1],
                  P.w("s_f1e", wdx + 1), [P.i("s_f1r")])

    GP(lambda e: None, P.w("s_f1r", 384), [P.i("s_pr")])
    rw = add_collective(1, lambda e: (
        e.tensor_reduce(scl[0:1, 16:17], mrg[:, 0:1], axis=AX.XYZWC, op=ALU.max),
        e.tensor_reduce(scl[0:1, 17:18], mrg[:, 1:2], axis=AX.XYZWC, op=ALU.max))[-1],
        [(0, 3), (1, 6)], P.w("s_pr", 2), 3)
    V(lambda e: (
        e.tensor_scalar(scl[:, 4:5], scl[:, 3:4], QMAXI, 1e-12, ALU.mult, ALU.max),
        e.reciprocal(scl[:, 5:6], scl[:, 4:5]),
        e.tensor_scalar(scl[:, 7:8], scl[:, 6:7], QMAXI, 1e-12, ALU.mult, ALU.max),
        e.reciprocal(scl[:, 8:9], scl[:, 7:8]),
        e.tensor_single_scalar(scl[:, 18:19], scl[:, 4:5], -MAGIC, ALU.mult))[-1],
      rw, [P.i("s_scl")])

    dma("gpsimd", "d_gbb", SG[:, 2048:3072],
        modh[5:6, :].partition_broadcast(128).squeeze(1), waits=P.w("s_f1m", 384))
    for mi in range(MI):
        dmaT("sync", "d_wdt", R2[:, mi * 1024:(mi + 1) * 1024],
             wdq_hbm[:, mi * 128:(mi + 1) * 128],
             waits=P.wd("d_dw", 144) + P.w("s_f1m", 384) +
             (P.w("s_wd", mi - 1) if mi >= 2 else []))
        V(lambda e, mi=mi:
          e.tensor_tensor(R2[:, mi * 1024:(mi + 1) * 1024],
                          R2[:, mi * 1024:(mi + 1) * 1024], SG[:, 2048:3072],
                          op=ALU.mult),
          P.wd("d_wdt", mi + 1) + P.wd("d_gbb", 1), [P.i("s_wd")])

    barrier()

    NU2 = MI * 4
    for u in range(NU2):
        mi, qq = divmod(u, 4)
        rin = (u % 4) * 1024
        psA = (PS0, PS1)[u % 2]
        frb = (u % 2) * 1024
        dma("sync", "d_f2r", R1[:, rin:rin + 1024],
            graw_hbm[mi * 128:(mi + 1) * 128, qq * 1024:(qq + 1) * 1024],
            waits=(P.wd("d_f2r", u - 1) if u >= 2 else []) +
            (P.wd("d_f2w", u - 3) if u >= 4 else []))
        SC(lambda e, rin=rin, psA=psA, frb=frb: (
            e.activation(psA[:, :], R1[:, rin:rin + 1024], AF.Identity,
                         bias=MAGIC, scale=scl[:, 5:6]),
            e.activation(FR[:, frb:frb + 1024], psA[:, :], AF.Sigmoid,
                         bias=scl[:, 18:19], scale=scl[:, 4:5]))[-1],
           P.wd("d_f2r", u + 1) + P.w("s_scl", 10) +
           (P.w("s_2v", u - 1) if u >= 2 else []),
           [P.i("s_2s")])
        V(lambda e, rin=rin, psA=psA, frb=frb: (
            e.tensor_scalar(FR[:, frb:frb + 1024], FR[:, frb:frb + 1024],
                            65536.0, MAGIC, ALU.mult, ALU.add),
            e.tensor_scalar(FR[:, frb:frb + 1024], FR[:, frb:frb + 1024],
                            -MAGIC, 65535.0, ALU.add, ALU.min),
            e.scalar_tensor_tensor(R1[:, rin:rin + 1024], psA[:, :], -MAGIC,
                                   FR[:, frb:frb + 1024], ALU.add, ALU.mult))[-1],
          P.w("s_2s", u + 1), [P.i("s_2v")])
        for v in ([u - 1] if u >= 1 else []) + ([u] if u == NU2 - 1 else []):
            vm, vq = divmod(v, 4)
            vr = (v % 4) * 1024
            V(lambda e, v=v, vr=vr: e.tensor_reduce(
                deq[:, v:v + 1], R1[:, vr:vr + 1024],
                axis=AX.X, op=ALU.max, apply_absolute_value=True),
              P.w("s_2v", v + 1), [P.i("s_2r")])
            dma("gpsimd", "d_f2w",
                prod_hbm[vm * 128:(vm + 1) * 128, vq * 1024:(vq + 1) * 1024],
                R1[:, vr:vr + 1024],
                waits=P.w("s_2r", v + 1) + (P.wd("d_f2w", v - 1) if v >= 2 else []))

    V(lambda e: e.tensor_reduce(mrg[:, 2:3], deq[:, 0:NU2], axis=AX.X,
                                op=ALU.max),
      P.w("s_2r", NU2), [P.i("s_2r")])
    GP(lambda e: None, P.w("s_2r", NU2 + 1), [P.i("s_pr")])
    rw = add_collective(2, lambda e: e.tensor_reduce(
        scl[0:1, 16:17], mrg[:, 2:3], axis=AX.XYZWC, op=ALU.max),
        [(0, 9)], P.w("s_pr", 4), 4)
    V(lambda e: (
        e.tensor_tensor(scl[:, 15:16], scl[:, 9:10], scl[:, 4:5], op=ALU.mult),
        e.tensor_scalar(scl[:, 10:11], scl[:, 15:16], QMAXI / 65536.0, 1e-12,
                        ALU.mult, ALU.max),
        e.tensor_single_scalar(scl[:, 15:16], scl[:, 10:11], 65536.0, ALU.mult),
        e.reciprocal(scl[:, 11:12], scl[:, 15:16]),
        e.tensor_tensor(scl[:, 11:12], scl[:, 11:12], scl[:, 4:5],
                        op=ALU.mult),
        e.tensor_single_scalar(scl[:, 19:20], scl[:, 10:11], -MAGIC,
                               ALU.mult))[-1],
      rw, [P.i("s_scl")])

    barrier()

    for u in range(NU2):
        mi, qq = divmod(u, 4)
        rp = (u % 4) * 1024
        ru = 4096 + (u % 4) * 1024
        fra = (u % 2) * 1024
        psB = (PS0, PS1)[u % 2]
        dma("sync", "d_f3r", R1[:, rp:rp + 1024],
            prod_hbm[mi * 128:(mi + 1) * 128, qq * 1024:(qq + 1) * 1024],
            waits=(P.wd("d_f3r", u - 1) if u >= 2 else []) +
            (P.wd("d_f3w", u - 3) if u >= 4 else []))
        dma("sync", "d_f3u", R1[:, ru:ru + 1024],
            uraw_hbm[mi * 128:(mi + 1) * 128, qq * 1024:(qq + 1) * 1024],
            waits=(P.wd("d_f3u", u - 1) if u >= 2 else []) +
            (P.w("s_31", u - 3) if u >= 4 else []))
        SC(lambda e, rp=rp, ru=ru, fra=fra, psB=psB: (
            e.activation(psB[:, :], R1[:, rp:rp + 1024], AF.Identity,
                         bias=MAGIC, scale=scl[:, 11:12]),
            e.activation(FR[:, fra:fra + 1024], R1[:, ru:ru + 1024], AF.Identity,
                         bias=MAGIC, scale=scl[:, 8:9]))[-1],
           P.wd("d_f3r", u + 1) + P.wd("d_f3u", u + 1) + P.w("s_scl", 11) +
           (P.w("s_3v", u - 1) if u >= 2 else []),
           [P.i("s_31")])
        V(lambda e, rp=rp, fra=fra, psB=psB: (
            e.tensor_scalar(FR[:, fra:fra + 1024], FR[:, fra:fra + 1024],
                            -MAGIC, scl[:, 7:8], ALU.add, ALU.mult),
            e.scalar_tensor_tensor(R1[:, rp:rp + 1024], psB[:, :], -MAGIC,
                                   FR[:, fra:fra + 1024], ALU.add, ALU.mult))[-1],
          P.w("s_31", u + 1), [P.i("s_3v")])
        for v in ([u - 1] if u >= 1 else []) + ([u] if u == NU2 - 1 else []):
            vm, vq = divmod(v, 4)
            vr = (v % 4) * 1024
            V(lambda e, v=v, vr=vr: e.tensor_reduce(
                deq[:, v:v + 1], R1[:, vr:vr + 1024],
                axis=AX.X, op=ALU.max, apply_absolute_value=True),
              P.w("s_3v", v + 1), [P.i("s_3r")])
            dma("gpsimd", "d_f3w",
                usc_hbm[vm * 128:(vm + 1) * 128, vq * 1024:(vq + 1) * 1024],
                R1[:, vr:vr + 1024],
                waits=P.w("s_3r", v + 1) + (P.wd("d_f3w", v - 1) if v >= 2 else []))

    V(lambda e: e.tensor_reduce(mrg[:, 3:4], deq[:, 0:NU2], axis=AX.X,
                                op=ALU.max),
      P.w("s_3r", NU2), [P.i("s_3r")])
    GP(lambda e: None, P.w("s_3r", NU2 + 1), [P.i("s_pr")])
    rw = add_collective(3, lambda e: e.tensor_reduce(
        scl[0:1, 16:17], mrg[:, 3:4], axis=AX.XYZWC, op=ALU.max),
        [(0, 12)], P.w("s_pr", 6), 5)
    V(lambda e: (
        e.tensor_tensor(scl[:, 21:22], scl[:, 12:13], scl[:, 10:11], op=ALU.mult),
        e.tensor_scalar(scl[:, 13:14], scl[:, 21:22], QMAXI, 1e-12, ALU.mult, ALU.max),
        e.reciprocal(scl[:, 14:15], scl[:, 13:14]),
        e.tensor_tensor(scl[:, 20:21], scl[:, 14:15], scl[:, 10:11],
                        op=ALU.mult))[-1],
      rw, [P.i("s_scl")])

    barrier()

    def emit_f4q(tc):
        sp = (tc % 2) * 12288
        for mi in range(MI):
            un = tc * MI + mi
            p = un % 2
            dma("sync", "d_u4", SG[:, p * 512:(p + 1) * 512],
                usc_hbm[mi * 128:(mi + 1) * 128, tc * 512:(tc + 1) * 512],
                waits=(P.w("s_41", un - 1) if un >= 2 else []))
            SC(lambda e, p=p:
               e.activation(PS2[:, p * 512:(p + 1) * 512],
                            SG[:, p * 512:(p + 1) * 512], AF.Identity,
                            bias=MAGIC, scale=scl[:, 20:21]),
               P.wd("d_u4", un + 1) + P.w("s_scl", 12) +
               (P.w("s_42", un - 1) if un >= 2 else []), [P.i("s_41")])
            V(lambda e, sp=sp, mi=mi, p=p:
              e.tensor_scalar(R3[:, sp + mi * 512:sp + (mi + 1) * 512],
                              PS2[:, p * 512:(p + 1) * 512],
                              -MAGIC, scl[:, 13:14], ALU.add, ALU.mult),
              P.w("s_41", un + 1) +
              (P.w("s_f4m", (tc - 1) * 4) if tc >= 2 and mi == 0 else []),
              [P.i("s_42")])

    def emit_f4mm(tc):
        sp = (tc % 2) * 12288
        for tt in range(4):
            tg = tc * 4 + tt
            ps = (PS0, PS1)[tg % 2]
            p = tg % 2
            dma("sync", "d_x2r", FR[:, p * 1024:(p + 1) * 1024],
                x2_hbm[tg * 128:(tg + 1) * 128, :],
                waits=(P.w("s_f4e", tg - 1) + P.wd("d_out", tg - 1)
                       if tg >= 2 else P.wd("d_x2w", NT)))

            def f4mm(e, sp=sp, tt=tt, ps=ps):
                last = None
                for wz in range(2):
                    for mi in range(MI):
                        last = e.matmul(
                            ps[:, wz * 512:(wz + 1) * 512],
                            R3[:, sp + mi * 512 + tt * 128:sp + mi * 512 + (tt + 1) * 128],
                            R2[:, mi * 1024 + wz * 512:mi * 1024 + (wz + 1) * 512],
                            start=(mi == 0), stop=(mi == MI - 1))
                return last
            TE(f4mm, P.w("s_42", (tc + 1) * MI) + P.w("s_wd", MI) +
               P.w("s_f4e", tg - 1), [P.i("s_f4m")])
            V(lambda e, p=p, ps=ps:
              e.scalar_tensor_tensor(FR[:, p * 1024:(p + 1) * 1024],
                                     ps[:, :], 1.0, FR[:, p * 1024:(p + 1) * 1024],
                                     ALU.mult, ALU.add),
              P.w("s_f4m", tg + 1) + P.wd("d_x2r", tg + 1),
              [P.i("s_f4e")])
            dma("scalar", "d_out", out_hbm[tg * 128:(tg + 1) * 128, :],
                FR[:, p * 1024:(p + 1) * 1024], waits=P.w("s_f4e", tg + 1))

    for tc in range(8):
        emit_f4q(tc)
        if tc >= 1:
            emit_f4mm(tc - 1)
    emit_f4mm(7)

    for name, dsttap in tap_outs.items():
        src = tap_srcs[name]
        dma("gpsimd", "d_tap", dsttap[:, :], src[:, :],
            waits=P.wd("d_out") + P.wd("d_f3w") + P.wd("d_dw") + P.wd("d_f2w") +
            P.wd("d_f1w") + P.wd("d_mr") + P.wd("d_mw") + P.wd("d_cw") +
            P.wd("d_x2w") + P.wd("d_tw") + P.wd("d_vw") + P.wd("d_h1w") +
            P.wd("d_wow"))

    SC(lambda e: None, P.wd("d_out"))
    if taps:
        GP(lambda e: None, P.wd("d_tap"))

    for E in ENGS:
        for waits, emit, incs in (
                (s for s in P.steps[E] if not isinstance(s[1], tuple))):
            for (n, v) in waits:
                if n == "__tpos__":
                    continue
                assert v <= P.cnt.get(n, 0), (
                    f"{E}: wait {n}>={v} unreachable (final {P.cnt.get(n, 0)})")

    inc_engs = {}
    for E in ENGS:
        for step in P.steps[E]:
            if isinstance(step[1], tuple):
                continue
            for (n, v) in step[2]:
                inc_engs.setdefault(n, set()).add(E)
    mergeable = {
        n for n, es in inc_engs.items()
        if len(es) == 1 and next(iter(es)) != "gpsimd"
        and not n.startswith("d_") and n not in NOMERGE
    }
    trans = {}
    phys_final = {E: 0 for E in ENGS}
    for E in ENGS:
        pos = 0
        lcnt = {}
        for step in P.steps[E]:
            if isinstance(step[1], tuple):
                continue
            stepm = [(n, v) for (n, v) in step[2] if n in mergeable]
            pos_end = pos + sum(v for (n, v) in stepm)
            for (n, v) in stepm:
                lcnt[n] = lcnt.get(n, 0) + v
                trans[(n, lcnt[n])] = pos_end
            pos = pos_end
        phys_final[E] = pos
    eng_of = {n: next(iter(es)) for n, es in inc_engs.items() if n in mergeable}

    sems = {}
    for E in ENGS:
        sems["@" + E] = e_(nc.semaphore("se_" + E))
    for n in P.cnt:
        if n not in mergeable:
            sems[n] = e_(nc.semaphore(n))
    sbar = e_(nc.semaphore("sbar"))

    def xlate(n, v):
        if n == "__tpos__":
            return ("@tensor", v)
        if n in mergeable:
            return ("@" + eng_of[n], trans[(n, v)])
        return (n, v)

    _NODRAIN = {"dma_start", "dma_start_transpose", "collective_compute",
                "wait_ge", "sem_inc", "drain", "nop", "engine_nop"}

    class _DrainProxy:

        def __init__(self, e):
            object.__setattr__(self, "_e", e)

        def __getattr__(self, name):
            f = getattr(self._e, name)
            if not callable(f) or name in _NODRAIN or name.startswith("_"):
                return f

            def wrap(*a, **kw):
                r = f(*a, **kw)
                if r is not None and hasattr(r, "then_inc"):
                    self._e.drain()
                return r
            return wrap

    def make_closure(E):
        def closure(raw_e):
            e = raw_e if E == "tensor" else _DrainProxy(raw_e)
            for waits, emit, incs in P.steps[E]:
                if isinstance(emit, tuple) and emit[0] == "BARRIER":
                    for (n, v) in emit[2]:
                        if v > 0:
                            raw_e.wait_ge(sems[n], v)
                    raw_e.sem_inc(sbar, 1)
                    raw_e.wait_ge(sbar, 5 * emit[1])
                    continue
                for (n, v) in waits:
                    if v > 0:
                        pn, pv = xlate(n, v)
                        raw_e.wait_ge(sems[pn], pv)
                inst = emit(e)
                agg = {}
                for (n, v) in incs:
                    pn = "@" + eng_of[n] if n in mergeable else n
                    agg[pn] = agg.get(pn, 0) + v
                if inst is None and agg:
                    inst = raw_e.nop()
                for pn, v in agg.items():
                    inst = inst.then_inc(sems[pn], v)
        return closure

    block = e_(nc.Block())
    block.sync(make_closure("sync"))
    block.vector(make_closure("vector"))
    block.scalar(make_closure("scalar"))
    block.tensor(make_closure("tensor"))
    block.gpsimd(make_closure("gpsimd"))

    st_.close()
    return nc


_NC_CACHE = {}


def _get_nc(taps=()):
    key = tuple(sorted(taps))
    if key not in _NC_CACHE:
        _NC_CACHE[key] = build_nc(taps=key)
    return _NC_CACHE[key]


def make_in_maps(inputs):
    y = np.ascontiguousarray(np.asarray(inputs["L0_attn_y_all"], dtype=np.float32))
    c = np.asarray(inputs["L0_c"], dtype=np.float32)
    base = {
        "wmod": np.ascontiguousarray(np.asarray(inputs["L0_wmod"], np.float32)),
        "wq": np.ascontiguousarray(np.asarray(inputs["L0_attn_Wq"], np.float32)),
        "wk": np.ascontiguousarray(np.asarray(inputs["L0_attn_Wk"], np.float32)),
        "wv": np.ascontiguousarray(np.asarray(inputs["L0_attn_Wv"], np.float32)),
        "wo": np.ascontiguousarray(np.asarray(inputs["L0_attn_Wo"], np.float32)),
        "bq": np.ascontiguousarray(np.asarray(inputs["L0_attn_bq"], np.float32).reshape(1, H)),
        "bk": np.ascontiguousarray(np.asarray(inputs["L0_attn_bk"], np.float32).reshape(1, H)),
        "bv": np.ascontiguousarray(np.asarray(inputs["L0_attn_bv"], np.float32).reshape(1, H)),
        "bo": np.ascontiguousarray(np.asarray(inputs["L0_attn_bo"], np.float32).reshape(1, H)),
        "wg": np.ascontiguousarray(np.asarray(inputs["L0_ffn_w_gate"], np.float32)),
        "wu": np.ascontiguousarray(np.asarray(inputs["L0_ffn_w_up"], np.float32)),
        "wd": np.ascontiguousarray(np.asarray(inputs["L0_ffn_w_down"], np.float32)),
    }
    in_maps = []
    for b in range(8):
        m = dict(base)
        m["x"] = np.ascontiguousarray(y[b])
        m["ccol"] = np.ascontiguousarray(c[b].reshape(8, 128).T)
        in_maps.append(m)
    return in_maps


def kernel(**inputs):
    nc = _get_nc()
    in_maps = make_in_maps(inputs)
    res = run_bass_kernel_spmd(nc, in_maps, core_ids=list(range(8)))
    return np.stack([np.asarray(r["out"], dtype=np.float32) for r in res.results])


# version 18
# speedup vs baseline: 1.0682x; 1.0062x over previous
import numpy as np
from contextlib import ExitStack

import concourse.bass as bass
import concourse.mybir as mybir
from concourse.bass_utils import run_bass_kernel_spmd

F32 = mybir.dt.float32
BF16 = mybir.dt.bfloat16
AF = mybir.ActivationFunctionType
ALU = mybir.AluOpType
AX = mybir.AxisListType

MAGIC = 12582912.0
NEG_BIG = -1.0e30
T, NT, H, KC, I3, MI = 4096, 32, 1024, 8, 3072, 24
EPS = 1e-5
QMAXI = 1.0 / 32767.0
ENGS = ("sync", "vector", "scalar", "tensor", "gpsimd")

PARITY = {
    "d_x", "d_h1w", "d_vw", "d_cw", "d_xr", "d_x2w", "d_tw", "d_d", "d_dw",
    "d_tt", "d_wdt", "d_f1w", "d_f2r", "d_f2w", "d_f3r", "d_f3u", "d_f3w",
    "d_u4", "d_x2r", "d_out", "d_da", "d_daw", "d_ds", "d_d2",
}
NOMERGE = {"s_ev2"}


class Prog:
    def __init__(self):
        self.cnt = {}
        self.tpos = 0
        self.steps = {E: [] for E in ENGS}
        self.stream_eng = {}

    def _reg(self, n):
        if n not in self.cnt:
            self.cnt[n] = 0

    def w(self, n, k=None):
        self._reg(n)
        c = self.cnt[n] if k is None else k
        return [(n, c)]

    def i(self, n):
        self._reg(n)
        self.cnt[n] += 1
        return (n, 1)

    def dcount(self, n):
        if n in PARITY:
            self._reg(n + "_A")
            self._reg(n + "_B")
            return (self.cnt[n + "_A"] + self.cnt[n + "_B"]) // 16
        self._reg(n)
        return self.cnt[n] // 16

    def dinc(self, n):
        tot = self.dcount(n)
        if n in PARITY:
            pn = n + ("_A" if tot % 2 == 0 else "_B")
            self.cnt[pn] += 16
            return (pn, 16)
        self.cnt[n] += 16
        return (n, 16)

    def wd(self, n, k=None):
        tot = self.dcount(n)
        if k is None:
            k = tot
        k = max(0, k)
        if k == 0:
            return []
        if n in PARITY:
            ka = (k + 1) // 2
            kb = k // 2
            out = [(n + "_A", 16 * ka)]
            if kb > 0:
                out.append((n + "_B", 16 * kb))
            return out
        return [(n, 16 * k)]

    def st(self, eng, emit, waits=(), incs=()):
        waits = tuple(waits)
        if eng == "tensor":
            ncomp = sum(v for (n, v) in incs if not n.startswith("d_"))
            if ncomp:
                if self.tpos > 0:
                    waits = waits + (("__tpos__", self.tpos),)
                self.tpos += ncomp
        self.steps[eng].append((waits, emit, tuple(incs)))


def att_chunks(i):
    out, j0 = [], 0
    while j0 <= i:
        nj = min(8, i + 1 - j0)
        out.append((j0, nj))
        j0 += nj
    return out


def build_nc(taps=()):
    nc = bass.Bass("TRN2", target_bir_lowering=False)

    x_in = nc.dram_tensor("x", [T, H], F32, kind="ExternalInput")
    ccol_in = nc.dram_tensor("ccol", [128, 8], F32, kind="ExternalInput")
    wmod_in = nc.dram_tensor("wmod", [H, 6 * H], F32, kind="ExternalInput")
    wq_in = nc.dram_tensor("wq", [H, H], F32, kind="ExternalInput")
    wk_in = nc.dram_tensor("wk", [H, H], F32, kind="ExternalInput")
    wv_in = nc.dram_tensor("wv", [H, H], F32, kind="ExternalInput")
    wo_in = nc.dram_tensor("wo", [H, H], F32, kind="ExternalInput")
    bq_in = nc.dram_tensor("bq", [1, H], F32, kind="ExternalInput")
    bk_in = nc.dram_tensor("bk", [1, H], F32, kind="ExternalInput")
    bv_in = nc.dram_tensor("bv", [1, H], F32, kind="ExternalInput")
    bo_in = nc.dram_tensor("bo", [1, H], F32, kind="ExternalInput")
    wg_in = nc.dram_tensor("wg", [I3, H], F32, kind="ExternalInput")
    wu_in = nc.dram_tensor("wu", [I3, H], F32, kind="ExternalInput")
    wd_in = nc.dram_tensor("wd", [H, I3], F32, kind="ExternalInput")
    out_hbm = nc.dram_tensor("out", [T, H], F32, kind="ExternalOutput")

    h1_hbm = nc.dram_tensor("h1s", [T, H], BF16)
    v_hbm = nc.dram_tensor("vs", [T, H], BF16)
    ctx_hbm = nc.dram_tensor("ctxs", [T, H], BF16)
    x2_hbm = nc.dram_tensor("x2s", [T, H], F32)
    t_hbm = nc.dram_tensor("ts_", [T, H], BF16)
    wo2_hbm = nc.dram_tensor("wo2s", [128, 8 * H], BF16)
    modraw = nc.dram_tensor("modraw", [1, 6 * H], F32)
    modh = nc.dram_tensor("modh", [12, H], F32)
    wgq_hbm = nc.dram_tensor("wgq", [I3, H], BF16)
    wuq_hbm = nc.dram_tensor("wuq", [I3, H], BF16)
    wdq_hbm = nc.dram_tensor("wdq", [H, I3], BF16)
    graw_hbm = nc.dram_tensor("graw", [I3, T], BF16)
    uraw_hbm = nc.dram_tensor("uraw", [I3, T], BF16)
    prod_hbm = nc.dram_tensor("prod", [I3, T], BF16)
    usc_hbm = nc.dram_tensor("usc", [I3, T], BF16)
    cci = [nc.dram_tensor(f"cci{j}", [8, 1], F32) for j in range(4)]
    cco = [nc.dram_tensor(f"cco{j}", [8, 1], F32) for j in range(4)]

    tap_srcs = {
        "h1": h1_hbm, "v": v_hbm, "ctx": ctx_hbm, "x2": x2_hbm,
        "modraw": modraw, "modh": modh, "wgq": wgq_hbm, "wuq": wuq_hbm,
        "wdq": wdq_hbm, "graw": graw_hbm, "uraw": uraw_hbm, "prod": prod_hbm,
        "usc": usc_hbm, "wo2": wo2_hbm,
    }
    tap_outs = {
        name: nc.dram_tensor("tap_" + name, list(tap_srcs[name].shape),
                             tap_srcs[name].dtype, kind="ExternalOutput")
        for name in taps
    }

    st_ = ExitStack()
    e_ = st_.enter_context

    R1 = e_(nc.sbuf_tensor("R1", [128, 32768], BF16))
    R2 = e_(nc.sbuf_tensor("R2", [128, 32768], BF16))
    R3 = e_(nc.sbuf_tensor("R3", [128, 32768], BF16))
    SG = e_(nc.sbuf_tensor("SGb", [128, 3072], BF16))
    FR = e_(nc.sbuf_tensor("FRf", [128, 2048], F32))
    idn = e_(nc.sbuf_tensor("idn", [128, 128], BF16))
    maskd = e_(nc.sbuf_tensor("maskd", [128, 128], BF16))
    onesrow = e_(nc.sbuf_tensor("onesrow", [1, 128], BF16))
    ccol = e_(nc.sbuf_tensor("ccolS", [128, 8], F32))
    cbf = e_(nc.sbuf_tensor("cbfS", [128, 8], BF16))
    ca1c = e_(nc.sbuf_tensor("ca1c", [128, 8], F32))
    sacbf = e_(nc.sbuf_tensor("sacbf", [128, 8], BF16))
    cb1c = e_(nc.sbuf_tensor("cb1c", [128, 8], F32))
    sbc = e_(nc.sbuf_tensor("sbc", [128, 8], F32))
    bqc = e_(nc.sbuf_tensor("bqc", [128, 8], F32))
    bkc = e_(nc.sbuf_tensor("bkc", [128, 8], F32))
    mxs = e_(nc.sbuf_tensor("mxs", [128, 8], F32))
    stats = e_(nc.sbuf_tensor("stats", [128, 16], F32))
    scl = e_(nc.sbuf_tensor("scl", [128, 24], F32))
    mrg = e_(nc.sbuf_tensor("mrg", [128, 8], F32))
    bnb = e_(nc.sbuf_tensor("bnb", [128, 24], F32))
    deq = e_(nc.sbuf_tensor("deq", [128, 128], F32))
    cst = e_(nc.sbuf_tensor("cst", [128, 6], F32))
    nc.const_aps.register(F32, EPS, cst[:, 0:1])
    nc.const_aps.register(F32, MAGIC, cst[:, 1:2])
    nc.const_aps.register(F32, 65536.0, cst[:, 2:3])
    nc.const_aps.register(F32, -MAGIC, cst[:, 3:4])
    nc.const_aps.register(F32, -1.0, cst[:, 4:5])

    PS0 = e_(nc.psum_tensor("PS0", [128, 1024], F32))
    PS1 = e_(nc.psum_tensor("PS1", [128, 1024], F32))
    PS2 = e_(nc.psum_tensor("PS2", [128, 1024], F32))
    PS3 = e_(nc.psum_tensor("PS3", [128, 2048], BF16))

    P = Prog()

    def r1(kc, a, b):
        return R1[:, kc * 4096 + a: kc * 4096 + b]

    def r2(kc, a, b):
        return R2[:, kc * 4096 + a: kc * 4096 + b]

    def r3(off, n):
        return R3[:, off: off + n]

    def fr(row, a, b):
        return FR[row:row + 1, a:b]

    def dma(eng, stream, out, in_, waits=(), slow=False):
        def em(e, out=out, in_=in_, slow=slow):
            if slow:
                return e.dma_start(out=out, in_=in_, allow_slow_non_contiguous=True)
            return e.dma_start(out=out, in_=in_)
        P.stream_eng[stream] = eng
        P.st(eng, em, waits, [P.dinc(stream)])

    def dmaT(eng, stream, out, in_, waits=()):
        P.stream_eng[stream] = eng
        P.st(eng, lambda e, o=out, i=in_: e.dma_start_transpose(o, i), waits,
             [P.dinc(stream)])

    nbar = [0]

    def barrier():
        nbar[0] += 1
        for E in ENGS:
            flush = []
            for sname, seng in P.stream_eng.items():
                if seng == E:
                    flush.extend(P.wd(sname))
            P.st(E, ("BARRIER", nbar[0], tuple(flush)))

    def V(emit, waits=(), incs=()):
        P.st("vector", emit, waits, incs)

    def SC(emit, waits=(), incs=()):
        P.st("scalar", emit, waits, incs)

    def TE(emit, waits=(), incs=()):
        P.st("tensor", emit, waits, incs)

    def GP(emit, waits=(), incs=()):
        P.st("gpsimd", emit, waits, incs)

    GP(lambda e: e.memset(idn[:, :], 0.0), [], [P.i("s_cst")])
    GP(lambda e: e.affine_select(out=idn[:, :], in_=idn[:, :],
                                 compare_op=ALU.not_equal, fill=1.0, base=0,
                                 pattern=[[-1, 128]], channel_multiplier=1),
       P.w("s_cst", 1), [P.i("s_cst")])
    GP(lambda e: e.memset(maskd[:, :], 0.0), [], [P.i("s_cst")])
    GP(lambda e: e.affine_select(out=maskd[:, :], in_=maskd[:, :],
                                 compare_op=ALU.is_ge, fill=NEG_BIG, base=0,
                                 pattern=[[-1, 128]], channel_multiplier=1),
       P.w("s_cst", 3), [P.i("s_cst")])
    GP(lambda e: e.memset(onesrow[:, :], 1.0), [], [P.i("s_cst")])
    GP(lambda e: e.memset(scl[:, 16:24], 0.0), [], [P.i("s_cst")])
    GP(lambda e: e.memset(mrg[:, 0:8], 0.0), [], [P.i("s_cst")])
    GP(lambda e: e.memset(cst[:, 0:1], EPS), [], [P.i("s_cst")])
    GP(lambda e: e.memset(cst[:, 1:2], MAGIC), [], [P.i("s_cst")])
    GP(lambda e: e.memset(cst[:, 2:3], 65536.0), [], [P.i("s_cst")])
    GP(lambda e: e.memset(cst[:, 3:4], -MAGIC), [], [P.i("s_cst")])
    GP(lambda e: e.memset(cst[:, 4:5], -1.0), [], [P.i("s_cst")])
    dma("sync", "d_p0", ccol[:, :], ccol_in[:, :])
    SC(lambda e: e.activation(cbf[:, :], ccol[:, :], AF.Silu),
       P.wd("d_p0", 1), [P.i("s_silu")])
    for kc in range(KC):
        dma("gpsimd", "d_wm", r3(kc * 2048, 2048),
            wmod_in[kc * 128:(kc + 1) * 128, 0:2048])

    for r in range(3):
        def mm0(e, r=r):
            last = None
            for w in range(4):
                ps = PS0 if w < 2 else PS1
                wo_ = (w % 2) * 512
                for kc in range(KC):
                    last = e.matmul(
                        ps[0:1, wo_:wo_ + 512], cbf[:, kc:kc + 1],
                        r3((r % 2) * 16384 + kc * 2048 + (w // 2) * 1024 + wo_, 512),
                        start=(kc == 0), stop=(kc == KC - 1))
            return last
        TE(mm0, P.wd("d_wm", 8 * (r + 1)) + P.w("s_silu", 1) + P.w("s_ev0", r),
           [P.i("s_mm0")])

    for t in range(NT):
        p = t % 2
        dma("sync", "d_x", FR[:, p * 1024:(p + 1) * 1024],
            x_in[t * 128:(t + 1) * 128, :],
            waits=(P.w("s_ln", t - 1) if t >= 2 else []))

        def bnst(e, p=p):
            e.bn_stats(bnb[:, p * 12:p * 12 + 6], FR[:, p * 1024:p * 1024 + 512])
            e.bn_stats(bnb[:, p * 12 + 6:p * 12 + 12],
                       FR[:, p * 1024 + 512:(p + 1) * 1024])
            return e.bn_aggr(stats[:, p * 2:p * 2 + 2],
                             bnb[:, p * 12:p * 12 + 12])
        V(bnst, P.wd("d_x", t + 1), [P.i("s_st")])
        SC(lambda e, p=p: e.activation(stats[:, 4 + p:5 + p],
                                       stats[:, p * 2 + 1:p * 2 + 2],
                                       AF.Sqrt, bias=EPS, scale=1.0),
           P.w("s_st", t + 1) + P.w("s_cst", 10), [P.i("s_sq")])

        def lnt(e, p=p):
            e.reciprocal(stats[:, 6 + p:7 + p], stats[:, 4 + p:5 + p])
            return e.tensor_scalar(
                SG[:, p * 1024:(p + 1) * 1024], FR[:, p * 1024:(p + 1) * 1024],
                stats[:, p * 2:p * 2 + 1], stats[:, 6 + p:7 + p],
                ALU.subtract, ALU.mult)
        V(lnt, P.w("s_sq", t + 1) + (P.wd("d_h1w", t - 1) if t >= 2 else []),
          [P.i("s_ln")])
        dma("gpsimd", "d_h1w", h1_hbm[t * 128:(t + 1) * 128, :],
            SG[:, p * 1024:(p + 1) * 1024], waits=P.w("s_ln", t + 1))

    for r in (1, 2):
        for kc in range(KC):
            dma("gpsimd", "d_wm", r3((r % 2) * 16384 + kc * 2048, 2048),
                wmod_in[kc * 128:(kc + 1) * 128, r * 2048:(r + 1) * 2048],
                waits=P.w("s_mm0", r))

    for r in range(3):
        V(lambda e: (e.tensor_copy(fr(0, 0, 1024), PS0[0:1, :]),
                     e.tensor_copy(fr(0, 1024, 2048), PS1[0:1, :]))[-1],
          P.w("s_mm0", r + 1) + P.w("s_ln", NT) + P.wd("d_mr", r),
          [P.i("s_ev0")])
        dma("scalar", "d_mr", modraw[0:1, r * 2048:(r + 1) * 2048],
            FR[0:1, 0:2048], waits=P.w("s_ev0", r + 1) + P.wd("d_mr", r))

    AS, BS = (0, 1024), (1024, 2048)

    def _row_sign_max(e, a, b):
        g = fr(0, *a)
        sgn = fr(0, *b)
        e.tensor_single_scalar(sgn, g, 0.0, ALU.is_ge)
        e.tensor_scalar(sgn, sgn, 2.0, -1.0, ALU.mult, ALU.add)
        e.scalar_tensor_tensor(g, g, -1.0, g, ALU.mult, ALU.max)
        e.tensor_scalar_max(g, g, 0.25)
        return e.tensor_tensor(g, g, sgn, op=ALU.mult)

    dma("sync", "d_rr", fr(0, *AS), modraw[0:1, 2 * H:3 * H], waits=P.wd("d_mr", 3))
    V(lambda e: _row_sign_max(e, AS, BS), P.wd("d_rr", 1), [P.i("s_row")])
    dma("scalar", "d_mw", modh[4:5, :], fr(0, *AS), waits=P.w("s_row", 1))
    dma("sync", "d_rr", fr(0, *AS), modraw[0:1, 5 * H:6 * H],
        waits=P.w("s_row", 1) + P.wd("d_mw", 1))
    V(lambda e: _row_sign_max(e, AS, BS), P.wd("d_rr", 2), [P.i("s_row")])
    dma("scalar", "d_mw", modh[5:6, :], fr(0, *AS), waits=P.w("s_row", 2))
    dma("sync", "d_rr", fr(0, *AS), modraw[0:1, H:2 * H],
        waits=P.w("s_row", 2) + P.wd("d_mw", 2))
    V(lambda e: e.tensor_scalar_add(fr(0, *AS), fr(0, *AS), 1.0),
      P.wd("d_rr", 3), [P.i("s_row")])
    dma("scalar", "d_mw", modh[0:1, :], fr(0, *AS), waits=P.w("s_row", 3))
    dma("sync", "d_rr", fr(0, *AS), modraw[0:1, 4 * H:5 * H],
        waits=P.w("s_row", 3) + P.wd("d_mw", 3))
    V(lambda e: e.tensor_scalar_add(fr(0, *AS), fr(0, *AS), 1.0),
      P.wd("d_rr", 4), [P.i("s_row")])
    dma("scalar", "d_mw", modh[2:3, :], fr(0, *AS), waits=P.w("s_row", 4))
    dma("gpsimd", "d_md", modh[1:2, :], modraw[0:1, 0:H], waits=P.wd("d_mr", 3))
    dma("gpsimd", "d_md", modh[3:4, :], modraw[0:1, 3 * H:4 * H],
        waits=P.wd("d_mr", 3))
    dma("sync", "d_rr", fr(0, *AS), bo_in[0:1, :],
        waits=P.w("s_row", 4) + P.wd("d_mw", 4))
    dma("sync", "d_rr", fr(0, *BS), modh[4:5, :],
        waits=P.w("s_row", 4) + P.wd("d_mw", 4))
    V(lambda e: e.tensor_tensor(fr(0, *AS), fr(0, *AS), fr(0, *BS), op=ALU.mult),
      P.wd("d_rr", 6), [P.i("s_row")])
    dma("scalar", "d_mw", modh[9:10, :], fr(0, *AS), waits=P.w("s_row", 5))
    for dst, row in ((ca1c, 0), (cb1c, 2), (sbc, 3)):
        dma("sync", "d_cols", dst[:, :],
            bass.AP(modh, row * H, [[1, 128], [128, 8]]),
            waits=P.wd("d_mw", 5) + P.wd("d_md", 2), slow=True)
    dma("gpsimd", "d_sac", sacbf[:, :],
        bass.AP(modh, 1 * H, [[1, 128], [128, 8]]),
        waits=P.wd("d_mw", 5) + P.wd("d_md", 2), slow=True)

    barrier()

    WSL = {"q": 8192, "k": 16384, "v": 24576}

    DUNITS = []
    for wsrc, wdst, npt, wcols in ((wg_in, wgq_hbm, 24, 1024),
                                   (wu_in, wuq_hbm, 24, 1024),
                                   (wd_in, wdq_hbm, 8, 3072)):
        for pt in range(npt):
            for hh in range(wcols // 512):
                DUNITS.append((wsrc, wdst, pt, hh))
    ND = len(DUNITS)
    nd_ld, nd_cp, npt_ = [0], [0], [0]

    NDA = 72

    def dslot(u):
        return (u % 4) * 512 if u < NDA else 1024 + (u % 2) * 512

    def emit_dload(u):
        wsrc, wdst, pt, hh = DUNITS[u]
        fro = dslot(u)
        if u < NDA:
            rw = ((P.wd("d_d", u - 1) if u >= 2 else
                   P.w("s_w2", 5) + P.wd("d_bw", 3)) +
                  (P.wd("d_dw", u - 3) if u >= 4 else []))
            dma("gpsimd", "d_d", FR[:, fro:fro + 512],
                wsrc[pt * 128:(pt + 1) * 128, hh * 512:(hh + 1) * 512], waits=rw)
        else:
            k = u - NDA
            rw = ((P.wd("d_d2", k - 1) if k >= 2 else P.wd("d_dw", NDA)) +
                  (P.wd("d_dw", u - 1) if k >= 2 else []))
            dma("sync", "d_d2", FR[:, fro:fro + 512],
                wsrc[pt * 128:(pt + 1) * 128, hh * 512:(hh + 1) * 512], waits=rw)

    def emit_dcomp(u):
        wsrc, wdst, pt, hh = DUNITS[u]
        fro = dslot(u)
        c0 = (u % 4) * 32

        def dv(e, fro=fro, c0=c0):
            e.tensor_reduce(deq[:, c0:c0 + 16],
                            FR[:, fro:fro + 512].rearrange("q (b j) -> q b j", j=32),
                            axis=AX.X, op=ALU.max, apply_absolute_value=True)
            e.tensor_scalar(deq[:, c0:c0 + 16], deq[:, c0:c0 + 16],
                            1.0 / 7.0, 1e-12, ALU.mult, ALU.max)
            return e.reciprocal(deq[:, c0 + 16:c0 + 32], deq[:, c0:c0 + 16])
        ldw = (P.wd("d_d", u + 1) if u < NDA else P.wd("d_d2", u - NDA + 1))
        V(dv, ldw + (P.w("s_dg", u - 3) if u >= 4 else []),
          [P.i("s_da")])
        if u < NDA:
            GP(lambda e, fro=fro, c0=c0:
               e.tensor_tensor(FR[:, fro:fro + 512].rearrange("q (b j) -> q b j", j=32),
                               FR[:, fro:fro + 512].rearrange("q (b j) -> q b j", j=32),
                               deq[:, c0 + 16:c0 + 32].unsqueeze(2)
                               .broadcast_to((128, 16, 32)), op=ALU.mult),
               P.w("s_da", u + 1) + P.w("s_dg1", u), [P.i("s_dg1")])
            V(lambda e, fro=fro, c0=c0: (
                e.tensor_scalar(FR[:, fro:fro + 512], FR[:, fro:fro + 512],
                                MAGIC, -MAGIC, ALU.add, ALU.add),
                e.tensor_tensor(FR[:, fro:fro + 512].rearrange("q (b j) -> q b j", j=32),
                                FR[:, fro:fro + 512].rearrange("q (b j) -> q b j", j=32),
                                deq[:, c0:c0 + 16].unsqueeze(2)
                                .broadcast_to((128, 16, 32)), op=ALU.mult))[-1],
              P.w("s_dg1", u + 1) + P.w("s_dg", u), [P.i("s_dg")])
        else:
            GP(lambda e, fro=fro, c0=c0: (
                e.tensor_tensor(FR[:, fro:fro + 512].rearrange("q (b j) -> q b j", j=32),
                                FR[:, fro:fro + 512].rearrange("q (b j) -> q b j", j=32),
                                deq[:, c0 + 16:c0 + 32].unsqueeze(2)
                                .broadcast_to((128, 16, 32)), op=ALU.mult),
                e.tensor_tensor(FR[:, fro:fro + 512], FR[:, fro:fro + 512],
                                cst[:, 1:2].broadcast_to((128, 512)), op=ALU.add),
                e.tensor_tensor(FR[:, fro:fro + 512], FR[:, fro:fro + 512],
                                cst[:, 1:2].broadcast_to((128, 512)),
                                op=ALU.subtract),
                e.tensor_tensor(FR[:, fro:fro + 512].rearrange("q (b j) -> q b j", j=32),
                                FR[:, fro:fro + 512].rearrange("q (b j) -> q b j", j=32),
                                deq[:, c0:c0 + 16].unsqueeze(2)
                                .broadcast_to((128, 16, 32)), op=ALU.mult))[-1],
               P.w("s_da", u + 1) + P.w("s_dg", u), [P.i("s_dg")])
        dma("gpsimd", "d_dw",
            wdst[pt * 128:(pt + 1) * 128, hh * 512:(hh + 1) * 512],
            FR[:, fro:fro + 512], waits=P.w("s_dg", u + 1) +
            (P.wd("d_dw", u - 1) if u >= 2 else []))

    def dpoint():
        p = npt_[0]
        npt_[0] += 1
        lt = min(NDA, 2 + (p + 1) // 2)
        ct = min(NDA, 2 + (p - 1) // 2) if p >= 2 else 0
        while nd_ld[0] < lt:
            emit_dload(nd_ld[0])
            nd_ld[0] += 1
        while nd_cp[0] < ct:
            emit_dcomp(nd_cp[0])
            nd_cp[0] += 1

    def dflush():
        while nd_ld[0] < NDA:
            emit_dload(nd_ld[0])
            nd_ld[0] += 1
        while nd_cp[0] < NDA:
            emit_dcomp(nd_cp[0])
            nd_cp[0] += 1

    for wn, wsrc in (("q", wq_in), ("k", wk_in), ("v", wv_in)):
        dma("gpsimd", "d_w", r3(WSL[wn], 8192).rearrange("p (k c) -> p k c", c=1024),
            wsrc.ap().rearrange("(k p) c -> p k c", p=128))

    for wi, (wn, b_in, mrow, divq) in enumerate(
            (("q", bq_in, 6, True), ("k", bk_in, 7, False), ("v", bv_in, 8, False))):
        def sawmm(e, wn=wn):
            last = None
            for kc in range(KC):
                e.matmul(PS2[0:1, 0:512], sacbf[:, kc:kc + 1],
                         r3(WSL[wn] + kc * 1024, 512),
                         start=(kc == 0), stop=(kc == KC - 1))
                last = e.matmul(PS2[0:1, 512:1024], sacbf[:, kc:kc + 1],
                                r3(WSL[wn] + kc * 1024 + 512, 512),
                                start=(kc == 0), stop=(kc == KC - 1))
            return last
        TE(sawmm, P.wd("d_w", 3) + P.wd("d_cols", 3) + P.wd("d_sac", 1) + P.w("s_w2", 2 * wi),
           [P.i("s_saw")])
        dma("sync", "d_br", fr(0, *AS), b_in[0:1, :],
            waits=P.w("s_row", 5) + P.w("s_w2", 2 * wi))

        def sawev(e, divq=divq):
            last = e.scalar_tensor_tensor(fr(0, *BS), PS2[0:1, :], 1.0, fr(0, *AS),
                                          ALU.mult, ALU.add)
            if divq:
                last = e.tensor_scalar_mul(fr(0, *BS), fr(0, *BS), 1.0 / 32.0)
            return last
        V(sawev, P.w("s_saw", wi + 1) + P.wd("d_br", wi + 1) + P.wd("d_bw", wi),
          [P.i("s_w2")])
        dma("scalar", "d_bw", modh[mrow:mrow + 1, :], fr(0, *BS),
            waits=P.w("s_w2", 2 * wi + 1))

        def wscale(e, wn=wn, divq=divq):
            last = None
            for kc in range(KC):
                if divq:
                    last = e.tensor_scalar(
                        r3(WSL[wn] + kc * 1024, 1024),
                        r3(WSL[wn] + kc * 1024, 1024),
                        ca1c[:, kc:kc + 1], 1.0 / 32.0, ALU.mult, ALU.mult)
                else:
                    last = e.tensor_single_scalar(
                        r3(WSL[wn] + kc * 1024, 1024),
                        r3(WSL[wn] + kc * 1024, 1024),
                        ca1c[:, kc:kc + 1], ALU.mult)
            return last
        V(wscale, P.w("s_saw", wi + 1), [P.i("s_w2")])

    dma("sync", "d_bc", bqc[:, :],
        bass.AP(modh, 6 * H, [[1, 128], [128, 8]]), waits=P.wd("d_bw", 3), slow=True)
    dma("sync", "d_bc", bkc[:, :],
        bass.AP(modh, 7 * H, [[1, 128], [128, 8]]), waits=P.wd("d_bw", 3), slow=True)
    dma("gpsimd", "d_bvb", SG[:, 2048:3072],
        modh[8:9, :].partition_broadcast(128).squeeze(1), waits=P.wd("d_bw", 3))

    nwin = [0]
    WINFO = []
    nsc, nvv = [0], [0]

    def _reuse_wait(wdx):
        if wdx < 4:
            return []
        k, o = WINFO[wdx - 4]
        return P.w("s_ev2s" if k == "s" else "s_ev2v", o)

    for tq in range(4):
        for kc in range(KC):
            dmaT("sync", "d_h1t", r3(kc * 1024, 1024),
                 h1_hbm[tq * 1024:(tq + 1) * 1024, kc * 128:(kc + 1) * 128],
                 waits=P.wd("d_h1w", 8 * (tq + 1)) +
                 (P.w("s_saw", 3 + tq) if tq >= 1 else []))
        for wn, dstR in (("q", r1), ("k", r2)):
            bcol = bqc if wn == "q" else bkc
            for mc in range(KC):
                for wz in range(2):
                    wdx = nwin[0]
                    nwin[0] += 1
                    ps = (PS0, PS1)[(wdx // 2) % 2]
                    po = (wdx % 2) * 512
                    rw = _reuse_wait(wdx)
                    nsc[0] += 1
                    WINFO.append(("s", nsc[0]))

                    def qkm(e, wn=wn, mc=mc, wz=wz, ps=ps, po=po):
                        last = None
                        for kc in range(KC):
                            last = e.matmul(ps[:, po:po + 512],
                                            r3(WSL[wn] + kc * 1024 + mc * 128, 128),
                                            r3(kc * 1024 + wz * 512, 512),
                                            start=(kc == 0), stop=(kc == KC - 1))
                        return last
                    TE(qkm, P.wd("d_h1t", 8 * (tq + 1)) +
                       P.w("s_w2", 2 if wn == "q" else 4) + rw, [P.i("s_mm2")])
                    SC(lambda e, dstR=dstR, mc=mc, tq=tq, wz=wz, ps=ps, po=po, bcol=bcol:
                       e.activation(dstR(mc, tq * 1024 + wz * 512,
                                         tq * 1024 + wz * 512 + 512),
                                    ps[:, po:po + 512], AF.Identity,
                                    bias=bcol[:, mc:mc + 1], scale=1.0),
                       P.w("s_mm2", wdx + 1) + P.wd("d_bc", 2), [P.i("s_ev2s")])
                    dpoint()
        for tt in range(8):
            gt = tq * 8 + tt
            for wz in range(2):
                wdx = nwin[0]
                nwin[0] += 1
                ps = (PS0, PS1)[(wdx // 2) % 2]
                po = (wdx % 2) * 512
                rw = _reuse_wait(wdx)
                nvv[0] += 1
                WINFO.append(("v", nvv[0]))

                def vmm(e, tt=tt, wz=wz, ps=ps, po=po):
                    last = None
                    for kc in range(KC):
                        last = e.matmul(ps[:, po:po + 512],
                                        r3(kc * 1024 + tt * 128, 128),
                                        r3(WSL["v"] + kc * 1024 + wz * 512, 512),
                                        start=(kc == 0), stop=(kc == KC - 1))
                    return last
                TE(vmm, P.w("s_w2", 6) + rw, [P.i("s_mm2")])
                V(lambda e, gt=gt, wz=wz, ps=ps, po=po:
                  e.scalar_tensor_tensor(
                      SG[:, (gt % 2) * 1024 + wz * 512:(gt % 2) * 1024 + wz * 512 + 512],
                      ps[:, po:po + 512], 1.0,
                      SG[:, 2048 + wz * 512:2048 + (wz + 1) * 512],
                      ALU.mult, ALU.add),
                  P.w("s_mm2", wdx + 1) + P.wd("d_bvb", 1) +
                  (P.wd("d_vw", gt - 1) if gt >= 2 else []),
                  [P.i("s_ev2v")])
                dpoint()
            dma("gpsimd", "d_vw", v_hbm[gt * 128:(gt + 1) * 128, :],
                SG[:, (gt % 2) * 1024:(gt % 2) * 1024 + 1024],
                waits=P.w("s_ev2v", 16 * tq + 2 * tt + 2))
            dpoint()
        TE(lambda e: None, [], [P.i("s_saw")])

    dma("gpsimd", "d_w", r3(WSL["q"], 8192).rearrange("p (k c) -> p k c", c=1024),
        wo_in.ap().rearrange("(k p) c -> p k c", p=128), waits=P.w("s_mm2", 192))
    dma("gpsimd", "d_gab", SG[:, 0:1024],
        modh[4:5, :].partition_broadcast(128).squeeze(1), waits=P.wd("d_vw", NT))

    def woscale(e):
        last = None
        for kc in range(KC):
            last = e.tensor_tensor(r3(WSL["q"] + kc * 1024, 1024),
                                   r3(WSL["q"] + kc * 1024, 1024),
                                   SG[:, 0:1024], op=ALU.mult)
        return last
    V(woscale, P.wd("d_w", 4) + P.wd("d_gab", 1), [P.i("s_w2")])
    dma("gpsimd", "d_wow", wo2_hbm[:, :], r3(WSL["q"], 8192),
        waits=P.w("s_w2", 7))
    dflush()

    barrier()

    for jj in range(8):
        dma("sync", "d_vr", r3(jj * 4096, 4096).rearrange("p (j c) -> p j c", c=1024),
            v_hbm[jj * 512:(jj + 1) * 512, :].rearrange("(j p) c -> p j c", p=128),
            waits=P.wd("d_vw", NT) + P.wd("d_wow", 1))

    CHUNKS = []
    for i in range(NT):
        cs = att_chunks(i)
        for ci, (j0, nj) in enumerate(cs):
            CHUNKS.append((i, j0, nj, ci == 0, ci == len(cs) - 1))
    G = len(CHUNKS)
    NI = [0]


    TE_ORDER = []
    for g in range(G):
        TE_ORDER.append(("S", g))
        if g >= 1:
            TE_ORDER.append(("ptt", g - 1))
        if g >= 2:
            TE_ORDER.append(("ctx", g - 2))
    TE_ORDER += [("ptt", G - 1), ("ctx", G - 2), ("ctx", G - 1)]
    TPOS = {kg: i + 1 for i, kg in enumerate(TE_ORDER)}

    V_ORDER = []
    for g in range(G):
        V_ORDER += [("vmax", g), ("PTevac", g)]
        if g >= 1:
            V_ORDER.append(("vmerge", g - 1))
    V_ORDER.append(("vmerge", G - 1))
    VPOS = {kg: i + 1 for i, kg in enumerate(V_ORDER)}

    def emit_S(g):
        i, j0, nj, first, last = CHUNKS[g]
        ps = (PS0, PS1)[g % 2]

        def Smm(e, i=i, j0=j0, nj=nj, ps=ps):
            last_ = None
            for kc in range(KC):
                c0 = 0
                while c0 < nj * 128:
                    cw = min(512, nj * 128 - c0)
                    last_ = e.matmul(ps[:, c0:c0 + cw],
                                     r1(kc, i * 128, (i + 1) * 128),
                                     r2(kc, j0 * 128 + c0, j0 * 128 + c0 + cw),
                                     start=(kc == 0), stop=(kc == KC - 1))
                    c0 += cw
            return last_
        TE(Smm, P.w("s_exp", g - 1), [P.i("s_T3")])

    def emit_ptt(g):
        i, j0, nj, first, last = CHUNKS[g]
        pg = g % 2

        def ptt(e, nj=nj, pg=pg):
            last_ = None
            for jj in range(nj):
                last_ = e.transpose(
                    PS3[:, pg * 1024 + jj * 128:pg * 1024 + (jj + 1) * 128],
                    SG[:, pg * 1024 + jj * 128:pg * 1024 + (jj + 1) * 128],
                    idn[:, :])
            return last_
        TE(ptt, P.w("s_exp", g + 1) +
           (P.w("s_v3", VPOS[("PTevac", g - 1)]) if g >= 1 else []),
           [P.i("s_T3")])

    def emit_ctx(g):
        i, j0, nj, first, last = CHUNKS[g]
        pg = g % 2

        def ctxmm(e, j0=j0, nj=nj, pg=pg):
            last_ = None
            for jj in range(nj):
                for wz in range(2):
                    last_ = e.matmul(
                        PS2[:, wz * 512:(wz + 1) * 512],
                        SG[:, pg * 1024 + jj * 128:pg * 1024 + (jj + 1) * 128],
                        r3((j0 + jj) * 1024 + wz * 512, 512),
                        start=(jj == 0), stop=(jj == nj - 1))
            return last_
        cw = (P.w("s_v3", VPOS[("vmerge", g - 1)]) if g >= 1
              else P.w("s_v3", VPOS[("PTevac", 0)]))
        TE(ctxmm, cw + P.wd("d_vr", 8), [P.i("s_T3")])

    def emit_vme(g):
        i, j0, nj, first, last = CHUNKS[g]
        ps = (PS0, PS1)[g % 2]
        pg = g % 2
        w = nj * 128

        def vmax(e, i=i, j0=j0, first=first, last=last, ps=ps, pg=pg, w=w):
            if last:
                dof = (i - j0) * 128
                e.tensor_tensor(ps[:, dof:dof + 128], ps[:, dof:dof + 128],
                                maskd[:, :], op=ALU.add)
            if first:
                e.tensor_reduce(stats[:, 14 + pg:15 + pg], ps[:, 0:w], axis=AX.X,
                                op=ALU.max)
            else:
                e.tensor_reduce(stats[:, 4 + pg:5 + pg], ps[:, 0:w], axis=AX.X,
                                op=ALU.max)
                e.tensor_single_scalar(stats[:, 14 + pg:15 + pg],
                                       stats[:, 4 + pg:5 + pg], stats[:, 0:1],
                                       ALU.max)
                e.tensor_single_scalar(stats[:, 8 + pg:9 + pg], stats[:, 0:1],
                                       stats[:, 14 + pg:15 + pg], ALU.subtract)
            return e.tensor_copy(stats[:, 0:1], stats[:, 14 + pg:15 + pg])
        V(vmax, P.w("s_T3", TPOS[("S", g)]), [P.i("s_v3")])

        def sexp(e, first=first, ps=ps, pg=pg, w=w):
            e.activation(stats[:, 10 + pg:11 + pg], stats[:, 14 + pg:15 + pg],
                         AF.Identity, scale=-1.0)
            if not first:
                e.activation(stats[:, 12 + pg:13 + pg], stats[:, 8 + pg:9 + pg], AF.Exp)
            return e.activation(SG[:, pg * 1024:pg * 1024 + w], ps[:, 0:w], AF.Exp,
                                bias=stats[:, 10 + pg:11 + pg], scale=1.0,
                                accum_out=stats[:, 6 + pg:7 + pg])
        sw = P.w("s_v3", VPOS[("vmax", g)])
        if g >= 2:
            sw = sw + P.w("s_T3", TPOS[("ctx", g - 2)])
        SC(sexp, sw, [P.i("s_exp")])

        V(lambda e, nj=nj, pg=pg:
          e.tensor_copy(SG[:, pg * 1024:pg * 1024 + nj * 128],
                        PS3[:, pg * 1024:pg * 1024 + nj * 128]),
          P.w("s_T3", TPOS[("ptt", g)]), [P.i("s_v3")])

    def emit_vmerge(g):
        i, j0, nj, first, last = CHUNKS[g]
        pg = g % 2

        def vmerge(e, i=i, first=first, last=last, pg=pg):
            last_ = None
            if first:
                e.tensor_copy(FR[:, 0:1024], PS2[:, :])
                last_ = e.tensor_copy(stats[:, 1:2], stats[:, 6 + pg:7 + pg])
            else:
                e.scalar_tensor_tensor(FR[:, 0:1024], FR[:, 0:1024],
                                       stats[:, 12 + pg:13 + pg], PS2[:, :],
                                       ALU.mult, ALU.add)
                last_ = e.scalar_tensor_tensor(stats[:, 1:2], stats[:, 1:2],
                                               stats[:, 12 + pg:13 + pg],
                                               stats[:, 6 + pg:7 + pg],
                                               ALU.mult, ALU.add)
            if last:
                e.reciprocal(stats[:, 3:4], stats[:, 1:2])
                last_ = e.tensor_single_scalar(SG[:, 2048:3072], FR[:, 0:1024],
                                               stats[:, 3:4], ALU.mult)
            return last_
        mwaits = P.w("s_T3", TPOS[("ctx", g)])
        if last and i >= 1:
            mwaits = mwaits + P.wd("d_cw", i)
        V(vmerge, mwaits, [P.i("s_v3")] + ([P.i("s_cbf")] if last else []))
        if last:
            NI[0] += 1
            dma("gpsimd", "d_cw", ctx_hbm[i * 128:(i + 1) * 128, :],
                SG[:, 2048:3072], waits=P.w("s_cbf", NI[0]))

    for g in range(G):
        emit_S(g)
        if g >= 1:
            emit_ptt(g - 1)
        if g >= 2:
            emit_ctx(g - 2)
        emit_vme(g)
        if g >= 1:
            emit_vmerge(g - 1)
        t2 = NDA + min(ND - NDA, max(0, g - 7))
        while nd_ld[0] < t2:
            emit_dload(nd_ld[0])
            nd_ld[0] += 1
            emit_dcomp(nd_cp[0])
            nd_cp[0] += 1
    while nd_cp[0] < ND:
        if nd_ld[0] < ND:
            emit_dload(nd_ld[0])
            nd_ld[0] += 1
        emit_dcomp(nd_cp[0])
        nd_cp[0] += 1
    emit_ptt(G - 1)
    emit_ctx(G - 2)
    emit_ctx(G - 1)
    emit_vmerge(G - 1)

    barrier()

    for kc in range(KC):
        dmaT("sync", "d_ct", r2(kc, 0, 4096),
             ctx_hbm[:, kc * 128:(kc + 1) * 128], waits=P.wd("d_cw", NT))
    dma("sync", "d_p4c", r3(WSL["q"], 8192), wo2_hbm[:, :], waits=P.wd("d_wow", 1))
    dma("gpsimd", "d_bor", SG[0:1, 2048:3072], modh[9:10, :],
        waits=P.wd("d_cw", NT))

    def emit_p4t(tt):
        p = tt % 2

        def ptt4(e, tt=tt, p=p):
            last = None
            for kc in range(KC):
                last = e.transpose(
                    PS3[:, p * 1024 + kc * 128:p * 1024 + (kc + 1) * 128],
                    SG[:, p * 1024 + kc * 128:p * 1024 + (kc + 1) * 128],
                    idn[:, :])
            return last
        TE(ptt4, P.w("s_ln", NT + tt + 1) +
           (P.w("s_p4e", tt - 1) if tt >= 2 else []), [P.i("s_p4t")])
        V(lambda e, tt=tt, p=p: e.tensor_copy(
            R1[:, :].rearrange("q (k c) -> q k c", c=4096)[:, :, tt * 128:(tt + 1) * 128],
            PS3[:, p * 1024:(p + 1) * 1024].rearrange("q (k c) -> q k c", c=128)),
          P.w("s_p4t", tt + 1), [P.i("s_p4e")])

    for tt in range(NT):
        ps = (PS0, PS1)[tt % 2]
        p = tt % 2
        dma("sync", "d_xr", FR[:, p * 1024:(p + 1) * 1024],
            x_in[tt * 128:(tt + 1) * 128, :],
            waits=(P.w("s_ln", NT + tt - 1) + P.wd("d_x2w", tt - 1)
                   if tt >= 2 else []))

        def atmm(e, tt=tt, ps=ps):
            last = None
            for wz in range(2):
                for kc in range(KC):
                    e.matmul(ps[:, wz * 512:(wz + 1) * 512],
                             r2(kc, tt * 128, (tt + 1) * 128),
                             r3(WSL["q"] + kc * 1024 + wz * 512, 512),
                             start=(kc == 0), stop=False)
                last = e.matmul(ps[:, wz * 512:(wz + 1) * 512], onesrow[0:1, :],
                                SG[0:1, 2048 + wz * 512:2048 + (wz + 1) * 512],
                                start=False, stop=True)
            return last
        TE(atmm, P.wd("d_ct", 8) + P.wd("d_p4c", 1) + P.wd("d_bor", 1) + P.w("s_x2", tt - 1),
           [P.i("s_at")])

        def x2ev(e, tt=tt, ps=ps, p=p):
            e.scalar_tensor_tensor(FR[:, p * 1024:(p + 1) * 1024],
                                   ps[:, :], 1.0, FR[:, p * 1024:(p + 1) * 1024],
                                   ALU.mult, ALU.add)
            e.bn_stats(bnb[:, p * 12:p * 12 + 6], FR[:, p * 1024:p * 1024 + 512])
            e.bn_stats(bnb[:, p * 12 + 6:p * 12 + 12],
                       FR[:, p * 1024 + 512:(p + 1) * 1024])
            return e.bn_aggr(stats[:, p * 2:p * 2 + 2],
                             bnb[:, p * 12:p * 12 + 12])
        V(x2ev, P.w("s_at", tt + 1) + P.wd("d_xr", tt + 1),
          [P.i("s_x2"), P.i("s_st")])
        SC(lambda e, p=p: e.activation(stats[:, 4 + p:5 + p],
                                       stats[:, p * 2 + 1:p * 2 + 2],
                                       AF.Sqrt, bias=EPS, scale=1.0),
           P.w("s_st", NT + tt + 1), [P.i("s_sq")])

        def tev(e, p=p):
            e.reciprocal(stats[:, 6 + p:7 + p], stats[:, 4 + p:5 + p])
            return e.tensor_scalar(
                SG[:, p * 1024:(p + 1) * 1024], FR[:, p * 1024:(p + 1) * 1024],
                stats[:, p * 2:p * 2 + 1], stats[:, 6 + p:7 + p],
                ALU.subtract, ALU.mult)
        V(tev, P.w("s_sq", NT + tt + 1) + (P.w("s_p4t", tt - 1) if tt >= 2 else []),
          [P.i("s_ln")])
        dma("gpsimd", "d_x2w", x2_hbm[tt * 128:(tt + 1) * 128, :],
            FR[:, p * 1024:(p + 1) * 1024], waits=P.w("s_x2", tt + 1))
        if tt >= 1:
            emit_p4t(tt - 1)
    emit_p4t(NT - 1)

    barrier()

    for kc in range(KC):
        V(lambda e, kc=kc: (
            e.tensor_scalar(r1(kc, 0, 4096), r1(kc, 0, 4096),
                            cb1c[:, kc:kc + 1], sbc[:, kc:kc + 1],
                            ALU.mult, ALU.add),
            e.tensor_reduce(mxs[:, kc:kc + 1], r1(kc, 0, 4096), axis=AX.X,
                            op=ALU.max, apply_absolute_value=True))[-1],
          P.w("s_p4e", NT), [P.i("s_f0a")])
    V(lambda e: e.tensor_reduce(stats[:, 14:15], mxs[:, :], axis=AX.X, op=ALU.max),
      P.w("s_f0a", 8), [P.i("s_f0a")])

    def add_collective(idx, reduce_emit, outcols, prwait, repk):
        GP(reduce_emit, prwait, [P.i("s_pr")])
        dma("gpsimd", "d_ccw", cci[idx][0:8, :], scl[0:1, 16:24],
            waits=P.w("s_pr", 2 * idx + 1))
        GP(lambda e, idx=idx: e.collective_compute(
            "AllReduce", ALU.max, replica_groups=[list(range(8))],
            ins=[cci[idx].ap().opt()], outs=[cco[idx].ap().opt()]),
           P.wd("d_ccw", idx + 1), [P.i("s_cc")])
        for rrow, dcol in outcols:
            dma("sync", "d_rep", scl[:, dcol:dcol + 1],
                cco[idx][rrow:rrow + 1, :].partition_broadcast(128).squeeze(1),
                waits=P.w("s_cc", idx + 1))
        return P.wd("d_rep", repk)

    rw = add_collective(0, lambda e: e.tensor_reduce(
        scl[0:1, 16:17], stats[:, 14:15], axis=AX.XYZWC, op=ALU.max),
        [(0, 0)], P.w("s_f0a", 9), 1)
    V(lambda e: (
        e.tensor_scalar(scl[:, 1:2], scl[:, 0:1], QMAXI, 1e-12, ALU.mult, ALU.max),
        e.reciprocal(scl[:, 2:3], scl[:, 1:2]))[-1],
      rw, [P.i("s_scl")])

    for kc in range(KC):
        V(lambda e, kc=kc: (
            e.tensor_scalar(FR[:, 0:2048], r1(kc, 0, 2048),
                            scl[:, 2:3], MAGIC, ALU.mult, ALU.add),
            e.tensor_scalar(r1(kc, 0, 2048), FR[:, 0:2048],
                            -MAGIC, scl[:, 1:2], ALU.add, ALU.mult),
            e.tensor_scalar(FR[:, 0:2048], r1(kc, 2048, 4096),
                            scl[:, 2:3], MAGIC, ALU.mult, ALU.add),
            e.tensor_scalar(r1(kc, 2048, 4096), FR[:, 0:2048],
                            -MAGIC, scl[:, 1:2], ALU.add, ALU.mult))[-1],
          P.w("s_scl", 1), [P.i("s_scl")])

    for kc in range(KC):
        dmaT("sync", "d_wgt", r2(kc, 0, 3072),
             wgq_hbm[:, kc * 128:(kc + 1) * 128], waits=P.wd("d_dw", 144))
    for kc in range(KC):
        dmaT("sync", "d_wgt", r3(kc * 4096, 3072),
             wuq_hbm[:, kc * 128:(kc + 1) * 128], waits=P.wd("d_dw", 144))

    barrier()

    f1w = [0]
    for mi in range(MI):
        for tc in range(8):
            for srcname, dhbm, mcol in (("g", graw_hbm, 0), ("u", uraw_hbm, 1)):
                wdx = f1w[0]
                f1w[0] += 1
                ps = (PS0, PS1, PS2)[(wdx // 2) % 3]
                po = (wdx % 2) * 512

                def f1mm(e, srcname=srcname, mi=mi, tc=tc, ps=ps, po=po):
                    last = None
                    for kc in range(KC):
                        if srcname == "g":
                            lhs = r2(kc, mi * 128, (mi + 1) * 128)
                        else:
                            lhs = r3(kc * 4096 + mi * 128, 128)
                        last = e.matmul(ps[:, po:po + 512], lhs,
                                        r1(kc, tc * 512, (tc + 1) * 512),
                                        start=(kc == 0), stop=(kc == KC - 1))
                    return last
                TE(f1mm, P.wd("d_wgt", 16) + P.w("s_scl", 9) +
                   P.w("s_f1e", wdx - 5), [P.i("s_f1m")])
                sgo = (wdx % 4) * 512
                SC(lambda e, ps=ps, po=po, sgo=sgo:
                   e.copy(SG[:, sgo:sgo + 512], ps[:, po:po + 512]),
                   P.w("s_f1m", wdx + 1) +
                   (P.wd("d_f1w", wdx - 3) + P.w("s_f1r", wdx - 3) if wdx >= 4 else []),
                   [P.i("s_f1e")])
                dma("scalar", "d_f1w",
                    dhbm[mi * 128:(mi + 1) * 128, tc * 512:(tc + 1) * 512],
                    SG[:, sgo:sgo + 512],
                    waits=(P.wd("d_f1w", wdx - 1) if wdx >= 2 else []))
                V(lambda e, sgo=sgo, mcol=mcol: (
                    e.tensor_reduce(mrg[:, 4:5], SG[:, sgo:sgo + 512], axis=AX.X,
                                    op=ALU.max, apply_absolute_value=True),
                    e.tensor_tensor(mrg[:, mcol:mcol + 1], mrg[:, mcol:mcol + 1],
                                    mrg[:, 4:5], op=ALU.max))[-1],
                  P.w("s_f1e", wdx + 1), [P.i("s_f1r")])

    GP(lambda e: None, P.w("s_f1r", 384), [P.i("s_pr")])
    rw = add_collective(1, lambda e: (
        e.tensor_reduce(scl[0:1, 16:17], mrg[:, 0:1], axis=AX.XYZWC, op=ALU.max),
        e.tensor_reduce(scl[0:1, 17:18], mrg[:, 1:2], axis=AX.XYZWC, op=ALU.max))[-1],
        [(0, 3), (1, 6)], P.w("s_pr", 2), 3)
    V(lambda e: (
        e.tensor_scalar(scl[:, 4:5], scl[:, 3:4], QMAXI, 1e-12, ALU.mult, ALU.max),
        e.reciprocal(scl[:, 5:6], scl[:, 4:5]),
        e.tensor_scalar(scl[:, 7:8], scl[:, 6:7], QMAXI, 1e-12, ALU.mult, ALU.max),
        e.reciprocal(scl[:, 8:9], scl[:, 7:8]),
        e.tensor_single_scalar(scl[:, 18:19], scl[:, 4:5], -MAGIC, ALU.mult))[-1],
      rw, [P.i("s_scl")])

    dma("gpsimd", "d_gbb", SG[:, 2048:3072],
        modh[5:6, :].partition_broadcast(128).squeeze(1))
    wd_mis = [m for m in range(MI) if m % 4 == 3] + \
        [m for m in range(MI) if m % 4 != 3]
    for idx, mi in enumerate(wd_mis):
        dmaT("sync", "d_wdt", R2[:, mi * 1024:(mi + 1) * 1024],
             wdq_hbm[:, mi * 128:(mi + 1) * 128],
             waits=P.wd("d_dw", 144) +
             ([] if mi % 4 == 3 else P.w("s_f1m", 384)) +
             (P.w("s_wd", idx - 1) if idx >= 2 else []))
        V(lambda e, mi=mi:
          e.tensor_tensor(R2[:, mi * 1024:(mi + 1) * 1024],
                          R2[:, mi * 1024:(mi + 1) * 1024], SG[:, 2048:3072],
                          op=ALU.mult),
          P.wd("d_wdt", idx + 1) + P.wd("d_gbb", 1), [P.i("s_wd")])

    barrier()

    NU2 = MI * 4
    for u in range(NU2):
        mi, qq = divmod(u, 4)
        rin = (u % 4) * 1024
        psA = (PS0, PS1)[u % 2]
        frb = (u % 2) * 1024
        dma("sync", "d_f2r", R1[:, rin:rin + 1024],
            graw_hbm[mi * 128:(mi + 1) * 128, qq * 1024:(qq + 1) * 1024],
            waits=(P.wd("d_f2r", u - 1) if u >= 2 else []) +
            (P.wd("d_f2w", u - 3) if u >= 4 else []))
        SC(lambda e, rin=rin, psA=psA, frb=frb: (
            e.activation(psA[:, :], R1[:, rin:rin + 1024], AF.Identity,
                         bias=MAGIC, scale=scl[:, 5:6]),
            e.activation(FR[:, frb:frb + 1024], psA[:, :], AF.Sigmoid,
                         bias=scl[:, 18:19], scale=scl[:, 4:5]))[-1],
           P.wd("d_f2r", u + 1) + P.w("s_scl", 10) +
           (P.w("s_2v", u - 1) if u >= 2 else []),
           [P.i("s_2s")])
        V(lambda e, rin=rin, psA=psA, frb=frb: (
            e.tensor_scalar(FR[:, frb:frb + 1024], FR[:, frb:frb + 1024],
                            65536.0, MAGIC, ALU.mult, ALU.add),
            e.tensor_scalar(FR[:, frb:frb + 1024], FR[:, frb:frb + 1024],
                            -MAGIC, 65535.0, ALU.add, ALU.min),
            e.scalar_tensor_tensor(R1[:, rin:rin + 1024], psA[:, :], -MAGIC,
                                   FR[:, frb:frb + 1024], ALU.add, ALU.mult))[-1],
          P.w("s_2s", u + 1), [P.i("s_2v")])
        for v in ([u - 1] if u >= 1 else []) + ([u] if u == NU2 - 1 else []):
            vm, vq = divmod(v, 4)
            vr = (v % 4) * 1024
            V(lambda e, v=v, vr=vr: e.tensor_reduce(
                deq[:, v:v + 1], R1[:, vr:vr + 1024],
                axis=AX.X, op=ALU.max, apply_absolute_value=True),
              P.w("s_2v", v + 1), [P.i("s_2r")])
            dma("gpsimd", "d_f2w",
                prod_hbm[vm * 128:(vm + 1) * 128, vq * 1024:(vq + 1) * 1024],
                R1[:, vr:vr + 1024],
                waits=P.w("s_2r", v + 1) + (P.wd("d_f2w", v - 1) if v >= 2 else []))

    V(lambda e: e.tensor_reduce(mrg[:, 2:3], deq[:, 0:NU2], axis=AX.X,
                                op=ALU.max),
      P.w("s_2r", NU2), [P.i("s_2r")])
    GP(lambda e: None, P.w("s_2r", NU2 + 1), [P.i("s_pr")])
    rw = add_collective(2, lambda e: e.tensor_reduce(
        scl[0:1, 16:17], mrg[:, 2:3], axis=AX.XYZWC, op=ALU.max),
        [(0, 9)], P.w("s_pr", 4), 4)
    V(lambda e: (
        e.tensor_tensor(scl[:, 15:16], scl[:, 9:10], scl[:, 4:5], op=ALU.mult),
        e.tensor_scalar(scl[:, 10:11], scl[:, 15:16], QMAXI / 65536.0, 1e-12,
                        ALU.mult, ALU.max),
        e.tensor_single_scalar(scl[:, 15:16], scl[:, 10:11], 65536.0, ALU.mult),
        e.reciprocal(scl[:, 11:12], scl[:, 15:16]),
        e.tensor_tensor(scl[:, 11:12], scl[:, 11:12], scl[:, 4:5],
                        op=ALU.mult),
        e.tensor_single_scalar(scl[:, 19:20], scl[:, 10:11], -MAGIC,
                               ALU.mult))[-1],
      rw, [P.i("s_scl")])

    barrier()

    for u in range(NU2):
        mi, qq = divmod(u, 4)
        rp = (u % 4) * 1024
        ru = 4096 + (u % 4) * 1024
        fra = (u % 2) * 1024
        psB = (PS0, PS1)[u % 2]
        dma("sync", "d_f3r", R1[:, rp:rp + 1024],
            prod_hbm[mi * 128:(mi + 1) * 128, qq * 1024:(qq + 1) * 1024],
            waits=(P.wd("d_f3r", u - 1) if u >= 2 else []) +
            (P.wd("d_f3w", u - 3) if u >= 4 else []))
        dma("sync", "d_f3u", R1[:, ru:ru + 1024],
            uraw_hbm[mi * 128:(mi + 1) * 128, qq * 1024:(qq + 1) * 1024],
            waits=(P.wd("d_f3u", u - 1) if u >= 2 else []) +
            (P.w("s_31", u - 3) if u >= 4 else []))
        SC(lambda e, rp=rp, ru=ru, fra=fra, psB=psB: (
            e.activation(psB[:, :], R1[:, rp:rp + 1024], AF.Identity,
                         bias=MAGIC, scale=scl[:, 11:12]),
            e.activation(FR[:, fra:fra + 1024], R1[:, ru:ru + 1024], AF.Identity,
                         bias=MAGIC, scale=scl[:, 8:9]))[-1],
           P.wd("d_f3r", u + 1) + P.wd("d_f3u", u + 1) + P.w("s_scl", 11) +
           (P.w("s_3v", u - 1) if u >= 2 else []),
           [P.i("s_31")])
        V(lambda e, rp=rp, fra=fra, psB=psB: (
            e.tensor_scalar(FR[:, fra:fra + 1024], FR[:, fra:fra + 1024],
                            -MAGIC, scl[:, 7:8], ALU.add, ALU.mult),
            e.scalar_tensor_tensor(R1[:, rp:rp + 1024], psB[:, :], -MAGIC,
                                   FR[:, fra:fra + 1024], ALU.add, ALU.mult))[-1],
          P.w("s_31", u + 1), [P.i("s_3v")])
        for v in ([u - 1] if u >= 1 else []) + ([u] if u == NU2 - 1 else []):
            vm, vq = divmod(v, 4)
            vr = (v % 4) * 1024
            V(lambda e, v=v, vr=vr: e.tensor_reduce(
                deq[:, v:v + 1], R1[:, vr:vr + 1024],
                axis=AX.X, op=ALU.max, apply_absolute_value=True),
              P.w("s_3v", v + 1), [P.i("s_3r")])
            dma("gpsimd", "d_f3w",
                usc_hbm[vm * 128:(vm + 1) * 128, vq * 1024:(vq + 1) * 1024],
                R1[:, vr:vr + 1024],
                waits=P.w("s_3r", v + 1) + (P.wd("d_f3w", v - 1) if v >= 2 else []))

    V(lambda e: e.tensor_reduce(mrg[:, 3:4], deq[:, 0:NU2], axis=AX.X,
                                op=ALU.max),
      P.w("s_3r", NU2), [P.i("s_3r")])
    GP(lambda e: None, P.w("s_3r", NU2 + 1), [P.i("s_pr")])
    rw = add_collective(3, lambda e: e.tensor_reduce(
        scl[0:1, 16:17], mrg[:, 3:4], axis=AX.XYZWC, op=ALU.max),
        [(0, 12)], P.w("s_pr", 6), 5)
    V(lambda e: (
        e.tensor_tensor(scl[:, 21:22], scl[:, 12:13], scl[:, 10:11], op=ALU.mult),
        e.tensor_scalar(scl[:, 13:14], scl[:, 21:22], QMAXI, 1e-12, ALU.mult, ALU.max),
        e.reciprocal(scl[:, 14:15], scl[:, 13:14]),
        e.tensor_tensor(scl[:, 20:21], scl[:, 14:15], scl[:, 10:11],
                        op=ALU.mult))[-1],
      rw, [P.i("s_scl")])

    barrier()

    def emit_f4q(tc):
        sp = (tc % 2) * 12288
        for mi in range(MI):
            un = tc * MI + mi
            p = un % 2
            dma("sync", "d_u4", SG[:, p * 512:(p + 1) * 512],
                usc_hbm[mi * 128:(mi + 1) * 128, tc * 512:(tc + 1) * 512],
                waits=(P.w("s_41", un - 1) if un >= 2 else []))
            SC(lambda e, p=p:
               e.activation(PS2[:, p * 512:(p + 1) * 512],
                            SG[:, p * 512:(p + 1) * 512], AF.Identity,
                            bias=MAGIC, scale=scl[:, 20:21]),
               P.wd("d_u4", un + 1) + P.w("s_scl", 12) +
               (P.w("s_42", un - 1) if un >= 2 else []), [P.i("s_41")])
            V(lambda e, sp=sp, mi=mi, p=p:
              e.tensor_scalar(R3[:, sp + mi * 512:sp + (mi + 1) * 512],
                              PS2[:, p * 512:(p + 1) * 512],
                              -MAGIC, scl[:, 13:14], ALU.add, ALU.mult),
              P.w("s_41", un + 1) +
              (P.w("s_f4m", (tc - 1) * 4) if tc >= 2 and mi == 0 else []),
              [P.i("s_42")])

    def emit_f4mm(tc):
        sp = (tc % 2) * 12288
        for tt in range(4):
            tg = tc * 4 + tt
            ps = (PS0, PS1)[tg % 2]
            p = tg % 2
            dma("sync", "d_x2r", FR[:, p * 1024:(p + 1) * 1024],
                x2_hbm[tg * 128:(tg + 1) * 128, :],
                waits=(P.w("s_f4e", tg - 1) + P.wd("d_out", tg - 1)
                       if tg >= 2 else P.wd("d_x2w", NT)))

            def f4mm(e, sp=sp, tt=tt, ps=ps):
                last = None
                for wz in range(2):
                    for mi in range(MI):
                        last = e.matmul(
                            ps[:, wz * 512:(wz + 1) * 512],
                            R3[:, sp + mi * 512 + tt * 128:sp + mi * 512 + (tt + 1) * 128],
                            R2[:, mi * 1024 + wz * 512:mi * 1024 + (wz + 1) * 512],
                            start=(mi == 0), stop=(mi == MI - 1))
                return last
            TE(f4mm, P.w("s_42", (tc + 1) * MI) + P.w("s_wd", MI) +
               P.w("s_f4e", tg - 1), [P.i("s_f4m")])
            V(lambda e, p=p, ps=ps:
              e.scalar_tensor_tensor(FR[:, p * 1024:(p + 1) * 1024],
                                     ps[:, :], 1.0, FR[:, p * 1024:(p + 1) * 1024],
                                     ALU.mult, ALU.add),
              P.w("s_f4m", tg + 1) + P.wd("d_x2r", tg + 1),
              [P.i("s_f4e")])
            dma("scalar", "d_out", out_hbm[tg * 128:(tg + 1) * 128, :],
                FR[:, p * 1024:(p + 1) * 1024], waits=P.w("s_f4e", tg + 1))

    for tc in range(8):
        emit_f4q(tc)
        if tc >= 1:
            emit_f4mm(tc - 1)
    emit_f4mm(7)

    for name, dsttap in tap_outs.items():
        src = tap_srcs[name]
        dma("gpsimd", "d_tap", dsttap[:, :], src[:, :],
            waits=P.wd("d_out") + P.wd("d_f3w") + P.wd("d_dw") + P.wd("d_f2w") +
            P.wd("d_f1w") + P.wd("d_mr") + P.wd("d_mw") + P.wd("d_cw") +
            P.wd("d_x2w") + P.wd("d_tw") + P.wd("d_vw") + P.wd("d_h1w") +
            P.wd("d_wow"))

    SC(lambda e: None, P.wd("d_out"))
    if taps:
        GP(lambda e: None, P.wd("d_tap"))

    for E in ENGS:
        for waits, emit, incs in (
                (s for s in P.steps[E] if not isinstance(s[1], tuple))):
            for (n, v) in waits:
                if n == "__tpos__":
                    continue
                assert v <= P.cnt.get(n, 0), (
                    f"{E}: wait {n}>={v} unreachable (final {P.cnt.get(n, 0)})")

    inc_engs = {}
    for E in ENGS:
        for step in P.steps[E]:
            if isinstance(step[1], tuple):
                continue
            for (n, v) in step[2]:
                inc_engs.setdefault(n, set()).add(E)
    mergeable = {
        n for n, es in inc_engs.items()
        if len(es) == 1 and next(iter(es)) != "gpsimd"
        and not n.startswith("d_") and n not in NOMERGE
    }
    trans = {}
    phys_final = {E: 0 for E in ENGS}
    for E in ENGS:
        pos = 0
        lcnt = {}
        for step in P.steps[E]:
            if isinstance(step[1], tuple):
                continue
            stepm = [(n, v) for (n, v) in step[2] if n in mergeable]
            pos_end = pos + sum(v for (n, v) in stepm)
            for (n, v) in stepm:
                lcnt[n] = lcnt.get(n, 0) + v
                trans[(n, lcnt[n])] = pos_end
            pos = pos_end
        phys_final[E] = pos
    eng_of = {n: next(iter(es)) for n, es in inc_engs.items() if n in mergeable}

    sems = {}
    for E in ENGS:
        sems["@" + E] = e_(nc.semaphore("se_" + E))
    for n in P.cnt:
        if n not in mergeable:
            sems[n] = e_(nc.semaphore(n))
    sbar = e_(nc.semaphore("sbar"))

    def xlate(n, v):
        if n == "__tpos__":
            return ("@tensor", v)
        if n in mergeable:
            return ("@" + eng_of[n], trans[(n, v)])
        return (n, v)

    _NODRAIN = {"dma_start", "dma_start_transpose", "collective_compute",
                "wait_ge", "sem_inc", "drain", "nop", "engine_nop"}

    class _DrainProxy:

        def __init__(self, e):
            object.__setattr__(self, "_e", e)

        def __getattr__(self, name):
            f = getattr(self._e, name)
            if not callable(f) or name in _NODRAIN or name.startswith("_"):
                return f

            def wrap(*a, **kw):
                r = f(*a, **kw)
                if r is not None and hasattr(r, "then_inc"):
                    self._e.drain()
                return r
            return wrap

    def make_closure(E):
        def closure(raw_e):
            e = raw_e if E == "tensor" else _DrainProxy(raw_e)
            for waits, emit, incs in P.steps[E]:
                if isinstance(emit, tuple) and emit[0] == "BARRIER":
                    for (n, v) in emit[2]:
                        if v > 0:
                            raw_e.wait_ge(sems[n], v)
                    raw_e.sem_inc(sbar, 1)
                    raw_e.wait_ge(sbar, 5 * emit[1])
                    continue
                for (n, v) in waits:
                    if v > 0:
                        pn, pv = xlate(n, v)
                        raw_e.wait_ge(sems[pn], pv)
                inst = emit(e)
                agg = {}
                for (n, v) in incs:
                    pn = "@" + eng_of[n] if n in mergeable else n
                    agg[pn] = agg.get(pn, 0) + v
                if inst is None and agg:
                    inst = raw_e.nop()
                for pn, v in agg.items():
                    inst = inst.then_inc(sems[pn], v)
        return closure

    block = e_(nc.Block())
    block.sync(make_closure("sync"))
    block.vector(make_closure("vector"))
    block.scalar(make_closure("scalar"))
    block.tensor(make_closure("tensor"))
    block.gpsimd(make_closure("gpsimd"))

    st_.close()
    return nc


_NC_CACHE = {}


def _get_nc(taps=()):
    key = tuple(sorted(taps))
    if key not in _NC_CACHE:
        _NC_CACHE[key] = build_nc(taps=key)
    return _NC_CACHE[key]


def make_in_maps(inputs):
    y = np.ascontiguousarray(np.asarray(inputs["L0_attn_y_all"], dtype=np.float32))
    c = np.asarray(inputs["L0_c"], dtype=np.float32)
    base = {
        "wmod": np.ascontiguousarray(np.asarray(inputs["L0_wmod"], np.float32)),
        "wq": np.ascontiguousarray(np.asarray(inputs["L0_attn_Wq"], np.float32)),
        "wk": np.ascontiguousarray(np.asarray(inputs["L0_attn_Wk"], np.float32)),
        "wv": np.ascontiguousarray(np.asarray(inputs["L0_attn_Wv"], np.float32)),
        "wo": np.ascontiguousarray(np.asarray(inputs["L0_attn_Wo"], np.float32)),
        "bq": np.ascontiguousarray(np.asarray(inputs["L0_attn_bq"], np.float32).reshape(1, H)),
        "bk": np.ascontiguousarray(np.asarray(inputs["L0_attn_bk"], np.float32).reshape(1, H)),
        "bv": np.ascontiguousarray(np.asarray(inputs["L0_attn_bv"], np.float32).reshape(1, H)),
        "bo": np.ascontiguousarray(np.asarray(inputs["L0_attn_bo"], np.float32).reshape(1, H)),
        "wg": np.ascontiguousarray(np.asarray(inputs["L0_ffn_w_gate"], np.float32)),
        "wu": np.ascontiguousarray(np.asarray(inputs["L0_ffn_w_up"], np.float32)),
        "wd": np.ascontiguousarray(np.asarray(inputs["L0_ffn_w_down"], np.float32)),
    }
    in_maps = []
    for b in range(8):
        m = dict(base)
        m["x"] = np.ascontiguousarray(y[b])
        m["ccol"] = np.ascontiguousarray(c[b].reshape(8, 128).T)
        in_maps.append(m)
    return in_maps


def kernel(**inputs):
    nc = _get_nc()
    in_maps = make_in_maps(inputs)
    res = run_bass_kernel_spmd(nc, in_maps, core_ids=list(range(8)))
    return np.stack([np.asarray(r["out"], dtype=np.float32) for r in res.results])
